# version 30
# baseline (speedup 1.0000x reference)
import numpy as np
import ml_dtypes
import concourse.bass as bass
import concourse.mybir as mybir
from concourse.bass_utils import run_bass_kernel_spmd

F32 = mybir.dt.float32
BF16 = mybir.dt.bfloat16
AF = mybir.ActivationFunctionType
ALU = mybir.AluOpType
AX = mybir.AxisListType

MAGIC = 12582912.0
S, DM, H, HD, INNER = 2048, 1024, 16, 64, 4096
NT = S // 128
NCORES = 8
LNEPS = 1e-5

ENGINES = ("tensor", "vector", "scalar", "gpsimd", "sync")


class _Step:
    __slots__ = ("idx", "eng", "fn", "deps", "marked", "semval", "dma",
                 "dma_sem", "dma_val", "tag", "sync_out", "flush")

    def __init__(self, idx, eng, fn, deps, dma, tag, sync_out=False,
                 flush=None):
        self.idx, self.eng, self.fn, self.deps = idx, eng, fn, deps
        self.marked = False
        self.semval = None
        self.dma = dma
        self.dma_sem = None
        self.dma_val = None
        self.tag = tag
        self.sync_out = sync_out
        self.flush = flush


class Res:
    __slots__ = ("writers", "readers")

    def __init__(self, fence=()):
        self.writers = list(fence)
        self.readers = []


class Prog:
    def __init__(self):
        self.steps = []
        self.per_engine = {e: [] for e in ENGINES}
        self.fence = []

    def res(self):
        return Res(self.fence)

    def step(self, eng, fn, reads=(), writes=(), dma=False, tag="",
             extra_deps=(), sync_out=False, flush=None):
        deps = list(extra_deps)
        for r in reads:
            deps += r.writers
        for w in writes:
            deps += w.writers
            deps += w.readers
        s = _Step(len(self.steps), eng, fn, deps, dma, tag, sync_out, flush)
        self.steps.append(s)
        self.per_engine[eng].append(s)
        for r in reads:
            r.readers.append(s)
        for w in writes:
            w.writers = [s]
            w.readers = []
        return s

    def add_fence(self, res_list):
        toks = []
        for r in res_list:
            toks += r.writers
            toks += r.readers
        toks += self.fence
        latest = {}
        dmas = {}
        for t in toks:
            if t.dma:
                dmas[id(t)] = t
            else:
                if t.eng not in latest or t.idx > latest[t.eng].idx:
                    latest[t.eng] = t
        self.fence = list(latest.values()) + list(dmas.values())

    def finalize(self):
        for s in self.steps:
            dd = []
            seen = set()
            for d in s.deps:
                if d is s or id(d) in seen:
                    continue
                seen.add(id(d))
                dd.append(d)
                if d.dma:
                    pass
                else:
                    d.marked = True
            s.deps = dd
        for eng in ENGINES:
            c = 0
            for s in self.per_engine[eng]:
                if s.marked:
                    c += 1
                    s.semval = c

    def emit(self, nc, stack, n_dma_sems=48):
        psem = {e: stack.enter_context(nc.semaphore(f"prog_{e}"))
                for e in ENGINES}
        dsems = [stack.enter_context(nc.semaphore(f"dma{i}"))
                 for i in range(n_dma_sems)]
        dcount = [0] * n_dma_sems
        pool = {"sync": list(range(0, 32)), "gpsimd": list(range(32, 48))}
        di = {"sync": 0, "gpsimd": 0}
        for s in self.steps:
            if s.dma:
                pl = pool[s.eng]
                i = pl[di[s.eng]]
                s.dma_sem = i
                dcount[i] += 1
                s.dma_val = 16 * dcount[i]
                di[s.eng] = (di[s.eng] + 1) % len(pl)
        self.finalize()
        block = nc.Block()

        def make_body(eng_name):
            def body(eng):
                hw = {e: 0 for e in ENGINES}
                dhw = [0] * n_dma_sems
                for s in self.per_engine[eng_name]:
                    ew = {}
                    dw = {}
                    for d in s.deps:
                        if d.dma:
                            dw[d.dma_sem] = max(dw.get(d.dma_sem, 0),
                                                d.dma_val)
                        else:
                            ew[d.eng] = max(ew.get(d.eng, 0), d.semval)
                    if s.dma and s.dma_val > 16:
                        dw[s.dma_sem] = max(dw.get(s.dma_sem, 0),
                                            s.dma_val - 16)
                    for e, v in sorted(ew.items()):
                        if v > hw[e]:
                            eng.wait_ge(psem[e], v)
                            hw[e] = v
                    for q, v in sorted(dw.items()):
                        if v > dhw[q]:
                            eng.wait_ge(dsems[q], v)
                            dhw[q] = v
                    inst = s.fn(eng)
                    if s.dma:
                        inst.then_inc(dsems[s.dma_sem], 16)
                    if s.marked:
                        inst.then_inc(psem[eng_name], 1)
            return body

        with block:
            block.tensor(make_body("tensor"))
            block.vector(make_body("vector"))
            block.scalar(make_body("scalar"))
            block.gpsimd(make_body("gpsimd"))
            block.sync(make_body("sync"))


def _host_consts():
    t = np.arange(S, dtype=np.float64)
    inv = 1.0 / 10000.0 ** (np.arange(0, HD, 2, dtype=np.float64) / HD)
    fr = t[:, None] * inv[None, :]
    emb = np.concatenate([fr, fr], axis=1)
    cos = np.cos(emb).astype(np.float32)
    sin = np.sin(emb).astype(np.float32)
    sinn = sin.copy()
    sinn[:, :32] *= -1.0
    cosF = np.tile(cos, (1, H)).astype(np.float32)
    sinnF = np.tile(sinn, (1, H)).astype(np.float32)
    trimask = np.triu(np.ones((128, 128), np.float32)).astype(ml_dtypes.bfloat16)
    ident = np.eye(128, dtype=ml_dtypes.bfloat16)
    identf = np.eye(128, dtype=np.float32)
    onesf = np.ones((1, 128), np.float32)
    ecat = np.zeros((128, 256), np.float32)
    for h in range(16):
        ecat[:, h * 16 + h] = 1.0
    ecat = ecat.astype(ml_dtypes.bfloat16)
    onescol = np.ones((128, 1), np.float32)
    sel = np.zeros((16, 8 * 128), np.float32)
    for hp in range(8):
        sel[2 * hp, hp * 128:hp * 128 + 64] = 1.0
        sel[2 * hp + 1, hp * 128 + 64:hp * 128 + 128] = 1.0
    cvec = np.zeros((128, 2), np.float32)
    cvec[:, 0] = LNEPS
    cvec[:, 1] = MAGIC
    return dict(cosF=cosF, sinnF=sinnF, trimask=trimask, ident=ident,
                identf=identf, onesf=onesf, ecat=ecat, cvec=cvec, sel=sel,
                onescol=onescol)


def build_program(stage="full"):
    from contextlib import ExitStack
    nc = bass.Bass("TRN2", target_bir_lowering=False)
    p = Prog()
    top = ExitStack()

    def din(name, shape, dtype=F32):
        return nc.dram_tensor(name, shape, dtype, kind="ExternalInput").ap()

    x_in = din("x", [S, DM])
    ln1g_in = din("ln1_g", [DM]); ln1b_in = din("ln1_b", [DM])
    wqkv_in = din("wqkv", [DM, 3 * DM]); bqkv_in = din("bqkv", [3 * DM])
    wo_in = din("wo", [DM, DM]); bo_in = din("bo", [DM])
    ln2g_in = din("ln2_g", [DM]); ln2b_in = din("ln2_b", [DM])
    w1_in = din("w1", [DM, INNER]); b1_in = din("b1", [INNER])
    w2_in = din("w2", [INNER, DM]); b2_in = din("b2", [DM])
    wg_in = din("wg", [INNER, DM]); wu_in = din("wu", [INNER, DM])
    wd_in = din("wd", [DM, INNER])
    cos_in = din("cosF", [S, DM]); sinn_in = din("sinnF", [S, DM])
    tri_in = din("trimask", [128, 128], BF16)
    ident_in = din("ident", [128, 128], BF16)
    identf_in = din("identf", [128, 128], F32)
    onesf_in = din("onesf", [1, 128])
    ecat_in = din("ecat", [128, 256], BF16)
    cvec_in = din("cvec", [128, 2])
    sel_in = din("sel", [16, 8 * 128])
    onescol_in = din("onescol", [128, 1])

    out_t = nc.dram_tensor("out", [S, DM], F32, kind="ExternalOutput").ap()
    dbg = {}

    def dbg_out(name, shape, dtype=F32):
        ap = nc.dram_tensor(name, shape, dtype, kind="ExternalOutput").ap()
        dbg[name] = ap
        return ap

    scr_x1 = nc.dram_tensor("scr_x1", [S, DM], F32, kind="Internal").ap()
    scr_x2 = nc.dram_tensor("scr_x2", [S, DM], F32, kind="Internal").ap()
    scr_xq2 = nc.dram_tensor("scr_xq2", [S, INNER], BF16, kind="Internal").ap()
    scr_wgt = nc.dram_tensor("scr_wgt", [DM, INNER], BF16, kind="Internal").ap()
    scr_wut = nc.dram_tensor("scr_wut", [DM, INNER], BF16, kind="Internal").ap()
    scr_wdt = nc.dram_tensor("scr_wdt", [INNER, DM], BF16, kind="Internal").ap()
    scr_row = nc.dram_tensor("scr_row", [16, 512], F32, kind="Internal").ap()
    scr_vec = nc.dram_tensor("scr_vec", [1, 64], F32, kind="Internal").ap()

    def sb(name, shape, dtype=F32, stack=None):
        h = (stack or top).enter_context(nc.sbuf_tensor(name, shape, dtype))
        return h, p.res()

    def ps(name, shape, dtype=F32, stack=None):
        h = (stack or top).enter_context(nc.psum_tensor(name, shape, dtype))
        return h, p.res()

    def dma(dst_ap, src_ap, reads=(), writes=(), eng="sync", tag=""):
        if eng == "sync" and "DRam" in type(dst_ap.tensor).__name__:
            eng = "gpsimd"
        return p.step(eng, lambda e: e.dma_start(out=dst_ap, in_=src_ap),
                      reads=reads, writes=writes, dma=True, tag=tag)

    r_sx1 = [p.res() for _ in range(NT)]
    r_sx2 = [p.res() for _ in range(NT)]
    r_sxq2 = [p.res() for _ in range(NT)]
    r_swg = p.res(); r_swu = p.res(); r_swd = p.res()
    r_srow = p.res(); r_svec = p.res()

    s_ident, r_ident = sb("c_ident", [128, 128], BF16)
    s_identf, r_identf = sb("c_identf", [128, 128], F32)
    s_tri, r_tri = sb("c_tri", [128, 128], BF16)
    s_onesf, r_onesf = sb("c_onesf", [1, 128], F32)
    s_ecat, r_ecat = sb("c_ecat", [128, 256], BF16)
    s_gb1, r_gb1 = sb("c_gb1", [128, 16], F32)
    s_gb2, r_gb2 = sb("c_gb2", [128, 16], F32)
    s_b1c, r_b1c = sb("c_b1c", [128, 32], F32)
    s_b2b, r_b2b = sb("c_b2b", [128, DM], F32)
    s_mu1, r_mu1 = sb("g_mu1", [128, 16], F32)
    s_rs1, r_rs1 = sb("g_rs1", [128, 16], F32)
    s_nrm, r_nrm = sb("g_nrm", [128, 32], F32)
    s_bneg, r_bneg = sb("g_bneg", [128, 16], F32)
    s_wsum, r_wsum = sb("g_wsum", [1, 8], F32)
    s_wrow, r_wrow = sb("g_wrow", [1, 8], F32)
    s_wcol, r_wcol = sb("g_wcol", [128, 8], F32)
    s_cbd, r_cbd = sb("g_cbd", [128, 16], F32)
    s_row1, r_row1 = sb("g_row1", [1, DM], F32)
    s_gbt, r_gbt = sb("g_gbt", [32, 128], F32)
    s_dummy = [sb(f"g_dummy{i}", [128, DM], F32) for i in range(2)]
    s_cvec, r_cvec = sb("c_cvec", [128, 2], F32)
    s_onescol, r_onescol = sb("c_onescol", [128, 1], F32)
    s_wacc, r_wacc = sb("g_wacc", [128, 4], F32)
    s_w31, r_w31 = sb("g_w31", [3, 1], F32)
    s_vfl, _r_vfl = sb("g_vfl", [128, 16], F32)
    s_sfl, _r_sfl = sb("g_sfl", [128, 16], F32)
    s_smal, r_smal = sb("g_smal", [128, 48], F32)
    s_hrow, r_hrow = sb("g_hrow", [1, 32], F32)
    s_xt = [sb(f"g_xt{i}", [128, DM], F32) for i in range(2)]
    s_c1sum, r_c1sum = sb("g_c1sum", [128, 16], F32)
    s_c1sq, r_c1sq = sb("g_c1sq", [128, 16], F32)
    s_c3sum, r_c3sum = sb("g_c3sum", [128, 16], F32)
    s_c3sq, r_c3sq = sb("g_c3sq", [128, 16], F32)
    s_mu3, r_mu3 = sb("g_mu3", [128, 16], F32)
    s_rs3, r_rs3 = sb("g_rs3", [128, 16], F32)
    s_axm, r_axm = sb("g_axm", [128, 16], F32)
    s_dsx, r_dsx = sb("g_dsx", [128, 16], F32)
    s_dcg, r_dcg = sb("g_dcg", [128, 16], F32)
    s_dcu, r_dcu = sb("g_dcu", [128, 16], F32)

    stQKV = ExitStack()
    s_qt, r_qt = sb("a_QT", [128, 8 * S], BF16, stack=stQKV)
    s_kt, r_kt = sb("a_KT", [128, 8 * S], BF16, stack=stQKV)
    s_v, r_v = sb("a_V", [128, NT * DM], BF16, stack=stQKV)

    pro = ExitStack()
    p_mix, r_pmix = ps("p_mix", [128, 512], F32, stack=pro)

    dma(s_ident[:], ident_in, writes=[r_ident])
    dma(s_identf[:], identf_in, writes=[r_identf])
    dma(s_tri[:], tri_in, writes=[r_tri])
    dma(s_onesf[:], onesf_in, writes=[r_onesf])
    dma(s_ecat[:], ecat_in, writes=[r_ecat])
    dma(s_cvec[:], cvec_in, writes=[r_cvec])
    dma(s_onescol[:], onescol_in, writes=[r_onescol])

    def load_cols(dsts, src_ap, nchunks):
        dst, dst_res, c0 = dsts
        dma(s_gbt[0:nchunks, :], src_ap.rearrange("(c q) -> c q", q=128),
            writes=[r_gbt])
        p.step("tensor",
               lambda e: e.matmul(p_mix[:, 0:nchunks], s_gbt[0:nchunks, :],
                                  s_identf[0:nchunks, 0:nchunks],
                                  start=True, stop=True),
               reads=[r_gbt, r_identf], writes=[r_pmix])
        p.step("vector",
               lambda e: e.tensor_copy(dst[:, c0:c0 + nchunks],
                                       p_mix[:, 0:nchunks]),
               reads=[r_pmix], writes=[dst_res],
               flush=lambda e: e.tensor_copy(s_vfl[:, 0:1],
                                             dst[:, c0:c0 + 1]))

    load_cols((s_gb1, r_gb1, 0), ln1g_in, 8)
    load_cols((s_gb1, r_gb1, 8), ln1b_in, 8)
    load_cols((s_gb2, r_gb2, 0), ln2g_in, 8)
    load_cols((s_gb2, r_gb2, 8), ln2b_in, 8)
    load_cols((s_b1c, r_b1c, 0), b1_in, 32)

    def bcast_row(dst, dst_res, src_row_ap, n):
        dma(s_row1[0:1, 0:n], src_row_ap, writes=[r_row1])
        for j in range(0, n, 512):
            w = min(512, n - j)
            def mk(j=j, w=w):
                return lambda e: e.matmul(p_mix[:, 0:w], s_onesf[:],
                                          s_row1[0:1, j:j + w],
                                          start=True, stop=True)
            p.step("tensor", mk(), reads=[r_row1, r_onesf], writes=[r_pmix])
            def cp(j=j, w=w):
                return lambda e: e.tensor_copy(dst[:, j:j + w],
                                               p_mix[:, 0:w])
            p.step("vector", cp(), reads=[r_pmix], writes=[dst_res])

    bcast_row(s_b2b, r_b2b, b2_in.rearrange("(a n) -> a n", a=1), DM)

    stA = ExitStack()
    s_wph, r_wph = sb("a_wph", [128, 8 * 1024], BF16, stack=stA)
    s_pbias, r_pbias = sb("a_pbias", [128, 1024], F32, stack=stA)
    s_xn = [sb(f"a_xn{i}", [128, 1024], BF16, stack=stA) for i in range(2)]
    s_xnt = [sb(f"a_xnt{i}", [128, 1024], BF16, stack=stA) for i in range(2)]
    s_qb = [sb(f"a_qb{i}", [128, 1024], F32, stack=stA) for i in range(2)]
    s_cost = [sb(f"a_cost{i}", [128, 1024], F32, stack=stA) for i in range(2)]
    s_sint = [sb(f"a_sint{i}", [128, 1024], F32, stack=stA) for i in range(2)]
    _tc = sb("a_tcos", [128, 1024], F32, stack=stA)
    _ts_ = sb("a_tsin", [128, 1024], F32, stack=stA)
    s_tcos = [_tc, _tc]
    s_tsin = [_ts_, _ts_]
    s_rope = [sb(f"a_rope{i}", [128, 1024], BF16, stack=stA) for i in range(2)]
    _sq_ = sb("a_sqbf", [128, 1024], BF16, stack=stA)
    s_sqbf = [_sq_, _sq_]
    s_nrmt = [sb(f"a_nrmt{i}", [128, 16], F32, stack=stA) for i in range(2)]
    s_asum, r_asum = sb("a_asum", [128, 16], F32, stack=stA)
    s_asq, r_asq = sb("a_asq", [128, 16], F32, stack=stA)
    s_wst = [sb(f"a_wst{i}", [128, 1024], F32, stack=stA) for i in range(2)]

    p_m = [ps(f"p_m{i}", [128, 1024], F32, stack=stA) for i in range(2)]
    p_tr = [ps(f"p_tr{i}", [128, 1024], BF16, stack=stA) for i in range(2)]


    p.step("vector", lambda e: e.memset(s_nrm[:], 0.0), writes=[r_nrm])


    wq_jobs = []
    for widx, w_ap in enumerate((wg_in, wu_in, wd_in)):
        for ch in range(32):
            wq_jobs.append((widx, w_ap, ch))
    wq_pos = [0]

    def wmean_pump(n):
        for _ in range(n):
            if wq_pos[0] >= len(wq_jobs):
                return
            widx, w_ap, ch = wq_jobs[wq_pos[0]]
            wq_pos[0] += 1
            buf, rbuf = s_wst[wq_pos[0] % 2]
            if widx < 2:
                src = w_ap[ch * 128:(ch + 1) * 128, :]
            else:
                src = w_ap.rearrange("m (c n) -> (m c) n", c=4)[
                    ch * 128:(ch + 1) * 128, :]
            dma(buf[:], src, writes=[rbuf], tag=f"wm{widx}.{ch}")
            p.step("vector",
                   lambda e, b=buf: e.tensor_reduce(
                       s_smal[:, 40:41], b[:], AX.X, ALU.add,
                       apply_absolute_value=True),
                   reads=[rbuf], writes=[r_smal], tag=f"wabs{widx}.{ch}",
                   sync_out=True)
            if ch == 0:
                def acc0(widx=widx):
                    return lambda e: e.tensor_copy(
                        s_wacc[:, widx:widx + 1], s_smal[:, 40:41])
            else:
                def acc0(widx=widx):
                    return lambda e: e.tensor_add(
                        s_wacc[:, widx:widx + 1], s_wacc[:, widx:widx + 1],
                        s_smal[:, 40:41])
            p.step("vector", acc0(), reads=[r_smal, r_wacc], writes=[r_wacc],
                   tag=f"wacc{widx}.{ch}")

    for t in range(NT):
        xbuf, rxbuf = s_xt[t % 2]
        dy, rdy = s_dummy[t % 2]
        dma(xbuf[:], x_in[t * 128:(t + 1) * 128, :], writes=[rxbuf],
            tag=f"a0x{t}")
        p.step("scalar",
               lambda e, xb=xbuf, dy=dy: e.activation(dy[:], xb[:],
                                                      AF.Square),
               reads=[rxbuf], writes=[rdy], tag=f"a0sq{t}")
        p.step("vector",
               lambda e, dy=dy, t=t: e.tensor_reduce(
                   s_asq[:, t:t + 1], dy[:], AX.X, ALU.add),
               reads=[rdy], writes=[r_asq], tag=f"a0qr{t}")
        p.step("vector",
               lambda e, xb=xbuf, t=t: e.tensor_reduce(
                   s_asum[:, t:t + 1], xb[:], AX.X, ALU.add),
               reads=[rxbuf], writes=[r_asum], tag=f"a0sr{t}")
        wmean_pump(2)

    def batch_stats(src_sum, r_srcsum, src_sq, r_srcsq, dst_mu, r_dstmu,
                    dst_rs, r_dstrs, pre):
        p.step("vector",
               lambda e: e.tensor_scalar(dst_mu[:], src_sum[:], 1.0 / DM,
                                         None, ALU.mult),
               reads=[r_srcsum], writes=[r_dstmu], tag=pre + "mu")
        p.step("vector",
               lambda e: e.tensor_scalar(s_smal[:, 0:16], src_sq[:],
                                         1.0 / DM, None, ALU.mult),
               reads=[r_srcsq], writes=[r_smal], tag=pre + "ms")
        p.step("vector",
               lambda e: e.tensor_mul(s_smal[:, 16:32], dst_mu[:],
                                      dst_mu[:]),
               reads=[r_dstmu], writes=[r_smal], tag=pre + "mm")
        p.step("vector",
               lambda e: e.tensor_sub(s_smal[:, 0:16], s_smal[:, 0:16],
                                      s_smal[:, 16:32]),
               reads=[r_smal], writes=[r_smal], tag=pre + "v")
        p.step("scalar",
               lambda e: e.activation(s_smal[:, 16:32], s_smal[:, 0:16],
                                      AF.Ln, bias=s_cvec[:, 0:1]),
               reads=[r_smal, r_cvec], writes=[r_smal], tag=pre + "l")
        p.step("scalar",
               lambda e: e.activation(dst_rs[:], s_smal[:, 16:32], AF.Exp,
                                      scale=-0.5),
               reads=[r_smal], writes=[r_dstrs], tag=pre + "r")

    batch_stats(s_asum, r_asum, s_asq, r_asq, s_mu1, r_mu1, s_rs1, r_rs1,
                "a0")

    for phase in range(3):
        dma(s_wph[:].rearrange("p (c n) -> p c n", c=8),
            wqkv_in[:, phase * 1024:(phase + 1) * 1024].rearrange(
                "(c q) n -> q c n", q=128),
            writes=[r_wph], eng="gpsimd", tag=f"wqkv{phase}")
        bcast_row(s_pbias, r_pbias,
                  bqkv_in.rearrange("(a n) -> a n", a=1)[
                      0:1, phase * 1024:(phase + 1) * 1024], DM)

        def a_stage1(t, phase=phase):
            pr = t % 2
            xbuf, rxbuf = s_xt[pr]
            dma(xbuf[:], x_in[t * 128:(t + 1) * 128, :], writes=[rxbuf],
                tag=f"x{phase}.{t}")
            if phase != 2:
                cbuf, rcbuf = s_cost[pr]
                sbuf_, rsbuf_ = s_sint[pr]
                dma(cbuf[:], cos_in[t * 128:(t + 1) * 128, :],
                    writes=[rcbuf], tag=f"cos{phase}.{t}")
                dma(sbuf_[:], sinn_in[t * 128:(t + 1) * 128, :],
                    writes=[rsbuf_], tag=f"sin{phase}.{t}")
            xn, rxn = s_xn[pr]
            xnt, rxnt = s_xnt[pr]
            ptr, rptr = p_tr[pr]
            p.step("vector",
                   lambda e, xb=xbuf, xn=xn, t=t: e.tensor_scalar(
                       xn[:], xb[:], s_mu1[:, t:t + 1], s_rs1[:, t:t + 1],
                       ALU.subtract, ALU.mult),
                   reads=[rxbuf, r_mu1, r_rs1], writes=[rxn],
                   tag=f"xn{phase}.{t}")
            for c in range(8):
                def tr(c=c, xn=xn, ptr=ptr):
                    return lambda e: e.transpose(
                        ptr[:, c * 128:(c + 1) * 128],
                        xn[:, c * 128:(c + 1) * 128], s_ident[:])
                p.step("tensor", tr(), reads=[rxn, r_ident], writes=[rptr],
                       tag=f"trx{phase}.{t}.{c}")
            for c in range(8):
                def ev(c=c, ptr=ptr, xnt=xnt):
                    return lambda e: e.activation(
                        xnt[:, c * 128:(c + 1) * 128],
                        ptr[:, c * 128:(c + 1) * 128], AF.Identity,
                        scale=s_gb1[:, c:c + 1],
                        bias=s_gb1[:, 8 + c:9 + c])
                p.step("scalar", ev(), reads=[rptr, r_gb1], writes=[rxnt],
                       tag=f"evx{phase}.{t}.{c}")
            pm, rpm = p_m[pr]
            for dinc in range(8):
                for h2 in range(2):
                    def mm(dinc=dinc, h2=h2, pm=pm, xnt=xnt):
                        return lambda e: e.matmul(
                            pm[:, h2 * 512:(h2 + 1) * 512],
                            xnt[:, dinc * 128:(dinc + 1) * 128],
                            s_wph[:, dinc * 1024 + h2 * 512:
                                  dinc * 1024 + (h2 + 1) * 512],
                            start=(dinc == 0), stop=(dinc == 7),
                            skip_group_check=True)
                    p.step("tensor", mm(), reads=[rxnt, r_wph],
                           writes=[rpm], tag=f"qkv{phase}.{t}.{dinc}.{h2}")

        def a_stage2(t, phase=phase):
            pr = t % 2
            pm, rpm = p_m[pr]
            ptr, rptr = p_tr[pr]
            if phase == 2:
                for h2 in range(2):
                    def ev2(h2=h2, t=t, pm=pm):
                        return lambda e: e.tensor_add(
                            s_v[:, t * 1024 + h2 * 512:
                                t * 1024 + (h2 + 1) * 512],
                            pm[:, h2 * 512:(h2 + 1) * 512],
                            s_pbias[:, h2 * 512:(h2 + 1) * 512])
                    p.step("vector", ev2(), reads=[rpm, r_pbias],
                           writes=[r_v], tag=f"vev{t}.{h2}")
                return
            cbuf, rcbuf = s_cost[pr]
            sbuf_, rsbuf_ = s_sint[pr]
            qb, rqb = s_qb[pr]
            p.step("vector",
                   lambda e, pm=pm, qb=qb: e.tensor_add(qb[:], pm[:],
                                                        s_pbias[:]),
                   reads=[rpm, r_pbias], writes=[rqb],
                   tag=f"qbev{phase}.{t}")
            sq, rsq = s_sqbf[pr]
            nt_, rnt_ = s_nrmt[pr]
            p.step("scalar",
                   lambda e, qb=qb, sq=sq: e.activation(sq[:], qb[:],
                                                        AF.Square),
                   reads=[rqb], writes=[rsq], tag=f"qsq{phase}.{t}")
            p.step("vector",
                   lambda e, sq=sq, nt_=nt_: e.tensor_reduce(
                       nt_[:], sq[:].rearrange("q (h c) -> q h c", h=16,
                                               c=64),
                       AX.X, ALU.add),
                   reads=[rsq], writes=[rnt_], tag=f"qnr{phase}.{t}",
                   sync_out=True)
            c0 = phase * 16
            p.step("vector",
                   lambda e, c0=c0, nt_=nt_: e.tensor_max(
                       s_nrm[:, c0:c0 + 16], s_nrm[:, c0:c0 + 16], nt_[:]),
                   reads=[rnt_, r_nrm], writes=[r_nrm],
                   tag=f"qmx{phase}.{t}")
            tc, rtc = s_tcos[pr]
            tsn, rtsn = s_tsin[pr]
            rp, rrp = s_rope[pr]
            p.step("vector",
                   lambda e, cbuf=cbuf, qb=qb, tc=tc: e.tensor_mul(
                       tc[:], qb[:], cbuf[:]),
                   reads=[rqb, rcbuf], writes=[rtc], tag=f"rc{phase}.{t}")
            qv = qb[:].rearrange("q (h t2 c) -> q h t2 c", h=16, t2=2, c=32)
            sv = sbuf_[:].rearrange("q (h t2 c) -> q h t2 c", h=16, t2=2,
                                    c=32)
            tv = tsn[:].rearrange("q (h t2 c) -> q h t2 c", h=16, t2=2,
                                  c=32)
            p.step("vector",
                   lambda e, qv=qv, sv=sv, tv=tv: e.tensor_mul(
                       tv[:, :, 0, :], qv[:, :, 1, :], sv[:, :, 0, :]),
                   reads=[rqb, rsbuf_], writes=[rtsn],
                   tag=f"rs0{phase}.{t}")
            p.step("vector",
                   lambda e, qv=qv, sv=sv, tv=tv: e.tensor_mul(
                       tv[:, :, 1, :], qv[:, :, 0, :], sv[:, :, 1, :]),
                   reads=[rqb, rsbuf_], writes=[rtsn],
                   tag=f"rs1{phase}.{t}")
            p.step("vector",
                   lambda e, tc=tc, tsn=tsn, rp=rp: e.tensor_add(
                       rp[:], tc[:], tsn[:]),
                   reads=[rtc, rtsn], writes=[rrp], tag=f"ra{phase}.{t}")
            for c in range(8):
                def tr2(c=c, rp=rp, ptr=ptr):
                    return lambda e: e.transpose(
                        ptr[:, c * 128:(c + 1) * 128],
                        rp[:, c * 128:(c + 1) * 128], s_ident[:])
                p.step("tensor", tr2(), reads=[rrp, r_ident],
                       writes=[rptr], tag=f"trq{phase}.{t}.{c}")
            dstT, rdstT = (s_qt, r_qt) if phase == 0 else (s_kt, r_kt)
            for half in range(2):
                dv = dstT[:].rearrange("d (c q) -> d c q", c=8, q=S)[
                    :, half * 4:(half + 1) * 4, t * 128:(t + 1) * 128]
                def cp(half=half, dv=dv, ptr=ptr):
                    return lambda e: e.activation(
                        dv, ptr[:, half * 512:(half + 1) * 512],
                        AF.Identity)
                p.step("scalar", cp(), reads=[rptr], writes=[rdstT],
                       tag=f"qtc{phase}.{t}.{half}")

        for t in range(NT):
            a_stage1(t)
            if t >= 1:
                a_stage2(t - 1)
            wmean_pump(2)
        a_stage2(NT - 1)
    wmean_pump(100)

    p.step("tensor",
           lambda e: e.matmul(p_mix[0:32, 0:128], s_nrm[:], s_identf[:],
                              start=True, stop=True),
           reads=[r_nrm, r_identf], writes=[r_pmix], tag="nrmT")
    s_hn, r_hn = sb("a_hn", [32, 1], F32, stack=stA)
    p.step("vector",
           lambda e: e.tensor_reduce(s_hn[:], p_mix[0:32, 0:128], AX.X,
                                     ALU.max),
           reads=[r_pmix], writes=[r_hn], tag="nrmmax", sync_out=True)
    dma(scr_vec[0:1, 0:32].rearrange("a (q b) -> q (a b)", q=32, b=1),
        s_hn[:], reads=[r_hn], writes=[r_svec], tag="bnc0")
    dma(s_hrow[:], scr_vec[0:1, 0:32], reads=[r_svec], writes=[r_hrow],
        tag="bnc1")
    p.step("vector",
           lambda e: e.tensor_add(s_hrow[0:1, 0:16], s_hrow[0:1, 0:16],
                                  s_hrow[0:1, 16:32]),
           reads=[r_hrow], writes=[r_hrow], tag="bsum")
    p.step("vector",
           lambda e: e.tensor_scalar(s_hrow[0:1, 0:16], s_hrow[0:1, 0:16],
                                     -1.0 / 16.0, None, ALU.mult),
           reads=[r_hrow], writes=[r_hrow], tag="bneg",
           flush=lambda e: e.tensor_copy(s_vfl[0:1, 0:8], s_hrow[0:1, 0:8]))
    p.step("tensor",
           lambda e: e.matmul(p_mix[:, 0:16], s_onesf[:], s_hrow[0:1, 0:16],
                              start=True, stop=True),
           reads=[r_hrow, r_onesf], writes=[r_pmix], tag="bnegT")
    p.step("vector",
           lambda e: e.tensor_copy(s_bneg[:], p_mix[:, 0:16]),
           reads=[r_pmix], writes=[r_bneg], tag="bnegC",
           flush=lambda e: e.tensor_copy(s_vfl[:, 0:1], s_bneg[:, 0:1]))

    p.step("tensor",
           lambda e: e.matmul(p_mix[0:3, 0:1], s_wacc[:, 0:3],
                              s_onescol[:], start=True, stop=True),
           reads=[r_wacc, r_onescol], writes=[r_pmix], tag="wsumT")
    p.step("vector",
           lambda e: e.tensor_copy(s_w31[:], p_mix[0:3, 0:1]),
           reads=[r_pmix], writes=[r_w31], tag="wsumC")
    p.step("tensor",
           lambda e: e.matmul(p_mix[0:1, 4:7], s_w31[:],
                              s_identf[0:3, 0:3], start=True, stop=True),
           reads=[r_w31, r_identf], writes=[r_pmix], tag="wrowT")
    p.step("vector",
           lambda e: e.tensor_copy(s_wsum[0:1, 0:3], p_mix[0:1, 4:7]),
           reads=[r_pmix], writes=[r_wsum], tag="wrowC")
    p.step("vector",
           lambda e: e.tensor_scalar(s_wrow[0:1, 0:3], s_wsum[0:1, 0:3],
                                     1.0 / (INNER * DM), 1e-5, ALU.mult,
                                     ALU.max),
           reads=[r_wsum], writes=[r_wrow], tag="wmean", sync_out=True)
    p.step("vector",
           lambda e: e.reciprocal(s_wrow[0:1, 3:6], s_wrow[0:1, 0:3]),
           reads=[r_wrow], writes=[r_wrow], tag="wrec", sync_out=True)
    p.step("vector",
           lambda e: e.reciprocal(s_wrow[0:1, 6:8], s_wrow[0:1, 3:5]),
           reads=[r_wrow], writes=[r_wrow], tag="wrec2", sync_out=True)
    p.step("vector",
           lambda e: e.reciprocal(s_wrow[0:1, 0:1], s_wrow[0:1, 5:6]),
           reads=[r_wrow], writes=[r_wrow], tag="wrec3", sync_out=True,
           flush=lambda e: e.tensor_copy(s_vfl[0:1, 0:8], s_wrow[0:1, 0:8]))
    p.step("tensor",
           lambda e: e.matmul(p_mix[:, 0:8], s_onesf[:], s_wrow[0:1, 0:8],
                              start=True, stop=True),
           reads=[r_wrow, r_onesf], writes=[r_pmix], tag="wcolT")
    p.step("vector",
           lambda e: e.tensor_copy(s_wcol[:], p_mix[:, 0:8]),
           reads=[r_pmix], writes=[r_wcol], tag="wcolC",
           flush=lambda e: e.tensor_copy(s_vfl[:, 0:1], s_wcol[:, 0:1]))

    if stage == "A":
        d_qt = dbg_out("d_qt", [128, 8 * S])
        d_kt = dbg_out("d_kt", [128, 8 * S])
        d_v = dbg_out("d_v", [128, NT * DM])
        d_bneg = dbg_out("d_bneg", [128, 16])
        d_wrow = dbg_out("d_wrow", [1, 8])
        d_mu = dbg_out("d_mu", [128, 16])
        d_rs = dbg_out("d_rs", [128, 16])
        d_wph = dbg_out("d_wph", [128, 8 * 1024])
        d_qb = dbg_out("d_qb", [128, 1024])
        d_xnt = dbg_out("d_xnt", [128, 1024])
        d_rope = dbg_out("d_rope", [128, 1024])
        d_pbias = dbg_out("d_pbias", [128, 1024])
        toks = [
            dma(d_qt, s_qt[:], reads=[r_qt], eng="gpsimd", tag="dbgq"),
            dma(d_kt, s_kt[:], reads=[r_kt], eng="gpsimd", tag="dbgk"),
            dma(d_v, s_v[:], reads=[r_v], eng="gpsimd", tag="dbgv"),
            dma(d_bneg, s_bneg[:], reads=[r_bneg], tag="dbgb"),
            dma(d_wrow, s_wrow[:], reads=[r_wrow], tag="dbgw"),
            dma(d_mu, s_mu1[:], reads=[r_mu1], tag="dbgmu"),
            dma(d_rs, s_rs1[:], reads=[r_rs1], tag="dbgrs"),
            dma(d_wph, s_wph[:], reads=[r_wph], eng="gpsimd", tag="dbgwp"),
            dma(d_qb, s_qb[1][0][:], reads=[s_qb[1][1]], tag="dbgqb"),
            dma(d_xnt, s_xnt[1][0][:], reads=[s_xnt[1][1]], eng="gpsimd", tag="dbgxnt"),
            dma(d_rope, s_rope[1][0][:], reads=[s_rope[1][1]], eng="gpsimd", tag="dbgro"),
            dma(d_pbias, s_pbias[:], reads=[r_pbias], tag="dbgpb"),
        ]
        p.step("sync", lambda e: e.nop(), extra_deps=toks, tag="fin")
        stA.close(); pro.close()
        p.emit(nc, top)
        return nc, dbg, top

    stA.close()
    p.add_fence([r_pmix])
    pro.close()
    p.add_fence([r_wph, r_pbias, p_m[0][1], p_m[1][1],
                 p_tr[0][1], p_tr[1][1], r_asum, r_asq]
                + [b[1] for b in (s_cost + s_sint + s_wst + s_xn + s_xnt
                                  + s_qb + s_tcos + s_tsin + s_rope
                                  + s_sqbf + s_nrmt)])

    stB = ExitStack()
    s_pta, r_pta = sb("b_pta", [128, 16 * 512], BF16, stack=stB)
    s_ptb, r_ptb = sb("b_ptb", [128, 16 * 512], BF16, stack=stB)
    s_attn, r_attn = sb("b_attn", [128, 8 * 512], BF16, stack=stB)
    s_wob, r_wob = sb("b_wob", [128, 8 * 1024], BF16, stack=stB)
    s_bob, r_bob = sb("b_bob", [128, DM], F32, stack=stB)
    s_r16, r_r16 = sb("b_r16", [16, 512], F32, stack=stB)
    s_sel, r_sel = sb("b_sel", [16, 8 * 128], F32, stack=stB)
    s_t1, r_t1 = sb("b_t1", [128, DM], F32, stack=stB)
    s_xp = [sb(f"b_xp{i}", [128, DM], F32, stack=stB) for i in range(2)]

    dma(s_wob[:].rearrange("p (c n) -> p c n", c=8),
        wo_in.rearrange("(c q) n -> q c n", q=128),
        writes=[r_wob], eng="gpsimd", tag="ldwo")
    dma(s_sel[:], sel_in, writes=[r_sel], tag="ldsel")
    bcast2 = ExitStack()
    p_bc2, r_pbc2 = ps("p_bc2", [128, 512], F32, stack=bcast2)
    dma(s_row1[0:1, 0:DM], bo_in.rearrange("(a n) -> a n", a=1),
        writes=[r_row1], tag="ldbo")
    for j in range(2):
        def mk(j=j):
            return lambda e: e.matmul(p_bc2[:, 0:512], s_onesf[:],
                                      s_row1[0:1, j * 512:(j + 1) * 512],
                                      start=True, stop=True)
        p.step("tensor", mk(), reads=[r_row1, r_onesf], writes=[r_pbc2])
        def cp(j=j):
            return lambda e: e.tensor_copy(s_bob[:, j * 512:(j + 1) * 512],
                                           p_bc2[:, 0:512])
        p.step("vector", cp(), reads=[r_pbc2], writes=[r_bob])
    bcast2.close()
    p.add_fence([r_pbc2])

    p_stab = [ps(f"p_st{sd}", [128, 1536], F32, stack=stB) for sd in range(2)]
    p_o, r_po = ps("p_o", [128, 512], F32, stack=stB)
    p_rs, r_prs = ps("p_rs", [128, 512], F32, stack=stB)

    PT = [(s_pta, r_pta), (s_ptb, r_ptb)]

    r_ptk = [[p.res() for _ in range(16)], [p.res() for _ in range(16)]]

    for qc in range(4):
        nkt = 4 * qc + 4
        first_rs = [True]
        kts = list(range(nkt))
        groups = [kts[i:i + 3] for i in range(0, nkt, 3)]

        def emit_st_group(hp, g, qc=qc):
            for side in range(2):
                head = 2 * hp + side
                pst, rpst = p_stab[side]
                for j, kt in enumerate(g):
                    def mmst(j=j, kt=kt, side=side, hp=hp, qc=qc, pst=pst):
                        return lambda e: e.matmul(
                            pst[:, j * 512:(j + 1) * 512],
                            s_kt[side * 64:(side + 1) * 64,
                                 hp * 2048 + kt * 128:
                                 hp * 2048 + (kt + 1) * 128],
                            s_qt[side * 64:(side + 1) * 64,
                                 hp * 2048 + qc * 512:
                                 hp * 2048 + (qc + 1) * 512],
                            start=True, stop=True,
                            tile_position=(side * 64, 0),
                            skip_group_check=True)
                    p.step("tensor", mmst(), reads=[r_kt, r_qt],
                           writes=[rpst], tag=f"st{qc}.{hp}.{side}.{kt}")
                ptb, _ = PT[side]
                def expf(g=g, side=side, head=head, pst=pst, ptb=ptb):
                    ln = len(g)
                    return lambda e: e.activation(
                        ptb[:, g[0] * 512:(g[0] + ln) * 512],
                        pst[:, 0:ln * 512], AF.Exp, scale=0.125,
                        bias=s_bneg[:, head:head + 1])
                p.step("scalar", expf(), reads=[rpst, r_bneg],
                       writes=[r_ptk[side][kt2] for kt2 in g],
                       tag=f"exp{qc}.{hp}.{side}")
            for kt in g:
                if kt < 4 * qc:
                    continue
                dcol = kt * 128 - qc * 512
                for side in range(2):
                    ptb, _ = PT[side]
                    if dcol > 0:
                        def msf(kt=kt, dcol=dcol, ptb=ptb):
                            return lambda e: e.memset(
                                ptb[:, kt * 512:kt * 512 + dcol], 0.0)
                        p.step("vector", msf(), writes=[r_ptk[side][kt]],
                               tag=f"ms{qc}.{hp}.{side}.{kt}")
                    def trif(kt=kt, dcol=dcol, ptb=ptb):
                        return lambda e: e.tensor_mul(
                            ptb[:, kt * 512 + dcol:kt * 512 + dcol + 128],
                            ptb[:, kt * 512 + dcol:kt * 512 + dcol + 128],
                            s_tri[:])
                    p.step("vector", trif(), reads=[r_tri],
                           writes=[r_ptk[side][kt]],
                           tag=f"tri{qc}.{hp}.{side}.{kt}")

        def emit_pv_group(hp, g, qc=qc):
            for kt in g:
                for side in range(2):
                    head = 2 * hp + side
                    ptb, _ = PT[side]
                    def pvf(kt=kt, side=side, head=head, ptb=ptb):
                        return lambda e: e.matmul(
                            p_o[side * 64:(side + 1) * 64, :],
                            s_v[:, kt * 1024 + head * 64:
                                kt * 1024 + (head + 1) * 64],
                            ptb[:, kt * 512:(kt + 1) * 512],
                            start=(kt == 0), stop=False,
                            tile_position=(0, side * 64),
                            skip_group_check=True)
                    p.step("tensor", pvf(), reads=[r_ptk[side][kt], r_v],
                           writes=[r_po], tag=f"pv{qc}.{hp}.{side}.{kt}")
                    def rsf(kt=kt, side=side, head=head, ptb=ptb,
                            st=first_rs[0]):
                        return lambda e: e.matmul(
                            p_rs[0:16, :],
                            s_ecat[:, head * 16:(head + 1) * 16],
                            ptb[:, kt * 512:(kt + 1) * 512],
                            start=st, stop=False,
                            skip_group_check=True)
                    p.step("tensor", rsf(), reads=[r_ptk[side][kt], r_ecat],
                           writes=[r_prs], tag=f"rs{qc}.{hp}.{side}.{kt}")
                    first_rs[0] = False
            if g is groups[-1]:
                def evo(hp=hp):
                    return lambda e: e.tensor_copy(
                        s_attn[:, hp * 512:(hp + 1) * 512], p_o[:])
                p.step("vector", evo(), reads=[r_po], writes=[r_attn],
                       tag=f"aev{qc}.{hp}")

        jobs = [(hp, g) for hp in range(8) for g in groups]
        for i, (hp, g) in enumerate(jobs):
            emit_st_group(hp, g)
            if i >= 1:
                emit_pv_group(*jobs[i - 1])
        emit_pv_group(*jobs[-1])

        p.step("vector",
               lambda e: e.reciprocal(s_r16[:], p_rs[0:16, :]),
               reads=[r_prs], writes=[r_r16], tag=f"recip{qc}")
        for hp in range(8):
            def rb0(hp=hp):
                return lambda e: e.matmul(
                    p_o[:], s_sel[:, hp * 128:(hp + 1) * 128], s_r16[:],
                    start=True, stop=True, skip_group_check=True)
            p.step("tensor", rb0(), reads=[r_r16, r_sel], writes=[r_po],
                   tag=f"rb{qc}.{hp}")
            def divf(hp=hp):
                return lambda e: e.tensor_mul(
                    s_attn[:, hp * 512:(hp + 1) * 512],
                    s_attn[:, hp * 512:(hp + 1) * 512], p_o[:])
            p.step("vector", divf(), reads=[r_po, r_attn], writes=[r_attn],
                   tag=f"div{qc}.{hp}")
        p_wo = p_stab[0][0]
        r_pwo = p_stab[0][1]
        for qsub in range(4):
            t = qc * 4 + qsub
            xbuf, rxbuf = s_xt[t % 2]
            dma(xbuf[:], x_in[t * 128:(t + 1) * 128, :], writes=[rxbuf],
                tag=f"xb{t}")
            for dmc in range(8):
                for h2 in range(2):
                    def mwo(dmc=dmc, h2=h2, qsub=qsub):
                        return lambda e: e.matmul(
                            p_wo[:, h2 * 512:(h2 + 1) * 512],
                            s_attn[:, dmc * 512 + qsub * 128:
                                   dmc * 512 + (qsub + 1) * 128],
                            s_wob[:, dmc * 1024 + h2 * 512:
                                  dmc * 1024 + (h2 + 1) * 512],
                            start=(dmc == 0), stop=(dmc == 7),
                            skip_group_check=True)
                    p.step("tensor", mwo(), reads=[r_attn, r_wob],
                           writes=[r_pwo], tag=f"wo{t}.{dmc}.{h2}")
            p.step("vector",
                   lambda e: e.tensor_add(s_t1[:], p_wo[:, 0:1024],
                                          s_bob[:]),
                   reads=[r_pwo, r_bob], writes=[r_t1], tag=f"woev{t}")
            xpb, rxpb = s_xp[t % 2]
            p.step("vector",
                   lambda e, xb=xbuf, xp=xpb: e.tensor_add(xp[:], s_t1[:],
                                                           xb[:]),
                   reads=[r_t1, rxbuf], writes=[rxpb], tag=f"res1.{t}")
            dma(scr_x1[t * 128:(t + 1) * 128, :], xpb[:], reads=[rxpb],
                writes=[r_sx1[t]], tag=f"sx1.{t}")
            dy, rdy = s_dummy[t % 2]
            p.step("scalar",
                   lambda e, xp=xpb, dy=dy: e.activation(dy[:], xp[:],
                                                         AF.Square),
                   reads=[rxpb], writes=[rdy], tag=f"bsq{t}")
            p.step("vector",
                   lambda e, dy=dy, t=t: e.tensor_reduce(
                       s_c1sq[:, t:t + 1], dy[:], AX.X, ALU.add),
                   reads=[rdy], writes=[r_c1sq], tag=f"bqr{t}")
            p.step("vector",
                   lambda e, xp=xpb, t=t: e.tensor_reduce(
                       s_c1sum[:, t:t + 1], xp[:], AX.X, ALU.add),
                   reads=[rxpb], writes=[r_c1sum], tag=f"bsr{t}")

    if stage == "B":
        d_x1 = dbg_out("d_x1", [S, DM])
        toks = [dma(d_x1, scr_x1, reads=r_sx1[:], tag="dbgx1")]
        p.step("sync", lambda e: e.nop(), extra_deps=toks, tag="fin")
        stB.close()
        p.emit(nc, top)
        return nc, dbg, top

    stB.close()
    p.add_fence([r_pta, r_ptb, r_attn, r_wob, r_bob, r_r16, r_sel, r_t1,
                 r_po, r_prs, p_stab[0][1], p_stab[1][1],
                 s_xp[0][1], s_xp[1][1]])
    stQKV.close()
    p.add_fence([r_qt, r_kt, r_v])

    stC = ExitStack()
    s_w1b, r_w1b = sb("c_w1b", [128, 8 * INNER], BF16, stack=stC)
    s_w2b, r_w2b = sb("c_w2b", [128, 32 * DM], BF16, stack=stC)
    s_mu2, r_mu2 = sb("c_mu2", [128, 16], F32, stack=stC)
    s_rs2, r_rs2 = sb("c_rs2", [128, 16], F32, stack=stC)
    s_c1x = [sb(f"c_x{i}", [128, DM], F32, stack=stC) for i in range(4)]
    _xn2 = sb("c_xn2", [128, DM], BF16, stack=stC)
    s_xn2 = [_xn2, _xn2]
    s_xn2t = [sb(f"c_xn2t{i}", [128, 8 * 256], BF16, stack=stC)
              for i in range(2)]
    s_h1c = [sb(f"c_h1c{i}", [128, 256], BF16, stack=stC) for i in range(3)]
    _xpp = sb("c_xpp", [128, DM], F32, stack=stC)
    s_xpp = [_xpp, _xpp]
    s_wq1, r_wq1 = sb("c_wq1", [128, 1024], F32, stack=stC)
    s_wqbf, r_wqbf = sb("c_wqbf", [128, 1024], BF16, stack=stC)
    s_wtx, r_wtx = sb("c_wtx", [128, 1024], BF16, stack=stC)
    s_wstc = [sb(f"c_wst{i}", [128, 1024], F32, stack=stC) for i in range(2)]
    p_h = [ps(f"p_h{i}", [128, 512], F32, stack=stC) for i in range(3)]
    p_mlp = [ps(f"p_mlp{i}", [128, 1024], F32, stack=stC) for i in range(2)]
    p_trc, r_ptrc = ps("p_trc", [128, 1024], BF16, stack=stC)

    dma(s_w1b[:].rearrange("p (c n) -> p c n", c=8),
        w1_in.rearrange("(c q) n -> q c n", q=128),
        writes=[r_w1b], eng="gpsimd", tag="ldw1")
    dma(s_w2b[:].rearrange("p (c n) -> p c n", c=32),
        w2_in.rearrange("(c q) n -> q c n", q=128),
        writes=[r_w2b], eng="gpsimd", tag="ldw2")

    wq2_jobs = []
    for widx, (w_ap, scr, rscr) in enumerate(
            ((wg_in, scr_wgt, r_swg), (wu_in, scr_wut, r_swu),
             (wd_in, scr_wdt, r_swd))):
        for ch in range(32):
            wq2_jobs.append((widx, w_ap, scr, rscr, ch))
    wq2_pos = [0]

    def wq2_pump(n):
        for _ in range(n):
            if wq2_pos[0] >= len(wq2_jobs):
                return
            widx, w_ap, scr, rscr, ch = wq2_jobs[wq2_pos[0]]
            wq2_pos[0] += 1
            buf, rbuf = s_wstc[wq2_pos[0] % 2]
            scol = 3 + widx if widx < 2 else 5
            if widx < 2:
                src = w_ap[ch * 128:(ch + 1) * 128, :]
                nin = 1024
            else:
                src = w_ap[(ch % 8) * 128:((ch % 8) + 1) * 128,
                           (ch // 8) * 1024:((ch // 8) + 1) * 1024]
                nin = 1024
            dma(buf[:], src, writes=[rbuf], tag=f"w2l{widx}.{ch}")
            def q1(buf=buf, scol=scol):
                return lambda e: e.tensor_scalar(
                    s_wq1[:], buf[:], s_wcol[:, scol:scol + 1], None,
                    ALU.mult)
            p.step("vector", q1(), reads=[rbuf, r_wcol], writes=[r_wq1],
                   tag=f"w2q1.{widx}.{ch}")
            p.step("vector",
                   lambda e: e.tensor_scalar(s_wq1[:], s_wq1[:], MAGIC,
                                             None, ALU.add),
                   reads=[r_wq1], writes=[r_wq1], tag=f"w2q1b.{widx}.{ch}")
            p.step("vector",
                   lambda e: e.tensor_scalar(s_wq1[:], s_wq1[:], MAGIC - 1.0,
                                             MAGIC + 1.0, ALU.max, ALU.min),
                   reads=[r_wq1], writes=[r_wq1], tag=f"w2q2.{widx}.{ch}")
            p.step("vector",
                   lambda e: e.tensor_scalar(s_wqbf[:], s_wq1[:], MAGIC,
                                             None, ALU.subtract),
                   reads=[r_wq1], writes=[r_wqbf], tag=f"w2q3.{widx}.{ch}")
            for c in range(8):
                def tr(c=c):
                    return lambda e: e.transpose(
                        p_trc[:, c * 128:(c + 1) * 128],
                        s_wqbf[:, c * 128:(c + 1) * 128], s_ident[:])
                p.step("tensor", tr(), reads=[r_wqbf, r_ident],
                       writes=[r_ptrc], tag=f"w2t.{widx}.{ch}.{c}")
            for half in range(2):
                def cp(half=half):
                    return lambda e: e.tensor_copy(
                        s_wtx[:, half * 512:(half + 1) * 512],
                        p_trc[:, half * 512:(half + 1) * 512])
                p.step("vector", cp(), reads=[r_ptrc], writes=[r_wtx],
                       tag=f"w2c.{widx}.{ch}.{half}")
            if widx < 2:
                dst = scr[:, ch * 128:(ch + 1) * 128].rearrange(
                    "(c q) o -> q c o", q=128)
            else:
                dst = scr[(ch // 8) * 1024:(ch // 8 + 1) * 1024,
                          (ch % 8) * 128:((ch % 8) + 1) * 128].rearrange(
                    "(c q) o -> q c o", q=128)
            dma(dst, s_wtx[:].rearrange("p (c o) -> p c o", c=8),
                reads=[r_wtx], writes=[rscr],
                tag=f"w2s{widx}.{ch}")

    batch_stats(s_c1sum, r_c1sum, s_c1sq, r_c1sq, s_mu2, r_mu2, s_rs2,
                r_rs2, "c1")

    xtiles_by_qg = {}

    def c2_prep(qg):
        xnt2, rxnt2 = s_xn2t[qg % 2]
        xtiles = []
        for qs in range(2):
            t = qg * 2 + qs
            xbuf, rxbuf = s_c1x[(qg % 2) * 2 + qs]
            dma(xbuf[:], scr_x1[t * 128:(t + 1) * 128, :], reads=[r_sx1[t]],
                writes=[rxbuf], tag=f"c2x{t}")
            xtiles.append((xbuf, rxbuf))
            xn2, rxn2 = s_xn2[qs]
            p.step("vector",
                   lambda e, xb=xbuf, t=t, xn2=xn2: e.tensor_scalar(
                       xn2[:], xb[:], s_mu2[:, t:t + 1],
                       s_rs2[:, t:t + 1], ALU.subtract, ALU.mult),
                   reads=[rxbuf, r_mu2, r_rs2], writes=[rxn2],
                   tag=f"c2n{t}")
            for c in range(8):
                def tr(c=c, xn2=xn2):
                    return lambda e: e.transpose(
                        p_trc[:, c * 128:(c + 1) * 128],
                        xn2[:, c * 128:(c + 1) * 128], s_ident[:])
                p.step("tensor", tr(), reads=[rxn2, r_ident],
                       writes=[r_ptrc], tag=f"c2t{t}.{c}")
            for c in range(8):
                def ev(c=c, qs=qs, xnt2=xnt2):
                    return lambda e: e.activation(
                        xnt2[:, c * 256 + qs * 128:c * 256 + qs * 128 + 128],
                        p_trc[:, c * 128:(c + 1) * 128], AF.Identity,
                        scale=s_gb2[:, c:c + 1], bias=s_gb2[:, 8 + c:9 + c])
                p.step("scalar", ev(), reads=[r_ptrc, r_gb2],
                       writes=[rxnt2], tag=f"c2e{t}.{c}")
        xtiles_by_qg[qg] = xtiles

    def c2_w1(qg, ic):
        xnt2, rxnt2 = s_xn2t[qg % 2]
        ph, rph = p_h[ic % 3]
        h1c, rh1c = s_h1c[ic % 3]
        for dinc in range(8):
            def m1(dinc=dinc, ic=ic, ph=ph, xnt2=xnt2):
                return lambda e: e.matmul(
                    ph[:, 0:256],
                    s_w1b[:, dinc * INNER + ic * 128:
                          dinc * INNER + (ic + 1) * 128],
                    xnt2[:, dinc * 256:(dinc + 1) * 256],
                    start=(dinc == 0), stop=(dinc == 7),
                    skip_group_check=True)
            p.step("tensor", m1(), reads=[r_w1b, rxnt2], writes=[rph],
                   tag=f"w1.{qg}.{ic}.{dinc}")
        def gel(ic=ic, ph=ph, h1c=h1c):
            return lambda e: e.activation(h1c[:], ph[:, 0:256], AF.Gelu,
                                          bias=s_b1c[:, ic:ic + 1])
        p.step("scalar", gel(), reads=[rph, r_b1c], writes=[rh1c],
               tag=f"gel{qg}.{ic}")

    def c2_w2(qg, ic):
        h1c, rh1c = s_h1c[ic % 3]
        for qs in range(2):
            pm, rpm = p_mlp[qs]
            for h2 in range(2):
                def m2(ic=ic, qs=qs, h2=h2, pm=pm, h1c=h1c):
                    return lambda e: e.matmul(
                        pm[:, h2 * 512:(h2 + 1) * 512],
                        h1c[:, qs * 128:(qs + 1) * 128],
                        s_w2b[:, ic * DM + h2 * 512:
                              ic * DM + (h2 + 1) * 512],
                        start=(ic == 0), stop=(ic == 31),
                        skip_group_check=True)
                p.step("tensor", m2(), reads=[rh1c, r_w2b],
                       writes=[rpm], tag=f"w2m{qg}.{ic}.{qs}.{h2}")

    def c2_evac(qg):
        for qs in range(2):
            t = qg * 2 + qs
            pm, rpm = p_mlp[qs]
            xb, rxb = xtiles_by_qg[qg][qs]
            xpp, rxpp = s_xpp[qs]
            x0b, rx0b = s_xt[qs]
            dma(x0b[:], x_in[t * 128:(t + 1) * 128, :], writes=[rx0b],
                tag=f"c2x0{t}")
            p.step("vector",
                   lambda e, pm=pm, xpp=xpp: e.tensor_add(
                       xpp[:], pm[:], s_b2b[:]),
                   reads=[rpm, r_b2b], writes=[rxpp], tag=f"c2b{t}")
            p.step("vector",
                   lambda e, xpp=xpp, xb=xb: e.tensor_add(xpp[:], xpp[:],
                                                          xb[:]),
                   reads=[rxb, rxpp], writes=[rxpp], tag=f"c2r{t}")
            p.step("vector",
                   lambda e, xpp=xpp, x0b=x0b: e.tensor_add(xpp[:], xpp[:],
                                                            x0b[:]),
                   reads=[rx0b, rxpp], writes=[rxpp], tag=f"c2r2{t}")
            dma(scr_x1[t * 128:(t + 1) * 128, :], xpp[:], reads=[rxpp],
                writes=[r_sx1[t]], tag=f"c2s{t}")

    NIC = INNER // 128
    c2_prep(0)
    jobs = [(qg, ic) for qg in range(8) for ic in range(NIC)]
    DEPTH = 2
    for i, (qg, ic) in enumerate(jobs):
        if i >= DEPTH:
            pqg, pic = jobs[i - DEPTH]
            c2_w2(pqg, pic)
            if pic == NIC - 1:
                c2_evac(pqg)
        if ic == NIC - 1 and qg < 7:
            c2_prep(qg + 1)
        c2_w1(qg, ic)
        wq2_pump(1)
    for i in range(len(jobs) - DEPTH, len(jobs)):
        pqg, pic = jobs[i]
        c2_w2(pqg, pic)
        if pic == NIC - 1:
            c2_evac(pqg)
    wq2_pump(200)


    stC.close()
    p.add_fence([r_w1b, r_w2b, r_wq1, r_wqbf, r_wtx, _xn2[1],
                 p_h[0][1], p_h[1][1], p_h[2][1], p_mlp[0][1], p_mlp[1][1], r_ptrc]
                + [b[1] for b in (s_c1x + s_xpp + s_wstc + s_xn2t
                                  + s_h1c)])

    stD = ExitStack()
    s_wgtb, r_wgtb = sb("d_wgtb", [128, 8 * INNER], BF16, stack=stD)
    s_wutb, r_wutb = sb("d_wutb", [128, 8 * INNER], BF16, stack=stD)
    s_x2t = [sb(f"d_x2t{i}", [128, DM], F32, stack=stD) for i in range(2)]
    _q1 = sb("d_q1", [128, DM], F32, stack=stD)
    s_q1 = [_q1, _q1]
    s_xq = [sb(f"d_xq{i}", [128, DM], BF16, stack=stD) for i in range(2)]
    s_xqt = [sb(f"d_xqt{i}", [128, DM], BF16, stack=stD) for i in range(3)]
    s_sg, r_sg = sb("d_sg", [128, 512], F32, stack=stD)
    s_h, r_h = sb("d_h", [128, INNER], F32, stack=stD)
    _q2 = sb("d_q2", [128, 1024], F32, stack=stD)
    s_q2 = [_q2, _q1]
    s_xq2 = [sb(f"d_xq2{i}", [128, 1024], BF16, stack=stD) for i in range(2)]
    s_hm, r_hm = sb("d_hm", [128, 16], F32, stack=stD)
    p_g = [ps(f"p_g{i}", [128, 512], F32, stack=stD) for i in range(2)]
    p_u = [ps(f"p_u{i}", [128, 512], F32, stack=stD) for i in range(2)]
    p_trd = [ps(f"p_trd{i}", [128, 1024], BF16, stack=stD) for i in range(3)]

    r_wgtc = [p.res() for _ in range(8)]
    r_wutc = [p.res() for _ in range(8)]
    for c in range(8):
        dma(s_wgtb[:, c * INNER:(c + 1) * INNER],
            scr_wgt[c * 128:(c + 1) * 128, :],
            reads=[r_swg], writes=[r_wgtc[c]], tag=f"ldwgt{c}")
        dma(s_wutb[:, c * INNER:(c + 1) * INNER],
            scr_wut[c * 128:(c + 1) * 128, :],
            reads=[r_swu], writes=[r_wutc[c]], tag=f"ldwut{c}")




    for t in range(NT):
        xbuf, rxbuf = s_x2t[t % 2]
        dy, rdy = s_dummy[t % 2]
        dma(xbuf[:], scr_x1[t * 128:(t + 1) * 128, :], reads=[r_sx1[t]],
            writes=[rxbuf], tag=f"c3x{t}")
        p.step("scalar",
               lambda e, xb=xbuf, dy=dy: e.activation(dy[:], xb[:],
                                                      AF.Square),
               reads=[rxbuf], writes=[rdy], tag=f"c3sq{t}")
        p.step("vector",
               lambda e, dy=dy, t=t: e.tensor_reduce(
                   s_c3sq[:, t:t + 1], dy[:], AX.X, ALU.add),
               reads=[rdy], writes=[r_c3sq], tag=f"c3qr{t}")
        p.step("vector",
               lambda e, xb=xbuf, t=t: e.tensor_reduce(
                   s_c3sum[:, t:t + 1], xb[:], AX.X, ALU.add),
               reads=[rxbuf], writes=[r_c3sum], tag=f"c3sr{t}")
        p.step("vector",
               lambda e, t=t: e.tensor_scalar(
                   s_mu3[:, t:t + 1], s_c3sum[:, t:t + 1], 1.0 / DM, None,
                   ALU.mult),
               reads=[r_c3sum], writes=[r_mu3], tag=f"c3mu{t}",
               sync_out=True)
        p.step("vector",
               lambda e, xb=xbuf, dy=dy, t=t: e.tensor_scalar(
                   dy[:], xb[:], s_mu3[:, t:t + 1], None, ALU.subtract),
               reads=[rxbuf, r_mu3, rdy], writes=[rdy], tag=f"c3d{t}")
        p.step("vector",
               lambda e, dy=dy, t=t: e.tensor_reduce(
                   s_axm[:, t:t + 1], dy[:], AX.X, ALU.max,
                   apply_absolute_value=True),
               reads=[rdy], writes=[r_axm], tag=f"c3am{t}")
    p.step("vector",
           lambda e: e.tensor_scalar(s_smal[:, 16:32], s_c3sq[:], 1.0 / DM,
                                     None, ALU.mult),
           reads=[r_c3sq], writes=[r_smal], tag="c3ms")
    p.step("vector",
           lambda e: e.tensor_mul(s_smal[:, 32:48], s_mu3[:], s_mu3[:]),
           reads=[r_mu3, r_smal], writes=[r_smal], tag="c3m2")
    p.step("vector",
           lambda e: e.scalar_tensor_tensor(
               s_smal[:, 16:32], s_smal[:, 16:32], LNEPS,
               s_smal[:, 32:48], ALU.add, ALU.subtract),
           reads=[r_smal], writes=[r_smal], tag="c3vp", sync_out=True)
    p.step("vector",
           lambda e: e.reciprocal(s_smal[:, 32:48], s_smal[:, 16:32]),
           reads=[r_smal], writes=[r_smal], tag="c3rc", sync_out=True)
    p.step("scalar",
           lambda e: e.activation(s_smal[:, 32:48], s_smal[:, 32:48],
                                  AF.Sqrt),
           reads=[r_smal], writes=[r_smal], tag="c3rq", sync_out=True)
    p.step("vector",
           lambda e: e.tensor_mul(s_smal[:, 0:16], s_smal[:, 32:48],
                                  s_smal[:, 32:48]),
           reads=[r_smal], writes=[r_smal], tag="c3n1")
    p.step("vector",
           lambda e: e.tensor_mul(s_smal[:, 0:16], s_smal[:, 0:16],
                                  s_smal[:, 16:32]),
           reads=[r_smal], writes=[r_smal], tag="c3n2")
    p.step("vector",
           lambda e: e.tensor_scalar(s_smal[:, 0:16], s_smal[:, 0:16],
                                     -0.5, 1.5, ALU.mult, ALU.add),
           reads=[r_smal], writes=[r_smal], tag="c3n3")
    p.step("vector",
           lambda e: e.tensor_mul(s_rs3[:], s_smal[:, 32:48],
                                  s_smal[:, 0:16]),
           reads=[r_smal], writes=[r_rs3], tag="c3n4", sync_out=True)
    p.step("vector",
           lambda e: e.tensor_mul(s_axm[:], s_axm[:], s_rs3[:]),
           reads=[r_rs3, r_axm], writes=[r_axm], tag="c3ax")
    p.step("vector",
           lambda e: e.tensor_scalar(s_smal[:, 0:16], s_axm[:], 1e-5, None,
                                     ALU.max),
           reads=[r_axm], writes=[r_smal], tag="dqmc")
    p.step("vector",
           lambda e: e.reciprocal(s_smal[:, 16:32], s_smal[:, 0:16]),
           reads=[r_smal], writes=[r_smal], tag="dqrc", sync_out=True)
    p.step("vector",
           lambda e: e.tensor_scalar(s_dsx[:], s_smal[:, 16:32], 127.0,
                                     None, ALU.mult),
           reads=[r_smal], writes=[r_dsx], tag="dqsx")
    p.step("vector",
           lambda e: e.tensor_scalar(s_smal[:, 32:48], s_smal[:, 0:16],
                                     1.0 / 127.0, None, ALU.mult),
           reads=[r_smal], writes=[r_smal], tag="dqos")
    p.step("vector",
           lambda e: e.tensor_scalar(s_dcg[:], s_smal[:, 32:48],
                                     s_wcol[:, 6:7], None, ALU.mult),
           reads=[r_smal, r_wcol], writes=[r_dcg], tag="dqcg")
    p.step("vector",
           lambda e: e.tensor_scalar(s_dcu[:], s_smal[:, 32:48],
                                     s_wcol[:, 7:8], None, ALU.mult),
           reads=[r_smal, r_wcol], writes=[r_dcu], tag="dqcu")

    def d2_pre(t):
        xbuf, rxbuf = s_x2t[t % 2]
        dma(xbuf[:], scr_x1[t * 128:(t + 1) * 128, :], reads=[r_sx1[t]],
            writes=[rxbuf], tag=f"d2x{t}")
        p.step("vector",
               lambda e, xb=xbuf, t=t: e.tensor_scalar(
                   xb[:], xb[:], s_mu3[:, t:t + 1], s_rs3[:, t:t + 1],
                   ALU.subtract, ALU.mult),
               reads=[r_mu3, r_rs3, rxbuf], writes=[rxbuf], tag=f"c3n{t}")
        dma(scr_x2[t * 128:(t + 1) * 128, :], xbuf[:], reads=[rxbuf],
            writes=[r_sx2[t]], tag=f"c3s{t}")

    def d2_stage1a(t):
        xbuf, rxbuf = s_x2t[t % 2]
        q1, rq1 = s_q1[t % 2]
        xq, rxq = s_xq[t % 2]
        p.step("scalar",
               lambda e, xb=xbuf, q1=q1, t=t: e.activation(
                   q1[:], xb[:], AF.Identity, scale=s_dsx[:, t:t + 1]),
               reads=[rxbuf, r_dsx], writes=[rq1], tag=f"d2q1{t}")
        p.step("vector",
               lambda e, q1=q1: e.tensor_scalar(q1[:], q1[:], MAGIC, None,
                                                ALU.add),
               reads=[rq1], writes=[rq1], tag=f"d2q1b{t}")
        p.step("vector",
               lambda e, q1=q1, xq=xq: e.tensor_scalar(
                   xq[:], q1[:], MAGIC, None, ALU.subtract),
               reads=[rq1], writes=[rxq], tag=f"d2q2{t}")

    def d2_stage1b(t):
        pr = t % 3
        xq, rxq = s_xq[t % 2]
        xqt, rxqt = s_xqt[pr]
        ptrd, rptrd = p_trd[pr]
        for c in range(8):
            def tr(c=c, xq=xq, ptrd=ptrd):
                return lambda e: e.transpose(ptrd[:, c * 128:(c + 1) * 128],
                                             xq[:, c * 128:(c + 1) * 128],
                                             s_ident[:])
            p.step("tensor", tr(), reads=[rxq, r_ident], writes=[rptrd],
                   tag=f"d2t{t}.{c}")
        for half in range(2):
            def cp(half=half, xqt=xqt, ptrd=ptrd):
                return lambda e: e.tensor_copy(
                    xqt[:, half * 512:(half + 1) * 512],
                    ptrd[:, half * 512:(half + 1) * 512])
            p.step("vector", cp(), reads=[rptrd], writes=[rxqt],
                   tag=f"d2tc{t}.{half}")

    def d2_stage2(t):
        pr = t % 3
        xqt, rxqt = s_xqt[pr]
        for oc in range(8):
            pg, rpg = p_g[oc % 2]
            pu, rpu = p_u[oc % 2]
            for dinc in range(8):
                def mg(dinc=dinc, oc=oc, pg=pg, xqt=xqt):
                    return lambda e: e.matmul(
                        pg[:],
                        xqt[:, dinc * 128:(dinc + 1) * 128],
                        s_wgtb[:, dinc * INNER + oc * 512:
                               dinc * INNER + (oc + 1) * 512],
                        start=(dinc == 0), stop=(dinc == 7),
                        skip_group_check=True)
                p.step("tensor", mg(), reads=[rxqt, r_wgtc[dinc]],
                       writes=[rpg], tag=f"d2g{t}.{oc}.{dinc}")
                def mu_(dinc=dinc, oc=oc, pu=pu, xqt=xqt):
                    return lambda e: e.matmul(
                        pu[:],
                        xqt[:, dinc * 128:(dinc + 1) * 128],
                        s_wutb[:, dinc * INNER + oc * 512:
                               dinc * INNER + (oc + 1) * 512],
                        start=(dinc == 0), stop=(dinc == 7),
                        skip_group_check=True)
                p.step("tensor", mu_(), reads=[rxqt, r_wutc[dinc]],
                       writes=[rpu], tag=f"d2u{t}.{oc}.{dinc}")
            def sil(pg=pg, t=t):
                return lambda e: e.activation(s_sg[:], pg[:], AF.Silu,
                                              scale=s_dcg[:, t:t + 1])
            p.step("scalar", sil(), reads=[rpg, r_dcg], writes=[r_sg],
                   tag=f"d2s{t}.{oc}")
            def hmul(oc=oc, pu=pu, t=t):
                return lambda e: e.scalar_tensor_tensor(
                    s_h[:, oc * 512:(oc + 1) * 512], pu[:],
                    s_dcu[:, t:t + 1], s_sg[:], ALU.mult, ALU.mult)
            p.step("vector", hmul(), reads=[rpu, r_dcu, r_sg],
                   writes=[r_h], tag=f"d2h{t}.{oc}")
            def hab(oc=oc, t=t):
                return lambda e: e.tensor_reduce(
                    s_smal[:, 8 + (oc % 2):9 + (oc % 2)],
                    s_h[:, oc * 512:(oc + 1) * 512], AX.X, ALU.max,
                    apply_absolute_value=True)
            p.step("vector", hab(), reads=[r_h], writes=[r_smal],
                   tag=f"d2ha{t}.{oc}", sync_out=True)
            def habm(oc=oc, t=t):
                if oc == 0:
                    return lambda e: e.tensor_copy(s_hm[:, t:t + 1],
                                                   s_smal[:, 8:9])
                return lambda e: e.tensor_max(
                    s_hm[:, t:t + 1], s_hm[:, t:t + 1],
                    s_smal[:, 8 + (oc % 2):9 + (oc % 2)])
            p.step("vector", habm(), reads=[r_smal, r_hm], writes=[r_hm],
                   tag=f"d2hm{t}.{oc}")
        p.step("vector",
               lambda e, t=t: e.tensor_scalar(s_smal[:, 1:2],
                                              s_hm[:, t:t + 1], 1e-5, None,
                                              ALU.max),
               reads=[r_hm], writes=[r_smal], tag=f"d2mc2{t}")
        p.step("vector",
               lambda e: e.reciprocal(s_smal[:, 2:3], s_smal[:, 1:2]),
               reads=[r_smal], writes=[r_smal], tag=f"d2rc2{t}",
               sync_out=True)
        p.step("vector",
               lambda e: e.tensor_scalar(s_smal[:, 3:4], s_smal[:, 2:3],
                                         127.0, None, ALU.mult),
               reads=[r_smal], writes=[r_smal], tag=f"d2sx2{t}",
               sync_out=True)
        p.step("vector",
               lambda e: e.tensor_scalar(s_smal[:, 4:5], s_smal[:, 1:2],
                                         1.0 / 127.0, None, ALU.mult),
               reads=[r_smal], writes=[r_smal], tag=f"d2os2{t}")
        p.step("vector",
               lambda e, t=t: e.tensor_mul(s_cbd[:, t:t + 1],
                                           s_smal[:, 4:5], s_wcol[:, 0:1]),
               reads=[r_smal, r_wcol], writes=[r_cbd], tag=f"d2cd{t}")
        for oc in range(4):
            xq2, rxq2 = s_xq2[oc % 2]
            q2, rq2 = s_q2[oc % 2]
            def q1f(oc=oc, q2=q2):
                return lambda e: e.activation(
                    q2[:], s_h[:, oc * 1024:(oc + 1) * 1024], AF.Identity,
                    scale=s_smal[:, 3:4])
            p.step("scalar", q1f(), reads=[r_h, r_smal], writes=[rq2],
                   tag=f"d2p1{t}.{oc}")
            p.step("vector",
                   lambda e, q2=q2: e.tensor_scalar(q2[:], q2[:], MAGIC,
                                                    None, ALU.add),
                   reads=[rq2], writes=[rq2], tag=f"d2p1b{t}.{oc}")
            p.step("vector", lambda e, xq2=xq2, q2=q2: e.tensor_scalar(
                xq2[:], q2[:], MAGIC, None,
                ALU.subtract), reads=[rq2], writes=[rxq2],
                tag=f"d2p2{t}.{oc}")
            dma(scr_xq2[t * 128:(t + 1) * 128,
                        oc * 1024:(oc + 1) * 1024], xq2[:],
                reads=[rxq2], writes=[r_sxq2[t]], tag=f"d2sx{t}.{oc}")

    d2_pre(0)
    d2_stage1a(0)
    d2_stage1b(0)
    d2_pre(1)
    d2_stage1a(1)
    d2_stage1b(1)
    for t in range(NT):
        if t + 2 < NT:
            d2_pre(t + 2)
            d2_stage1a(t + 2)
        d2_stage2(t)
        if t + 2 < NT:
            d2_stage1b(t + 2)

    if stage == "C":
        d_x2 = dbg_out("d_x2", [S, DM])
        d_wgt = dbg_out("d_wgt", [DM, INNER], BF16)
        d_wdt = dbg_out("d_wdt", [INNER, DM], BF16)
        toks = [dma(d_x2, scr_x2, reads=r_sx2[:], tag="dbgx2"),
                dma(d_wgt, scr_wgt, reads=[r_swg], tag="dbgwgt"),
                dma(d_wdt, scr_wdt, reads=[r_swd], tag="dbgwdt")]
        p.step("sync", lambda e: e.nop(), extra_deps=toks, tag="fin")
        p.emit(nc, top)
        return nc, dbg, top


    stD.close()
    p.add_fence(r_wgtc + r_wutc + [r_sg, r_h, r_hm,
                 p_g[0][1], p_g[1][1], p_u[0][1], p_u[1][1],
                 p_trd[0][1], p_trd[1][1], p_trd[2][1]]
                + [b[1] for b in (s_x2t + s_q1 + s_xq + s_xqt + s_xq2
                                  + [_q2])])

    stE = ExitStack()
    s_wdtb, r_wdtb = sb("e_wdtb", [128, 32 * DM], BF16, stack=stE)
    s_xq2t = [sb(f"e_xq2t{i}", [128, INNER], BF16, stack=stE)
              for i in range(2)]
    s_xq2T = [sb(f"e_xq2T{i}", [128, 32 * 128], BF16, stack=stE)
              for i in range(4)]
    s_ex2 = [sb(f"e_x2{i}", [128, DM], F32, stack=stE) for i in range(4)]
    s_y, r_y = sb("e_y", [128, DM], F32, stack=stE)
    s_o = [sb(f"e_o{i}", [128, DM], F32, stack=stE) for i in range(2)]
    p_d = [ps(f"p_d{i}", [128, 1024], F32, stack=stE) for i in range(2)]
    p_tre = [ps(f"p_tre{i}", [128, 1024], BF16, stack=stE) for i in range(2)]

    r_wdtc = [p.res() for _ in range(32)]
    for c in range(32):
        dma(s_wdtb[:, c * DM:(c + 1) * DM],
            scr_wdt[c * 128:(c + 1) * 128, :],
            reads=[r_swd], writes=[r_wdtc[c]], tag=f"ldwdt{c}")

    out_toks = []

    def d3_prep(t):
        qbuf, rqbuf = s_xq2t[t % 2]
        dma(qbuf[:], scr_xq2[t * 128:(t + 1) * 128, :], reads=[r_sxq2[t]],
            writes=[rqbuf], tag=f"d3q{t}")
        xqT, rxqT = s_xq2T[t % 4]
        for quad in range(4):
            ptr, rptr = p_tre[quad % 2]
            for c in range(8):
                cc = quad * 8 + c
                def tr(cc=cc, c=c, ptr=ptr, qbuf=qbuf):
                    return lambda e: e.transpose(
                        ptr[:, c * 128:(c + 1) * 128],
                        qbuf[:, cc * 128:(cc + 1) * 128], s_ident[:])
                p.step("tensor", tr(), reads=[rqbuf, r_ident], writes=[rptr],
                       tag=f"d3t{t}.{cc}")
            for half in range(2):
                def cp(quad=quad, half=half, ptr=ptr, xqT=xqT):
                    return lambda e: e.tensor_copy(
                        xqT[:, quad * 1024 + half * 512:
                            quad * 1024 + (half + 1) * 512],
                        ptr[:, half * 512:(half + 1) * 512])
                p.step("vector", cp(), reads=[rptr], writes=[rxqT],
                       tag=f"d3c{t}.{quad}.{half}")
        xbuf, rxbuf = s_ex2[t % 4]
        dma(xbuf[:], scr_x2[t * 128:(t + 1) * 128, :], reads=[r_sx2[t]],
            writes=[rxbuf], tag=f"d3x{t}")

    def d3_main(t):
        xqT, rxqT = s_xq2T[t % 4]
        xbuf, rxbuf = s_ex2[t % 4]
        pd, rpd = p_d[t % 2]
        for i2c in range(32):
            for h2 in range(2):
                def md(i2c=i2c, h2=h2, pd=pd, xqT=xqT):
                    return lambda e: e.matmul(
                        pd[:, h2 * 512:(h2 + 1) * 512],
                        xqT[:, i2c * 128:(i2c + 1) * 128],
                        s_wdtb[:, i2c * DM + h2 * 512:
                               i2c * DM + (h2 + 1) * 512],
                        start=(i2c == 0), stop=(i2c == 31),
                        skip_group_check=True)
                p.step("tensor", md(), reads=[rxqT, r_wdtc[i2c]],
                       writes=[rpd], tag=f"d3m{t}.{i2c}.{h2}")
        p.step("vector",
               lambda e, pd=pd, xb=xbuf, t=t: e.scalar_tensor_tensor(
                   s_y[:], pd[:], s_cbd[:, t:t + 1], xb[:], ALU.mult,
                   ALU.add),
               reads=[rpd, r_cbd, rxbuf], writes=[r_y], tag=f"d3y{t}")
        dy3, rdy3 = s_dummy[t % 2]
        p.step("scalar",
               lambda e, dy3=dy3: e.activation(dy3[:], s_y[:], AF.Square),
               reads=[r_y], writes=[rdy3], tag=f"d3sq{t}")
        p.step("vector",
               lambda e, dy3=dy3: e.tensor_reduce(s_smal[:, 0:1], dy3[:],
                                                  AX.X, ALU.add),
               reads=[rdy3], writes=[r_smal], tag=f"d3sq{t}r",
               sync_out=True)
        p.step("vector",
               lambda e: e.tensor_reduce(s_smal[:, 1:2], s_y[:], AX.X,
                                         ALU.add),
               reads=[r_y], writes=[r_smal], tag=f"d3su{t}", sync_out=True)
        p.step("vector",
               lambda e: e.tensor_scalar(s_smal[:, 6:7], s_smal[:, 1:2],
                                         1.0 / DM, None, ALU.mult),
               reads=[r_smal], writes=[r_smal], tag=f"d3mu{t}")
        p.step("vector",
               lambda e: e.tensor_scalar(s_smal[:, 2:3], s_smal[:, 0:1],
                                         1.0 / DM, None, ALU.mult),
               reads=[r_smal], writes=[r_smal], tag=f"d3ms{t}")
        p.step("vector",
               lambda e: e.tensor_mul(s_smal[:, 3:4], s_smal[:, 6:7],
                                      s_smal[:, 6:7]),
               reads=[r_smal], writes=[r_smal], tag=f"d3mm{t}")
        p.step("vector",
               lambda e: e.tensor_sub(s_smal[:, 4:5], s_smal[:, 2:3],
                                      s_smal[:, 3:4]),
               reads=[r_smal], writes=[r_smal], tag=f"d3v{t}",
               flush=lambda e: e.tensor_copy(s_vfl[:, 0:1], s_smal[:, 4:5]))
        p.step("scalar",
               lambda e: e.activation(s_smal[:, 5:6], s_smal[:, 4:5], AF.Ln,
                                      bias=s_cvec[:, 0:1]),
               reads=[r_smal, r_cvec], writes=[r_smal], tag=f"d3l{t}")
        p.step("scalar",
               lambda e: e.activation(s_smal[:, 7:8], s_smal[:, 5:6],
                                      AF.Exp, scale=-0.5),
               reads=[r_smal], writes=[r_smal], tag=f"d3r{t}",
               flush=lambda e: e.copy(s_sfl[:, 0:1], s_smal[:, 7:8]))
        obuf, robuf = s_o[t % 2]
        p.step("vector",
               lambda e, ob=obuf: e.tensor_scalar(
                   ob[:], s_y[:], s_smal[:, 6:7], s_smal[:, 7:8],
                   ALU.subtract, ALU.mult),
               reads=[r_y, r_smal], writes=[robuf], tag=f"d3n{t}")
        out_toks.append(dma(out_t[t * 128:(t + 1) * 128, :], obuf[:],
                            reads=[robuf], tag=f"out{t}"))

    d3_prep(0)
    d3_prep(1)
    for tp in range(0, NT, 2):
        if tp + 2 < NT:
            d3_prep(tp + 2)
            d3_prep(tp + 3)
        d3_main(tp)
        d3_main(tp + 1)
    p.step("sync", lambda e: e.nop(), extra_deps=out_toks, tag="fin")
    stE.close()
    p.emit(nc, top)
    return nc, dbg, top


_CACHE = {}


def _get_program(stage="full"):
    if stage not in _CACHE:
        _CACHE[stage] = build_program(stage)
    return _CACHE[stage]


def make_in_maps(inputs):
    c = _host_consts()
    f32 = np.float32
    base = dict(
        ln1_g=np.ascontiguousarray(inputs["L0_attn_ln1_g"], f32),
        ln1_b=np.ascontiguousarray(inputs["L0_attn_ln1_b"], f32),
        wqkv=np.ascontiguousarray(inputs["L0_attn_Wqkv"], f32),
        bqkv=np.ascontiguousarray(inputs["L0_attn_bqkv"], f32),
        wo=np.ascontiguousarray(inputs["L0_attn_Wo"], f32),
        bo=np.ascontiguousarray(inputs["L0_attn_bo"], f32),
        ln2_g=np.ascontiguousarray(inputs["L0_attn_ln2_g"], f32),
        ln2_b=np.ascontiguousarray(inputs["L0_attn_ln2_b"], f32),
        w1=np.ascontiguousarray(inputs["L0_attn_W1"], f32),
        b1=np.ascontiguousarray(inputs["L0_attn_b1"], f32),
        w2=np.ascontiguousarray(inputs["L0_attn_W2"], f32),
        b2=np.ascontiguousarray(inputs["L0_attn_b2"], f32),
        wg=np.ascontiguousarray(inputs["L0_ffn_w_gate"], f32),
        wu=np.ascontiguousarray(inputs["L0_ffn_w_up"], f32),
        wd=np.ascontiguousarray(inputs["L0_ffn_w_down"], f32),
        cosF=c["cosF"], sinnF=c["sinnF"], trimask=c["trimask"],
        ident=c["ident"], identf=c["identf"], onesf=c["onesf"],
        ecat=c["ecat"], cvec=c["cvec"], sel=c["sel"], onescol=c["onescol"],
    )
    x = np.ascontiguousarray(inputs["L0_attn_x"], f32)
    maps = []
    for core in range(NCORES):
        m = dict(base)
        m["x"] = np.ascontiguousarray(x[core])
        maps.append(m)
    return maps


def kernel(**inputs):
    nc, dbg, top = _get_program("full")
    maps = make_in_maps(inputs)
    res = run_bass_kernel_spmd(nc, maps, core_ids=list(range(NCORES)))
    out = np.stack([res.results[i]["out"] for i in range(NCORES)], axis=0)
    return out.astype(np.float32)


# version 32
# speedup vs baseline: 1.0039x; 1.0039x over previous
import numpy as np
import ml_dtypes
import concourse.bass as bass
import concourse.mybir as mybir
from concourse.bass_utils import run_bass_kernel_spmd

F32 = mybir.dt.float32
BF16 = mybir.dt.bfloat16
AF = mybir.ActivationFunctionType
ALU = mybir.AluOpType
AX = mybir.AxisListType

MAGIC = 12582912.0
S, DM, H, HD, INNER = 2048, 1024, 16, 64, 4096
NT = S // 128
NCORES = 8
LNEPS = 1e-5

ENGINES = ("tensor", "vector", "scalar", "gpsimd", "sync")


class _Step:
    __slots__ = ("idx", "eng", "fn", "deps", "marked", "semval", "dma",
                 "dma_sem", "dma_val", "tag", "sync_out", "flush")

    def __init__(self, idx, eng, fn, deps, dma, tag, sync_out=False,
                 flush=None):
        self.idx, self.eng, self.fn, self.deps = idx, eng, fn, deps
        self.marked = False
        self.semval = None
        self.dma = dma
        self.dma_sem = None
        self.dma_val = None
        self.tag = tag
        self.sync_out = sync_out
        self.flush = flush


class Res:
    __slots__ = ("writers", "readers")

    def __init__(self, fence=()):
        self.writers = list(fence)
        self.readers = []


class Prog:
    def __init__(self):
        self.steps = []
        self.per_engine = {e: [] for e in ENGINES}
        self.fence = []

    def res(self):
        return Res(self.fence)

    def step(self, eng, fn, reads=(), writes=(), dma=False, tag="",
             extra_deps=(), sync_out=False, flush=None):
        deps = list(extra_deps)
        for r in reads:
            deps += r.writers
        for w in writes:
            deps += w.writers
            deps += w.readers
        s = _Step(len(self.steps), eng, fn, deps, dma, tag, sync_out, flush)
        self.steps.append(s)
        self.per_engine[eng].append(s)
        for r in reads:
            r.readers.append(s)
        for w in writes:
            w.writers = [s]
            w.readers = []
        return s

    def add_fence(self, res_list):
        toks = []
        for r in res_list:
            toks += r.writers
            toks += r.readers
        toks += self.fence
        latest = {}
        dmas = {}
        for t in toks:
            if t.dma:
                dmas[id(t)] = t
            else:
                if t.eng not in latest or t.idx > latest[t.eng].idx:
                    latest[t.eng] = t
        self.fence = list(latest.values()) + list(dmas.values())

    def finalize(self):
        for s in self.steps:
            dd = []
            seen = set()
            for d in s.deps:
                if d is s or id(d) in seen:
                    continue
                seen.add(id(d))
                dd.append(d)
                if d.dma:
                    pass
                else:
                    d.marked = True
            s.deps = dd
        for eng in ENGINES:
            c = 0
            for s in self.per_engine[eng]:
                if s.marked:
                    c += 1
                    s.semval = c

    def emit(self, nc, stack, n_dma_sems=48):
        psem = {e: stack.enter_context(nc.semaphore(f"prog_{e}"))
                for e in ENGINES}
        dsems = [stack.enter_context(nc.semaphore(f"dma{i}"))
                 for i in range(n_dma_sems)]
        dcount = [0] * n_dma_sems
        pool = {"sync": list(range(0, 32)), "gpsimd": list(range(32, 48))}
        di = {"sync": 0, "gpsimd": 0}
        for s in self.steps:
            if s.dma:
                pl = pool[s.eng]
                i = pl[di[s.eng]]
                s.dma_sem = i
                dcount[i] += 1
                s.dma_val = 16 * dcount[i]
                di[s.eng] = (di[s.eng] + 1) % len(pl)
        self.finalize()
        block = nc.Block()

        def make_body(eng_name):
            def body(eng):
                hw = {e: 0 for e in ENGINES}
                dhw = [0] * n_dma_sems
                for s in self.per_engine[eng_name]:
                    ew = {}
                    dw = {}
                    for d in s.deps:
                        if d.dma:
                            dw[d.dma_sem] = max(dw.get(d.dma_sem, 0),
                                                d.dma_val)
                        else:
                            ew[d.eng] = max(ew.get(d.eng, 0), d.semval)
                    if s.dma and s.dma_val > 16:
                        dw[s.dma_sem] = max(dw.get(s.dma_sem, 0),
                                            s.dma_val - 16)
                    for e, v in sorted(ew.items()):
                        if v > hw[e]:
                            eng.wait_ge(psem[e], v)
                            hw[e] = v
                    for q, v in sorted(dw.items()):
                        if v > dhw[q]:
                            eng.wait_ge(dsems[q], v)
                            dhw[q] = v
                    inst = s.fn(eng)
                    if s.dma:
                        inst.then_inc(dsems[s.dma_sem], 16)
                    if s.marked:
                        inst.then_inc(psem[eng_name], 1)
            return body

        with block:
            block.tensor(make_body("tensor"))
            block.vector(make_body("vector"))
            block.scalar(make_body("scalar"))
            block.gpsimd(make_body("gpsimd"))
            block.sync(make_body("sync"))


def _host_consts():
    t = np.arange(S, dtype=np.float64)
    inv = 1.0 / 10000.0 ** (np.arange(0, HD, 2, dtype=np.float64) / HD)
    fr = t[:, None] * inv[None, :]
    emb = np.concatenate([fr, fr], axis=1)
    cos = np.cos(emb).astype(np.float32)
    sin = np.sin(emb).astype(np.float32)
    sinn = sin.copy()
    sinn[:, :32] *= -1.0
    cosF = np.tile(cos, (1, H)).astype(np.float32)
    sinnF = np.tile(sinn, (1, H)).astype(np.float32)
    trimask = np.triu(np.ones((128, 128), np.float32)).astype(ml_dtypes.bfloat16)
    ident = np.eye(128, dtype=ml_dtypes.bfloat16)
    identf = np.eye(128, dtype=np.float32)
    onesf = np.ones((1, 128), np.float32)
    ecat = np.zeros((128, 256), np.float32)
    for h in range(16):
        ecat[:, h * 16 + h] = 1.0
    ecat = ecat.astype(ml_dtypes.bfloat16)
    onescol = np.ones((128, 1), np.float32)
    sel = np.zeros((16, 8 * 128), np.float32)
    for hp in range(8):
        sel[2 * hp, hp * 128:hp * 128 + 64] = 1.0
        sel[2 * hp + 1, hp * 128 + 64:hp * 128 + 128] = 1.0
    cvec = np.zeros((128, 2), np.float32)
    cvec[:, 0] = LNEPS
    cvec[:, 1] = MAGIC
    return dict(cosF=cosF, sinnF=sinnF, trimask=trimask, ident=ident,
                identf=identf, onesf=onesf, ecat=ecat, cvec=cvec, sel=sel,
                onescol=onescol)


def build_program(stage="full"):
    from contextlib import ExitStack
    nc = bass.Bass("TRN2", target_bir_lowering=False)
    p = Prog()
    top = ExitStack()

    def din(name, shape, dtype=F32):
        return nc.dram_tensor(name, shape, dtype, kind="ExternalInput").ap()

    x_in = din("x", [S, DM])
    ln1g_in = din("ln1_g", [DM]); ln1b_in = din("ln1_b", [DM])
    wqkv_in = din("wqkv", [DM, 3 * DM]); bqkv_in = din("bqkv", [3 * DM])
    wo_in = din("wo", [DM, DM]); bo_in = din("bo", [DM])
    ln2g_in = din("ln2_g", [DM]); ln2b_in = din("ln2_b", [DM])
    w1_in = din("w1", [DM, INNER]); b1_in = din("b1", [INNER])
    w2_in = din("w2", [INNER, DM]); b2_in = din("b2", [DM])
    wg_in = din("wg", [INNER, DM]); wu_in = din("wu", [INNER, DM])
    wd_in = din("wd", [DM, INNER])
    cos_in = din("cosF", [S, DM]); sinn_in = din("sinnF", [S, DM])
    tri_in = din("trimask", [128, 128], BF16)
    ident_in = din("ident", [128, 128], BF16)
    identf_in = din("identf", [128, 128], F32)
    onesf_in = din("onesf", [1, 128])
    ecat_in = din("ecat", [128, 256], BF16)
    cvec_in = din("cvec", [128, 2])
    sel_in = din("sel", [16, 8 * 128])
    onescol_in = din("onescol", [128, 1])

    out_t = nc.dram_tensor("out", [S, DM], F32, kind="ExternalOutput").ap()
    dbg = {}

    def dbg_out(name, shape, dtype=F32):
        ap = nc.dram_tensor(name, shape, dtype, kind="ExternalOutput").ap()
        dbg[name] = ap
        return ap

    scr_x1 = nc.dram_tensor("scr_x1", [S, DM], F32, kind="Internal").ap()
    scr_x2 = nc.dram_tensor("scr_x2", [S, DM], F32, kind="Internal").ap()
    scr_xq2 = nc.dram_tensor("scr_xq2", [S, INNER], BF16, kind="Internal").ap()
    scr_wgt = nc.dram_tensor("scr_wgt", [DM, INNER], BF16, kind="Internal").ap()
    scr_wut = nc.dram_tensor("scr_wut", [DM, INNER], BF16, kind="Internal").ap()
    scr_wdt = nc.dram_tensor("scr_wdt", [INNER, DM], BF16, kind="Internal").ap()
    scr_row = nc.dram_tensor("scr_row", [16, 512], F32, kind="Internal").ap()
    scr_vec = nc.dram_tensor("scr_vec", [1, 64], F32, kind="Internal").ap()

    def sb(name, shape, dtype=F32, stack=None):
        h = (stack or top).enter_context(nc.sbuf_tensor(name, shape, dtype))
        return h, p.res()

    def ps(name, shape, dtype=F32, stack=None):
        h = (stack or top).enter_context(nc.psum_tensor(name, shape, dtype))
        return h, p.res()

    def dma(dst_ap, src_ap, reads=(), writes=(), eng="sync", tag=""):
        if eng == "sync" and "DRam" in type(dst_ap.tensor).__name__:
            eng = "gpsimd"
        return p.step(eng, lambda e: e.dma_start(out=dst_ap, in_=src_ap),
                      reads=reads, writes=writes, dma=True, tag=tag)

    r_sx1 = [p.res() for _ in range(NT)]
    r_sx2 = [p.res() for _ in range(NT)]
    r_sxq2 = [p.res() for _ in range(NT)]
    r_swg = p.res(); r_swu = p.res(); r_swd = p.res()
    r_srow = p.res(); r_svec = p.res()

    s_ident, r_ident = sb("c_ident", [128, 128], BF16)
    s_identf, r_identf = sb("c_identf", [128, 128], F32)
    s_tri, r_tri = sb("c_tri", [128, 128], BF16)
    s_onesf, r_onesf = sb("c_onesf", [1, 128], F32)
    s_ecat, r_ecat = sb("c_ecat", [128, 256], BF16)
    s_gb1, r_gb1 = sb("c_gb1", [128, 16], F32)
    s_gb2, r_gb2 = sb("c_gb2", [128, 16], F32)
    s_b1c, r_b1c = sb("c_b1c", [128, 32], F32)
    s_b2b, r_b2b = sb("c_b2b", [128, DM], F32)
    s_mu1, r_mu1 = sb("g_mu1", [128, 16], F32)
    s_rs1, r_rs1 = sb("g_rs1", [128, 16], F32)
    s_nrm, r_nrm = sb("g_nrm", [128, 32], F32)
    s_bneg, r_bneg = sb("g_bneg", [128, 16], F32)
    s_wsum, r_wsum = sb("g_wsum", [1, 8], F32)
    s_wrow, r_wrow = sb("g_wrow", [1, 8], F32)
    s_wcol, r_wcol = sb("g_wcol", [128, 8], F32)
    s_cbd, r_cbd = sb("g_cbd", [128, 16], F32)
    s_row1, r_row1 = sb("g_row1", [1, DM], F32)
    s_gbt, r_gbt = sb("g_gbt", [32, 128], F32)
    s_dummy = [sb(f"g_dummy{i}", [128, DM], F32) for i in range(2)]
    s_cvec, r_cvec = sb("c_cvec", [128, 2], F32)
    s_onescol, r_onescol = sb("c_onescol", [128, 1], F32)
    s_wacc, r_wacc = sb("g_wacc", [128, 4], F32)
    s_w31, r_w31 = sb("g_w31", [3, 1], F32)
    s_vfl, _r_vfl = sb("g_vfl", [128, 16], F32)
    s_sfl, _r_sfl = sb("g_sfl", [128, 16], F32)
    s_smal, r_smal = sb("g_smal", [128, 48], F32)
    s_hrow, r_hrow = sb("g_hrow", [1, 32], F32)
    s_xt = [sb(f"g_xt{i}", [128, DM], F32) for i in range(2)]
    s_c1sum, r_c1sum = sb("g_c1sum", [128, 16], F32)
    s_c1sq, r_c1sq = sb("g_c1sq", [128, 16], F32)
    s_c3sum, r_c3sum = sb("g_c3sum", [128, 16], F32)
    s_c3sq, r_c3sq = sb("g_c3sq", [128, 16], F32)
    s_mu3, r_mu3 = sb("g_mu3", [128, 16], F32)
    s_rs3, r_rs3 = sb("g_rs3", [128, 16], F32)
    s_axm, r_axm = sb("g_axm", [128, 16], F32)
    s_dsx, r_dsx = sb("g_dsx", [128, 16], F32)
    s_dcg, r_dcg = sb("g_dcg", [128, 16], F32)
    s_dcu, r_dcu = sb("g_dcu", [128, 16], F32)

    stQKV = ExitStack()
    s_qt, r_qt = sb("a_QT", [128, 8 * S], BF16, stack=stQKV)
    s_kt, r_kt = sb("a_KT", [128, 8 * S], BF16, stack=stQKV)
    s_v, r_v = sb("a_V", [128, NT * DM], BF16, stack=stQKV)

    pro = ExitStack()
    p_mix, r_pmix = ps("p_mix", [128, 512], F32, stack=pro)

    dma(s_ident[:], ident_in, writes=[r_ident])
    dma(s_identf[:], identf_in, writes=[r_identf])
    dma(s_tri[:], tri_in, writes=[r_tri])
    dma(s_onesf[:], onesf_in, writes=[r_onesf])
    dma(s_ecat[:], ecat_in, writes=[r_ecat])
    dma(s_cvec[:], cvec_in, writes=[r_cvec])
    dma(s_onescol[:], onescol_in, writes=[r_onescol])

    def load_cols(dsts, src_ap, nchunks):
        dst, dst_res, c0 = dsts
        dma(s_gbt[0:nchunks, :], src_ap.rearrange("(c q) -> c q", q=128),
            writes=[r_gbt])
        p.step("tensor",
               lambda e: e.matmul(p_mix[:, 0:nchunks], s_gbt[0:nchunks, :],
                                  s_identf[0:nchunks, 0:nchunks],
                                  start=True, stop=True),
               reads=[r_gbt, r_identf], writes=[r_pmix])
        p.step("vector",
               lambda e: e.tensor_copy(dst[:, c0:c0 + nchunks],
                                       p_mix[:, 0:nchunks]),
               reads=[r_pmix], writes=[dst_res],
               flush=lambda e: e.tensor_copy(s_vfl[:, 0:1],
                                             dst[:, c0:c0 + 1]))

    load_cols((s_gb1, r_gb1, 0), ln1g_in, 8)
    load_cols((s_gb1, r_gb1, 8), ln1b_in, 8)
    load_cols((s_gb2, r_gb2, 0), ln2g_in, 8)
    load_cols((s_gb2, r_gb2, 8), ln2b_in, 8)
    load_cols((s_b1c, r_b1c, 0), b1_in, 32)

    def bcast_row(dst, dst_res, src_row_ap, n):
        dma(s_row1[0:1, 0:n], src_row_ap, writes=[r_row1])
        for j in range(0, n, 512):
            w = min(512, n - j)
            def mk(j=j, w=w):
                return lambda e: e.matmul(p_mix[:, 0:w], s_onesf[:],
                                          s_row1[0:1, j:j + w],
                                          start=True, stop=True)
            p.step("tensor", mk(), reads=[r_row1, r_onesf], writes=[r_pmix])
            def cp(j=j, w=w):
                return lambda e: e.tensor_copy(dst[:, j:j + w],
                                               p_mix[:, 0:w])
            p.step("vector", cp(), reads=[r_pmix], writes=[dst_res])

    bcast_row(s_b2b, r_b2b, b2_in.rearrange("(a n) -> a n", a=1), DM)

    stA = ExitStack()
    s_wph, r_wph = sb("a_wph", [128, 8 * 1024], BF16, stack=stA)
    s_pbias, r_pbias = sb("a_pbias", [128, 1024], F32, stack=stA)
    s_xn = [sb(f"a_xn{i}", [128, 1024], BF16, stack=stA) for i in range(2)]
    s_xnt = [sb(f"a_xnt{i}", [128, 1024], BF16, stack=stA) for i in range(2)]
    xnt_cres = [[p.res() for _ in range(8)] for _ in range(2)]
    ptr_cres = [[p.res() for _ in range(8)] for _ in range(2)]
    s_a0b = [sb(f"a_a0b{i}", [128, 1024], F32, stack=stA) for i in range(2)]
    s_qb = [sb(f"a_qb{i}", [128, 1024], F32, stack=stA) for i in range(2)]
    s_cost = [sb(f"a_cost{i}", [128, 1024], F32, stack=stA) for i in range(2)]
    s_sint = [sb(f"a_sint{i}", [128, 1024], F32, stack=stA) for i in range(2)]
    _tc = sb("a_tcos", [128, 1024], F32, stack=stA)
    _ts_ = sb("a_tsin", [128, 1024], F32, stack=stA)
    s_tcos = [_tc, _tc]
    s_tsin = [_ts_, _ts_]
    s_rope = [sb(f"a_rope{i}", [128, 1024], BF16, stack=stA) for i in range(2)]
    _sq_ = sb("a_sqbf", [128, 1024], BF16, stack=stA)
    s_sqbf = [_sq_, _sq_]
    s_nrmt = [sb(f"a_nrmt{i}", [128, 16], F32, stack=stA) for i in range(2)]
    s_asum, r_asum = sb("a_asum", [128, 16], F32, stack=stA)
    s_asq, r_asq = sb("a_asq", [128, 16], F32, stack=stA)
    s_wst = [sb(f"a_wst{i}", [128, 1024], F32, stack=stA) for i in range(2)]

    p_m = [ps(f"p_m{i}", [128, 1024], F32, stack=stA) for i in range(2)]
    p_tr = [ps(f"p_tr{i}", [128, 1024], BF16, stack=stA) for i in range(2)]


    p.step("vector", lambda e: e.memset(s_nrm[:], 0.0), writes=[r_nrm])


    wq_jobs = []
    for widx, w_ap in enumerate((wg_in, wu_in, wd_in)):
        for ch in range(32):
            wq_jobs.append((widx, w_ap, ch))
    wq_pos = [0]

    def wmean_pump(n):
        for _ in range(n):
            if wq_pos[0] >= len(wq_jobs):
                return
            widx, w_ap, ch = wq_jobs[wq_pos[0]]
            wq_pos[0] += 1
            buf, rbuf = s_wst[wq_pos[0] % 2]
            if widx < 2:
                src = w_ap[ch * 128:(ch + 1) * 128, :]
            else:
                src = w_ap.rearrange("m (c n) -> (m c) n", c=4)[
                    ch * 128:(ch + 1) * 128, :]
            dma(buf[:], src, writes=[rbuf], tag=f"wm{widx}.{ch}")
            p.step("vector",
                   lambda e, b=buf: e.tensor_reduce(
                       s_smal[:, 40:41], b[:], AX.X, ALU.add,
                       apply_absolute_value=True),
                   reads=[rbuf], writes=[r_smal], tag=f"wabs{widx}.{ch}",
                   sync_out=True)
            if ch == 0:
                def acc0(widx=widx):
                    return lambda e: e.tensor_copy(
                        s_wacc[:, widx:widx + 1], s_smal[:, 40:41])
            else:
                def acc0(widx=widx):
                    return lambda e: e.tensor_add(
                        s_wacc[:, widx:widx + 1], s_wacc[:, widx:widx + 1],
                        s_smal[:, 40:41])
            p.step("vector", acc0(), reads=[r_smal, r_wacc], writes=[r_wacc],
                   tag=f"wacc{widx}.{ch}")

    a0bufs = [s_xt[0], s_xt[1], s_a0b[0], s_a0b[1]]
    for t in range(NT):
        xbuf, rxbuf = a0bufs[t % 4]
        dy, rdy = s_dummy[t % 2]
        dma(xbuf[:], x_in[t * 128:(t + 1) * 128, :], writes=[rxbuf],
            tag=f"a0x{t}")
        p.step("scalar",
               lambda e, xb=xbuf, dy=dy: e.activation(dy[:], xb[:],
                                                      AF.Square),
               reads=[rxbuf], writes=[rdy], tag=f"a0sq{t}")
        p.step("vector",
               lambda e, dy=dy, t=t: e.tensor_reduce(
                   s_asq[:, t:t + 1], dy[:], AX.X, ALU.add),
               reads=[rdy], writes=[r_asq], tag=f"a0qr{t}")
        p.step("vector",
               lambda e, xb=xbuf, t=t: e.tensor_reduce(
                   s_asum[:, t:t + 1], xb[:], AX.X, ALU.add),
               reads=[rxbuf], writes=[r_asum], tag=f"a0sr{t}")
        wmean_pump(2)

    def batch_stats(src_sum, r_srcsum, src_sq, r_srcsq, dst_mu, r_dstmu,
                    dst_rs, r_dstrs, pre):
        p.step("vector",
               lambda e: e.tensor_scalar(dst_mu[:], src_sum[:], 1.0 / DM,
                                         None, ALU.mult),
               reads=[r_srcsum], writes=[r_dstmu], tag=pre + "mu")
        p.step("vector",
               lambda e: e.tensor_scalar(s_smal[:, 0:16], src_sq[:],
                                         1.0 / DM, None, ALU.mult),
               reads=[r_srcsq], writes=[r_smal], tag=pre + "ms")
        p.step("vector",
               lambda e: e.tensor_mul(s_smal[:, 16:32], dst_mu[:],
                                      dst_mu[:]),
               reads=[r_dstmu], writes=[r_smal], tag=pre + "mm")
        p.step("vector",
               lambda e: e.tensor_sub(s_smal[:, 0:16], s_smal[:, 0:16],
                                      s_smal[:, 16:32]),
               reads=[r_smal], writes=[r_smal], tag=pre + "v")
        p.step("scalar",
               lambda e: e.activation(s_smal[:, 16:32], s_smal[:, 0:16],
                                      AF.Ln, bias=s_cvec[:, 0:1]),
               reads=[r_smal, r_cvec], writes=[r_smal], tag=pre + "l")
        p.step("scalar",
               lambda e: e.activation(dst_rs[:], s_smal[:, 16:32], AF.Exp,
                                      scale=-0.5),
               reads=[r_smal], writes=[r_dstrs], tag=pre + "r")

    batch_stats(s_asum, r_asum, s_asq, r_asq, s_mu1, r_mu1, s_rs1, r_rs1,
                "a0")

    for phase in range(3):
        dma(s_wph[:].rearrange("p (c n) -> p c n", c=8),
            wqkv_in[:, phase * 1024:(phase + 1) * 1024].rearrange(
                "(c q) n -> q c n", q=128),
            writes=[r_wph], eng="gpsimd", tag=f"wqkv{phase}")
        bcast_row(s_pbias, r_pbias,
                  bqkv_in.rearrange("(a n) -> a n", a=1)[
                      0:1, phase * 1024:(phase + 1) * 1024], DM)

        def a_stage1(t, phase=phase):
            pr = t % 2
            xbuf, rxbuf = s_xt[pr]
            dma(xbuf[:], x_in[t * 128:(t + 1) * 128, :], writes=[rxbuf],
                tag=f"x{phase}.{t}")
            if phase != 2:
                cbuf, rcbuf = s_cost[pr]
                sbuf_, rsbuf_ = s_sint[pr]
                dma(cbuf[:], cos_in[t * 128:(t + 1) * 128, :],
                    writes=[rcbuf], tag=f"cos{phase}.{t}")
                dma(sbuf_[:], sinn_in[t * 128:(t + 1) * 128, :],
                    writes=[rsbuf_], tag=f"sin{phase}.{t}")
            xn, rxn = s_xn[pr]
            xnt, rxnt = s_xnt[pr]
            ptr, rptr = p_tr[pr]
            p.step("vector",
                   lambda e, xb=xbuf, xn=xn, t=t: e.tensor_scalar(
                       xn[:], xb[:], s_mu1[:, t:t + 1], s_rs1[:, t:t + 1],
                       ALU.subtract, ALU.mult),
                   reads=[rxbuf, r_mu1, r_rs1], writes=[rxn],
                   tag=f"xn{phase}.{t}")
            for c in range(8):
                def tr(c=c, xn=xn, ptr=ptr):
                    return lambda e: e.transpose(
                        ptr[:, c * 128:(c + 1) * 128],
                        xn[:, c * 128:(c + 1) * 128], s_ident[:])
                p.step("tensor", tr(), reads=[rxn, r_ident],
                       writes=[rptr], tag=f"trx{phase}.{t}.{c}")
                def ev(c=c, ptr=ptr, xnt=xnt):
                    return lambda e: e.activation(
                        xnt[:, c * 128:(c + 1) * 128],
                        ptr[:, c * 128:(c + 1) * 128], AF.Identity,
                        scale=s_gb1[:, c:c + 1],
                        bias=s_gb1[:, 8 + c:9 + c])
                p.step("scalar", ev(), reads=[rptr, r_gb1],
                       writes=[xnt_cres[pr][c]], tag=f"evx{phase}.{t}.{c}")
            pm, rpm = p_m[pr]
            for dinc in range(8):
                for h2 in range(2):
                    def mm(dinc=dinc, h2=h2, pm=pm, xnt=xnt):
                        return lambda e: e.matmul(
                            pm[:, h2 * 512:(h2 + 1) * 512],
                            xnt[:, dinc * 128:(dinc + 1) * 128],
                            s_wph[:, dinc * 1024 + h2 * 512:
                                  dinc * 1024 + (h2 + 1) * 512],
                            start=(dinc == 0), stop=(dinc == 7),
                            skip_group_check=True)
                    p.step("tensor", mm(),
                           reads=[xnt_cres[pr][dinc], r_wph],
                           writes=[rpm], tag=f"qkv{phase}.{t}.{dinc}.{h2}")

        def a_stage2(t, phase=phase):
            pr = t % 2
            pm, rpm = p_m[pr]
            ptr, rptr = p_tr[pr]
            if phase == 2:
                for h2 in range(2):
                    def ev2(h2=h2, t=t, pm=pm):
                        return lambda e: e.tensor_add(
                            s_v[:, t * 1024 + h2 * 512:
                                t * 1024 + (h2 + 1) * 512],
                            pm[:, h2 * 512:(h2 + 1) * 512],
                            s_pbias[:, h2 * 512:(h2 + 1) * 512])
                    p.step("vector", ev2(), reads=[rpm, r_pbias],
                           writes=[r_v], tag=f"vev{t}.{h2}")
                return
            cbuf, rcbuf = s_cost[pr]
            sbuf_, rsbuf_ = s_sint[pr]
            qb, rqb = s_qb[pr]
            p.step("vector",
                   lambda e, pm=pm, qb=qb: e.tensor_add(qb[:], pm[:],
                                                        s_pbias[:]),
                   reads=[rpm, r_pbias], writes=[rqb],
                   tag=f"qbev{phase}.{t}")
            sq, rsq = s_sqbf[pr]
            nt_, rnt_ = s_nrmt[pr]
            p.step("scalar",
                   lambda e, qb=qb, sq=sq: e.activation(sq[:], qb[:],
                                                        AF.Square),
                   reads=[rqb], writes=[rsq], tag=f"qsq{phase}.{t}")
            p.step("vector",
                   lambda e, sq=sq, nt_=nt_: e.tensor_reduce(
                       nt_[:], sq[:].rearrange("q (h c) -> q h c", h=16,
                                               c=64),
                       AX.X, ALU.add),
                   reads=[rsq], writes=[rnt_], tag=f"qnr{phase}.{t}",
                   sync_out=True)
            c0 = phase * 16
            p.step("vector",
                   lambda e, c0=c0, nt_=nt_: e.tensor_max(
                       s_nrm[:, c0:c0 + 16], s_nrm[:, c0:c0 + 16], nt_[:]),
                   reads=[rnt_, r_nrm], writes=[r_nrm],
                   tag=f"qmx{phase}.{t}")
            tc, rtc = s_tcos[pr]
            tsn, rtsn = s_tsin[pr]
            rp, rrp = s_rope[pr]
            p.step("vector",
                   lambda e, cbuf=cbuf, qb=qb, tc=tc: e.tensor_mul(
                       tc[:], qb[:], cbuf[:]),
                   reads=[rqb, rcbuf], writes=[rtc], tag=f"rc{phase}.{t}")
            qv = qb[:].rearrange("q (h t2 c) -> q h t2 c", h=16, t2=2, c=32)
            sv = sbuf_[:].rearrange("q (h t2 c) -> q h t2 c", h=16, t2=2,
                                    c=32)
            tv = tsn[:].rearrange("q (h t2 c) -> q h t2 c", h=16, t2=2,
                                  c=32)
            p.step("vector",
                   lambda e, qv=qv, sv=sv, tv=tv: e.tensor_mul(
                       tv[:, :, 0, :], qv[:, :, 1, :], sv[:, :, 0, :]),
                   reads=[rqb, rsbuf_], writes=[rtsn],
                   tag=f"rs0{phase}.{t}")
            p.step("vector",
                   lambda e, qv=qv, sv=sv, tv=tv: e.tensor_mul(
                       tv[:, :, 1, :], qv[:, :, 0, :], sv[:, :, 1, :]),
                   reads=[rqb, rsbuf_], writes=[rtsn],
                   tag=f"rs1{phase}.{t}")
            p.step("vector",
                   lambda e, tc=tc, tsn=tsn, rp=rp: e.tensor_add(
                       rp[:], tc[:], tsn[:]),
                   reads=[rtc, rtsn], writes=[rrp], tag=f"ra{phase}.{t}")
            for c in range(8):
                def tr2(c=c, rp=rp, ptr=ptr):
                    return lambda e: e.transpose(
                        ptr[:, c * 128:(c + 1) * 128],
                        rp[:, c * 128:(c + 1) * 128], s_ident[:])
                p.step("tensor", tr2(), reads=[rrp, r_ident],
                       writes=[rptr], tag=f"trq{phase}.{t}.{c}")
            dstT, rdstT = (s_qt, r_qt) if phase == 0 else (s_kt, r_kt)
            for half in range(2):
                dv = dstT[:].rearrange("d (c q) -> d c q", c=8, q=S)[
                    :, half * 4:(half + 1) * 4, t * 128:(t + 1) * 128]
                def cp(half=half, dv=dv, ptr=ptr):
                    return lambda e: e.activation(
                        dv, ptr[:, half * 512:(half + 1) * 512],
                        AF.Identity)
                p.step("scalar", cp(), reads=[rptr],
                       writes=[rdstT], tag=f"qtc{phase}.{t}.{half}")

        for t in range(NT):
            a_stage1(t)
            if t >= 1:
                a_stage2(t - 1)
            wmean_pump(2)
        a_stage2(NT - 1)
    wmean_pump(100)

    p.step("tensor",
           lambda e: e.matmul(p_mix[0:32, 0:128], s_nrm[:], s_identf[:],
                              start=True, stop=True),
           reads=[r_nrm, r_identf], writes=[r_pmix], tag="nrmT")
    s_hn, r_hn = sb("a_hn", [32, 1], F32, stack=stA)
    p.step("vector",
           lambda e: e.tensor_reduce(s_hn[:], p_mix[0:32, 0:128], AX.X,
                                     ALU.max),
           reads=[r_pmix], writes=[r_hn], tag="nrmmax", sync_out=True)
    dma(scr_vec[0:1, 0:32].rearrange("a (q b) -> q (a b)", q=32, b=1),
        s_hn[:], reads=[r_hn], writes=[r_svec], tag="bnc0")
    dma(s_hrow[:], scr_vec[0:1, 0:32], reads=[r_svec], writes=[r_hrow],
        tag="bnc1")
    p.step("vector",
           lambda e: e.tensor_add(s_hrow[0:1, 0:16], s_hrow[0:1, 0:16],
                                  s_hrow[0:1, 16:32]),
           reads=[r_hrow], writes=[r_hrow], tag="bsum")
    p.step("vector",
           lambda e: e.tensor_scalar(s_hrow[0:1, 0:16], s_hrow[0:1, 0:16],
                                     -1.0 / 16.0, None, ALU.mult),
           reads=[r_hrow], writes=[r_hrow], tag="bneg",
           flush=lambda e: e.tensor_copy(s_vfl[0:1, 0:8], s_hrow[0:1, 0:8]))
    p.step("tensor",
           lambda e: e.matmul(p_mix[:, 0:16], s_onesf[:], s_hrow[0:1, 0:16],
                              start=True, stop=True),
           reads=[r_hrow, r_onesf], writes=[r_pmix], tag="bnegT")
    p.step("vector",
           lambda e: e.tensor_copy(s_bneg[:], p_mix[:, 0:16]),
           reads=[r_pmix], writes=[r_bneg], tag="bnegC",
           flush=lambda e: e.tensor_copy(s_vfl[:, 0:1], s_bneg[:, 0:1]))

    p.step("tensor",
           lambda e: e.matmul(p_mix[0:3, 0:1], s_wacc[:, 0:3],
                              s_onescol[:], start=True, stop=True),
           reads=[r_wacc, r_onescol], writes=[r_pmix], tag="wsumT")
    p.step("vector",
           lambda e: e.tensor_copy(s_w31[:], p_mix[0:3, 0:1]),
           reads=[r_pmix], writes=[r_w31], tag="wsumC")
    p.step("tensor",
           lambda e: e.matmul(p_mix[0:1, 4:7], s_w31[:],
                              s_identf[0:3, 0:3], start=True, stop=True),
           reads=[r_w31, r_identf], writes=[r_pmix], tag="wrowT")
    p.step("vector",
           lambda e: e.tensor_copy(s_wsum[0:1, 0:3], p_mix[0:1, 4:7]),
           reads=[r_pmix], writes=[r_wsum], tag="wrowC")
    p.step("vector",
           lambda e: e.tensor_scalar(s_wrow[0:1, 0:3], s_wsum[0:1, 0:3],
                                     1.0 / (INNER * DM), 1e-5, ALU.mult,
                                     ALU.max),
           reads=[r_wsum], writes=[r_wrow], tag="wmean", sync_out=True)
    p.step("vector",
           lambda e: e.reciprocal(s_wrow[0:1, 3:6], s_wrow[0:1, 0:3]),
           reads=[r_wrow], writes=[r_wrow], tag="wrec", sync_out=True)
    p.step("vector",
           lambda e: e.reciprocal(s_wrow[0:1, 6:8], s_wrow[0:1, 3:5]),
           reads=[r_wrow], writes=[r_wrow], tag="wrec2", sync_out=True)
    p.step("vector",
           lambda e: e.reciprocal(s_wrow[0:1, 0:1], s_wrow[0:1, 5:6]),
           reads=[r_wrow], writes=[r_wrow], tag="wrec3", sync_out=True,
           flush=lambda e: e.tensor_copy(s_vfl[0:1, 0:8], s_wrow[0:1, 0:8]))
    p.step("tensor",
           lambda e: e.matmul(p_mix[:, 0:8], s_onesf[:], s_wrow[0:1, 0:8],
                              start=True, stop=True),
           reads=[r_wrow, r_onesf], writes=[r_pmix], tag="wcolT")
    p.step("vector",
           lambda e: e.tensor_copy(s_wcol[:], p_mix[:, 0:8]),
           reads=[r_pmix], writes=[r_wcol], tag="wcolC",
           flush=lambda e: e.tensor_copy(s_vfl[:, 0:1], s_wcol[:, 0:1]))

    if stage == "A":
        d_qt = dbg_out("d_qt", [128, 8 * S])
        d_kt = dbg_out("d_kt", [128, 8 * S])
        d_v = dbg_out("d_v", [128, NT * DM])
        d_bneg = dbg_out("d_bneg", [128, 16])
        d_wrow = dbg_out("d_wrow", [1, 8])
        d_mu = dbg_out("d_mu", [128, 16])
        d_rs = dbg_out("d_rs", [128, 16])
        d_wph = dbg_out("d_wph", [128, 8 * 1024])
        d_qb = dbg_out("d_qb", [128, 1024])
        d_xnt = dbg_out("d_xnt", [128, 1024])
        d_rope = dbg_out("d_rope", [128, 1024])
        d_pbias = dbg_out("d_pbias", [128, 1024])
        toks = [
            dma(d_qt, s_qt[:], reads=[r_qt], eng="gpsimd", tag="dbgq"),
            dma(d_kt, s_kt[:], reads=[r_kt], eng="gpsimd", tag="dbgk"),
            dma(d_v, s_v[:], reads=[r_v], eng="gpsimd", tag="dbgv"),
            dma(d_bneg, s_bneg[:], reads=[r_bneg], tag="dbgb"),
            dma(d_wrow, s_wrow[:], reads=[r_wrow], tag="dbgw"),
            dma(d_mu, s_mu1[:], reads=[r_mu1], tag="dbgmu"),
            dma(d_rs, s_rs1[:], reads=[r_rs1], tag="dbgrs"),
            dma(d_wph, s_wph[:], reads=[r_wph], eng="gpsimd", tag="dbgwp"),
            dma(d_qb, s_qb[1][0][:], reads=[s_qb[1][1]], tag="dbgqb"),
            dma(d_xnt, s_xnt[1][0][:], reads=xnt_cres[1], eng="gpsimd", tag="dbgxnt"),
            dma(d_rope, s_rope[1][0][:], reads=[s_rope[1][1]], eng="gpsimd", tag="dbgro"),
            dma(d_pbias, s_pbias[:], reads=[r_pbias], tag="dbgpb"),
        ]
        p.step("sync", lambda e: e.nop(), extra_deps=toks, tag="fin")
        stA.close(); pro.close()
        p.emit(nc, top)
        return nc, dbg, top

    stA.close()
    p.add_fence([r_pmix])
    pro.close()
    p.add_fence([r_wph, r_pbias, p_m[0][1], p_m[1][1],
                 r_asum, r_asq]
                + ptr_cres[0] + ptr_cres[1] + xnt_cres[0] + xnt_cres[1]
                + [b[1] for b in (s_cost + s_sint + s_wst + s_xn + s_xnt
                                  + s_a0b
                                  + s_qb + s_tcos + s_tsin + s_rope
                                  + s_sqbf + s_nrmt)])

    stB = ExitStack()
    s_pta, r_pta = sb("b_pta", [128, 16 * 512], BF16, stack=stB)
    s_ptb, r_ptb = sb("b_ptb", [128, 16 * 512], BF16, stack=stB)
    s_attn, r_attn = sb("b_attn", [128, 8 * 512], BF16, stack=stB)
    s_wob, r_wob = sb("b_wob", [128, 8 * 1024], BF16, stack=stB)
    s_bob, r_bob = sb("b_bob", [128, DM], F32, stack=stB)
    s_r16, r_r16 = sb("b_r16", [16, 512], F32, stack=stB)
    s_sel, r_sel = sb("b_sel", [16, 8 * 128], F32, stack=stB)
    s_t1, r_t1 = sb("b_t1", [128, DM], F32, stack=stB)
    s_xp = [sb(f"b_xp{i}", [128, DM], F32, stack=stB) for i in range(2)]

    dma(s_wob[:].rearrange("p (c n) -> p c n", c=8),
        wo_in.rearrange("(c q) n -> q c n", q=128),
        writes=[r_wob], eng="gpsimd", tag="ldwo")
    dma(s_sel[:], sel_in, writes=[r_sel], tag="ldsel")
    bcast2 = ExitStack()
    p_bc2, r_pbc2 = ps("p_bc2", [128, 512], F32, stack=bcast2)
    dma(s_row1[0:1, 0:DM], bo_in.rearrange("(a n) -> a n", a=1),
        writes=[r_row1], tag="ldbo")
    for j in range(2):
        def mk(j=j):
            return lambda e: e.matmul(p_bc2[:, 0:512], s_onesf[:],
                                      s_row1[0:1, j * 512:(j + 1) * 512],
                                      start=True, stop=True)
        p.step("tensor", mk(), reads=[r_row1, r_onesf], writes=[r_pbc2])
        def cp(j=j):
            return lambda e: e.tensor_copy(s_bob[:, j * 512:(j + 1) * 512],
                                           p_bc2[:, 0:512])
        p.step("vector", cp(), reads=[r_pbc2], writes=[r_bob])
    bcast2.close()
    p.add_fence([r_pbc2])

    p_stab = [ps(f"p_st{sd}", [128, 1536], F32, stack=stB) for sd in range(2)]
    p_o, r_po = ps("p_o", [128, 512], F32, stack=stB)
    p_rs, r_prs = ps("p_rs", [128, 512], F32, stack=stB)

    PT = [(s_pta, r_pta), (s_ptb, r_ptb)]

    r_ptk = [[p.res() for _ in range(16)], [p.res() for _ in range(16)]]

    for qc in range(4):
        nkt = 4 * qc + 4
        first_rs = [True]
        kts = list(range(nkt))
        groups = [kts[i:i + 3] for i in range(0, nkt, 3)]

        def emit_st_group(hp, g, qc=qc):
            for side in range(2):
                head = 2 * hp + side
                pst, rpst = p_stab[side]
                for j, kt in enumerate(g):
                    def mmst(j=j, kt=kt, side=side, hp=hp, qc=qc, pst=pst):
                        return lambda e: e.matmul(
                            pst[:, j * 512:(j + 1) * 512],
                            s_kt[side * 64:(side + 1) * 64,
                                 hp * 2048 + kt * 128:
                                 hp * 2048 + (kt + 1) * 128],
                            s_qt[side * 64:(side + 1) * 64,
                                 hp * 2048 + qc * 512:
                                 hp * 2048 + (qc + 1) * 512],
                            start=True, stop=True,
                            tile_position=(side * 64, 0),
                            skip_group_check=True)
                    p.step("tensor", mmst(), reads=[r_kt, r_qt],
                           writes=[rpst], tag=f"st{qc}.{hp}.{side}.{kt}")
                ptb, _ = PT[side]
                def expf(g=g, side=side, head=head, pst=pst, ptb=ptb):
                    ln = len(g)
                    return lambda e: e.activation(
                        ptb[:, g[0] * 512:(g[0] + ln) * 512],
                        pst[:, 0:ln * 512], AF.Exp, scale=0.125,
                        bias=s_bneg[:, head:head + 1])
                p.step("scalar", expf(), reads=[rpst, r_bneg],
                       writes=[r_ptk[side][kt2] for kt2 in g],
                       tag=f"exp{qc}.{hp}.{side}")
            for kt in g:
                if kt < 4 * qc:
                    continue
                dcol = kt * 128 - qc * 512
                for side in range(2):
                    ptb, _ = PT[side]
                    if dcol > 0:
                        def msf(kt=kt, dcol=dcol, ptb=ptb):
                            return lambda e: e.memset(
                                ptb[:, kt * 512:kt * 512 + dcol], 0.0)
                        p.step("vector", msf(), writes=[r_ptk[side][kt]],
                               tag=f"ms{qc}.{hp}.{side}.{kt}")
                    def trif(kt=kt, dcol=dcol, ptb=ptb):
                        return lambda e: e.tensor_mul(
                            ptb[:, kt * 512 + dcol:kt * 512 + dcol + 128],
                            ptb[:, kt * 512 + dcol:kt * 512 + dcol + 128],
                            s_tri[:])
                    p.step("vector", trif(), reads=[r_tri],
                           writes=[r_ptk[side][kt]],
                           tag=f"tri{qc}.{hp}.{side}.{kt}")

        def emit_pv_group(hp, g, qc=qc):
            for kt in g:
                for side in range(2):
                    head = 2 * hp + side
                    ptb, _ = PT[side]
                    def pvf(kt=kt, side=side, head=head, ptb=ptb):
                        return lambda e: e.matmul(
                            p_o[side * 64:(side + 1) * 64, :],
                            s_v[:, kt * 1024 + head * 64:
                                kt * 1024 + (head + 1) * 64],
                            ptb[:, kt * 512:(kt + 1) * 512],
                            start=(kt == 0), stop=False,
                            tile_position=(0, side * 64),
                            skip_group_check=True)
                    p.step("tensor", pvf(), reads=[r_ptk[side][kt], r_v],
                           writes=[r_po], tag=f"pv{qc}.{hp}.{side}.{kt}")
                    def rsf(kt=kt, side=side, head=head, ptb=ptb,
                            st=first_rs[0]):
                        return lambda e: e.matmul(
                            p_rs[0:16, :],
                            s_ecat[:, head * 16:(head + 1) * 16],
                            ptb[:, kt * 512:(kt + 1) * 512],
                            start=st, stop=False,
                            skip_group_check=True)
                    p.step("tensor", rsf(), reads=[r_ptk[side][kt], r_ecat],
                           writes=[r_prs], tag=f"rs{qc}.{hp}.{side}.{kt}")
                    first_rs[0] = False
            if g is groups[-1]:
                def evo(hp=hp):
                    return lambda e: e.tensor_copy(
                        s_attn[:, hp * 512:(hp + 1) * 512], p_o[:])
                p.step("vector", evo(), reads=[r_po], writes=[r_attn],
                       tag=f"aev{qc}.{hp}")

        jobs = [(hp, g) for hp in range(8) for g in groups]
        for i, (hp, g) in enumerate(jobs):
            emit_st_group(hp, g)
            if i >= 1:
                emit_pv_group(*jobs[i - 1])
        emit_pv_group(*jobs[-1])

        p.step("vector",
               lambda e: e.reciprocal(s_r16[:], p_rs[0:16, :]),
               reads=[r_prs], writes=[r_r16], tag=f"recip{qc}")
        for hp in range(8):
            def rb0(hp=hp):
                return lambda e: e.matmul(
                    p_o[:], s_sel[:, hp * 128:(hp + 1) * 128], s_r16[:],
                    start=True, stop=True, skip_group_check=True)
            p.step("tensor", rb0(), reads=[r_r16, r_sel], writes=[r_po],
                   tag=f"rb{qc}.{hp}")
            def divf(hp=hp):
                return lambda e: e.tensor_mul(
                    s_attn[:, hp * 512:(hp + 1) * 512],
                    s_attn[:, hp * 512:(hp + 1) * 512], p_o[:])
            p.step("vector", divf(), reads=[r_po, r_attn], writes=[r_attn],
                   tag=f"div{qc}.{hp}")
        p_wo = p_stab[0][0]
        r_pwo = p_stab[0][1]
        for qsub in range(4):
            t = qc * 4 + qsub
            xbuf, rxbuf = s_xt[t % 2]
            dma(xbuf[:], x_in[t * 128:(t + 1) * 128, :], writes=[rxbuf],
                tag=f"xb{t}")
            for dmc in range(8):
                for h2 in range(2):
                    def mwo(dmc=dmc, h2=h2, qsub=qsub):
                        return lambda e: e.matmul(
                            p_wo[:, h2 * 512:(h2 + 1) * 512],
                            s_attn[:, dmc * 512 + qsub * 128:
                                   dmc * 512 + (qsub + 1) * 128],
                            s_wob[:, dmc * 1024 + h2 * 512:
                                  dmc * 1024 + (h2 + 1) * 512],
                            start=(dmc == 0), stop=(dmc == 7),
                            skip_group_check=True)
                    p.step("tensor", mwo(), reads=[r_attn, r_wob],
                           writes=[r_pwo], tag=f"wo{t}.{dmc}.{h2}")
            p.step("vector",
                   lambda e: e.tensor_add(s_t1[:], p_wo[:, 0:1024],
                                          s_bob[:]),
                   reads=[r_pwo, r_bob], writes=[r_t1], tag=f"woev{t}")
            xpb, rxpb = s_xp[t % 2]
            p.step("vector",
                   lambda e, xb=xbuf, xp=xpb: e.tensor_add(xp[:], s_t1[:],
                                                           xb[:]),
                   reads=[r_t1, rxbuf], writes=[rxpb], tag=f"res1.{t}")
            dma(scr_x1[t * 128:(t + 1) * 128, :], xpb[:], reads=[rxpb],
                writes=[r_sx1[t]], tag=f"sx1.{t}")
            dy, rdy = s_dummy[t % 2]
            p.step("scalar",
                   lambda e, xp=xpb, dy=dy: e.activation(dy[:], xp[:],
                                                         AF.Square),
                   reads=[rxpb], writes=[rdy], tag=f"bsq{t}")
            p.step("vector",
                   lambda e, dy=dy, t=t: e.tensor_reduce(
                       s_c1sq[:, t:t + 1], dy[:], AX.X, ALU.add),
                   reads=[rdy], writes=[r_c1sq], tag=f"bqr{t}")
            p.step("vector",
                   lambda e, xp=xpb, t=t: e.tensor_reduce(
                       s_c1sum[:, t:t + 1], xp[:], AX.X, ALU.add),
                   reads=[rxpb], writes=[r_c1sum], tag=f"bsr{t}")

    if stage == "B":
        d_x1 = dbg_out("d_x1", [S, DM])
        toks = [dma(d_x1, scr_x1, reads=r_sx1[:], tag="dbgx1")]
        p.step("sync", lambda e: e.nop(), extra_deps=toks, tag="fin")
        stB.close()
        p.emit(nc, top)
        return nc, dbg, top

    stB.close()
    p.add_fence([r_pta, r_ptb, r_attn, r_wob, r_bob, r_r16, r_sel, r_t1,
                 r_po, r_prs, p_stab[0][1], p_stab[1][1],
                 s_xp[0][1], s_xp[1][1]])
    stQKV.close()
    p.add_fence([r_qt, r_kt, r_v])

    stC = ExitStack()
    s_w1b, r_w1b = sb("c_w1b", [128, 8 * INNER], BF16, stack=stC)
    s_w2b, r_w2b = sb("c_w2b", [128, 32 * DM], BF16, stack=stC)
    s_mu2, r_mu2 = sb("c_mu2", [128, 16], F32, stack=stC)
    s_rs2, r_rs2 = sb("c_rs2", [128, 16], F32, stack=stC)
    s_c1x = [sb(f"c_x{i}", [128, DM], F32, stack=stC) for i in range(4)]
    _xn2 = sb("c_xn2", [128, DM], BF16, stack=stC)
    s_xn2 = [_xn2, _xn2]
    s_xn2t = [sb(f"c_xn2t{i}", [128, 8 * 256], BF16, stack=stC)
              for i in range(2)]
    s_h1c = [sb(f"c_h1c{i}", [128, 256], BF16, stack=stC) for i in range(3)]
    _xpp = sb("c_xpp", [128, DM], F32, stack=stC)
    s_xpp = [_xpp, _xpp]
    s_wq1, r_wq1 = sb("c_wq1", [128, 1024], F32, stack=stC)
    s_wqbf, r_wqbf = sb("c_wqbf", [128, 1024], BF16, stack=stC)
    s_wtx, r_wtx = sb("c_wtx", [128, 1024], BF16, stack=stC)
    s_wstc = [sb(f"c_wst{i}", [128, 1024], F32, stack=stC) for i in range(2)]
    p_h = [ps(f"p_h{i}", [128, 512], F32, stack=stC) for i in range(3)]
    p_mlp = [ps(f"p_mlp{i}", [128, 1024], F32, stack=stC) for i in range(2)]
    p_trc, r_ptrc = ps("p_trc", [128, 1024], BF16, stack=stC)

    dma(s_w1b[:].rearrange("p (c n) -> p c n", c=8),
        w1_in.rearrange("(c q) n -> q c n", q=128),
        writes=[r_w1b], eng="gpsimd", tag="ldw1")
    dma(s_w2b[:].rearrange("p (c n) -> p c n", c=32),
        w2_in.rearrange("(c q) n -> q c n", q=128),
        writes=[r_w2b], eng="gpsimd", tag="ldw2")

    wq2_jobs = []
    for widx, (w_ap, scr, rscr) in enumerate(
            ((wg_in, scr_wgt, r_swg), (wu_in, scr_wut, r_swu),
             (wd_in, scr_wdt, r_swd))):
        for ch in range(32):
            wq2_jobs.append((widx, w_ap, scr, rscr, ch))
    wq2_pos = [0]

    def wq2_pump(n):
        for _ in range(n):
            if wq2_pos[0] >= len(wq2_jobs):
                return
            widx, w_ap, scr, rscr, ch = wq2_jobs[wq2_pos[0]]
            wq2_pos[0] += 1
            buf, rbuf = s_wstc[wq2_pos[0] % 2]
            scol = 3 + widx if widx < 2 else 5
            if widx < 2:
                src = w_ap[ch * 128:(ch + 1) * 128, :]
                nin = 1024
            else:
                src = w_ap[(ch % 8) * 128:((ch % 8) + 1) * 128,
                           (ch // 8) * 1024:((ch // 8) + 1) * 1024]
                nin = 1024
            dma(buf[:], src, writes=[rbuf], tag=f"w2l{widx}.{ch}")
            def q1(buf=buf, scol=scol):
                return lambda e: e.tensor_scalar(
                    s_wq1[:], buf[:], s_wcol[:, scol:scol + 1], None,
                    ALU.mult)
            p.step("vector", q1(), reads=[rbuf, r_wcol], writes=[r_wq1],
                   tag=f"w2q1.{widx}.{ch}")
            p.step("vector",
                   lambda e: e.tensor_scalar(s_wq1[:], s_wq1[:], MAGIC,
                                             None, ALU.add),
                   reads=[r_wq1], writes=[r_wq1], tag=f"w2q1b.{widx}.{ch}")
            p.step("vector",
                   lambda e: e.tensor_scalar(s_wq1[:], s_wq1[:], MAGIC - 1.0,
                                             MAGIC + 1.0, ALU.max, ALU.min),
                   reads=[r_wq1], writes=[r_wq1], tag=f"w2q2.{widx}.{ch}")
            p.step("vector",
                   lambda e: e.tensor_scalar(s_wqbf[:], s_wq1[:], MAGIC,
                                             None, ALU.subtract),
                   reads=[r_wq1], writes=[r_wqbf], tag=f"w2q3.{widx}.{ch}")
            for c in range(8):
                def tr(c=c):
                    return lambda e: e.transpose(
                        p_trc[:, c * 128:(c + 1) * 128],
                        s_wqbf[:, c * 128:(c + 1) * 128], s_ident[:])
                p.step("tensor", tr(), reads=[r_wqbf, r_ident],
                       writes=[r_ptrc], tag=f"w2t.{widx}.{ch}.{c}")
            for half in range(2):
                def cp(half=half):
                    return lambda e: e.tensor_copy(
                        s_wtx[:, half * 512:(half + 1) * 512],
                        p_trc[:, half * 512:(half + 1) * 512])
                p.step("vector", cp(), reads=[r_ptrc], writes=[r_wtx],
                       tag=f"w2c.{widx}.{ch}.{half}")
            if widx < 2:
                dst = scr[:, ch * 128:(ch + 1) * 128].rearrange(
                    "(c q) o -> q c o", q=128)
            else:
                dst = scr[(ch // 8) * 1024:(ch // 8 + 1) * 1024,
                          (ch % 8) * 128:((ch % 8) + 1) * 128].rearrange(
                    "(c q) o -> q c o", q=128)
            dma(dst, s_wtx[:].rearrange("p (c o) -> p c o", c=8),
                reads=[r_wtx], writes=[rscr],
                tag=f"w2s{widx}.{ch}")

    batch_stats(s_c1sum, r_c1sum, s_c1sq, r_c1sq, s_mu2, r_mu2, s_rs2,
                r_rs2, "c1")

    xtiles_by_qg = {}

    def c2_prep(qg):
        xnt2, rxnt2 = s_xn2t[qg % 2]
        xtiles = []
        for qs in range(2):
            t = qg * 2 + qs
            xbuf, rxbuf = s_c1x[(qg % 2) * 2 + qs]
            dma(xbuf[:], scr_x1[t * 128:(t + 1) * 128, :], reads=[r_sx1[t]],
                writes=[rxbuf], tag=f"c2x{t}")
            xtiles.append((xbuf, rxbuf))
            xn2, rxn2 = s_xn2[qs]
            p.step("vector",
                   lambda e, xb=xbuf, t=t, xn2=xn2: e.tensor_scalar(
                       xn2[:], xb[:], s_mu2[:, t:t + 1],
                       s_rs2[:, t:t + 1], ALU.subtract, ALU.mult),
                   reads=[rxbuf, r_mu2, r_rs2], writes=[rxn2],
                   tag=f"c2n{t}")
            for c in range(8):
                def tr(c=c, xn2=xn2):
                    return lambda e: e.transpose(
                        p_trc[:, c * 128:(c + 1) * 128],
                        xn2[:, c * 128:(c + 1) * 128], s_ident[:])
                p.step("tensor", tr(), reads=[rxn2, r_ident],
                       writes=[r_ptrc], tag=f"c2t{t}.{c}")
            for c in range(8):
                def ev(c=c, qs=qs, xnt2=xnt2):
                    return lambda e: e.activation(
                        xnt2[:, c * 256 + qs * 128:c * 256 + qs * 128 + 128],
                        p_trc[:, c * 128:(c + 1) * 128], AF.Identity,
                        scale=s_gb2[:, c:c + 1], bias=s_gb2[:, 8 + c:9 + c])
                p.step("scalar", ev(), reads=[r_ptrc, r_gb2],
                       writes=[rxnt2], tag=f"c2e{t}.{c}")
        xtiles_by_qg[qg] = xtiles

    def c2_w1(qg, ic):
        xnt2, rxnt2 = s_xn2t[qg % 2]
        ph, rph = p_h[ic % 3]
        h1c, rh1c = s_h1c[ic % 3]
        for dinc in range(8):
            def m1(dinc=dinc, ic=ic, ph=ph, xnt2=xnt2):
                return lambda e: e.matmul(
                    ph[:, 0:256],
                    s_w1b[:, dinc * INNER + ic * 128:
                          dinc * INNER + (ic + 1) * 128],
                    xnt2[:, dinc * 256:(dinc + 1) * 256],
                    start=(dinc == 0), stop=(dinc == 7),
                    skip_group_check=True)
            p.step("tensor", m1(), reads=[r_w1b, rxnt2], writes=[rph],
                   tag=f"w1.{qg}.{ic}.{dinc}")
        def gel(ic=ic, ph=ph, h1c=h1c):
            return lambda e: e.activation(h1c[:], ph[:, 0:256], AF.Gelu,
                                          bias=s_b1c[:, ic:ic + 1])
        p.step("scalar", gel(), reads=[rph, r_b1c], writes=[rh1c],
               tag=f"gel{qg}.{ic}")

    def c2_w2(qg, ic):
        h1c, rh1c = s_h1c[ic % 3]
        for qs in range(2):
            pm, rpm = p_mlp[qs]
            for h2 in range(2):
                def m2(ic=ic, qs=qs, h2=h2, pm=pm, h1c=h1c):
                    return lambda e: e.matmul(
                        pm[:, h2 * 512:(h2 + 1) * 512],
                        h1c[:, qs * 128:(qs + 1) * 128],
                        s_w2b[:, ic * DM + h2 * 512:
                              ic * DM + (h2 + 1) * 512],
                        start=(ic == 0), stop=(ic == 31),
                        skip_group_check=True)
                p.step("tensor", m2(), reads=[rh1c, r_w2b],
                       writes=[rpm], tag=f"w2m{qg}.{ic}.{qs}.{h2}")

    def c2_evac(qg):
        for qs in range(2):
            t = qg * 2 + qs
            pm, rpm = p_mlp[qs]
            xb, rxb = xtiles_by_qg[qg][qs]
            xpp, rxpp = s_xpp[qs]
            x0b, rx0b = s_xt[qs]
            dma(x0b[:], x_in[t * 128:(t + 1) * 128, :], writes=[rx0b],
                tag=f"c2x0{t}")
            p.step("vector",
                   lambda e, pm=pm, xpp=xpp: e.tensor_add(
                       xpp[:], pm[:], s_b2b[:]),
                   reads=[rpm, r_b2b], writes=[rxpp], tag=f"c2b{t}")
            p.step("vector",
                   lambda e, xpp=xpp, xb=xb: e.tensor_add(xpp[:], xpp[:],
                                                          xb[:]),
                   reads=[rxb, rxpp], writes=[rxpp], tag=f"c2r{t}")
            p.step("vector",
                   lambda e, xpp=xpp, x0b=x0b: e.tensor_add(xpp[:], xpp[:],
                                                            x0b[:]),
                   reads=[rx0b, rxpp], writes=[rxpp], tag=f"c2r2{t}")
            dma(scr_x1[t * 128:(t + 1) * 128, :], xpp[:], reads=[rxpp],
                writes=[r_sx1[t]], tag=f"c2s{t}")

    NIC = INNER // 128
    c2_prep(0)
    jobs = [(qg, ic) for qg in range(8) for ic in range(NIC)]
    DEPTH = 2
    for i, (qg, ic) in enumerate(jobs):
        if i >= DEPTH:
            pqg, pic = jobs[i - DEPTH]
            c2_w2(pqg, pic)
            if pic == NIC - 1:
                c2_evac(pqg)
        if ic == NIC - 1 and qg < 7:
            c2_prep(qg + 1)
        c2_w1(qg, ic)
        wq2_pump(1)
    for i in range(len(jobs) - DEPTH, len(jobs)):
        pqg, pic = jobs[i]
        c2_w2(pqg, pic)
        if pic == NIC - 1:
            c2_evac(pqg)
    wq2_pump(200)


    stC.close()
    p.add_fence([r_w1b, r_w2b, r_wq1, r_wqbf, r_wtx, _xn2[1],
                 p_h[0][1], p_h[1][1], p_h[2][1], p_mlp[0][1], p_mlp[1][1], r_ptrc]
                + [b[1] for b in (s_c1x + s_xpp + s_wstc + s_xn2t
                                  + s_h1c)])

    stD = ExitStack()
    s_wgtb, r_wgtb = sb("d_wgtb", [128, 8 * INNER], BF16, stack=stD)
    s_wutb, r_wutb = sb("d_wutb", [128, 8 * INNER], BF16, stack=stD)
    s_x2t = [sb(f"d_x2t{i}", [128, DM], F32, stack=stD) for i in range(2)]
    _q1 = sb("d_q1", [128, DM], F32, stack=stD)
    s_q1 = [_q1, _q1]
    s_xq = [sb(f"d_xq{i}", [128, DM], BF16, stack=stD) for i in range(2)]
    s_xqt = [sb(f"d_xqt{i}", [128, DM], BF16, stack=stD) for i in range(3)]
    s_sg, r_sg = sb("d_sg", [128, 512], F32, stack=stD)
    s_h, r_h = sb("d_h", [128, INNER], F32, stack=stD)
    _q2 = sb("d_q2", [128, 1024], F32, stack=stD)
    s_q2 = [_q2, _q1]
    s_xq2 = [sb(f"d_xq2{i}", [128, 1024], BF16, stack=stD) for i in range(2)]
    s_hm, r_hm = sb("d_hm", [128, 16], F32, stack=stD)
    p_g = [ps(f"p_g{i}", [128, 512], F32, stack=stD) for i in range(2)]
    p_u = [ps(f"p_u{i}", [128, 512], F32, stack=stD) for i in range(2)]
    p_trd = [ps(f"p_trd{i}", [128, 1024], BF16, stack=stD) for i in range(3)]

    r_wgtc = [p.res() for _ in range(8)]
    r_wutc = [p.res() for _ in range(8)]
    for c in range(8):
        dma(s_wgtb[:, c * INNER:(c + 1) * INNER],
            scr_wgt[c * 128:(c + 1) * 128, :],
            reads=[r_swg], writes=[r_wgtc[c]], tag=f"ldwgt{c}")
        dma(s_wutb[:, c * INNER:(c + 1) * INNER],
            scr_wut[c * 128:(c + 1) * 128, :],
            reads=[r_swu], writes=[r_wutc[c]], tag=f"ldwut{c}")




    for t in range(NT):
        xbuf, rxbuf = s_x2t[t % 2]
        dy, rdy = s_dummy[t % 2]
        dma(xbuf[:], scr_x1[t * 128:(t + 1) * 128, :], reads=[r_sx1[t]],
            writes=[rxbuf], tag=f"c3x{t}")
        p.step("scalar",
               lambda e, xb=xbuf, dy=dy: e.activation(dy[:], xb[:],
                                                      AF.Square),
               reads=[rxbuf], writes=[rdy], tag=f"c3sq{t}")
        p.step("vector",
               lambda e, dy=dy, t=t: e.tensor_reduce(
                   s_c3sq[:, t:t + 1], dy[:], AX.X, ALU.add),
               reads=[rdy], writes=[r_c3sq], tag=f"c3qr{t}")
        p.step("vector",
               lambda e, xb=xbuf, t=t: e.tensor_reduce(
                   s_c3sum[:, t:t + 1], xb[:], AX.X, ALU.add),
               reads=[rxbuf], writes=[r_c3sum], tag=f"c3sr{t}")
        p.step("vector",
               lambda e, t=t: e.tensor_scalar(
                   s_mu3[:, t:t + 1], s_c3sum[:, t:t + 1], 1.0 / DM, None,
                   ALU.mult),
               reads=[r_c3sum], writes=[r_mu3], tag=f"c3mu{t}",
               sync_out=True)
        p.step("vector",
               lambda e, xb=xbuf, dy=dy, t=t: e.tensor_scalar(
                   dy[:], xb[:], s_mu3[:, t:t + 1], None, ALU.subtract),
               reads=[rxbuf, r_mu3, rdy], writes=[rdy], tag=f"c3d{t}")
        p.step("vector",
               lambda e, dy=dy, t=t: e.tensor_reduce(
                   s_axm[:, t:t + 1], dy[:], AX.X, ALU.max,
                   apply_absolute_value=True),
               reads=[rdy], writes=[r_axm], tag=f"c3am{t}")
    p.step("vector",
           lambda e: e.tensor_scalar(s_smal[:, 16:32], s_c3sq[:], 1.0 / DM,
                                     None, ALU.mult),
           reads=[r_c3sq], writes=[r_smal], tag="c3ms")
    p.step("vector",
           lambda e: e.tensor_mul(s_smal[:, 32:48], s_mu3[:], s_mu3[:]),
           reads=[r_mu3, r_smal], writes=[r_smal], tag="c3m2")
    p.step("vector",
           lambda e: e.scalar_tensor_tensor(
               s_smal[:, 16:32], s_smal[:, 16:32], LNEPS,
               s_smal[:, 32:48], ALU.add, ALU.subtract),
           reads=[r_smal], writes=[r_smal], tag="c3vp", sync_out=True)
    p.step("vector",
           lambda e: e.reciprocal(s_smal[:, 32:48], s_smal[:, 16:32]),
           reads=[r_smal], writes=[r_smal], tag="c3rc", sync_out=True)
    p.step("scalar",
           lambda e: e.activation(s_smal[:, 32:48], s_smal[:, 32:48],
                                  AF.Sqrt),
           reads=[r_smal], writes=[r_smal], tag="c3rq", sync_out=True)
    p.step("vector",
           lambda e: e.tensor_mul(s_smal[:, 0:16], s_smal[:, 32:48],
                                  s_smal[:, 32:48]),
           reads=[r_smal], writes=[r_smal], tag="c3n1")
    p.step("vector",
           lambda e: e.tensor_mul(s_smal[:, 0:16], s_smal[:, 0:16],
                                  s_smal[:, 16:32]),
           reads=[r_smal], writes=[r_smal], tag="c3n2")
    p.step("vector",
           lambda e: e.tensor_scalar(s_smal[:, 0:16], s_smal[:, 0:16],
                                     -0.5, 1.5, ALU.mult, ALU.add),
           reads=[r_smal], writes=[r_smal], tag="c3n3")
    p.step("vector",
           lambda e: e.tensor_mul(s_rs3[:], s_smal[:, 32:48],
                                  s_smal[:, 0:16]),
           reads=[r_smal], writes=[r_rs3], tag="c3n4", sync_out=True)
    p.step("vector",
           lambda e: e.tensor_mul(s_axm[:], s_axm[:], s_rs3[:]),
           reads=[r_rs3, r_axm], writes=[r_axm], tag="c3ax")
    p.step("vector",
           lambda e: e.tensor_scalar(s_smal[:, 0:16], s_axm[:], 1e-5, None,
                                     ALU.max),
           reads=[r_axm], writes=[r_smal], tag="dqmc")
    p.step("vector",
           lambda e: e.reciprocal(s_smal[:, 16:32], s_smal[:, 0:16]),
           reads=[r_smal], writes=[r_smal], tag="dqrc", sync_out=True)
    p.step("vector",
           lambda e: e.tensor_scalar(s_dsx[:], s_smal[:, 16:32], 127.0,
                                     None, ALU.mult),
           reads=[r_smal], writes=[r_dsx], tag="dqsx")
    p.step("vector",
           lambda e: e.tensor_scalar(s_smal[:, 32:48], s_smal[:, 0:16],
                                     1.0 / 127.0, None, ALU.mult),
           reads=[r_smal], writes=[r_smal], tag="dqos")
    p.step("vector",
           lambda e: e.tensor_scalar(s_dcg[:], s_smal[:, 32:48],
                                     s_wcol[:, 6:7], None, ALU.mult),
           reads=[r_smal, r_wcol], writes=[r_dcg], tag="dqcg")
    p.step("vector",
           lambda e: e.tensor_scalar(s_dcu[:], s_smal[:, 32:48],
                                     s_wcol[:, 7:8], None, ALU.mult),
           reads=[r_smal, r_wcol], writes=[r_dcu], tag="dqcu")

    def d2_pre(t):
        xbuf, rxbuf = s_x2t[t % 2]
        dma(xbuf[:], scr_x1[t * 128:(t + 1) * 128, :], reads=[r_sx1[t]],
            writes=[rxbuf], tag=f"d2x{t}")
        p.step("vector",
               lambda e, xb=xbuf, t=t: e.tensor_scalar(
                   xb[:], xb[:], s_mu3[:, t:t + 1], s_rs3[:, t:t + 1],
                   ALU.subtract, ALU.mult),
               reads=[r_mu3, r_rs3, rxbuf], writes=[rxbuf], tag=f"c3n{t}")
        dma(scr_x2[t * 128:(t + 1) * 128, :], xbuf[:], reads=[rxbuf],
            writes=[r_sx2[t]], tag=f"c3s{t}")

    def d2_stage1a(t):
        xbuf, rxbuf = s_x2t[t % 2]
        q1, rq1 = s_q1[t % 2]
        xq, rxq = s_xq[t % 2]
        p.step("scalar",
               lambda e, xb=xbuf, q1=q1, t=t: e.activation(
                   q1[:], xb[:], AF.Identity, scale=s_dsx[:, t:t + 1]),
               reads=[rxbuf, r_dsx], writes=[rq1], tag=f"d2q1{t}")
        p.step("vector",
               lambda e, q1=q1: e.tensor_scalar(q1[:], q1[:], MAGIC, None,
                                                ALU.add),
               reads=[rq1], writes=[rq1], tag=f"d2q1b{t}")
        p.step("vector",
               lambda e, q1=q1, xq=xq: e.tensor_scalar(
                   xq[:], q1[:], MAGIC, None, ALU.subtract),
               reads=[rq1], writes=[rxq], tag=f"d2q2{t}")

    def d2_stage1b(t):
        pr = t % 3
        xq, rxq = s_xq[t % 2]
        xqt, rxqt = s_xqt[pr]
        ptrd, rptrd = p_trd[pr]
        for c in range(8):
            def tr(c=c, xq=xq, ptrd=ptrd):
                return lambda e: e.transpose(ptrd[:, c * 128:(c + 1) * 128],
                                             xq[:, c * 128:(c + 1) * 128],
                                             s_ident[:])
            p.step("tensor", tr(), reads=[rxq, r_ident], writes=[rptrd],
                   tag=f"d2t{t}.{c}")
        for half in range(2):
            def cp(half=half, xqt=xqt, ptrd=ptrd):
                return lambda e: e.tensor_copy(
                    xqt[:, half * 512:(half + 1) * 512],
                    ptrd[:, half * 512:(half + 1) * 512])
            p.step("vector", cp(), reads=[rptrd], writes=[rxqt],
                   tag=f"d2tc{t}.{half}")

    def d2_stage2(t):
        pr = t % 3
        xqt, rxqt = s_xqt[pr]
        for oc in range(8):
            pg, rpg = p_g[oc % 2]
            pu, rpu = p_u[oc % 2]
            for dinc in range(8):
                def mg(dinc=dinc, oc=oc, pg=pg, xqt=xqt):
                    return lambda e: e.matmul(
                        pg[:],
                        xqt[:, dinc * 128:(dinc + 1) * 128],
                        s_wgtb[:, dinc * INNER + oc * 512:
                               dinc * INNER + (oc + 1) * 512],
                        start=(dinc == 0), stop=(dinc == 7),
                        skip_group_check=True)
                p.step("tensor", mg(), reads=[rxqt, r_wgtc[dinc]],
                       writes=[rpg], tag=f"d2g{t}.{oc}.{dinc}")
                def mu_(dinc=dinc, oc=oc, pu=pu, xqt=xqt):
                    return lambda e: e.matmul(
                        pu[:],
                        xqt[:, dinc * 128:(dinc + 1) * 128],
                        s_wutb[:, dinc * INNER + oc * 512:
                               dinc * INNER + (oc + 1) * 512],
                        start=(dinc == 0), stop=(dinc == 7),
                        skip_group_check=True)
                p.step("tensor", mu_(), reads=[rxqt, r_wutc[dinc]],
                       writes=[rpu], tag=f"d2u{t}.{oc}.{dinc}")
            def sil(pg=pg, t=t):
                return lambda e: e.activation(s_sg[:], pg[:], AF.Silu,
                                              scale=s_dcg[:, t:t + 1])
            p.step("scalar", sil(), reads=[rpg, r_dcg], writes=[r_sg],
                   tag=f"d2s{t}.{oc}")
            def hmul(oc=oc, pu=pu, t=t):
                return lambda e: e.scalar_tensor_tensor(
                    s_h[:, oc * 512:(oc + 1) * 512], pu[:],
                    s_dcu[:, t:t + 1], s_sg[:], ALU.mult, ALU.mult)
            p.step("vector", hmul(), reads=[rpu, r_dcu, r_sg],
                   writes=[r_h], tag=f"d2h{t}.{oc}")
            def hab(oc=oc, t=t):
                return lambda e: e.tensor_reduce(
                    s_smal[:, 8 + (oc % 2):9 + (oc % 2)],
                    s_h[:, oc * 512:(oc + 1) * 512], AX.X, ALU.max,
                    apply_absolute_value=True)
            p.step("vector", hab(), reads=[r_h], writes=[r_smal],
                   tag=f"d2ha{t}.{oc}", sync_out=True)
            def habm(oc=oc, t=t):
                if oc == 0:
                    return lambda e: e.tensor_copy(s_hm[:, t:t + 1],
                                                   s_smal[:, 8:9])
                return lambda e: e.tensor_max(
                    s_hm[:, t:t + 1], s_hm[:, t:t + 1],
                    s_smal[:, 8 + (oc % 2):9 + (oc % 2)])
            p.step("vector", habm(), reads=[r_smal, r_hm], writes=[r_hm],
                   tag=f"d2hm{t}.{oc}")
        p.step("vector",
               lambda e, t=t: e.tensor_scalar(s_smal[:, 1:2],
                                              s_hm[:, t:t + 1], 1e-5, None,
                                              ALU.max),
               reads=[r_hm], writes=[r_smal], tag=f"d2mc2{t}")
        p.step("vector",
               lambda e: e.reciprocal(s_smal[:, 2:3], s_smal[:, 1:2]),
               reads=[r_smal], writes=[r_smal], tag=f"d2rc2{t}",
               sync_out=True)
        p.step("vector",
               lambda e: e.tensor_scalar(s_smal[:, 3:4], s_smal[:, 2:3],
                                         127.0, None, ALU.mult),
               reads=[r_smal], writes=[r_smal], tag=f"d2sx2{t}",
               sync_out=True)
        p.step("vector",
               lambda e: e.tensor_scalar(s_smal[:, 4:5], s_smal[:, 1:2],
                                         1.0 / 127.0, None, ALU.mult),
               reads=[r_smal], writes=[r_smal], tag=f"d2os2{t}")
        p.step("vector",
               lambda e, t=t: e.tensor_mul(s_cbd[:, t:t + 1],
                                           s_smal[:, 4:5], s_wcol[:, 0:1]),
               reads=[r_smal, r_wcol], writes=[r_cbd], tag=f"d2cd{t}")
        for oc in range(4):
            xq2, rxq2 = s_xq2[oc % 2]
            q2, rq2 = s_q2[oc % 2]
            def q1f(oc=oc, q2=q2):
                return lambda e: e.activation(
                    q2[:], s_h[:, oc * 1024:(oc + 1) * 1024], AF.Identity,
                    scale=s_smal[:, 3:4])
            p.step("scalar", q1f(), reads=[r_h, r_smal], writes=[rq2],
                   tag=f"d2p1{t}.{oc}")
            p.step("vector",
                   lambda e, q2=q2: e.tensor_scalar(q2[:], q2[:], MAGIC,
                                                    None, ALU.add),
                   reads=[rq2], writes=[rq2], tag=f"d2p1b{t}.{oc}")
            p.step("vector", lambda e, xq2=xq2, q2=q2: e.tensor_scalar(
                xq2[:], q2[:], MAGIC, None,
                ALU.subtract), reads=[rq2], writes=[rxq2],
                tag=f"d2p2{t}.{oc}")
            dma(scr_xq2[t * 128:(t + 1) * 128,
                        oc * 1024:(oc + 1) * 1024], xq2[:],
                reads=[rxq2], writes=[r_sxq2[t]], tag=f"d2sx{t}.{oc}")

    d2_pre(0)
    d2_stage1a(0)
    d2_stage1b(0)
    d2_pre(1)
    d2_stage1a(1)
    d2_stage1b(1)
    for t in range(NT):
        if t + 2 < NT:
            d2_pre(t + 2)
            d2_stage1a(t + 2)
        d2_stage2(t)
        if t + 2 < NT:
            d2_stage1b(t + 2)

    if stage == "C":
        d_x2 = dbg_out("d_x2", [S, DM])
        d_wgt = dbg_out("d_wgt", [DM, INNER], BF16)
        d_wdt = dbg_out("d_wdt", [INNER, DM], BF16)
        toks = [dma(d_x2, scr_x2, reads=r_sx2[:], tag="dbgx2"),
                dma(d_wgt, scr_wgt, reads=[r_swg], tag="dbgwgt"),
                dma(d_wdt, scr_wdt, reads=[r_swd], tag="dbgwdt")]
        p.step("sync", lambda e: e.nop(), extra_deps=toks, tag="fin")
        p.emit(nc, top)
        return nc, dbg, top


    stD.close()
    p.add_fence(r_wgtc + r_wutc + [r_sg, r_h, r_hm,
                 p_g[0][1], p_g[1][1], p_u[0][1], p_u[1][1],
                 p_trd[0][1], p_trd[1][1], p_trd[2][1]]
                + [b[1] for b in (s_x2t + s_q1 + s_xq + s_xqt + s_xq2
                                  + [_q2])])

    stE = ExitStack()
    s_wdtb, r_wdtb = sb("e_wdtb", [128, 32 * DM], BF16, stack=stE)
    s_xq2t = [sb(f"e_xq2t{i}", [128, INNER], BF16, stack=stE)
              for i in range(2)]
    s_xq2T = [sb(f"e_xq2T{i}", [128, 32 * 128], BF16, stack=stE)
              for i in range(4)]
    s_ex2 = [sb(f"e_x2{i}", [128, DM], F32, stack=stE) for i in range(4)]
    s_y, r_y = sb("e_y", [128, DM], F32, stack=stE)
    s_o = [sb(f"e_o{i}", [128, DM], F32, stack=stE) for i in range(2)]
    p_d = [ps(f"p_d{i}", [128, 1024], F32, stack=stE) for i in range(2)]
    p_tre = [ps(f"p_tre{i}", [128, 1024], BF16, stack=stE) for i in range(2)]

    r_wdtc = [p.res() for _ in range(32)]
    for c in range(32):
        dma(s_wdtb[:, c * DM:(c + 1) * DM],
            scr_wdt[c * 128:(c + 1) * 128, :],
            reads=[r_swd], writes=[r_wdtc[c]], tag=f"ldwdt{c}")

    out_toks = []

    def d3_prep(t):
        qbuf, rqbuf = s_xq2t[t % 2]
        dma(qbuf[:], scr_xq2[t * 128:(t + 1) * 128, :], reads=[r_sxq2[t]],
            writes=[rqbuf], tag=f"d3q{t}")
        xqT, rxqT = s_xq2T[t % 4]
        for quad in range(4):
            ptr, rptr = p_tre[quad % 2]
            for c in range(8):
                cc = quad * 8 + c
                def tr(cc=cc, c=c, ptr=ptr, qbuf=qbuf):
                    return lambda e: e.transpose(
                        ptr[:, c * 128:(c + 1) * 128],
                        qbuf[:, cc * 128:(cc + 1) * 128], s_ident[:])
                p.step("tensor", tr(), reads=[rqbuf, r_ident], writes=[rptr],
                       tag=f"d3t{t}.{cc}")
            for half in range(2):
                def cp(quad=quad, half=half, ptr=ptr, xqT=xqT):
                    return lambda e: e.tensor_copy(
                        xqT[:, quad * 1024 + half * 512:
                            quad * 1024 + (half + 1) * 512],
                        ptr[:, half * 512:(half + 1) * 512])
                p.step("vector", cp(), reads=[rptr], writes=[rxqT],
                       tag=f"d3c{t}.{quad}.{half}")
        xbuf, rxbuf = s_ex2[t % 4]
        dma(xbuf[:], scr_x2[t * 128:(t + 1) * 128, :], reads=[r_sx2[t]],
            writes=[rxbuf], tag=f"d3x{t}")

    def d3_main(t):
        xqT, rxqT = s_xq2T[t % 4]
        xbuf, rxbuf = s_ex2[t % 4]
        pd, rpd = p_d[t % 2]
        for i2c in range(32):
            for h2 in range(2):
                def md(i2c=i2c, h2=h2, pd=pd, xqT=xqT):
                    return lambda e: e.matmul(
                        pd[:, h2 * 512:(h2 + 1) * 512],
                        xqT[:, i2c * 128:(i2c + 1) * 128],
                        s_wdtb[:, i2c * DM + h2 * 512:
                               i2c * DM + (h2 + 1) * 512],
                        start=(i2c == 0), stop=(i2c == 31),
                        skip_group_check=True)
                p.step("tensor", md(), reads=[rxqT, r_wdtc[i2c]],
                       writes=[rpd], tag=f"d3m{t}.{i2c}.{h2}")
        p.step("vector",
               lambda e, pd=pd, xb=xbuf, t=t: e.scalar_tensor_tensor(
                   s_y[:], pd[:], s_cbd[:, t:t + 1], xb[:], ALU.mult,
                   ALU.add),
               reads=[rpd, r_cbd, rxbuf], writes=[r_y], tag=f"d3y{t}")
        dy3, rdy3 = s_dummy[t % 2]
        p.step("scalar",
               lambda e, dy3=dy3: e.activation(dy3[:], s_y[:], AF.Square),
               reads=[r_y], writes=[rdy3], tag=f"d3sq{t}")
        p.step("vector",
               lambda e, dy3=dy3: e.tensor_reduce(s_smal[:, 0:1], dy3[:],
                                                  AX.X, ALU.add),
               reads=[rdy3], writes=[r_smal], tag=f"d3sq{t}r",
               sync_out=True)
        p.step("vector",
               lambda e: e.tensor_reduce(s_smal[:, 1:2], s_y[:], AX.X,
                                         ALU.add),
               reads=[r_y], writes=[r_smal], tag=f"d3su{t}", sync_out=True)
        p.step("vector",
               lambda e: e.tensor_scalar(s_smal[:, 6:7], s_smal[:, 1:2],
                                         1.0 / DM, None, ALU.mult),
               reads=[r_smal], writes=[r_smal], tag=f"d3mu{t}")
        p.step("vector",
               lambda e: e.tensor_scalar(s_smal[:, 2:3], s_smal[:, 0:1],
                                         1.0 / DM, None, ALU.mult),
               reads=[r_smal], writes=[r_smal], tag=f"d3ms{t}")
        p.step("vector",
               lambda e: e.tensor_mul(s_smal[:, 3:4], s_smal[:, 6:7],
                                      s_smal[:, 6:7]),
               reads=[r_smal], writes=[r_smal], tag=f"d3mm{t}")
        p.step("vector",
               lambda e: e.tensor_sub(s_smal[:, 4:5], s_smal[:, 2:3],
                                      s_smal[:, 3:4]),
               reads=[r_smal], writes=[r_smal], tag=f"d3v{t}",
               flush=lambda e: e.tensor_copy(s_vfl[:, 0:1], s_smal[:, 4:5]))
        p.step("scalar",
               lambda e: e.activation(s_smal[:, 5:6], s_smal[:, 4:5], AF.Ln,
                                      bias=s_cvec[:, 0:1]),
               reads=[r_smal, r_cvec], writes=[r_smal], tag=f"d3l{t}")
        p.step("scalar",
               lambda e: e.activation(s_smal[:, 7:8], s_smal[:, 5:6],
                                      AF.Exp, scale=-0.5),
               reads=[r_smal], writes=[r_smal], tag=f"d3r{t}",
               flush=lambda e: e.copy(s_sfl[:, 0:1], s_smal[:, 7:8]))
        obuf, robuf = s_o[t % 2]
        p.step("vector",
               lambda e, ob=obuf: e.tensor_scalar(
                   ob[:], s_y[:], s_smal[:, 6:7], s_smal[:, 7:8],
                   ALU.subtract, ALU.mult),
               reads=[r_y, r_smal], writes=[robuf], tag=f"d3n{t}")
        out_toks.append(dma(out_t[t * 128:(t + 1) * 128, :], obuf[:],
                            reads=[robuf], tag=f"out{t}"))

    d3_prep(0)
    d3_prep(1)
    for tp in range(0, NT, 2):
        if tp + 2 < NT:
            d3_prep(tp + 2)
            d3_prep(tp + 3)
        d3_main(tp)
        d3_main(tp + 1)
    p.step("sync", lambda e: e.nop(), extra_deps=out_toks, tag="fin")
    stE.close()
    p.emit(nc, top)
    return nc, dbg, top


_CACHE = {}


def _get_program(stage="full"):
    if stage not in _CACHE:
        _CACHE[stage] = build_program(stage)
    return _CACHE[stage]


def make_in_maps(inputs):
    c = _host_consts()
    f32 = np.float32
    base = dict(
        ln1_g=np.ascontiguousarray(inputs["L0_attn_ln1_g"], f32),
        ln1_b=np.ascontiguousarray(inputs["L0_attn_ln1_b"], f32),
        wqkv=np.ascontiguousarray(inputs["L0_attn_Wqkv"], f32),
        bqkv=np.ascontiguousarray(inputs["L0_attn_bqkv"], f32),
        wo=np.ascontiguousarray(inputs["L0_attn_Wo"], f32),
        bo=np.ascontiguousarray(inputs["L0_attn_bo"], f32),
        ln2_g=np.ascontiguousarray(inputs["L0_attn_ln2_g"], f32),
        ln2_b=np.ascontiguousarray(inputs["L0_attn_ln2_b"], f32),
        w1=np.ascontiguousarray(inputs["L0_attn_W1"], f32),
        b1=np.ascontiguousarray(inputs["L0_attn_b1"], f32),
        w2=np.ascontiguousarray(inputs["L0_attn_W2"], f32),
        b2=np.ascontiguousarray(inputs["L0_attn_b2"], f32),
        wg=np.ascontiguousarray(inputs["L0_ffn_w_gate"], f32),
        wu=np.ascontiguousarray(inputs["L0_ffn_w_up"], f32),
        wd=np.ascontiguousarray(inputs["L0_ffn_w_down"], f32),
        cosF=c["cosF"], sinnF=c["sinnF"], trimask=c["trimask"],
        ident=c["ident"], identf=c["identf"], onesf=c["onesf"],
        ecat=c["ecat"], cvec=c["cvec"], sel=c["sel"], onescol=c["onescol"],
    )
    x = np.ascontiguousarray(inputs["L0_attn_x"], f32)
    maps = []
    for core in range(NCORES):
        m = dict(base)
        m["x"] = np.ascontiguousarray(x[core])
        maps.append(m)
    return maps


def kernel(**inputs):
    nc, dbg, top = _get_program("full")
    maps = make_in_maps(inputs)
    res = run_bass_kernel_spmd(nc, maps, core_ids=list(range(NCORES)))
    out = np.stack([res.results[i]["out"] for i in range(NCORES)], axis=0)
    return out.astype(np.float32)


# version 34
# speedup vs baseline: 1.0189x; 1.0149x over previous
import numpy as np
import ml_dtypes
import concourse.bass as bass
import concourse.mybir as mybir
from concourse.bass_utils import run_bass_kernel_spmd

F32 = mybir.dt.float32
BF16 = mybir.dt.bfloat16
AF = mybir.ActivationFunctionType
ALU = mybir.AluOpType
AX = mybir.AxisListType

MAGIC = 12582912.0
S, DM, H, HD, INNER = 2048, 1024, 16, 64, 4096
NT = S // 128
NCORES = 8
LNEPS = 1e-5

ENGINES = ("tensor", "vector", "scalar", "gpsimd", "sync")


class _Step:
    __slots__ = ("idx", "eng", "fn", "deps", "marked", "semval", "dma",
                 "dma_sem", "dma_val", "tag", "sync_out", "flush")

    def __init__(self, idx, eng, fn, deps, dma, tag, sync_out=False,
                 flush=None):
        self.idx, self.eng, self.fn, self.deps = idx, eng, fn, deps
        self.marked = False
        self.semval = None
        self.dma = dma
        self.dma_sem = None
        self.dma_val = None
        self.tag = tag
        self.sync_out = sync_out
        self.flush = flush


class Res:
    __slots__ = ("writers", "readers")

    def __init__(self, fence=()):
        self.writers = list(fence)
        self.readers = []


class Prog:
    def __init__(self):
        self.steps = []
        self.per_engine = {e: [] for e in ENGINES}
        self.fence = []

    def res(self):
        return Res(self.fence)

    def step(self, eng, fn, reads=(), writes=(), dma=False, tag="",
             extra_deps=(), sync_out=False, flush=None):
        deps = list(extra_deps)
        for r in reads:
            deps += r.writers
        for w in writes:
            deps += w.writers
            deps += w.readers
        s = _Step(len(self.steps), eng, fn, deps, dma, tag, sync_out, flush)
        self.steps.append(s)
        self.per_engine[eng].append(s)
        for r in reads:
            r.readers.append(s)
        for w in writes:
            w.writers = [s]
            w.readers = []
        return s

    def add_fence(self, res_list):
        toks = []
        for r in res_list:
            toks += r.writers
            toks += r.readers
        toks += self.fence
        latest = {}
        dmas = {}
        for t in toks:
            if t.dma:
                dmas[id(t)] = t
            else:
                if t.eng not in latest or t.idx > latest[t.eng].idx:
                    latest[t.eng] = t
        self.fence = list(latest.values()) + list(dmas.values())

    def finalize(self):
        for s in self.steps:
            dd = []
            seen = set()
            for d in s.deps:
                if d is s or id(d) in seen:
                    continue
                seen.add(id(d))
                dd.append(d)
                if d.dma:
                    pass
                else:
                    d.marked = True
            s.deps = dd
        for eng in ENGINES:
            c = 0
            for s in self.per_engine[eng]:
                if s.marked:
                    c += 1
                    s.semval = c

    def emit(self, nc, stack, n_dma_sems=48):
        psem = {e: stack.enter_context(nc.semaphore(f"prog_{e}"))
                for e in ENGINES}
        dsems = [stack.enter_context(nc.semaphore(f"dma{i}"))
                 for i in range(n_dma_sems)]
        dcount = [0] * n_dma_sems
        pool = {"sync": list(range(0, 32)), "gpsimd": list(range(32, 48))}
        di = {"sync": 0, "gpsimd": 0}
        for s in self.steps:
            if s.dma:
                pl = pool[s.eng]
                i = pl[di[s.eng]]
                s.dma_sem = i
                dcount[i] += 1
                s.dma_val = 16 * dcount[i]
                di[s.eng] = (di[s.eng] + 1) % len(pl)
        self.finalize()
        block = nc.Block()

        def make_body(eng_name):
            def body(eng):
                hw = {e: 0 for e in ENGINES}
                dhw = [0] * n_dma_sems
                for s in self.per_engine[eng_name]:
                    ew = {}
                    dw = {}
                    for d in s.deps:
                        if d.dma:
                            dw[d.dma_sem] = max(dw.get(d.dma_sem, 0),
                                                d.dma_val)
                        else:
                            ew[d.eng] = max(ew.get(d.eng, 0), d.semval)
                    if s.dma and s.dma_val > 16:
                        dw[s.dma_sem] = max(dw.get(s.dma_sem, 0),
                                            s.dma_val - 16)
                    for e, v in sorted(ew.items()):
                        if v > hw[e]:
                            eng.wait_ge(psem[e], v)
                            hw[e] = v
                    for q, v in sorted(dw.items()):
                        if v > dhw[q]:
                            eng.wait_ge(dsems[q], v)
                            dhw[q] = v
                    inst = s.fn(eng)
                    if s.dma:
                        inst.then_inc(dsems[s.dma_sem], 16)
                    if s.marked:
                        inst.then_inc(psem[eng_name], 1)
            return body

        with block:
            block.tensor(make_body("tensor"))
            block.vector(make_body("vector"))
            block.scalar(make_body("scalar"))
            block.gpsimd(make_body("gpsimd"))
            block.sync(make_body("sync"))


def _host_consts():
    t = np.arange(S, dtype=np.float64)
    inv = 1.0 / 10000.0 ** (np.arange(0, HD, 2, dtype=np.float64) / HD)
    fr = t[:, None] * inv[None, :]
    emb = np.concatenate([fr, fr], axis=1)
    cos = np.cos(emb).astype(np.float32)
    sin = np.sin(emb).astype(np.float32)
    sinn = sin.copy()
    sinn[:, :32] *= -1.0
    cosF = np.tile(cos, (1, H)).astype(np.float32)
    sinnF = np.tile(sinn, (1, H)).astype(np.float32)
    trimask = np.triu(np.ones((128, 128), np.float32)).astype(ml_dtypes.bfloat16)
    ident = np.eye(128, dtype=ml_dtypes.bfloat16)
    identf = np.eye(128, dtype=np.float32)
    onesf = np.ones((1, 128), np.float32)
    ecat = np.zeros((128, 256), np.float32)
    for h in range(16):
        ecat[:, h * 16 + h] = 1.0
    ecat = ecat.astype(ml_dtypes.bfloat16)
    onescol = np.ones((128, 1), np.float32)
    sel = np.zeros((16, 8 * 128), np.float32)
    for hp in range(8):
        sel[2 * hp, hp * 128:hp * 128 + 64] = 1.0
        sel[2 * hp + 1, hp * 128 + 64:hp * 128 + 128] = 1.0
    cvec = np.zeros((128, 2), np.float32)
    cvec[:, 0] = LNEPS
    cvec[:, 1] = MAGIC
    return dict(cosF=cosF, sinnF=sinnF, trimask=trimask, ident=ident,
                identf=identf, onesf=onesf, ecat=ecat, cvec=cvec, sel=sel,
                onescol=onescol)


def build_program(stage="full"):
    from contextlib import ExitStack
    nc = bass.Bass("TRN2", target_bir_lowering=False)
    p = Prog()
    top = ExitStack()

    def din(name, shape, dtype=F32):
        return nc.dram_tensor(name, shape, dtype, kind="ExternalInput").ap()

    x_in = din("x", [S, DM])
    ln1g_in = din("ln1_g", [DM]); ln1b_in = din("ln1_b", [DM])
    wqkv_in = din("wqkv", [DM, 3 * DM]); bqkv_in = din("bqkv", [3 * DM])
    wo_in = din("wo", [DM, DM]); bo_in = din("bo", [DM])
    ln2g_in = din("ln2_g", [DM]); ln2b_in = din("ln2_b", [DM])
    w1_in = din("w1", [DM, INNER]); b1_in = din("b1", [INNER])
    w2_in = din("w2", [INNER, DM]); b2_in = din("b2", [DM])
    wg_in = din("wg", [INNER, DM]); wu_in = din("wu", [INNER, DM])
    wd_in = din("wd", [DM, INNER])
    cos_in = din("cosF", [S, DM]); sinn_in = din("sinnF", [S, DM])
    tri_in = din("trimask", [128, 128], BF16)
    ident_in = din("ident", [128, 128], BF16)
    identf_in = din("identf", [128, 128], F32)
    onesf_in = din("onesf", [1, 128])
    ecat_in = din("ecat", [128, 256], BF16)
    cvec_in = din("cvec", [128, 2])
    sel_in = din("sel", [16, 8 * 128])
    onescol_in = din("onescol", [128, 1])

    out_t = nc.dram_tensor("out", [S, DM], F32, kind="ExternalOutput").ap()
    dbg = {}

    def dbg_out(name, shape, dtype=F32):
        ap = nc.dram_tensor(name, shape, dtype, kind="ExternalOutput").ap()
        dbg[name] = ap
        return ap

    scr_x1 = nc.dram_tensor("scr_x1", [S, DM], F32, kind="Internal").ap()
    scr_x2 = nc.dram_tensor("scr_x2", [S, DM], F32, kind="Internal").ap()
    scr_xq2 = nc.dram_tensor("scr_xq2", [S, INNER], BF16, kind="Internal").ap()
    scr_wgt = nc.dram_tensor("scr_wgt", [DM, INNER], BF16, kind="Internal").ap()
    scr_wut = nc.dram_tensor("scr_wut", [DM, INNER], BF16, kind="Internal").ap()
    scr_wdt = nc.dram_tensor("scr_wdt", [INNER, DM], BF16, kind="Internal").ap()
    scr_row = nc.dram_tensor("scr_row", [16, 512], F32, kind="Internal").ap()
    scr_vec = nc.dram_tensor("scr_vec", [1, 64], F32, kind="Internal").ap()

    def sb(name, shape, dtype=F32, stack=None):
        h = (stack or top).enter_context(nc.sbuf_tensor(name, shape, dtype))
        return h, p.res()

    def ps(name, shape, dtype=F32, stack=None):
        h = (stack or top).enter_context(nc.psum_tensor(name, shape, dtype))
        return h, p.res()

    def dma(dst_ap, src_ap, reads=(), writes=(), eng="sync", tag="",
            extra_deps=()):
        if eng == "sync" and "DRam" in type(dst_ap.tensor).__name__:
            eng = "gpsimd"
        return p.step(eng, lambda e: e.dma_start(out=dst_ap, in_=src_ap),
                      reads=reads, writes=writes, dma=True, tag=tag,
                      extra_deps=extra_deps)

    r_sx1 = [p.res() for _ in range(NT)]
    r_sx2 = [p.res() for _ in range(NT)]
    r_sxq2 = [p.res() for _ in range(NT)]
    r_swg = p.res(); r_swu = p.res(); r_swd = p.res()
    wq2_store_toks = [[], [], []]
    r_srow = p.res(); r_svec = p.res()

    s_ident, r_ident = sb("c_ident", [128, 128], BF16)
    s_identf, r_identf = sb("c_identf", [128, 128], F32)
    s_tri, r_tri = sb("c_tri", [128, 128], BF16)
    s_onesf, r_onesf = sb("c_onesf", [1, 128], F32)
    s_ecat, r_ecat = sb("c_ecat", [128, 256], BF16)
    s_gb1, r_gb1 = sb("c_gb1", [128, 16], F32)
    s_gb2, r_gb2 = sb("c_gb2", [128, 16], F32)
    s_b1c, r_b1c = sb("c_b1c", [128, 32], F32)
    s_b2b, r_b2b = sb("c_b2b", [128, DM], F32)
    s_mu1, r_mu1 = sb("g_mu1", [128, 16], F32)
    s_rs1, r_rs1 = sb("g_rs1", [128, 16], F32)
    s_nrm, r_nrm = sb("g_nrm", [128, 32], F32)
    s_bneg, r_bneg = sb("g_bneg", [128, 16], F32)
    s_wsum, r_wsum = sb("g_wsum", [1, 8], F32)
    s_wrow, r_wrow = sb("g_wrow", [1, 8], F32)
    s_wcol, r_wcol = sb("g_wcol", [128, 8], F32)
    s_cbd, r_cbd = sb("g_cbd", [128, 16], F32)
    s_row1, r_row1 = sb("g_row1", [1, DM], F32)
    s_gbt, r_gbt = sb("g_gbt", [32, 128], F32)
    s_dummy = [sb(f"g_dummy{i}", [128, DM], F32) for i in range(2)]
    s_cvec, r_cvec = sb("c_cvec", [128, 2], F32)
    s_onescol, r_onescol = sb("c_onescol", [128, 1], F32)
    s_wacc, r_wacc = sb("g_wacc", [128, 4], F32)
    s_w31, r_w31 = sb("g_w31", [3, 1], F32)
    s_vfl, _r_vfl = sb("g_vfl", [128, 16], F32)
    s_sfl, _r_sfl = sb("g_sfl", [128, 16], F32)
    s_smal, r_smal = sb("g_smal", [128, 48], F32)
    s_hrow, r_hrow = sb("g_hrow", [1, 32], F32)
    s_xt = [sb(f"g_xt{i}", [128, DM], F32) for i in range(2)]
    s_c1sum, r_c1sum = sb("g_c1sum", [128, 16], F32)
    s_c1sq, r_c1sq = sb("g_c1sq", [128, 16], F32)
    s_c3sum, r_c3sum = sb("g_c3sum", [128, 16], F32)
    s_c3sq, r_c3sq = sb("g_c3sq", [128, 16], F32)
    s_mu3, r_mu3 = sb("g_mu3", [128, 16], F32)
    s_rs3, r_rs3 = sb("g_rs3", [128, 16], F32)
    s_axm, r_axm = sb("g_axm", [128, 16], F32)
    s_dsx, r_dsx = sb("g_dsx", [128, 16], F32)
    s_dcg, r_dcg = sb("g_dcg", [128, 16], F32)
    s_dcu, r_dcu = sb("g_dcu", [128, 16], F32)

    stQKV = ExitStack()
    s_qt, r_qt = sb("a_QT", [128, 8 * S], BF16, stack=stQKV)
    s_kt, r_kt = sb("a_KT", [128, 8 * S], BF16, stack=stQKV)
    s_v, r_v = sb("a_V", [128, NT * DM], BF16, stack=stQKV)

    pro = ExitStack()
    p_mix, r_pmix = ps("p_mix", [128, 512], F32, stack=pro)

    dma(s_ident[:], ident_in, writes=[r_ident])
    dma(s_identf[:], identf_in, writes=[r_identf])
    dma(s_tri[:], tri_in, writes=[r_tri])
    dma(s_onesf[:], onesf_in, writes=[r_onesf])
    dma(s_ecat[:], ecat_in, writes=[r_ecat])
    dma(s_cvec[:], cvec_in, writes=[r_cvec])
    dma(s_onescol[:], onescol_in, writes=[r_onescol])

    def load_cols(dsts, src_ap, nchunks):
        dst, dst_res, c0 = dsts
        dma(s_gbt[0:nchunks, :], src_ap.rearrange("(c q) -> c q", q=128),
            writes=[r_gbt])
        p.step("tensor",
               lambda e: e.matmul(p_mix[:, 0:nchunks], s_gbt[0:nchunks, :],
                                  s_identf[0:nchunks, 0:nchunks],
                                  start=True, stop=True),
               reads=[r_gbt, r_identf], writes=[r_pmix])
        p.step("vector",
               lambda e: e.tensor_copy(dst[:, c0:c0 + nchunks],
                                       p_mix[:, 0:nchunks]),
               reads=[r_pmix], writes=[dst_res],
               flush=lambda e: e.tensor_copy(s_vfl[:, 0:1],
                                             dst[:, c0:c0 + 1]))

    load_cols((s_gb1, r_gb1, 0), ln1g_in, 8)
    load_cols((s_gb1, r_gb1, 8), ln1b_in, 8)
    load_cols((s_gb2, r_gb2, 0), ln2g_in, 8)
    load_cols((s_gb2, r_gb2, 8), ln2b_in, 8)
    load_cols((s_b1c, r_b1c, 0), b1_in, 32)

    def bcast_row(dst, dst_res, src_row_ap, n):
        dma(s_row1[0:1, 0:n], src_row_ap, writes=[r_row1])
        for j in range(0, n, 512):
            w = min(512, n - j)
            def mk(j=j, w=w):
                return lambda e: e.matmul(p_mix[:, 0:w], s_onesf[:],
                                          s_row1[0:1, j:j + w],
                                          start=True, stop=True)
            p.step("tensor", mk(), reads=[r_row1, r_onesf], writes=[r_pmix])
            def cp(j=j, w=w):
                return lambda e: e.tensor_copy(dst[:, j:j + w],
                                               p_mix[:, 0:w])
            p.step("vector", cp(), reads=[r_pmix], writes=[dst_res])

    bcast_row(s_b2b, r_b2b, b2_in.rearrange("(a n) -> a n", a=1), DM)

    stA = ExitStack()
    s_wph, r_wph = sb("a_wph", [128, 8 * 1024], BF16, stack=stA)
    s_pbias, r_pbias = sb("a_pbias", [128, 1024], F32, stack=stA)
    s_xn = [sb(f"a_xn{i}", [128, 1024], BF16, stack=stA) for i in range(2)]
    s_xnt = [sb(f"a_xnt{i}", [128, 1024], BF16, stack=stA) for i in range(2)]
    xnt_cres = [[p.res() for _ in range(8)] for _ in range(2)]
    ptr_cres = [[p.res() for _ in range(8)] for _ in range(2)]
    s_a0b = [sb(f"a_a0b{i}", [128, 1024], F32, stack=stA) for i in range(2)]
    s_qb = [sb(f"a_qb{i}", [128, 1024], F32, stack=stA) for i in range(2)]
    s_cost = [sb(f"a_cost{i}", [128, 1024], F32, stack=stA) for i in range(2)]
    s_sint = [sb(f"a_sint{i}", [128, 1024], F32, stack=stA) for i in range(2)]
    _tc = sb("a_tcos", [128, 1024], F32, stack=stA)
    _ts_ = sb("a_tsin", [128, 1024], F32, stack=stA)
    s_tcos = [_tc, _tc]
    s_tsin = [_ts_, _ts_]
    s_rope = [sb(f"a_rope{i}", [128, 1024], BF16, stack=stA) for i in range(2)]
    _sq_ = sb("a_sqbf", [128, 1024], BF16, stack=stA)
    s_sqbf = [_sq_, _sq_]
    s_nrmt = [sb(f"a_nrmt{i}", [128, 16], F32, stack=stA) for i in range(2)]
    s_asum, r_asum = sb("a_asum", [128, 16], F32, stack=stA)
    s_asq, r_asq = sb("a_asq", [128, 16], F32, stack=stA)
    s_wst = [sb(f"a_wst{i}", [128, 1024], F32, stack=stA) for i in range(2)]

    p_m = [ps(f"p_m{i}", [128, 1024], F32, stack=stA) for i in range(2)]
    p_tr = [ps(f"p_tr{i}", [128, 1024], BF16, stack=stA) for i in range(2)]


    p.step("vector", lambda e: e.memset(s_nrm[:], 0.0), writes=[r_nrm])


    wq_jobs = []
    for widx, w_ap in enumerate((wg_in, wu_in, wd_in)):
        for ch in range(32):
            wq_jobs.append((widx, w_ap, ch))
    wq_pos = [0]

    def wmean_pump(n):
        for _ in range(n):
            if wq_pos[0] >= len(wq_jobs):
                return
            widx, w_ap, ch = wq_jobs[wq_pos[0]]
            wq_pos[0] += 1
            buf, rbuf = s_wst[wq_pos[0] % 2]
            if widx < 2:
                src = w_ap[ch * 128:(ch + 1) * 128, :]
            else:
                src = w_ap.rearrange("m (c n) -> (m c) n", c=4)[
                    ch * 128:(ch + 1) * 128, :]
            dma(buf[:], src, writes=[rbuf], tag=f"wm{widx}.{ch}")
            p.step("vector",
                   lambda e, b=buf: e.tensor_reduce(
                       s_smal[:, 40:41], b[:], AX.X, ALU.add,
                       apply_absolute_value=True),
                   reads=[rbuf], writes=[r_smal], tag=f"wabs{widx}.{ch}",
                   sync_out=True)
            if ch == 0:
                def acc0(widx=widx):
                    return lambda e: e.tensor_copy(
                        s_wacc[:, widx:widx + 1], s_smal[:, 40:41])
            else:
                def acc0(widx=widx):
                    return lambda e: e.tensor_add(
                        s_wacc[:, widx:widx + 1], s_wacc[:, widx:widx + 1],
                        s_smal[:, 40:41])
            p.step("vector", acc0(), reads=[r_smal, r_wacc], writes=[r_wacc],
                   tag=f"wacc{widx}.{ch}")

    a0bufs = [s_xt[0], s_xt[1], s_a0b[0], s_a0b[1]]
    for t in range(NT):
        xbuf, rxbuf = a0bufs[t % 4]
        dy, rdy = s_dummy[t % 2]
        dma(xbuf[:], x_in[t * 128:(t + 1) * 128, :], writes=[rxbuf],
            tag=f"a0x{t}")
        p.step("scalar",
               lambda e, xb=xbuf, dy=dy: e.activation(dy[:], xb[:],
                                                      AF.Square),
               reads=[rxbuf], writes=[rdy], tag=f"a0sq{t}")
        p.step("vector",
               lambda e, dy=dy, t=t: e.tensor_reduce(
                   s_asq[:, t:t + 1], dy[:], AX.X, ALU.add),
               reads=[rdy], writes=[r_asq], tag=f"a0qr{t}")
        p.step("vector",
               lambda e, xb=xbuf, t=t: e.tensor_reduce(
                   s_asum[:, t:t + 1], xb[:], AX.X, ALU.add),
               reads=[rxbuf], writes=[r_asum], tag=f"a0sr{t}")

    def batch_stats(src_sum, r_srcsum, src_sq, r_srcsq, dst_mu, r_dstmu,
                    dst_rs, r_dstrs, pre):
        p.step("vector",
               lambda e: e.tensor_scalar(dst_mu[:], src_sum[:], 1.0 / DM,
                                         None, ALU.mult),
               reads=[r_srcsum], writes=[r_dstmu], tag=pre + "mu")
        p.step("vector",
               lambda e: e.tensor_scalar(s_smal[:, 0:16], src_sq[:],
                                         1.0 / DM, None, ALU.mult),
               reads=[r_srcsq], writes=[r_smal], tag=pre + "ms")
        p.step("vector",
               lambda e: e.tensor_mul(s_smal[:, 16:32], dst_mu[:],
                                      dst_mu[:]),
               reads=[r_dstmu], writes=[r_smal], tag=pre + "mm")
        p.step("vector",
               lambda e: e.tensor_sub(s_smal[:, 0:16], s_smal[:, 0:16],
                                      s_smal[:, 16:32]),
               reads=[r_smal], writes=[r_smal], tag=pre + "v")
        p.step("scalar",
               lambda e: e.activation(s_smal[:, 16:32], s_smal[:, 0:16],
                                      AF.Ln, bias=s_cvec[:, 0:1]),
               reads=[r_smal, r_cvec], writes=[r_smal], tag=pre + "l")
        p.step("scalar",
               lambda e: e.activation(dst_rs[:], s_smal[:, 16:32], AF.Exp,
                                      scale=-0.5),
               reads=[r_smal], writes=[r_dstrs], tag=pre + "r")

    batch_stats(s_asum, r_asum, s_asq, r_asq, s_mu1, r_mu1, s_rs1, r_rs1,
                "a0")

    for phase in range(3):
        dma(s_wph[:].rearrange("p (c n) -> p c n", c=8),
            wqkv_in[:, phase * 1024:(phase + 1) * 1024].rearrange(
                "(c q) n -> q c n", q=128),
            writes=[r_wph], eng="gpsimd", tag=f"wqkv{phase}")
        bcast_row(s_pbias, r_pbias,
                  bqkv_in.rearrange("(a n) -> a n", a=1)[
                      0:1, phase * 1024:(phase + 1) * 1024], DM)

        def a_stage1(t, phase=phase):
            pr = t % 2
            xbuf, rxbuf = s_xt[pr]
            dma(xbuf[:], x_in[t * 128:(t + 1) * 128, :], writes=[rxbuf],
                tag=f"x{phase}.{t}")
            if phase != 2:
                cbuf, rcbuf = s_cost[pr]
                sbuf_, rsbuf_ = s_sint[pr]
                dma(cbuf[:], cos_in[t * 128:(t + 1) * 128, :],
                    writes=[rcbuf], tag=f"cos{phase}.{t}")
                dma(sbuf_[:], sinn_in[t * 128:(t + 1) * 128, :],
                    writes=[rsbuf_], tag=f"sin{phase}.{t}")
            xn, rxn = s_xn[pr]
            xnt, rxnt = s_xnt[pr]
            ptr, rptr = p_tr[pr]
            p.step("vector",
                   lambda e, xb=xbuf, xn=xn, t=t: e.tensor_scalar(
                       xn[:], xb[:], s_mu1[:, t:t + 1], s_rs1[:, t:t + 1],
                       ALU.subtract, ALU.mult),
                   reads=[rxbuf, r_mu1, r_rs1], writes=[rxn],
                   tag=f"xn{phase}.{t}")
            for c in range(8):
                def tr(c=c, xn=xn, ptr=ptr):
                    return lambda e: e.transpose(
                        ptr[:, c * 128:(c + 1) * 128],
                        xn[:, c * 128:(c + 1) * 128], s_ident[:])
                p.step("tensor", tr(), reads=[rxn, r_ident],
                       writes=[rptr], tag=f"trx{phase}.{t}.{c}")
                def ev(c=c, ptr=ptr, xnt=xnt):
                    return lambda e: e.activation(
                        xnt[:, c * 128:(c + 1) * 128],
                        ptr[:, c * 128:(c + 1) * 128], AF.Identity,
                        scale=s_gb1[:, c:c + 1],
                        bias=s_gb1[:, 8 + c:9 + c])
                p.step("scalar", ev(), reads=[rptr, r_gb1],
                       writes=[xnt_cres[pr][c]], tag=f"evx{phase}.{t}.{c}")
            pm, rpm = p_m[pr]
            for dinc in range(8):
                for h2 in range(2):
                    def mm(dinc=dinc, h2=h2, pm=pm, xnt=xnt):
                        return lambda e: e.matmul(
                            pm[:, h2 * 512:(h2 + 1) * 512],
                            xnt[:, dinc * 128:(dinc + 1) * 128],
                            s_wph[:, dinc * 1024 + h2 * 512:
                                  dinc * 1024 + (h2 + 1) * 512],
                            start=(dinc == 0), stop=(dinc == 7),
                            skip_group_check=True)
                    p.step("tensor", mm(),
                           reads=[xnt_cres[pr][dinc], r_wph],
                           writes=[rpm], tag=f"qkv{phase}.{t}.{dinc}.{h2}")

        def a_stage2(t, phase=phase):
            pr = t % 2
            pm, rpm = p_m[pr]
            ptr, rptr = p_tr[pr]
            if phase == 2:
                for h2 in range(2):
                    def ev2(h2=h2, t=t, pm=pm):
                        return lambda e: e.tensor_add(
                            s_v[:, t * 1024 + h2 * 512:
                                t * 1024 + (h2 + 1) * 512],
                            pm[:, h2 * 512:(h2 + 1) * 512],
                            s_pbias[:, h2 * 512:(h2 + 1) * 512])
                    p.step("vector", ev2(), reads=[rpm, r_pbias],
                           writes=[r_v], tag=f"vev{t}.{h2}")
                return
            cbuf, rcbuf = s_cost[pr]
            sbuf_, rsbuf_ = s_sint[pr]
            qb, rqb = s_qb[pr]
            p.step("vector",
                   lambda e, pm=pm, qb=qb: e.tensor_add(qb[:], pm[:],
                                                        s_pbias[:]),
                   reads=[rpm, r_pbias], writes=[rqb],
                   tag=f"qbev{phase}.{t}")
            sq, rsq = s_sqbf[pr]
            nt_, rnt_ = s_nrmt[pr]
            p.step("scalar",
                   lambda e, qb=qb, sq=sq: e.activation(sq[:], qb[:],
                                                        AF.Square),
                   reads=[rqb], writes=[rsq], tag=f"qsq{phase}.{t}")
            p.step("vector",
                   lambda e, sq=sq, nt_=nt_: e.tensor_reduce(
                       nt_[:], sq[:].rearrange("q (h c) -> q h c", h=16,
                                               c=64),
                       AX.X, ALU.add),
                   reads=[rsq], writes=[rnt_], tag=f"qnr{phase}.{t}",
                   sync_out=True)
            c0 = phase * 16
            p.step("vector",
                   lambda e, c0=c0, nt_=nt_: e.tensor_max(
                       s_nrm[:, c0:c0 + 16], s_nrm[:, c0:c0 + 16], nt_[:]),
                   reads=[rnt_, r_nrm], writes=[r_nrm],
                   tag=f"qmx{phase}.{t}")
            tc, rtc = s_tcos[pr]
            tsn, rtsn = s_tsin[pr]
            rp, rrp = s_rope[pr]
            p.step("vector",
                   lambda e, cbuf=cbuf, qb=qb, tc=tc: e.tensor_mul(
                       tc[:], qb[:], cbuf[:]),
                   reads=[rqb, rcbuf], writes=[rtc], tag=f"rc{phase}.{t}")
            qv = qb[:].rearrange("q (h t2 c) -> q h t2 c", h=16, t2=2, c=32)
            sv = sbuf_[:].rearrange("q (h t2 c) -> q h t2 c", h=16, t2=2,
                                    c=32)
            tv = tsn[:].rearrange("q (h t2 c) -> q h t2 c", h=16, t2=2,
                                  c=32)
            p.step("vector",
                   lambda e, qv=qv, sv=sv, tv=tv: e.tensor_mul(
                       tv[:, :, 0, :], qv[:, :, 1, :], sv[:, :, 0, :]),
                   reads=[rqb, rsbuf_], writes=[rtsn],
                   tag=f"rs0{phase}.{t}")
            p.step("vector",
                   lambda e, qv=qv, sv=sv, tv=tv: e.tensor_mul(
                       tv[:, :, 1, :], qv[:, :, 0, :], sv[:, :, 1, :]),
                   reads=[rqb, rsbuf_], writes=[rtsn],
                   tag=f"rs1{phase}.{t}")
            p.step("vector",
                   lambda e, tc=tc, tsn=tsn, rp=rp: e.tensor_add(
                       rp[:], tc[:], tsn[:]),
                   reads=[rtc, rtsn], writes=[rrp], tag=f"ra{phase}.{t}")
            for c in range(8):
                def tr2(c=c, rp=rp, ptr=ptr):
                    return lambda e: e.transpose(
                        ptr[:, c * 128:(c + 1) * 128],
                        rp[:, c * 128:(c + 1) * 128], s_ident[:])
                p.step("tensor", tr2(), reads=[rrp, r_ident],
                       writes=[rptr], tag=f"trq{phase}.{t}.{c}")
            dstT, rdstT = (s_qt, r_qt) if phase == 0 else (s_kt, r_kt)
            for half in range(2):
                dv = dstT[:].rearrange("d (c q) -> d c q", c=8, q=S)[
                    :, half * 4:(half + 1) * 4, t * 128:(t + 1) * 128]
                def cp(half=half, dv=dv, ptr=ptr):
                    return lambda e: e.activation(
                        dv, ptr[:, half * 512:(half + 1) * 512],
                        AF.Identity)
                p.step("scalar", cp(), reads=[rptr],
                       writes=[rdstT], tag=f"qtc{phase}.{t}.{half}")

        for t in range(NT):
            a_stage1(t)
            if t >= 1:
                a_stage2(t - 1)
            wmean_pump(2)
        a_stage2(NT - 1)
    wmean_pump(100)

    p.step("tensor",
           lambda e: e.matmul(p_mix[0:32, 0:128], s_nrm[:], s_identf[:],
                              start=True, stop=True),
           reads=[r_nrm, r_identf], writes=[r_pmix], tag="nrmT")
    s_hn, r_hn = sb("a_hn", [32, 1], F32, stack=stA)
    p.step("vector",
           lambda e: e.tensor_reduce(s_hn[:], p_mix[0:32, 0:128], AX.X,
                                     ALU.max),
           reads=[r_pmix], writes=[r_hn], tag="nrmmax", sync_out=True)
    dma(scr_vec[0:1, 0:32].rearrange("a (q b) -> q (a b)", q=32, b=1),
        s_hn[:], reads=[r_hn], writes=[r_svec], tag="bnc0")
    dma(s_hrow[:], scr_vec[0:1, 0:32], reads=[r_svec], writes=[r_hrow],
        tag="bnc1")
    p.step("vector",
           lambda e: e.tensor_add(s_hrow[0:1, 0:16], s_hrow[0:1, 0:16],
                                  s_hrow[0:1, 16:32]),
           reads=[r_hrow], writes=[r_hrow], tag="bsum")
    p.step("vector",
           lambda e: e.tensor_scalar(s_hrow[0:1, 0:16], s_hrow[0:1, 0:16],
                                     -1.0 / 16.0, None, ALU.mult),
           reads=[r_hrow], writes=[r_hrow], tag="bneg",
           flush=lambda e: e.tensor_copy(s_vfl[0:1, 0:8], s_hrow[0:1, 0:8]))
    p.step("tensor",
           lambda e: e.matmul(p_mix[:, 0:16], s_onesf[:], s_hrow[0:1, 0:16],
                              start=True, stop=True),
           reads=[r_hrow, r_onesf], writes=[r_pmix], tag="bnegT")
    p.step("vector",
           lambda e: e.tensor_copy(s_bneg[:], p_mix[:, 0:16]),
           reads=[r_pmix], writes=[r_bneg], tag="bnegC",
           flush=lambda e: e.tensor_copy(s_vfl[:, 0:1], s_bneg[:, 0:1]))

    p.step("tensor",
           lambda e: e.matmul(p_mix[0:3, 0:1], s_wacc[:, 0:3],
                              s_onescol[:], start=True, stop=True),
           reads=[r_wacc, r_onescol], writes=[r_pmix], tag="wsumT")
    p.step("vector",
           lambda e: e.tensor_copy(s_w31[:], p_mix[0:3, 0:1]),
           reads=[r_pmix], writes=[r_w31], tag="wsumC")
    p.step("tensor",
           lambda e: e.matmul(p_mix[0:1, 4:7], s_w31[:],
                              s_identf[0:3, 0:3], start=True, stop=True),
           reads=[r_w31, r_identf], writes=[r_pmix], tag="wrowT")
    p.step("vector",
           lambda e: e.tensor_copy(s_wsum[0:1, 0:3], p_mix[0:1, 4:7]),
           reads=[r_pmix], writes=[r_wsum], tag="wrowC")
    p.step("vector",
           lambda e: e.tensor_scalar(s_wrow[0:1, 0:3], s_wsum[0:1, 0:3],
                                     1.0 / (INNER * DM), 1e-5, ALU.mult,
                                     ALU.max),
           reads=[r_wsum], writes=[r_wrow], tag="wmean", sync_out=True)
    p.step("vector",
           lambda e: e.reciprocal(s_wrow[0:1, 3:6], s_wrow[0:1, 0:3]),
           reads=[r_wrow], writes=[r_wrow], tag="wrec", sync_out=True)
    p.step("vector",
           lambda e: e.reciprocal(s_wrow[0:1, 6:8], s_wrow[0:1, 3:5]),
           reads=[r_wrow], writes=[r_wrow], tag="wrec2", sync_out=True)
    p.step("vector",
           lambda e: e.reciprocal(s_wrow[0:1, 0:1], s_wrow[0:1, 5:6]),
           reads=[r_wrow], writes=[r_wrow], tag="wrec3", sync_out=True,
           flush=lambda e: e.tensor_copy(s_vfl[0:1, 0:8], s_wrow[0:1, 0:8]))
    p.step("tensor",
           lambda e: e.matmul(p_mix[:, 0:8], s_onesf[:], s_wrow[0:1, 0:8],
                              start=True, stop=True),
           reads=[r_wrow, r_onesf], writes=[r_pmix], tag="wcolT")
    p.step("vector",
           lambda e: e.tensor_copy(s_wcol[:], p_mix[:, 0:8]),
           reads=[r_pmix], writes=[r_wcol], tag="wcolC",
           flush=lambda e: e.tensor_copy(s_vfl[:, 0:1], s_wcol[:, 0:1]))

    if stage == "A":
        d_qt = dbg_out("d_qt", [128, 8 * S])
        d_kt = dbg_out("d_kt", [128, 8 * S])
        d_v = dbg_out("d_v", [128, NT * DM])
        d_bneg = dbg_out("d_bneg", [128, 16])
        d_wrow = dbg_out("d_wrow", [1, 8])
        d_mu = dbg_out("d_mu", [128, 16])
        d_rs = dbg_out("d_rs", [128, 16])
        d_wph = dbg_out("d_wph", [128, 8 * 1024])
        d_qb = dbg_out("d_qb", [128, 1024])
        d_xnt = dbg_out("d_xnt", [128, 1024])
        d_rope = dbg_out("d_rope", [128, 1024])
        d_pbias = dbg_out("d_pbias", [128, 1024])
        toks = [
            dma(d_qt, s_qt[:], reads=[r_qt], eng="gpsimd", tag="dbgq"),
            dma(d_kt, s_kt[:], reads=[r_kt], eng="gpsimd", tag="dbgk"),
            dma(d_v, s_v[:], reads=[r_v], eng="gpsimd", tag="dbgv"),
            dma(d_bneg, s_bneg[:], reads=[r_bneg], tag="dbgb"),
            dma(d_wrow, s_wrow[:], reads=[r_wrow], tag="dbgw"),
            dma(d_mu, s_mu1[:], reads=[r_mu1], tag="dbgmu"),
            dma(d_rs, s_rs1[:], reads=[r_rs1], tag="dbgrs"),
            dma(d_wph, s_wph[:], reads=[r_wph], eng="gpsimd", tag="dbgwp"),
            dma(d_qb, s_qb[1][0][:], reads=[s_qb[1][1]], tag="dbgqb"),
            dma(d_xnt, s_xnt[1][0][:], reads=xnt_cres[1], eng="gpsimd", tag="dbgxnt"),
            dma(d_rope, s_rope[1][0][:], reads=[s_rope[1][1]], eng="gpsimd", tag="dbgro"),
            dma(d_pbias, s_pbias[:], reads=[r_pbias], tag="dbgpb"),
        ]
        p.step("sync", lambda e: e.nop(), extra_deps=toks, tag="fin")
        stA.close(); pro.close()
        p.emit(nc, top)
        return nc, dbg, top

    stA.close()
    p.add_fence([r_pmix])
    pro.close()
    p.add_fence([r_wph, r_pbias, p_m[0][1], p_m[1][1],
                 r_asum, r_asq]
                + ptr_cres[0] + ptr_cres[1] + xnt_cres[0] + xnt_cres[1]
                + [b[1] for b in (s_cost + s_sint + s_wst + s_xn + s_xnt
                                  + s_a0b
                                  + s_qb + s_tcos + s_tsin + s_rope
                                  + s_sqbf + s_nrmt)])

    stB = ExitStack()
    s_pta, r_pta = sb("b_pta", [128, 16 * 512], BF16, stack=stB)
    s_ptb, r_ptb = sb("b_ptb", [128, 16 * 512], BF16, stack=stB)
    s_attn, r_attn = sb("b_attn", [128, 8 * 512], BF16, stack=stB)
    s_wob, r_wob = sb("b_wob", [128, 8 * 1024], BF16, stack=stB)
    s_bob, r_bob = sb("b_bob", [128, DM], F32, stack=stB)
    s_r16, r_r16 = sb("b_r16", [16, 512], F32, stack=stB)
    s_sel, r_sel = sb("b_sel", [16, 8 * 128], F32, stack=stB)
    s_t1, r_t1 = sb("b_t1", [128, DM], F32, stack=stB)
    s_xp = [sb(f"b_xp{i}", [128, DM], F32, stack=stB) for i in range(2)]

    dma(s_wob[:].rearrange("p (c n) -> p c n", c=8),
        wo_in.rearrange("(c q) n -> q c n", q=128),
        writes=[r_wob], eng="gpsimd", tag="ldwo")
    dma(s_sel[:], sel_in, writes=[r_sel], tag="ldsel")
    bcast2 = ExitStack()
    p_bc2, r_pbc2 = ps("p_bc2", [128, 512], F32, stack=bcast2)
    dma(s_row1[0:1, 0:DM], bo_in.rearrange("(a n) -> a n", a=1),
        writes=[r_row1], tag="ldbo")
    for j in range(2):
        def mk(j=j):
            return lambda e: e.matmul(p_bc2[:, 0:512], s_onesf[:],
                                      s_row1[0:1, j * 512:(j + 1) * 512],
                                      start=True, stop=True)
        p.step("tensor", mk(), reads=[r_row1, r_onesf], writes=[r_pbc2])
        def cp(j=j):
            return lambda e: e.tensor_copy(s_bob[:, j * 512:(j + 1) * 512],
                                           p_bc2[:, 0:512])
        p.step("vector", cp(), reads=[r_pbc2], writes=[r_bob])
    bcast2.close()
    p.add_fence([r_pbc2])

    p_stab = [ps(f"p_st{sd}", [128, 1536], F32, stack=stB) for sd in range(2)]
    p_o, r_po = ps("p_o", [128, 512], F32, stack=stB)
    p_rs, r_prs = ps("p_rs", [128, 512], F32, stack=stB)

    PT = [(s_pta, r_pta), (s_ptb, r_ptb)]

    r_ptk = [[p.res() for _ in range(16)], [p.res() for _ in range(16)]]

    for qc in range(4):
        nkt = 4 * qc + 4
        first_rs = [True]
        kts = list(range(nkt))
        groups = [kts[i:i + 3] for i in range(0, nkt, 3)]

        def emit_st_group(hp, g, qc=qc):
            for side in range(2):
                head = 2 * hp + side
                pst, rpst = p_stab[side]
                for j, kt in enumerate(g):
                    def mmst(j=j, kt=kt, side=side, hp=hp, qc=qc, pst=pst):
                        return lambda e: e.matmul(
                            pst[:, j * 512:(j + 1) * 512],
                            s_kt[side * 64:(side + 1) * 64,
                                 hp * 2048 + kt * 128:
                                 hp * 2048 + (kt + 1) * 128],
                            s_qt[side * 64:(side + 1) * 64,
                                 hp * 2048 + qc * 512:
                                 hp * 2048 + (qc + 1) * 512],
                            start=True, stop=True,
                            tile_position=(side * 64, 0),
                            skip_group_check=True)
                    p.step("tensor", mmst(), reads=[r_kt, r_qt],
                           writes=[rpst], tag=f"st{qc}.{hp}.{side}.{kt}")
                ptb, _ = PT[side]
                def expf(g=g, side=side, head=head, pst=pst, ptb=ptb):
                    ln = len(g)
                    return lambda e: e.activation(
                        ptb[:, g[0] * 512:(g[0] + ln) * 512],
                        pst[:, 0:ln * 512], AF.Exp, scale=0.125,
                        bias=s_bneg[:, head:head + 1])
                p.step("scalar", expf(), reads=[rpst, r_bneg],
                       writes=[r_ptk[side][kt2] for kt2 in g],
                       tag=f"exp{qc}.{hp}.{side}")
            for kt in g:
                if kt < 4 * qc:
                    continue
                dcol = kt * 128 - qc * 512
                for side in range(2):
                    ptb, _ = PT[side]
                    if dcol > 0:
                        def msf(kt=kt, dcol=dcol, ptb=ptb):
                            return lambda e: e.memset(
                                ptb[:, kt * 512:kt * 512 + dcol], 0.0)
                        p.step("vector", msf(), writes=[r_ptk[side][kt]],
                               tag=f"ms{qc}.{hp}.{side}.{kt}")
                    def trif(kt=kt, dcol=dcol, ptb=ptb):
                        return lambda e: e.tensor_mul(
                            ptb[:, kt * 512 + dcol:kt * 512 + dcol + 128],
                            ptb[:, kt * 512 + dcol:kt * 512 + dcol + 128],
                            s_tri[:])
                    p.step("vector", trif(), reads=[r_tri],
                           writes=[r_ptk[side][kt]],
                           tag=f"tri{qc}.{hp}.{side}.{kt}")

        def emit_pv_group(hp, g, qc=qc):
            for kt in g:
                for side in range(2):
                    head = 2 * hp + side
                    ptb, _ = PT[side]
                    def pvf(kt=kt, side=side, head=head, ptb=ptb):
                        return lambda e: e.matmul(
                            p_o[side * 64:(side + 1) * 64, :],
                            s_v[:, kt * 1024 + head * 64:
                                kt * 1024 + (head + 1) * 64],
                            ptb[:, kt * 512:(kt + 1) * 512],
                            start=(kt == 0), stop=False,
                            tile_position=(0, side * 64),
                            skip_group_check=True)
                    p.step("tensor", pvf(), reads=[r_ptk[side][kt], r_v],
                           writes=[r_po], tag=f"pv{qc}.{hp}.{side}.{kt}")
                    def rsf(kt=kt, side=side, head=head, ptb=ptb,
                            st=first_rs[0]):
                        return lambda e: e.matmul(
                            p_rs[0:16, :],
                            s_ecat[:, head * 16:(head + 1) * 16],
                            ptb[:, kt * 512:(kt + 1) * 512],
                            start=st, stop=False,
                            skip_group_check=True)
                    p.step("tensor", rsf(), reads=[r_ptk[side][kt], r_ecat],
                           writes=[r_prs], tag=f"rs{qc}.{hp}.{side}.{kt}")
                    first_rs[0] = False
            if g is groups[-1]:
                def evo(hp=hp):
                    return lambda e: e.tensor_copy(
                        s_attn[:, hp * 512:(hp + 1) * 512], p_o[:])
                p.step("vector", evo(), reads=[r_po], writes=[r_attn],
                       tag=f"aev{qc}.{hp}")

        jobs = [(hp, g) for hp in range(8) for g in groups]
        for i, (hp, g) in enumerate(jobs):
            emit_st_group(hp, g)
            if i >= 1:
                emit_pv_group(*jobs[i - 1])
        emit_pv_group(*jobs[-1])

        p.step("vector",
               lambda e: e.reciprocal(s_r16[:], p_rs[0:16, :]),
               reads=[r_prs], writes=[r_r16], tag=f"recip{qc}")
        for hp in range(8):
            def rb0(hp=hp):
                return lambda e: e.matmul(
                    p_o[:], s_sel[:, hp * 128:(hp + 1) * 128], s_r16[:],
                    start=True, stop=True, skip_group_check=True)
            p.step("tensor", rb0(), reads=[r_r16, r_sel], writes=[r_po],
                   tag=f"rb{qc}.{hp}")
            def divf(hp=hp):
                return lambda e: e.tensor_mul(
                    s_attn[:, hp * 512:(hp + 1) * 512],
                    s_attn[:, hp * 512:(hp + 1) * 512], p_o[:])
            p.step("vector", divf(), reads=[r_po, r_attn], writes=[r_attn],
                   tag=f"div{qc}.{hp}")
        p_wo = p_stab[0][0]
        r_pwo = p_stab[0][1]
        for qsub in range(4):
            t = qc * 4 + qsub
            xbuf, rxbuf = s_xt[t % 2]
            dma(xbuf[:], x_in[t * 128:(t + 1) * 128, :], writes=[rxbuf],
                tag=f"xb{t}")
            for dmc in range(8):
                for h2 in range(2):
                    def mwo(dmc=dmc, h2=h2, qsub=qsub):
                        return lambda e: e.matmul(
                            p_wo[:, h2 * 512:(h2 + 1) * 512],
                            s_attn[:, dmc * 512 + qsub * 128:
                                   dmc * 512 + (qsub + 1) * 128],
                            s_wob[:, dmc * 1024 + h2 * 512:
                                  dmc * 1024 + (h2 + 1) * 512],
                            start=(dmc == 0), stop=(dmc == 7),
                            skip_group_check=True)
                    p.step("tensor", mwo(), reads=[r_attn, r_wob],
                           writes=[r_pwo], tag=f"wo{t}.{dmc}.{h2}")
            p.step("vector",
                   lambda e: e.tensor_add(s_t1[:], p_wo[:, 0:1024],
                                          s_bob[:]),
                   reads=[r_pwo, r_bob], writes=[r_t1], tag=f"woev{t}")
            xpb, rxpb = s_xp[t % 2]
            p.step("vector",
                   lambda e, xb=xbuf, xp=xpb: e.tensor_add(xp[:], s_t1[:],
                                                           xb[:]),
                   reads=[r_t1, rxbuf], writes=[rxpb], tag=f"res1.{t}")
            dma(scr_x1[t * 128:(t + 1) * 128, :], xpb[:], reads=[rxpb],
                writes=[r_sx1[t]], tag=f"sx1.{t}")
            dy, rdy = s_dummy[t % 2]
            p.step("scalar",
                   lambda e, xp=xpb, dy=dy: e.activation(dy[:], xp[:],
                                                         AF.Square),
                   reads=[rxpb], writes=[rdy], tag=f"bsq{t}")
            p.step("vector",
                   lambda e, dy=dy, t=t: e.tensor_reduce(
                       s_c1sq[:, t:t + 1], dy[:], AX.X, ALU.add),
                   reads=[rdy], writes=[r_c1sq], tag=f"bqr{t}")
            p.step("vector",
                   lambda e, xp=xpb, t=t: e.tensor_reduce(
                       s_c1sum[:, t:t + 1], xp[:], AX.X, ALU.add),
                   reads=[rxpb], writes=[r_c1sum], tag=f"bsr{t}")

    if stage == "B":
        d_x1 = dbg_out("d_x1", [S, DM])
        toks = [dma(d_x1, scr_x1, reads=r_sx1[:], tag="dbgx1")]
        p.step("sync", lambda e: e.nop(), extra_deps=toks, tag="fin")
        stB.close()
        p.emit(nc, top)
        return nc, dbg, top

    stB.close()
    p.add_fence([r_pta, r_ptb, r_attn, r_wob, r_bob, r_r16, r_sel, r_t1,
                 r_po, r_prs, p_stab[0][1], p_stab[1][1],
                 s_xp[0][1], s_xp[1][1]])
    stQKV.close()
    p.add_fence([r_qt, r_kt, r_v])

    stC = ExitStack()
    s_w1b, r_w1b = sb("c_w1b", [128, 8 * INNER], BF16, stack=stC)
    s_w2b, r_w2b = sb("c_w2b", [128, 32 * DM], BF16, stack=stC)
    s_mu2, r_mu2 = sb("c_mu2", [128, 16], F32, stack=stC)
    s_rs2, r_rs2 = sb("c_rs2", [128, 16], F32, stack=stC)
    s_c1x = [sb(f"c_x{i}", [128, DM], F32, stack=stC) for i in range(4)]
    _xn2 = sb("c_xn2", [128, DM], BF16, stack=stC)
    s_xn2 = [_xn2, _xn2]
    s_xn2t = [sb(f"c_xn2t{i}", [128, 8 * 256], BF16, stack=stC)
              for i in range(2)]
    s_h1c = [sb(f"c_h1c{i}", [128, 256], BF16, stack=stC) for i in range(3)]
    _xpp = sb("c_xpp", [128, DM], F32, stack=stC)
    s_xpp = [_xpp, _xpp]
    s_wq1, r_wq1 = sb("c_wq1", [128, 1024], F32, stack=stC)
    s_wqbf, r_wqbf = sb("c_wqbf", [128, 1024], BF16, stack=stC)
    s_wtx, r_wtx = sb("c_wtx", [128, 1024], BF16, stack=stC)
    s_wstc = [sb(f"c_wst{i}", [128, 1024], F32, stack=stC) for i in range(2)]
    p_h = [ps(f"p_h{i}", [128, 512], F32, stack=stC) for i in range(3)]
    p_mlp = [ps(f"p_mlp{i}", [128, 1024], F32, stack=stC) for i in range(2)]
    p_trc, r_ptrc = ps("p_trc", [128, 1024], BF16, stack=stC)

    dma(s_w1b[:].rearrange("p (c n) -> p c n", c=8),
        w1_in.rearrange("(c q) n -> q c n", q=128),
        writes=[r_w1b], eng="gpsimd", tag="ldw1")
    dma(s_w2b[:].rearrange("p (c n) -> p c n", c=32),
        w2_in.rearrange("(c q) n -> q c n", q=128),
        writes=[r_w2b], eng="gpsimd", tag="ldw2")

    wq2_jobs = []
    for widx, (w_ap, scr, rscr) in enumerate(
            ((wg_in, scr_wgt, r_swg), (wu_in, scr_wut, r_swu),
             (wd_in, scr_wdt, r_swd))):
        for ch in range(32):
            wq2_jobs.append((widx, w_ap, scr, rscr, ch))
    wq2_pos = [0]

    def wq2_pump(n):
        for _ in range(n):
            if wq2_pos[0] >= len(wq2_jobs):
                return
            widx, w_ap, scr, rscr, ch = wq2_jobs[wq2_pos[0]]
            wq2_pos[0] += 1
            buf, rbuf = s_wstc[wq2_pos[0] % 2]
            scol = 3 + widx if widx < 2 else 5
            if widx < 2:
                src = w_ap[ch * 128:(ch + 1) * 128, :]
                nin = 1024
            else:
                src = w_ap[(ch % 8) * 128:((ch % 8) + 1) * 128,
                           (ch // 8) * 1024:((ch // 8) + 1) * 1024]
                nin = 1024
            dma(buf[:], src, writes=[rbuf], tag=f"w2l{widx}.{ch}")
            def q1(buf=buf, scol=scol):
                return lambda e: e.tensor_scalar(
                    s_wq1[:], buf[:], s_wcol[:, scol:scol + 1], None,
                    ALU.mult)
            p.step("vector", q1(), reads=[rbuf, r_wcol], writes=[r_wq1],
                   tag=f"w2q1.{widx}.{ch}")
            p.step("vector",
                   lambda e: e.tensor_scalar(s_wq1[:], s_wq1[:], MAGIC,
                                             None, ALU.add),
                   reads=[r_wq1], writes=[r_wq1], tag=f"w2q1b.{widx}.{ch}")
            p.step("vector",
                   lambda e: e.tensor_scalar(s_wq1[:], s_wq1[:], MAGIC - 1.0,
                                             MAGIC + 1.0, ALU.max, ALU.min),
                   reads=[r_wq1], writes=[r_wq1], tag=f"w2q2.{widx}.{ch}")
            p.step("vector",
                   lambda e: e.tensor_scalar(s_wqbf[:], s_wq1[:], MAGIC,
                                             None, ALU.subtract),
                   reads=[r_wq1], writes=[r_wqbf], tag=f"w2q3.{widx}.{ch}")
            for c in range(8):
                def tr(c=c):
                    return lambda e: e.transpose(
                        p_trc[:, c * 128:(c + 1) * 128],
                        s_wqbf[:, c * 128:(c + 1) * 128], s_ident[:])
                p.step("tensor", tr(), reads=[r_wqbf, r_ident],
                       writes=[r_ptrc], tag=f"w2t.{widx}.{ch}.{c}")
            for half in range(2):
                def cp(half=half):
                    return lambda e: e.tensor_copy(
                        s_wtx[:, half * 512:(half + 1) * 512],
                        p_trc[:, half * 512:(half + 1) * 512])
                p.step("vector", cp(), reads=[r_ptrc], writes=[r_wtx],
                       tag=f"w2c.{widx}.{ch}.{half}")
            if widx < 2:
                dst = scr[:, ch * 128:(ch + 1) * 128].rearrange(
                    "(c q) o -> q c o", q=128)
            else:
                dst = scr[(ch // 8) * 1024:(ch // 8 + 1) * 1024,
                          (ch % 8) * 128:((ch % 8) + 1) * 128].rearrange(
                    "(c q) o -> q c o", q=128)
            tok = dma(dst, s_wtx[:].rearrange("p (c o) -> p c o", c=8),
                      reads=[r_wtx], writes=[rscr],
                      tag=f"w2s{widx}.{ch}")
            wq2_store_toks[widx].append(tok)

    batch_stats(s_c1sum, r_c1sum, s_c1sq, r_c1sq, s_mu2, r_mu2, s_rs2,
                r_rs2, "c1")

    xtiles_by_qg = {}

    def c2_prep(qg):
        xnt2, rxnt2 = s_xn2t[qg % 2]
        xtiles = []
        for qs in range(2):
            t = qg * 2 + qs
            xbuf, rxbuf = s_c1x[(qg % 2) * 2 + qs]
            dma(xbuf[:], scr_x1[t * 128:(t + 1) * 128, :], reads=[r_sx1[t]],
                writes=[rxbuf], tag=f"c2x{t}")
            xtiles.append((xbuf, rxbuf))
            xn2, rxn2 = s_xn2[qs]
            p.step("vector",
                   lambda e, xb=xbuf, t=t, xn2=xn2: e.tensor_scalar(
                       xn2[:], xb[:], s_mu2[:, t:t + 1],
                       s_rs2[:, t:t + 1], ALU.subtract, ALU.mult),
                   reads=[rxbuf, r_mu2, r_rs2], writes=[rxn2],
                   tag=f"c2n{t}")
            for c in range(8):
                def tr(c=c, xn2=xn2):
                    return lambda e: e.transpose(
                        p_trc[:, c * 128:(c + 1) * 128],
                        xn2[:, c * 128:(c + 1) * 128], s_ident[:])
                p.step("tensor", tr(), reads=[rxn2, r_ident],
                       writes=[r_ptrc], tag=f"c2t{t}.{c}")
            for c in range(8):
                def ev(c=c, qs=qs, xnt2=xnt2):
                    return lambda e: e.activation(
                        xnt2[:, c * 256 + qs * 128:c * 256 + qs * 128 + 128],
                        p_trc[:, c * 128:(c + 1) * 128], AF.Identity,
                        scale=s_gb2[:, c:c + 1], bias=s_gb2[:, 8 + c:9 + c])
                p.step("scalar", ev(), reads=[r_ptrc, r_gb2],
                       writes=[rxnt2], tag=f"c2e{t}.{c}")
        xtiles_by_qg[qg] = xtiles

    def c2_w1(qg, ic):
        xnt2, rxnt2 = s_xn2t[qg % 2]
        ph, rph = p_h[ic % 3]
        h1c, rh1c = s_h1c[ic % 3]
        for dinc in range(8):
            def m1(dinc=dinc, ic=ic, ph=ph, xnt2=xnt2):
                return lambda e: e.matmul(
                    ph[:, 0:256],
                    s_w1b[:, dinc * INNER + ic * 128:
                          dinc * INNER + (ic + 1) * 128],
                    xnt2[:, dinc * 256:(dinc + 1) * 256],
                    start=(dinc == 0), stop=(dinc == 7),
                    skip_group_check=True)
            p.step("tensor", m1(), reads=[r_w1b, rxnt2], writes=[rph],
                   tag=f"w1.{qg}.{ic}.{dinc}")
        def gel(ic=ic, ph=ph, h1c=h1c):
            return lambda e: e.activation(h1c[:], ph[:, 0:256], AF.Gelu,
                                          bias=s_b1c[:, ic:ic + 1])
        p.step("scalar", gel(), reads=[rph, r_b1c], writes=[rh1c],
               tag=f"gel{qg}.{ic}")

    def c2_w2(qg, ic):
        h1c, rh1c = s_h1c[ic % 3]
        for qs in range(2):
            pm, rpm = p_mlp[qs]
            for h2 in range(2):
                def m2(ic=ic, qs=qs, h2=h2, pm=pm, h1c=h1c):
                    return lambda e: e.matmul(
                        pm[:, h2 * 512:(h2 + 1) * 512],
                        h1c[:, qs * 128:(qs + 1) * 128],
                        s_w2b[:, ic * DM + h2 * 512:
                              ic * DM + (h2 + 1) * 512],
                        start=(ic == 0), stop=(ic == 31),
                        skip_group_check=True)
                p.step("tensor", m2(), reads=[rh1c, r_w2b],
                       writes=[rpm], tag=f"w2m{qg}.{ic}.{qs}.{h2}")

    def c2_evac(qg):
        for qs in range(2):
            t = qg * 2 + qs
            pm, rpm = p_mlp[qs]
            xb, rxb = xtiles_by_qg[qg][qs]
            xpp, rxpp = s_xpp[qs]
            x0b, rx0b = s_xt[qs]
            dma(x0b[:], x_in[t * 128:(t + 1) * 128, :], writes=[rx0b],
                tag=f"c2x0{t}")
            p.step("vector",
                   lambda e, pm=pm, xpp=xpp: e.tensor_add(
                       xpp[:], pm[:], s_b2b[:]),
                   reads=[rpm, r_b2b], writes=[rxpp], tag=f"c2b{t}")
            p.step("vector",
                   lambda e, xpp=xpp, xb=xb: e.tensor_add(xpp[:], xpp[:],
                                                          xb[:]),
                   reads=[rxb, rxpp], writes=[rxpp], tag=f"c2r{t}")
            p.step("vector",
                   lambda e, xpp=xpp, x0b=x0b: e.tensor_add(xpp[:], xpp[:],
                                                            x0b[:]),
                   reads=[rx0b, rxpp], writes=[rxpp], tag=f"c2r2{t}")
            dma(scr_x1[t * 128:(t + 1) * 128, :], xpp[:], reads=[rxpp],
                writes=[r_sx1[t]], tag=f"c2s{t}")

    NIC = INNER // 128
    c2_prep(0)
    jobs = [(qg, ic) for qg in range(8) for ic in range(NIC)]
    DEPTH = 2
    for i, (qg, ic) in enumerate(jobs):
        if i >= DEPTH:
            pqg, pic = jobs[i - DEPTH]
            c2_w2(pqg, pic)
            if pic == NIC - 1:
                c2_evac(pqg)
        if ic == NIC - 1 and qg < 7:
            c2_prep(qg + 1)
        c2_w1(qg, ic)
        wq2_pump(1)
    for i in range(len(jobs) - DEPTH, len(jobs)):
        pqg, pic = jobs[i]
        c2_w2(pqg, pic)
        if pic == NIC - 1:
            c2_evac(pqg)
    wq2_pump(200)


    stC.close()
    p.add_fence([r_w1b, r_w2b, r_wq1, r_wqbf, r_wtx, _xn2[1],
                 p_h[0][1], p_h[1][1], p_h[2][1], p_mlp[0][1], p_mlp[1][1], r_ptrc]
                + [b[1] for b in (s_c1x + s_xpp + s_wstc + s_xn2t
                                  + s_h1c)])

    stD = ExitStack()
    s_wgtb, r_wgtb = sb("d_wgtb", [128, 8 * INNER], BF16, stack=stD)
    s_wutb, r_wutb = sb("d_wutb", [128, 8 * INNER], BF16, stack=stD)
    s_x2t = [sb(f"d_x2t{i}", [128, DM], F32, stack=stD) for i in range(2)]
    _q1 = sb("d_q1", [128, DM], F32, stack=stD)
    s_q1 = [_q1, _q1]
    s_xq = [sb(f"d_xq{i}", [128, DM], BF16, stack=stD) for i in range(2)]
    s_xqt = [sb(f"d_xqt{i}", [128, DM], BF16, stack=stD) for i in range(3)]
    s_sg, r_sg = sb("d_sg", [128, 512], F32, stack=stD)
    s_h, r_h = sb("d_h", [128, INNER], F32, stack=stD)
    _q2 = sb("d_q2", [128, 1024], F32, stack=stD)
    s_q2 = [_q2, _q1]
    s_xq2 = [sb(f"d_xq2{i}", [128, 1024], BF16, stack=stD) for i in range(2)]
    s_hm, r_hm = sb("d_hm", [128, 16], F32, stack=stD)
    p_g = [ps(f"p_g{i}", [128, 512], F32, stack=stD) for i in range(2)]
    p_u = [ps(f"p_u{i}", [128, 512], F32, stack=stD) for i in range(2)]
    p_trd = [ps(f"p_trd{i}", [128, 1024], BF16, stack=stD) for i in range(3)]

    r_wgtc = [p.res() for _ in range(8)]
    r_wutc = [p.res() for _ in range(8)]

    def ldw_chunk(c):
        dma(s_wgtb[:, c * INNER:(c + 1) * INNER],
            scr_wgt[c * 128:(c + 1) * 128, :],
            reads=[r_swg], writes=[r_wgtc[c]], tag=f"ldwgt{c}",
            extra_deps=wq2_store_toks[0])
        dma(s_wutb[:, c * INNER:(c + 1) * INNER],
            scr_wut[c * 128:(c + 1) * 128, :],
            reads=[r_swu], writes=[r_wutc[c]], tag=f"ldwut{c}",
            extra_deps=wq2_store_toks[1])




    for t in range(NT):
        xbuf, rxbuf = s_x2t[t % 2]
        dy, rdy = s_dummy[t % 2]
        dma(xbuf[:], scr_x1[t * 128:(t + 1) * 128, :], reads=[r_sx1[t]],
            writes=[rxbuf], tag=f"c3x{t}")
        p.step("scalar",
               lambda e, xb=xbuf, dy=dy: e.activation(dy[:], xb[:],
                                                      AF.Square),
               reads=[rxbuf], writes=[rdy], tag=f"c3sq{t}")
        p.step("vector",
               lambda e, dy=dy, t=t: e.tensor_reduce(
                   s_c3sq[:, t:t + 1], dy[:], AX.X, ALU.add),
               reads=[rdy], writes=[r_c3sq], tag=f"c3qr{t}")
        p.step("vector",
               lambda e, xb=xbuf, t=t: e.tensor_reduce(
                   s_c3sum[:, t:t + 1], xb[:], AX.X, ALU.add),
               reads=[rxbuf], writes=[r_c3sum], tag=f"c3sr{t}")
        p.step("vector",
               lambda e, t=t: e.tensor_scalar(
                   s_mu3[:, t:t + 1], s_c3sum[:, t:t + 1], 1.0 / DM, None,
                   ALU.mult),
               reads=[r_c3sum], writes=[r_mu3], tag=f"c3mu{t}",
               sync_out=True)
        p.step("vector",
               lambda e, xb=xbuf, dy=dy, t=t: e.tensor_scalar(
                   dy[:], xb[:], s_mu3[:, t:t + 1], None, ALU.subtract),
               reads=[rxbuf, r_mu3, rdy], writes=[rdy], tag=f"c3d{t}")
        p.step("vector",
               lambda e, dy=dy, t=t: e.tensor_reduce(
                   s_axm[:, t:t + 1], dy[:], AX.X, ALU.max,
                   apply_absolute_value=True),
               reads=[rdy], writes=[r_axm], tag=f"c3am{t}")
        if t < 8:
            ldw_chunk(t)
    p.step("vector",
           lambda e: e.tensor_scalar(s_smal[:, 16:32], s_c3sq[:], 1.0 / DM,
                                     None, ALU.mult),
           reads=[r_c3sq], writes=[r_smal], tag="c3ms")
    p.step("vector",
           lambda e: e.tensor_mul(s_smal[:, 32:48], s_mu3[:], s_mu3[:]),
           reads=[r_mu3, r_smal], writes=[r_smal], tag="c3m2")
    p.step("vector",
           lambda e: e.scalar_tensor_tensor(
               s_smal[:, 16:32], s_smal[:, 16:32], LNEPS,
               s_smal[:, 32:48], ALU.add, ALU.subtract),
           reads=[r_smal], writes=[r_smal], tag="c3vp", sync_out=True)
    p.step("vector",
           lambda e: e.reciprocal(s_smal[:, 32:48], s_smal[:, 16:32]),
           reads=[r_smal], writes=[r_smal], tag="c3rc", sync_out=True)
    p.step("scalar",
           lambda e: e.activation(s_smal[:, 32:48], s_smal[:, 32:48],
                                  AF.Sqrt),
           reads=[r_smal], writes=[r_smal], tag="c3rq", sync_out=True)
    p.step("vector",
           lambda e: e.tensor_mul(s_smal[:, 0:16], s_smal[:, 32:48],
                                  s_smal[:, 32:48]),
           reads=[r_smal], writes=[r_smal], tag="c3n1")
    p.step("vector",
           lambda e: e.tensor_mul(s_smal[:, 0:16], s_smal[:, 0:16],
                                  s_smal[:, 16:32]),
           reads=[r_smal], writes=[r_smal], tag="c3n2")
    p.step("vector",
           lambda e: e.tensor_scalar(s_smal[:, 0:16], s_smal[:, 0:16],
                                     -0.5, 1.5, ALU.mult, ALU.add),
           reads=[r_smal], writes=[r_smal], tag="c3n3")
    p.step("vector",
           lambda e: e.tensor_mul(s_rs3[:], s_smal[:, 32:48],
                                  s_smal[:, 0:16]),
           reads=[r_smal], writes=[r_rs3], tag="c3n4", sync_out=True)
    p.step("vector",
           lambda e: e.tensor_mul(s_axm[:], s_axm[:], s_rs3[:]),
           reads=[r_rs3, r_axm], writes=[r_axm], tag="c3ax")
    p.step("vector",
           lambda e: e.tensor_scalar(s_smal[:, 0:16], s_axm[:], 1e-5, None,
                                     ALU.max),
           reads=[r_axm], writes=[r_smal], tag="dqmc")
    p.step("vector",
           lambda e: e.reciprocal(s_smal[:, 16:32], s_smal[:, 0:16]),
           reads=[r_smal], writes=[r_smal], tag="dqrc", sync_out=True)
    p.step("vector",
           lambda e: e.tensor_scalar(s_dsx[:], s_smal[:, 16:32], 127.0,
                                     None, ALU.mult),
           reads=[r_smal], writes=[r_dsx], tag="dqsx")
    p.step("vector",
           lambda e: e.tensor_scalar(s_smal[:, 32:48], s_smal[:, 0:16],
                                     1.0 / 127.0, None, ALU.mult),
           reads=[r_smal], writes=[r_smal], tag="dqos")
    p.step("vector",
           lambda e: e.tensor_scalar(s_dcg[:], s_smal[:, 32:48],
                                     s_wcol[:, 6:7], None, ALU.mult),
           reads=[r_smal, r_wcol], writes=[r_dcg], tag="dqcg")
    p.step("vector",
           lambda e: e.tensor_scalar(s_dcu[:], s_smal[:, 32:48],
                                     s_wcol[:, 7:8], None, ALU.mult),
           reads=[r_smal, r_wcol], writes=[r_dcu], tag="dqcu")

    rx2c = [[p.res() for _ in range(2)] for _ in range(2)]

    def d2_pre(t):
        xbuf, rxbuf = s_x2t[t % 2]
        for ch in range(2):
            dma(xbuf[:, ch * 512:(ch + 1) * 512],
                scr_x1[t * 128:(t + 1) * 128, ch * 512:(ch + 1) * 512],
                reads=[r_sx1[t]], writes=[rx2c[t % 2][ch], rxbuf],
                tag=f"d2x{t}.{ch}")
        p.step("vector",
               lambda e, xb=xbuf, t=t: e.tensor_scalar(
                   xb[:], xb[:], s_mu3[:, t:t + 1], s_rs3[:, t:t + 1],
                   ALU.subtract, ALU.mult),
               reads=[r_mu3, r_rs3, rx2c[t % 2][0], rx2c[t % 2][1],
                      rxbuf], writes=[rxbuf], tag=f"c3n{t}")
        dma(scr_x2[t * 128:(t + 1) * 128, :], xbuf[:], reads=[rxbuf],
            writes=[r_sx2[t]], tag=f"c3s{t}")

    def d2_stage1a(t):
        xbuf, rxbuf = s_x2t[t % 2]
        q1, rq1 = s_q1[t % 2]
        xq, rxq = s_xq[t % 2]
        p.step("scalar",
               lambda e, xb=xbuf, q1=q1, t=t: e.activation(
                   q1[:], xb[:], AF.Identity, scale=s_dsx[:, t:t + 1]),
               reads=[rxbuf, r_dsx], writes=[rq1], tag=f"d2q1{t}")
        p.step("vector",
               lambda e, q1=q1: e.tensor_scalar(q1[:], q1[:], MAGIC, None,
                                                ALU.add),
               reads=[rq1], writes=[rq1], tag=f"d2q1b{t}")
        p.step("vector",
               lambda e, q1=q1, xq=xq: e.tensor_scalar(
                   xq[:], q1[:], MAGIC, None, ALU.subtract),
               reads=[rq1], writes=[rxq], tag=f"d2q2{t}")

    def d2_stage1b(t):
        pr = t % 3
        xq, rxq = s_xq[t % 2]
        xqt, rxqt = s_xqt[pr]
        ptrd, rptrd = p_trd[pr]
        for c in range(8):
            def tr(c=c, xq=xq, ptrd=ptrd):
                return lambda e: e.transpose(ptrd[:, c * 128:(c + 1) * 128],
                                             xq[:, c * 128:(c + 1) * 128],
                                             s_ident[:])
            p.step("tensor", tr(), reads=[rxq, r_ident], writes=[rptrd],
                   tag=f"d2t{t}.{c}")
        for half in range(2):
            def cp(half=half, xqt=xqt, ptrd=ptrd):
                return lambda e: e.activation(
                    xqt[:, half * 512:(half + 1) * 512],
                    ptrd[:, half * 512:(half + 1) * 512], AF.Identity)
            p.step("scalar", cp(), reads=[rptrd], writes=[rxqt],
                   tag=f"d2tc{t}.{half}")

    def d2_stage2(t):
        pr = t % 3
        xqt, rxqt = s_xqt[pr]
        for oc in range(8):
            pg, rpg = p_g[oc % 2]
            pu, rpu = p_u[oc % 2]
            for dinc in range(8):
                def mg(dinc=dinc, oc=oc, pg=pg, xqt=xqt):
                    return lambda e: e.matmul(
                        pg[:],
                        xqt[:, dinc * 128:(dinc + 1) * 128],
                        s_wgtb[:, dinc * INNER + oc * 512:
                               dinc * INNER + (oc + 1) * 512],
                        start=(dinc == 0), stop=(dinc == 7),
                        skip_group_check=True)
                p.step("tensor", mg(), reads=[rxqt, r_wgtc[dinc]],
                       writes=[rpg], tag=f"d2g{t}.{oc}.{dinc}")
                def mu_(dinc=dinc, oc=oc, pu=pu, xqt=xqt):
                    return lambda e: e.matmul(
                        pu[:],
                        xqt[:, dinc * 128:(dinc + 1) * 128],
                        s_wutb[:, dinc * INNER + oc * 512:
                               dinc * INNER + (oc + 1) * 512],
                        start=(dinc == 0), stop=(dinc == 7),
                        skip_group_check=True)
                p.step("tensor", mu_(), reads=[rxqt, r_wutc[dinc]],
                       writes=[rpu], tag=f"d2u{t}.{oc}.{dinc}")
            def sil(pg=pg, t=t):
                return lambda e: e.activation(s_sg[:], pg[:], AF.Silu,
                                              scale=s_dcg[:, t:t + 1])
            p.step("scalar", sil(), reads=[rpg, r_dcg], writes=[r_sg],
                   tag=f"d2s{t}.{oc}")
            def hmul(oc=oc, pu=pu, t=t):
                return lambda e: e.scalar_tensor_tensor(
                    s_h[:, oc * 512:(oc + 1) * 512], pu[:],
                    s_dcu[:, t:t + 1], s_sg[:], ALU.mult, ALU.mult)
            p.step("vector", hmul(), reads=[rpu, r_dcu, r_sg],
                   writes=[r_h], tag=f"d2h{t}.{oc}")
            def hab(oc=oc, t=t):
                return lambda e: e.tensor_reduce(
                    s_smal[:, 8 + (oc % 2):9 + (oc % 2)],
                    s_h[:, oc * 512:(oc + 1) * 512], AX.X, ALU.max,
                    apply_absolute_value=True)
            p.step("vector", hab(), reads=[r_h], writes=[r_smal],
                   tag=f"d2ha{t}.{oc}", sync_out=True)
            def habm(oc=oc, t=t):
                if oc == 0:
                    return lambda e: e.tensor_copy(s_hm[:, t:t + 1],
                                                   s_smal[:, 8:9])
                return lambda e: e.tensor_max(
                    s_hm[:, t:t + 1], s_hm[:, t:t + 1],
                    s_smal[:, 8 + (oc % 2):9 + (oc % 2)])
            p.step("vector", habm(), reads=[r_smal, r_hm], writes=[r_hm],
                   tag=f"d2hm{t}.{oc}")
        p.step("vector",
               lambda e, t=t: e.tensor_scalar(s_smal[:, 1:2],
                                              s_hm[:, t:t + 1], 1e-5, None,
                                              ALU.max),
               reads=[r_hm], writes=[r_smal], tag=f"d2mc2{t}")
        p.step("vector",
               lambda e: e.reciprocal(s_smal[:, 2:3], s_smal[:, 1:2]),
               reads=[r_smal], writes=[r_smal], tag=f"d2rc2{t}",
               sync_out=True)
        p.step("vector",
               lambda e: e.tensor_scalar(s_smal[:, 3:4], s_smal[:, 2:3],
                                         127.0, None, ALU.mult),
               reads=[r_smal], writes=[r_smal], tag=f"d2sx2{t}",
               sync_out=True)
        p.step("vector",
               lambda e: e.tensor_scalar(s_smal[:, 4:5], s_smal[:, 1:2],
                                         1.0 / 127.0, None, ALU.mult),
               reads=[r_smal], writes=[r_smal], tag=f"d2os2{t}")
        p.step("vector",
               lambda e, t=t: e.tensor_mul(s_cbd[:, t:t + 1],
                                           s_smal[:, 4:5], s_wcol[:, 0:1]),
               reads=[r_smal, r_wcol], writes=[r_cbd], tag=f"d2cd{t}")
        for oc in range(4):
            xq2, rxq2 = s_xq2[oc % 2]
            q2, rq2 = s_q2[oc % 2]
            def q1f(oc=oc, q2=q2):
                return lambda e: e.activation(
                    q2[:], s_h[:, oc * 1024:(oc + 1) * 1024], AF.Identity,
                    scale=s_smal[:, 3:4])
            p.step("scalar", q1f(), reads=[r_h, r_smal], writes=[rq2],
                   tag=f"d2p1{t}.{oc}")
            p.step("vector",
                   lambda e, q2=q2: e.tensor_scalar(q2[:], q2[:], MAGIC,
                                                    None, ALU.add),
                   reads=[rq2], writes=[rq2], tag=f"d2p1b{t}.{oc}")
            p.step("vector", lambda e, xq2=xq2, q2=q2: e.tensor_scalar(
                xq2[:], q2[:], MAGIC, None,
                ALU.subtract), reads=[rq2], writes=[rxq2],
                tag=f"d2p2{t}.{oc}")
            dma(scr_xq2[t * 128:(t + 1) * 128,
                        oc * 1024:(oc + 1) * 1024], xq2[:],
                reads=[rxq2], writes=[r_sxq2[t]], tag=f"d2sx{t}.{oc}")

    d2_pre(0)
    d2_stage1a(0)
    d2_stage1b(0)
    d2_pre(1)
    d2_stage1a(1)
    d2_stage1b(1)
    for t in range(NT):
        if t + 2 < NT:
            d2_pre(t + 2)
            d2_stage1a(t + 2)
        d2_stage2(t)
        if t + 2 < NT:
            d2_stage1b(t + 2)

    if stage == "C":
        d_x2 = dbg_out("d_x2", [S, DM])
        d_wgt = dbg_out("d_wgt", [DM, INNER], BF16)
        d_wdt = dbg_out("d_wdt", [INNER, DM], BF16)
        toks = [dma(d_x2, scr_x2, reads=r_sx2[:], tag="dbgx2"),
                dma(d_wgt, scr_wgt, reads=[r_swg], tag="dbgwgt"),
                dma(d_wdt, scr_wdt, reads=[r_swd], tag="dbgwdt")]
        p.step("sync", lambda e: e.nop(), extra_deps=toks, tag="fin")
        p.emit(nc, top)
        return nc, dbg, top


    stD.close()
    p.add_fence(r_wgtc + r_wutc + [r_sg, r_h, r_hm,
                 p_g[0][1], p_g[1][1], p_u[0][1], p_u[1][1],
                 p_trd[0][1], p_trd[1][1], p_trd[2][1]]
                + [b[1] for b in (s_x2t + s_q1 + s_xq + s_xqt + s_xq2
                                  + [_q2])])

    stE = ExitStack()
    s_wdtb, r_wdtb = sb("e_wdtb", [128, 32 * DM], BF16, stack=stE)
    s_xq2t = [sb(f"e_xq2t{i}", [128, INNER], BF16, stack=stE)
              for i in range(2)]
    s_xq2T = [sb(f"e_xq2T{i}", [128, 32 * 128], BF16, stack=stE)
              for i in range(4)]
    s_ex2 = [sb(f"e_x2{i}", [128, DM], F32, stack=stE) for i in range(4)]
    s_y, r_y = sb("e_y", [128, DM], F32, stack=stE)
    s_o = [sb(f"e_o{i}", [128, DM], F32, stack=stE) for i in range(2)]
    p_d = [ps(f"p_d{i}", [128, 1024], F32, stack=stE) for i in range(2)]
    p_tre = [ps(f"p_tre{i}", [128, 1024], BF16, stack=stE) for i in range(2)]

    r_wdtc = [p.res() for _ in range(32)]
    for c in range(32):
        dma(s_wdtb[:, c * DM:(c + 1) * DM],
            scr_wdt[c * 128:(c + 1) * 128, :],
            reads=[r_swd], writes=[r_wdtc[c]], tag=f"ldwdt{c}",
            extra_deps=wq2_store_toks[2])

    out_toks = []
    rq_c = [[p.res() for _ in range(4)] for _ in range(2)]
    rex_c = [[p.res() for _ in range(2)] for _ in range(4)]

    def d3_prep(t):
        qbuf, rqbuf = s_xq2t[t % 2]
        for quad in range(4):
            dma(qbuf[:, quad * 1024:(quad + 1) * 1024],
                scr_xq2[t * 128:(t + 1) * 128,
                        quad * 1024:(quad + 1) * 1024],
                reads=[r_sxq2[t]], writes=[rq_c[t % 2][quad], rqbuf],
                tag=f"d3q{t}.{quad}")
        xqT, rxqT = s_xq2T[t % 4]
        for quad in range(4):
            ptr, rptr = p_tre[quad % 2]
            for c in range(8):
                cc = quad * 8 + c
                def tr(cc=cc, c=c, ptr=ptr, qbuf=qbuf):
                    return lambda e: e.transpose(
                        ptr[:, c * 128:(c + 1) * 128],
                        qbuf[:, cc * 128:(cc + 1) * 128], s_ident[:])
                p.step("tensor", tr(), reads=[rq_c[t % 2][quad], r_ident],
                       writes=[rptr], tag=f"d3t{t}.{cc}")
            for half in range(2):
                def cp(quad=quad, half=half, ptr=ptr, xqT=xqT):
                    return lambda e: e.tensor_copy(
                        xqT[:, quad * 1024 + half * 512:
                            quad * 1024 + (half + 1) * 512],
                        ptr[:, half * 512:(half + 1) * 512])
                p.step("vector", cp(), reads=[rptr], writes=[rxqT],
                       tag=f"d3c{t}.{quad}.{half}")
        xbuf, rxbuf = s_ex2[t % 4]
        for ch in range(2):
            dma(xbuf[:, ch * 512:(ch + 1) * 512],
                scr_x2[t * 128:(t + 1) * 128, ch * 512:(ch + 1) * 512],
                reads=[r_sx2[t]], writes=[rex_c[t % 4][ch], rxbuf],
                tag=f"d3x{t}.{ch}")

    def d3_main(t):
        xqT, rxqT = s_xq2T[t % 4]
        xbuf, rxbuf = s_ex2[t % 4]
        pd, rpd = p_d[t % 2]
        for i2c in range(32):
            for h2 in range(2):
                def md(i2c=i2c, h2=h2, pd=pd, xqT=xqT):
                    return lambda e: e.matmul(
                        pd[:, h2 * 512:(h2 + 1) * 512],
                        xqT[:, i2c * 128:(i2c + 1) * 128],
                        s_wdtb[:, i2c * DM + h2 * 512:
                               i2c * DM + (h2 + 1) * 512],
                        start=(i2c == 0), stop=(i2c == 31),
                        skip_group_check=True)
                p.step("tensor", md(), reads=[rxqT, r_wdtc[i2c]],
                       writes=[rpd], tag=f"d3m{t}.{i2c}.{h2}")
        p.step("vector",
               lambda e, pd=pd, xb=xbuf, t=t: e.scalar_tensor_tensor(
                   s_y[:], pd[:], s_cbd[:, t:t + 1], xb[:], ALU.mult,
                   ALU.add),
               reads=[rpd, r_cbd, rex_c[t % 4][0], rex_c[t % 4][1],
                      rxbuf], writes=[r_y], tag=f"d3y{t}")
        dy3, rdy3 = s_dummy[t % 2]
        p.step("scalar",
               lambda e, dy3=dy3: e.activation(dy3[:], s_y[:], AF.Square),
               reads=[r_y], writes=[rdy3], tag=f"d3sq{t}")
        p.step("vector",
               lambda e, dy3=dy3: e.tensor_reduce(s_smal[:, 0:1], dy3[:],
                                                  AX.X, ALU.add),
               reads=[rdy3], writes=[r_smal], tag=f"d3sq{t}r",
               sync_out=True)
        p.step("vector",
               lambda e: e.tensor_reduce(s_smal[:, 1:2], s_y[:], AX.X,
                                         ALU.add),
               reads=[r_y], writes=[r_smal], tag=f"d3su{t}", sync_out=True)
        p.step("vector",
               lambda e: e.tensor_scalar(s_smal[:, 6:7], s_smal[:, 1:2],
                                         1.0 / DM, None, ALU.mult),
               reads=[r_smal], writes=[r_smal], tag=f"d3mu{t}")
        p.step("vector",
               lambda e: e.tensor_scalar(s_smal[:, 2:3], s_smal[:, 0:1],
                                         1.0 / DM, None, ALU.mult),
               reads=[r_smal], writes=[r_smal], tag=f"d3ms{t}")
        p.step("vector",
               lambda e: e.tensor_mul(s_smal[:, 3:4], s_smal[:, 6:7],
                                      s_smal[:, 6:7]),
               reads=[r_smal], writes=[r_smal], tag=f"d3mm{t}")
        p.step("vector",
               lambda e: e.tensor_sub(s_smal[:, 4:5], s_smal[:, 2:3],
                                      s_smal[:, 3:4]),
               reads=[r_smal], writes=[r_smal], tag=f"d3v{t}",
               flush=lambda e: e.tensor_copy(s_vfl[:, 0:1], s_smal[:, 4:5]))
        p.step("scalar",
               lambda e: e.activation(s_smal[:, 5:6], s_smal[:, 4:5], AF.Ln,
                                      bias=s_cvec[:, 0:1]),
               reads=[r_smal, r_cvec], writes=[r_smal], tag=f"d3l{t}")
        p.step("scalar",
               lambda e: e.activation(s_smal[:, 7:8], s_smal[:, 5:6],
                                      AF.Exp, scale=-0.5),
               reads=[r_smal], writes=[r_smal], tag=f"d3r{t}",
               flush=lambda e: e.copy(s_sfl[:, 0:1], s_smal[:, 7:8]))
        obuf, robuf = s_o[t % 2]
        p.step("vector",
               lambda e, ob=obuf: e.tensor_scalar(
                   ob[:], s_y[:], s_smal[:, 6:7], s_smal[:, 7:8],
                   ALU.subtract, ALU.mult),
               reads=[r_y, r_smal], writes=[robuf], tag=f"d3n{t}")
        out_toks.append(dma(out_t[t * 128:(t + 1) * 128, :], obuf[:],
                            reads=[robuf], tag=f"out{t}"))

    d3_prep(0)
    d3_prep(1)
    for tp in range(0, NT, 2):
        if tp + 2 < NT:
            d3_prep(tp + 2)
            d3_prep(tp + 3)
        d3_main(tp)
        d3_main(tp + 1)
    p.step("sync", lambda e: e.nop(), extra_deps=out_toks, tag="fin")
    stE.close()
    p.emit(nc, top)
    return nc, dbg, top


_CACHE = {}


def _get_program(stage="full"):
    if stage not in _CACHE:
        _CACHE[stage] = build_program(stage)
    return _CACHE[stage]


def make_in_maps(inputs):
    c = _host_consts()
    f32 = np.float32
    base = dict(
        ln1_g=np.ascontiguousarray(inputs["L0_attn_ln1_g"], f32),
        ln1_b=np.ascontiguousarray(inputs["L0_attn_ln1_b"], f32),
        wqkv=np.ascontiguousarray(inputs["L0_attn_Wqkv"], f32),
        bqkv=np.ascontiguousarray(inputs["L0_attn_bqkv"], f32),
        wo=np.ascontiguousarray(inputs["L0_attn_Wo"], f32),
        bo=np.ascontiguousarray(inputs["L0_attn_bo"], f32),
        ln2_g=np.ascontiguousarray(inputs["L0_attn_ln2_g"], f32),
        ln2_b=np.ascontiguousarray(inputs["L0_attn_ln2_b"], f32),
        w1=np.ascontiguousarray(inputs["L0_attn_W1"], f32),
        b1=np.ascontiguousarray(inputs["L0_attn_b1"], f32),
        w2=np.ascontiguousarray(inputs["L0_attn_W2"], f32),
        b2=np.ascontiguousarray(inputs["L0_attn_b2"], f32),
        wg=np.ascontiguousarray(inputs["L0_ffn_w_gate"], f32),
        wu=np.ascontiguousarray(inputs["L0_ffn_w_up"], f32),
        wd=np.ascontiguousarray(inputs["L0_ffn_w_down"], f32),
        cosF=c["cosF"], sinnF=c["sinnF"], trimask=c["trimask"],
        ident=c["ident"], identf=c["identf"], onesf=c["onesf"],
        ecat=c["ecat"], cvec=c["cvec"], sel=c["sel"], onescol=c["onescol"],
    )
    x = np.ascontiguousarray(inputs["L0_attn_x"], f32)
    maps = []
    for core in range(NCORES):
        m = dict(base)
        m["x"] = np.ascontiguousarray(x[core])
        maps.append(m)
    return maps


def kernel(**inputs):
    nc, dbg, top = _get_program("full")
    maps = make_in_maps(inputs)
    res = run_bass_kernel_spmd(nc, maps, core_ids=list(range(NCORES)))
    out = np.stack([res.results[i]["out"] for i in range(NCORES)], axis=0)
    return out.astype(np.float32)


# version 35
# speedup vs baseline: 1.5189x; 1.4908x over previous
import numpy as np
import ml_dtypes
import concourse.bass as bass
import concourse.mybir as mybir
from concourse.bass_utils import run_bass_kernel_spmd

F32 = mybir.dt.float32
BF16 = mybir.dt.bfloat16
AF = mybir.ActivationFunctionType
ALU = mybir.AluOpType
AX = mybir.AxisListType

MAGIC = 12582912.0
S, DM, H, HD, INNER = 2048, 1024, 16, 64, 4096
NT = S // 128
NCORES = 8
LNEPS = 1e-5

ENGINES = ("tensor", "vector", "scalar", "gpsimd", "sync")


class _Step:
    __slots__ = ("idx", "eng", "fn", "deps", "marked", "semval", "dma",
                 "dma_sem", "dma_val", "tag", "sync_out", "flush")

    def __init__(self, idx, eng, fn, deps, dma, tag, sync_out=False,
                 flush=None):
        self.idx, self.eng, self.fn, self.deps = idx, eng, fn, deps
        self.marked = False
        self.semval = None
        self.dma = dma
        self.dma_sem = None
        self.dma_val = None
        self.tag = tag
        self.sync_out = sync_out
        self.flush = flush


class Res:
    __slots__ = ("writers", "readers")

    def __init__(self, fence=()):
        self.writers = list(fence)
        self.readers = []


class Prog:
    def __init__(self):
        self.steps = []
        self.per_engine = {e: [] for e in ENGINES}
        self.fence = []

    def res(self):
        return Res(self.fence)

    def step(self, eng, fn, reads=(), writes=(), dma=False, tag="",
             extra_deps=(), sync_out=False, flush=None):
        deps = list(extra_deps)
        for r in reads:
            deps += r.writers
        for w in writes:
            deps += w.writers
            deps += w.readers
        s = _Step(len(self.steps), eng, fn, deps, dma, tag, sync_out, flush)
        self.steps.append(s)
        self.per_engine[eng].append(s)
        for r in reads:
            r.readers.append(s)
        for w in writes:
            w.writers = [s]
            w.readers = []
        return s

    def add_fence(self, res_list):
        toks = []
        for r in res_list:
            toks += r.writers
            toks += r.readers
        toks += self.fence
        latest = {}
        dmas = {}
        for t in toks:
            if t.dma:
                dmas[id(t)] = t
            else:
                if t.eng not in latest or t.idx > latest[t.eng].idx:
                    latest[t.eng] = t
        self.fence = list(latest.values()) + list(dmas.values())

    def finalize(self):
        for s in self.steps:
            dd = []
            seen = set()
            for d in s.deps:
                if d is s or id(d) in seen:
                    continue
                seen.add(id(d))
                dd.append(d)
                if d.dma:
                    pass
                else:
                    d.marked = True
            s.deps = dd
        for eng in ENGINES:
            c = 0
            for s in self.per_engine[eng]:
                if s.marked:
                    c += 1
                    s.semval = c

    def emit(self, nc, stack, n_dma_sems=48):
        psem = {e: stack.enter_context(nc.semaphore(f"prog_{e}"))
                for e in ENGINES}
        dsems = [stack.enter_context(nc.semaphore(f"dma{i}"))
                 for i in range(n_dma_sems)]
        dcount = [0] * n_dma_sems
        pool = {"sync": list(range(0, 32)), "gpsimd": list(range(32, 48))}
        di = {"sync": 0, "gpsimd": 0}
        for s in self.steps:
            if s.dma:
                pl = pool[s.eng]
                i = pl[di[s.eng]]
                s.dma_sem = i
                dcount[i] += 1
                s.dma_val = 16 * dcount[i]
                di[s.eng] = (di[s.eng] + 1) % len(pl)
        self.finalize()
        block = nc.Block()

        def make_body(eng_name):
            def body(eng):
                hw = {e: 0 for e in ENGINES}
                dhw = [0] * n_dma_sems
                for s in self.per_engine[eng_name]:
                    ew = {}
                    dw = {}
                    for d in s.deps:
                        if d.dma:
                            dw[d.dma_sem] = max(dw.get(d.dma_sem, 0),
                                                d.dma_val)
                        elif d.eng == "tensor" and eng_name == "tensor":
                            continue
                        else:
                            ew[d.eng] = max(ew.get(d.eng, 0), d.semval)
                    if s.dma and s.dma_val > 16:
                        dw[s.dma_sem] = max(dw.get(s.dma_sem, 0),
                                            s.dma_val - 16)
                    for e, v in sorted(ew.items()):
                        if v > hw[e]:
                            eng.wait_ge(psem[e], v)
                            hw[e] = v
                    for q, v in sorted(dw.items()):
                        if v > dhw[q]:
                            eng.wait_ge(dsems[q], v)
                            dhw[q] = v
                    inst = s.fn(eng)
                    if s.dma:
                        inst.then_inc(dsems[s.dma_sem], 16)
                    if s.marked:
                        inst.then_inc(psem[eng_name], 1)
            return body

        with block:
            block.tensor(make_body("tensor"))
            block.vector(make_body("vector"))
            block.scalar(make_body("scalar"))
            block.gpsimd(make_body("gpsimd"))
            block.sync(make_body("sync"))


def _host_consts():
    t = np.arange(S, dtype=np.float64)
    inv = 1.0 / 10000.0 ** (np.arange(0, HD, 2, dtype=np.float64) / HD)
    fr = t[:, None] * inv[None, :]
    emb = np.concatenate([fr, fr], axis=1)
    cos = np.cos(emb).astype(np.float32)
    sin = np.sin(emb).astype(np.float32)
    sinn = sin.copy()
    sinn[:, :32] *= -1.0
    cosF = np.tile(cos, (1, H)).astype(np.float32)
    sinnF = np.tile(sinn, (1, H)).astype(np.float32)
    trimask = np.triu(np.ones((128, 128), np.float32)).astype(ml_dtypes.bfloat16)
    ident = np.eye(128, dtype=ml_dtypes.bfloat16)
    identf = np.eye(128, dtype=np.float32)
    onesf = np.ones((1, 128), np.float32)
    ecat = np.zeros((128, 256), np.float32)
    for h in range(16):
        ecat[:, h * 16 + h] = 1.0
    ecat = ecat.astype(ml_dtypes.bfloat16)
    onescol = np.ones((128, 1), np.float32)
    sel = np.zeros((16, 8 * 128), np.float32)
    for hp in range(8):
        sel[2 * hp, hp * 128:hp * 128 + 64] = 1.0
        sel[2 * hp + 1, hp * 128 + 64:hp * 128 + 128] = 1.0
    cvec = np.zeros((128, 2), np.float32)
    cvec[:, 0] = LNEPS
    cvec[:, 1] = MAGIC
    return dict(cosF=cosF, sinnF=sinnF, trimask=trimask, ident=ident,
                identf=identf, onesf=onesf, ecat=ecat, cvec=cvec, sel=sel,
                onescol=onescol)


def build_program(stage="full"):
    from contextlib import ExitStack
    nc = bass.Bass("TRN2", target_bir_lowering=False)
    p = Prog()
    top = ExitStack()

    def din(name, shape, dtype=F32):
        return nc.dram_tensor(name, shape, dtype, kind="ExternalInput").ap()

    x_in = din("x", [S, DM])
    ln1g_in = din("ln1_g", [DM]); ln1b_in = din("ln1_b", [DM])
    wqkv_in = din("wqkv", [DM, 3 * DM]); bqkv_in = din("bqkv", [3 * DM])
    wo_in = din("wo", [DM, DM]); bo_in = din("bo", [DM])
    ln2g_in = din("ln2_g", [DM]); ln2b_in = din("ln2_b", [DM])
    w1_in = din("w1", [DM, INNER]); b1_in = din("b1", [INNER])
    w2_in = din("w2", [INNER, DM]); b2_in = din("b2", [DM])
    wg_in = din("wg", [INNER, DM]); wu_in = din("wu", [INNER, DM])
    wd_in = din("wd", [DM, INNER])
    cos_in = din("cosF", [S, DM]); sinn_in = din("sinnF", [S, DM])
    tri_in = din("trimask", [128, 128], BF16)
    ident_in = din("ident", [128, 128], BF16)
    identf_in = din("identf", [128, 128], F32)
    onesf_in = din("onesf", [1, 128])
    ecat_in = din("ecat", [128, 256], BF16)
    cvec_in = din("cvec", [128, 2])
    sel_in = din("sel", [16, 8 * 128])
    onescol_in = din("onescol", [128, 1])

    out_t = nc.dram_tensor("out", [S, DM], F32, kind="ExternalOutput").ap()
    dbg = {}

    def dbg_out(name, shape, dtype=F32):
        ap = nc.dram_tensor(name, shape, dtype, kind="ExternalOutput").ap()
        dbg[name] = ap
        return ap

    scr_x1 = nc.dram_tensor("scr_x1", [S, DM], F32, kind="Internal").ap()
    scr_x2 = nc.dram_tensor("scr_x2", [S, DM], F32, kind="Internal").ap()
    scr_xq2 = nc.dram_tensor("scr_xq2", [S, INNER], BF16, kind="Internal").ap()
    scr_wgt = nc.dram_tensor("scr_wgt", [DM, INNER], BF16, kind="Internal").ap()
    scr_wut = nc.dram_tensor("scr_wut", [DM, INNER], BF16, kind="Internal").ap()
    scr_wdt = nc.dram_tensor("scr_wdt", [INNER, DM], BF16, kind="Internal").ap()
    scr_row = nc.dram_tensor("scr_row", [16, 512], F32, kind="Internal").ap()
    scr_vec = nc.dram_tensor("scr_vec", [1, 64], F32, kind="Internal").ap()

    def sb(name, shape, dtype=F32, stack=None):
        h = (stack or top).enter_context(nc.sbuf_tensor(name, shape, dtype))
        return h, p.res()

    def ps(name, shape, dtype=F32, stack=None):
        h = (stack or top).enter_context(nc.psum_tensor(name, shape, dtype))
        return h, p.res()

    def dma(dst_ap, src_ap, reads=(), writes=(), eng="sync", tag="",
            extra_deps=()):
        if eng == "sync" and "DRam" in type(dst_ap.tensor).__name__:
            eng = "gpsimd"
        return p.step(eng, lambda e: e.dma_start(out=dst_ap, in_=src_ap),
                      reads=reads, writes=writes, dma=True, tag=tag,
                      extra_deps=extra_deps)

    r_sx1 = [p.res() for _ in range(NT)]
    r_sx2 = [p.res() for _ in range(NT)]
    r_sxq2 = [p.res() for _ in range(NT)]
    r_swg = p.res(); r_swu = p.res(); r_swd = p.res()
    wq2_store_toks = [[], [], []]
    r_srow = p.res(); r_svec = p.res()

    s_ident, r_ident = sb("c_ident", [128, 128], BF16)
    s_identf, r_identf = sb("c_identf", [128, 128], F32)
    s_tri, r_tri = sb("c_tri", [128, 128], BF16)
    s_onesf, r_onesf = sb("c_onesf", [1, 128], F32)
    s_ecat, r_ecat = sb("c_ecat", [128, 256], BF16)
    s_gb1, r_gb1 = sb("c_gb1", [128, 16], F32)
    s_gb2, r_gb2 = sb("c_gb2", [128, 16], F32)
    s_b1c, r_b1c = sb("c_b1c", [128, 32], F32)
    s_b2b, r_b2b = sb("c_b2b", [128, DM], F32)
    s_mu1, r_mu1 = sb("g_mu1", [128, 16], F32)
    s_rs1, r_rs1 = sb("g_rs1", [128, 16], F32)
    s_nrm, r_nrm = sb("g_nrm", [128, 32], F32)
    s_bneg, r_bneg = sb("g_bneg", [128, 16], F32)
    s_wsum, r_wsum = sb("g_wsum", [1, 8], F32)
    s_wrow, r_wrow = sb("g_wrow", [1, 8], F32)
    s_wcol, r_wcol = sb("g_wcol", [128, 8], F32)
    s_cbd, r_cbd = sb("g_cbd", [128, 16], F32)
    s_row1, r_row1 = sb("g_row1", [1, DM], F32)
    s_gbt, r_gbt = sb("g_gbt", [32, 128], F32)
    s_dummy = [sb(f"g_dummy{i}", [128, DM], F32) for i in range(2)]
    s_cvec, r_cvec = sb("c_cvec", [128, 2], F32)
    s_onescol, r_onescol = sb("c_onescol", [128, 1], F32)
    s_wacc, r_wacc = sb("g_wacc", [128, 4], F32)
    s_w31, r_w31 = sb("g_w31", [3, 1], F32)
    s_vfl, _r_vfl = sb("g_vfl", [128, 16], F32)
    s_sfl, _r_sfl = sb("g_sfl", [128, 16], F32)
    s_smal, r_smal = sb("g_smal", [128, 48], F32)
    s_hrow, r_hrow = sb("g_hrow", [1, 32], F32)
    s_xt = [sb(f"g_xt{i}", [128, DM], F32) for i in range(2)]
    s_c1sum, r_c1sum = sb("g_c1sum", [128, 16], F32)
    s_c1sq, r_c1sq = sb("g_c1sq", [128, 16], F32)
    s_c3sum, r_c3sum = sb("g_c3sum", [128, 16], F32)
    s_c3sq, r_c3sq = sb("g_c3sq", [128, 16], F32)
    s_mu3, r_mu3 = sb("g_mu3", [128, 16], F32)
    s_rs3, r_rs3 = sb("g_rs3", [128, 16], F32)
    s_axm, r_axm = sb("g_axm", [128, 16], F32)
    s_dsx, r_dsx = sb("g_dsx", [128, 16], F32)
    s_dcg, r_dcg = sb("g_dcg", [128, 16], F32)
    s_dcu, r_dcu = sb("g_dcu", [128, 16], F32)

    stQKV = ExitStack()
    s_qt, r_qt = sb("a_QT", [128, 8 * S], BF16, stack=stQKV)
    s_kt, r_kt = sb("a_KT", [128, 8 * S], BF16, stack=stQKV)
    s_v, r_v = sb("a_V", [128, NT * DM], BF16, stack=stQKV)

    pro = ExitStack()
    p_mix, r_pmix = ps("p_mix", [128, 512], F32, stack=pro)

    dma(s_ident[:], ident_in, writes=[r_ident])
    dma(s_identf[:], identf_in, writes=[r_identf])
    dma(s_tri[:], tri_in, writes=[r_tri])
    dma(s_onesf[:], onesf_in, writes=[r_onesf])
    dma(s_ecat[:], ecat_in, writes=[r_ecat])
    dma(s_cvec[:], cvec_in, writes=[r_cvec])
    dma(s_onescol[:], onescol_in, writes=[r_onescol])

    def load_cols(dsts, src_ap, nchunks):
        dst, dst_res, c0 = dsts
        dma(s_gbt[0:nchunks, :], src_ap.rearrange("(c q) -> c q", q=128),
            writes=[r_gbt])
        p.step("tensor",
               lambda e: e.matmul(p_mix[:, 0:nchunks], s_gbt[0:nchunks, :],
                                  s_identf[0:nchunks, 0:nchunks],
                                  start=True, stop=True),
               reads=[r_gbt, r_identf], writes=[r_pmix])
        p.step("vector",
               lambda e: e.tensor_copy(dst[:, c0:c0 + nchunks],
                                       p_mix[:, 0:nchunks]),
               reads=[r_pmix], writes=[dst_res],
               flush=lambda e: e.tensor_copy(s_vfl[:, 0:1],
                                             dst[:, c0:c0 + 1]))

    load_cols((s_gb1, r_gb1, 0), ln1g_in, 8)
    load_cols((s_gb1, r_gb1, 8), ln1b_in, 8)
    load_cols((s_gb2, r_gb2, 0), ln2g_in, 8)
    load_cols((s_gb2, r_gb2, 8), ln2b_in, 8)
    load_cols((s_b1c, r_b1c, 0), b1_in, 32)

    def bcast_row(dst, dst_res, src_row_ap, n):
        dma(s_row1[0:1, 0:n], src_row_ap, writes=[r_row1])
        for j in range(0, n, 512):
            w = min(512, n - j)
            def mk(j=j, w=w):
                return lambda e: e.matmul(p_mix[:, 0:w], s_onesf[:],
                                          s_row1[0:1, j:j + w],
                                          start=True, stop=True)
            p.step("tensor", mk(), reads=[r_row1, r_onesf], writes=[r_pmix])
            def cp(j=j, w=w):
                return lambda e: e.tensor_copy(dst[:, j:j + w],
                                               p_mix[:, 0:w])
            p.step("vector", cp(), reads=[r_pmix], writes=[dst_res])

    bcast_row(s_b2b, r_b2b, b2_in.rearrange("(a n) -> a n", a=1), DM)

    stA = ExitStack()
    s_wph, r_wph = sb("a_wph", [128, 8 * 1024], BF16, stack=stA)
    s_pbias, r_pbias = sb("a_pbias", [128, 1024], F32, stack=stA)
    s_xn = [sb(f"a_xn{i}", [128, 1024], BF16, stack=stA) for i in range(2)]
    s_xnt = [sb(f"a_xnt{i}", [128, 1024], BF16, stack=stA) for i in range(2)]
    xnt_cres = [[p.res() for _ in range(8)] for _ in range(2)]
    ptr_cres = [[p.res() for _ in range(8)] for _ in range(2)]
    s_a0b = [sb(f"a_a0b{i}", [128, 1024], F32, stack=stA) for i in range(2)]
    s_qb = [sb(f"a_qb{i}", [128, 1024], F32, stack=stA) for i in range(2)]
    s_cost = [sb(f"a_cost{i}", [128, 1024], F32, stack=stA) for i in range(2)]
    s_sint = [sb(f"a_sint{i}", [128, 1024], F32, stack=stA) for i in range(2)]
    _tc = sb("a_tcos", [128, 1024], F32, stack=stA)
    _ts_ = sb("a_tsin", [128, 1024], F32, stack=stA)
    s_tcos = [_tc, _tc]
    s_tsin = [_ts_, _ts_]
    s_rope = [sb(f"a_rope{i}", [128, 1024], BF16, stack=stA) for i in range(2)]
    _sq_ = sb("a_sqbf", [128, 1024], BF16, stack=stA)
    s_sqbf = [_sq_, _sq_]
    s_nrmt = [sb(f"a_nrmt{i}", [128, 16], F32, stack=stA) for i in range(2)]
    s_asum, r_asum = sb("a_asum", [128, 16], F32, stack=stA)
    s_asq, r_asq = sb("a_asq", [128, 16], F32, stack=stA)
    s_wst = [sb(f"a_wst{i}", [128, 1024], F32, stack=stA) for i in range(2)]

    p_m = [ps(f"p_m{i}", [128, 1024], F32, stack=stA) for i in range(2)]
    p_tr = [ps(f"p_tr{i}", [128, 1024], BF16, stack=stA) for i in range(2)]


    p.step("vector", lambda e: e.memset(s_nrm[:], 0.0), writes=[r_nrm])


    wq_jobs = []
    for widx, w_ap in enumerate((wg_in, wu_in, wd_in)):
        for ch in range(32):
            wq_jobs.append((widx, w_ap, ch))
    wq_pos = [0]

    def wmean_pump(n):
        for _ in range(n):
            if wq_pos[0] >= len(wq_jobs):
                return
            widx, w_ap, ch = wq_jobs[wq_pos[0]]
            wq_pos[0] += 1
            buf, rbuf = s_wst[wq_pos[0] % 2]
            if widx < 2:
                src = w_ap[ch * 128:(ch + 1) * 128, :]
            else:
                src = w_ap.rearrange("m (c n) -> (m c) n", c=4)[
                    ch * 128:(ch + 1) * 128, :]
            dma(buf[:], src, writes=[rbuf], tag=f"wm{widx}.{ch}")
            p.step("vector",
                   lambda e, b=buf: e.tensor_reduce(
                       s_smal[:, 40:41], b[:], AX.X, ALU.add,
                       apply_absolute_value=True),
                   reads=[rbuf], writes=[r_smal], tag=f"wabs{widx}.{ch}",
                   sync_out=True)
            if ch == 0:
                def acc0(widx=widx):
                    return lambda e: e.tensor_copy(
                        s_wacc[:, widx:widx + 1], s_smal[:, 40:41])
            else:
                def acc0(widx=widx):
                    return lambda e: e.tensor_add(
                        s_wacc[:, widx:widx + 1], s_wacc[:, widx:widx + 1],
                        s_smal[:, 40:41])
            p.step("vector", acc0(), reads=[r_smal, r_wacc], writes=[r_wacc],
                   tag=f"wacc{widx}.{ch}")

    a0bufs = [s_xt[0], s_xt[1], s_a0b[0], s_a0b[1]]
    for t in range(NT):
        xbuf, rxbuf = a0bufs[t % 4]
        dy, rdy = s_dummy[t % 2]
        dma(xbuf[:], x_in[t * 128:(t + 1) * 128, :], writes=[rxbuf],
            tag=f"a0x{t}")
        p.step("scalar",
               lambda e, xb=xbuf, dy=dy: e.activation(dy[:], xb[:],
                                                      AF.Square),
               reads=[rxbuf], writes=[rdy], tag=f"a0sq{t}")
        p.step("vector",
               lambda e, dy=dy, t=t: e.tensor_reduce(
                   s_asq[:, t:t + 1], dy[:], AX.X, ALU.add),
               reads=[rdy], writes=[r_asq], tag=f"a0qr{t}")
        p.step("vector",
               lambda e, xb=xbuf, t=t: e.tensor_reduce(
                   s_asum[:, t:t + 1], xb[:], AX.X, ALU.add),
               reads=[rxbuf], writes=[r_asum], tag=f"a0sr{t}")

    def batch_stats(src_sum, r_srcsum, src_sq, r_srcsq, dst_mu, r_dstmu,
                    dst_rs, r_dstrs, pre):
        p.step("vector",
               lambda e: e.tensor_scalar(dst_mu[:], src_sum[:], 1.0 / DM,
                                         None, ALU.mult),
               reads=[r_srcsum], writes=[r_dstmu], tag=pre + "mu")
        p.step("vector",
               lambda e: e.tensor_scalar(s_smal[:, 0:16], src_sq[:],
                                         1.0 / DM, None, ALU.mult),
               reads=[r_srcsq], writes=[r_smal], tag=pre + "ms")
        p.step("vector",
               lambda e: e.tensor_mul(s_smal[:, 16:32], dst_mu[:],
                                      dst_mu[:]),
               reads=[r_dstmu], writes=[r_smal], tag=pre + "mm")
        p.step("vector",
               lambda e: e.tensor_sub(s_smal[:, 0:16], s_smal[:, 0:16],
                                      s_smal[:, 16:32]),
               reads=[r_smal], writes=[r_smal], tag=pre + "v")
        p.step("scalar",
               lambda e: e.activation(s_smal[:, 16:32], s_smal[:, 0:16],
                                      AF.Ln, bias=s_cvec[:, 0:1]),
               reads=[r_smal, r_cvec], writes=[r_smal], tag=pre + "l")
        p.step("scalar",
               lambda e: e.activation(dst_rs[:], s_smal[:, 16:32], AF.Exp,
                                      scale=-0.5),
               reads=[r_smal], writes=[r_dstrs], tag=pre + "r")

    batch_stats(s_asum, r_asum, s_asq, r_asq, s_mu1, r_mu1, s_rs1, r_rs1,
                "a0")

    for phase in range(3):
        dma(s_wph[:].rearrange("p (c n) -> p c n", c=8),
            wqkv_in[:, phase * 1024:(phase + 1) * 1024].rearrange(
                "(c q) n -> q c n", q=128),
            writes=[r_wph], eng="gpsimd", tag=f"wqkv{phase}")
        bcast_row(s_pbias, r_pbias,
                  bqkv_in.rearrange("(a n) -> a n", a=1)[
                      0:1, phase * 1024:(phase + 1) * 1024], DM)

        def a_stage1(t, phase=phase):
            pr = t % 2
            xbuf, rxbuf = s_xt[pr]
            dma(xbuf[:], x_in[t * 128:(t + 1) * 128, :], writes=[rxbuf],
                tag=f"x{phase}.{t}")
            if phase != 2:
                cbuf, rcbuf = s_cost[pr]
                sbuf_, rsbuf_ = s_sint[pr]
                dma(cbuf[:], cos_in[t * 128:(t + 1) * 128, :],
                    writes=[rcbuf], tag=f"cos{phase}.{t}")
                dma(sbuf_[:], sinn_in[t * 128:(t + 1) * 128, :],
                    writes=[rsbuf_], tag=f"sin{phase}.{t}")
            xn, rxn = s_xn[pr]
            xnt, rxnt = s_xnt[pr]
            ptr, rptr = p_tr[pr]
            p.step("vector",
                   lambda e, xb=xbuf, xn=xn, t=t: e.tensor_scalar(
                       xn[:], xb[:], s_mu1[:, t:t + 1], s_rs1[:, t:t + 1],
                       ALU.subtract, ALU.mult),
                   reads=[rxbuf, r_mu1, r_rs1], writes=[rxn],
                   tag=f"xn{phase}.{t}")
            for c in range(8):
                def tr(c=c, xn=xn, ptr=ptr):
                    return lambda e: e.transpose(
                        ptr[:, c * 128:(c + 1) * 128],
                        xn[:, c * 128:(c + 1) * 128], s_ident[:])
                p.step("tensor", tr(), reads=[rxn, r_ident],
                       writes=[rptr], tag=f"trx{phase}.{t}.{c}")
                def ev(c=c, ptr=ptr, xnt=xnt):
                    return lambda e: e.activation(
                        xnt[:, c * 128:(c + 1) * 128],
                        ptr[:, c * 128:(c + 1) * 128], AF.Identity,
                        scale=s_gb1[:, c:c + 1],
                        bias=s_gb1[:, 8 + c:9 + c])
                p.step("scalar", ev(), reads=[rptr, r_gb1],
                       writes=[xnt_cres[pr][c]], tag=f"evx{phase}.{t}.{c}")
            pm, rpm = p_m[pr]
            for dinc in range(8):
                for h2 in range(2):
                    def mm(dinc=dinc, h2=h2, pm=pm, xnt=xnt):
                        return lambda e: e.matmul(
                            pm[:, h2 * 512:(h2 + 1) * 512],
                            xnt[:, dinc * 128:(dinc + 1) * 128],
                            s_wph[:, dinc * 1024 + h2 * 512:
                                  dinc * 1024 + (h2 + 1) * 512],
                            start=(dinc == 0), stop=(dinc == 7),
                            skip_group_check=True)
                    p.step("tensor", mm(),
                           reads=[xnt_cres[pr][dinc], r_wph],
                           writes=[rpm], tag=f"qkv{phase}.{t}.{dinc}.{h2}")

        def a_stage2(t, phase=phase):
            pr = t % 2
            pm, rpm = p_m[pr]
            ptr, rptr = p_tr[pr]
            if phase == 2:
                for h2 in range(2):
                    def ev2(h2=h2, t=t, pm=pm):
                        return lambda e: e.tensor_add(
                            s_v[:, t * 1024 + h2 * 512:
                                t * 1024 + (h2 + 1) * 512],
                            pm[:, h2 * 512:(h2 + 1) * 512],
                            s_pbias[:, h2 * 512:(h2 + 1) * 512])
                    p.step("vector", ev2(), reads=[rpm, r_pbias],
                           writes=[r_v], tag=f"vev{t}.{h2}")
                return
            cbuf, rcbuf = s_cost[pr]
            sbuf_, rsbuf_ = s_sint[pr]
            qb, rqb = s_qb[pr]
            p.step("vector",
                   lambda e, pm=pm, qb=qb: e.tensor_add(qb[:], pm[:],
                                                        s_pbias[:]),
                   reads=[rpm, r_pbias], writes=[rqb],
                   tag=f"qbev{phase}.{t}")
            sq, rsq = s_sqbf[pr]
            nt_, rnt_ = s_nrmt[pr]
            p.step("scalar",
                   lambda e, qb=qb, sq=sq: e.activation(sq[:], qb[:],
                                                        AF.Square),
                   reads=[rqb], writes=[rsq], tag=f"qsq{phase}.{t}")
            p.step("vector",
                   lambda e, sq=sq, nt_=nt_: e.tensor_reduce(
                       nt_[:], sq[:].rearrange("q (h c) -> q h c", h=16,
                                               c=64),
                       AX.X, ALU.add),
                   reads=[rsq], writes=[rnt_], tag=f"qnr{phase}.{t}",
                   sync_out=True)
            c0 = phase * 16
            p.step("vector",
                   lambda e, c0=c0, nt_=nt_: e.tensor_max(
                       s_nrm[:, c0:c0 + 16], s_nrm[:, c0:c0 + 16], nt_[:]),
                   reads=[rnt_, r_nrm], writes=[r_nrm],
                   tag=f"qmx{phase}.{t}")
            tc, rtc = s_tcos[pr]
            tsn, rtsn = s_tsin[pr]
            rp, rrp = s_rope[pr]
            p.step("vector",
                   lambda e, cbuf=cbuf, qb=qb, tc=tc: e.tensor_mul(
                       tc[:], qb[:], cbuf[:]),
                   reads=[rqb, rcbuf], writes=[rtc], tag=f"rc{phase}.{t}")
            qv = qb[:].rearrange("q (h t2 c) -> q h t2 c", h=16, t2=2, c=32)
            sv = sbuf_[:].rearrange("q (h t2 c) -> q h t2 c", h=16, t2=2,
                                    c=32)
            tv = tsn[:].rearrange("q (h t2 c) -> q h t2 c", h=16, t2=2,
                                  c=32)
            p.step("vector",
                   lambda e, qv=qv, sv=sv, tv=tv: e.tensor_mul(
                       tv[:, :, 0, :], qv[:, :, 1, :], sv[:, :, 0, :]),
                   reads=[rqb, rsbuf_], writes=[rtsn],
                   tag=f"rs0{phase}.{t}")
            p.step("vector",
                   lambda e, qv=qv, sv=sv, tv=tv: e.tensor_mul(
                       tv[:, :, 1, :], qv[:, :, 0, :], sv[:, :, 1, :]),
                   reads=[rqb, rsbuf_], writes=[rtsn],
                   tag=f"rs1{phase}.{t}")
            p.step("vector",
                   lambda e, tc=tc, tsn=tsn, rp=rp: e.tensor_add(
                       rp[:], tc[:], tsn[:]),
                   reads=[rtc, rtsn], writes=[rrp], tag=f"ra{phase}.{t}")
            for c in range(8):
                def tr2(c=c, rp=rp, ptr=ptr):
                    return lambda e: e.transpose(
                        ptr[:, c * 128:(c + 1) * 128],
                        rp[:, c * 128:(c + 1) * 128], s_ident[:])
                p.step("tensor", tr2(), reads=[rrp, r_ident],
                       writes=[rptr], tag=f"trq{phase}.{t}.{c}")
            dstT, rdstT = (s_qt, r_qt) if phase == 0 else (s_kt, r_kt)
            for half in range(2):
                dv = dstT[:].rearrange("d (c q) -> d c q", c=8, q=S)[
                    :, half * 4:(half + 1) * 4, t * 128:(t + 1) * 128]
                def cp(half=half, dv=dv, ptr=ptr):
                    return lambda e: e.activation(
                        dv, ptr[:, half * 512:(half + 1) * 512],
                        AF.Identity)
                p.step("scalar", cp(), reads=[rptr],
                       writes=[rdstT], tag=f"qtc{phase}.{t}.{half}")

        for t in range(NT):
            a_stage1(t)
            if t >= 1:
                a_stage2(t - 1)
            wmean_pump(2)
        a_stage2(NT - 1)
    wmean_pump(100)

    p.step("tensor",
           lambda e: e.matmul(p_mix[0:32, 0:128], s_nrm[:], s_identf[:],
                              start=True, stop=True),
           reads=[r_nrm, r_identf], writes=[r_pmix], tag="nrmT")
    s_hn, r_hn = sb("a_hn", [32, 1], F32, stack=stA)
    p.step("vector",
           lambda e: e.tensor_reduce(s_hn[:], p_mix[0:32, 0:128], AX.X,
                                     ALU.max),
           reads=[r_pmix], writes=[r_hn], tag="nrmmax", sync_out=True)
    dma(scr_vec[0:1, 0:32].rearrange("a (q b) -> q (a b)", q=32, b=1),
        s_hn[:], reads=[r_hn], writes=[r_svec], tag="bnc0")
    dma(s_hrow[:], scr_vec[0:1, 0:32], reads=[r_svec], writes=[r_hrow],
        tag="bnc1")
    p.step("vector",
           lambda e: e.tensor_add(s_hrow[0:1, 0:16], s_hrow[0:1, 0:16],
                                  s_hrow[0:1, 16:32]),
           reads=[r_hrow], writes=[r_hrow], tag="bsum")
    p.step("vector",
           lambda e: e.tensor_scalar(s_hrow[0:1, 0:16], s_hrow[0:1, 0:16],
                                     -1.0 / 16.0, None, ALU.mult),
           reads=[r_hrow], writes=[r_hrow], tag="bneg",
           flush=lambda e: e.tensor_copy(s_vfl[0:1, 0:8], s_hrow[0:1, 0:8]))
    p.step("tensor",
           lambda e: e.matmul(p_mix[:, 0:16], s_onesf[:], s_hrow[0:1, 0:16],
                              start=True, stop=True),
           reads=[r_hrow, r_onesf], writes=[r_pmix], tag="bnegT")
    p.step("vector",
           lambda e: e.tensor_copy(s_bneg[:], p_mix[:, 0:16]),
           reads=[r_pmix], writes=[r_bneg], tag="bnegC",
           flush=lambda e: e.tensor_copy(s_vfl[:, 0:1], s_bneg[:, 0:1]))

    p.step("tensor",
           lambda e: e.matmul(p_mix[0:3, 0:1], s_wacc[:, 0:3],
                              s_onescol[:], start=True, stop=True),
           reads=[r_wacc, r_onescol], writes=[r_pmix], tag="wsumT")
    p.step("vector",
           lambda e: e.tensor_copy(s_w31[:], p_mix[0:3, 0:1]),
           reads=[r_pmix], writes=[r_w31], tag="wsumC")
    p.step("tensor",
           lambda e: e.matmul(p_mix[0:1, 4:7], s_w31[:],
                              s_identf[0:3, 0:3], start=True, stop=True),
           reads=[r_w31, r_identf], writes=[r_pmix], tag="wrowT")
    p.step("vector",
           lambda e: e.tensor_copy(s_wsum[0:1, 0:3], p_mix[0:1, 4:7]),
           reads=[r_pmix], writes=[r_wsum], tag="wrowC")
    p.step("vector",
           lambda e: e.tensor_scalar(s_wrow[0:1, 0:3], s_wsum[0:1, 0:3],
                                     1.0 / (INNER * DM), 1e-5, ALU.mult,
                                     ALU.max),
           reads=[r_wsum], writes=[r_wrow], tag="wmean", sync_out=True)
    p.step("vector",
           lambda e: e.reciprocal(s_wrow[0:1, 3:6], s_wrow[0:1, 0:3]),
           reads=[r_wrow], writes=[r_wrow], tag="wrec", sync_out=True)
    p.step("vector",
           lambda e: e.reciprocal(s_wrow[0:1, 6:8], s_wrow[0:1, 3:5]),
           reads=[r_wrow], writes=[r_wrow], tag="wrec2", sync_out=True)
    p.step("vector",
           lambda e: e.reciprocal(s_wrow[0:1, 0:1], s_wrow[0:1, 5:6]),
           reads=[r_wrow], writes=[r_wrow], tag="wrec3", sync_out=True,
           flush=lambda e: e.tensor_copy(s_vfl[0:1, 0:8], s_wrow[0:1, 0:8]))
    p.step("tensor",
           lambda e: e.matmul(p_mix[:, 0:8], s_onesf[:], s_wrow[0:1, 0:8],
                              start=True, stop=True),
           reads=[r_wrow, r_onesf], writes=[r_pmix], tag="wcolT")
    p.step("vector",
           lambda e: e.tensor_copy(s_wcol[:], p_mix[:, 0:8]),
           reads=[r_pmix], writes=[r_wcol], tag="wcolC",
           flush=lambda e: e.tensor_copy(s_vfl[:, 0:1], s_wcol[:, 0:1]))

    if stage == "A":
        d_qt = dbg_out("d_qt", [128, 8 * S])
        d_kt = dbg_out("d_kt", [128, 8 * S])
        d_v = dbg_out("d_v", [128, NT * DM])
        d_bneg = dbg_out("d_bneg", [128, 16])
        d_wrow = dbg_out("d_wrow", [1, 8])
        d_mu = dbg_out("d_mu", [128, 16])
        d_rs = dbg_out("d_rs", [128, 16])
        d_wph = dbg_out("d_wph", [128, 8 * 1024])
        d_qb = dbg_out("d_qb", [128, 1024])
        d_xnt = dbg_out("d_xnt", [128, 1024])
        d_rope = dbg_out("d_rope", [128, 1024])
        d_pbias = dbg_out("d_pbias", [128, 1024])
        toks = [
            dma(d_qt, s_qt[:], reads=[r_qt], eng="gpsimd", tag="dbgq"),
            dma(d_kt, s_kt[:], reads=[r_kt], eng="gpsimd", tag="dbgk"),
            dma(d_v, s_v[:], reads=[r_v], eng="gpsimd", tag="dbgv"),
            dma(d_bneg, s_bneg[:], reads=[r_bneg], tag="dbgb"),
            dma(d_wrow, s_wrow[:], reads=[r_wrow], tag="dbgw"),
            dma(d_mu, s_mu1[:], reads=[r_mu1], tag="dbgmu"),
            dma(d_rs, s_rs1[:], reads=[r_rs1], tag="dbgrs"),
            dma(d_wph, s_wph[:], reads=[r_wph], eng="gpsimd", tag="dbgwp"),
            dma(d_qb, s_qb[1][0][:], reads=[s_qb[1][1]], tag="dbgqb"),
            dma(d_xnt, s_xnt[1][0][:], reads=xnt_cres[1], eng="gpsimd", tag="dbgxnt"),
            dma(d_rope, s_rope[1][0][:], reads=[s_rope[1][1]], eng="gpsimd", tag="dbgro"),
            dma(d_pbias, s_pbias[:], reads=[r_pbias], tag="dbgpb"),
        ]
        p.step("sync", lambda e: e.nop(), extra_deps=toks, tag="fin")
        stA.close(); pro.close()
        p.emit(nc, top)
        return nc, dbg, top

    stA.close()
    p.add_fence([r_pmix])
    pro.close()
    p.add_fence([r_wph, r_pbias, p_m[0][1], p_m[1][1],
                 r_asum, r_asq]
                + ptr_cres[0] + ptr_cres[1] + xnt_cres[0] + xnt_cres[1]
                + [b[1] for b in (s_cost + s_sint + s_wst + s_xn + s_xnt
                                  + s_a0b
                                  + s_qb + s_tcos + s_tsin + s_rope
                                  + s_sqbf + s_nrmt)])

    stB = ExitStack()
    s_pta, r_pta = sb("b_pta", [128, 16 * 512], BF16, stack=stB)
    s_ptb, r_ptb = sb("b_ptb", [128, 16 * 512], BF16, stack=stB)
    s_attn, r_attn = sb("b_attn", [128, 8 * 512], BF16, stack=stB)
    s_wob, r_wob = sb("b_wob", [128, 8 * 1024], BF16, stack=stB)
    s_bob, r_bob = sb("b_bob", [128, DM], F32, stack=stB)
    s_r16, r_r16 = sb("b_r16", [16, 512], F32, stack=stB)
    s_sel, r_sel = sb("b_sel", [16, 8 * 128], F32, stack=stB)
    s_t1, r_t1 = sb("b_t1", [128, DM], F32, stack=stB)
    s_xp = [sb(f"b_xp{i}", [128, DM], F32, stack=stB) for i in range(2)]

    dma(s_wob[:].rearrange("p (c n) -> p c n", c=8),
        wo_in.rearrange("(c q) n -> q c n", q=128),
        writes=[r_wob], eng="gpsimd", tag="ldwo")
    dma(s_sel[:], sel_in, writes=[r_sel], tag="ldsel")
    bcast2 = ExitStack()
    p_bc2, r_pbc2 = ps("p_bc2", [128, 512], F32, stack=bcast2)
    dma(s_row1[0:1, 0:DM], bo_in.rearrange("(a n) -> a n", a=1),
        writes=[r_row1], tag="ldbo")
    for j in range(2):
        def mk(j=j):
            return lambda e: e.matmul(p_bc2[:, 0:512], s_onesf[:],
                                      s_row1[0:1, j * 512:(j + 1) * 512],
                                      start=True, stop=True)
        p.step("tensor", mk(), reads=[r_row1, r_onesf], writes=[r_pbc2])
        def cp(j=j):
            return lambda e: e.tensor_copy(s_bob[:, j * 512:(j + 1) * 512],
                                           p_bc2[:, 0:512])
        p.step("vector", cp(), reads=[r_pbc2], writes=[r_bob])
    bcast2.close()
    p.add_fence([r_pbc2])

    p_stab = [ps(f"p_st{sd}", [128, 1536], F32, stack=stB) for sd in range(2)]
    p_o, r_po = ps("p_o", [128, 512], F32, stack=stB)
    p_rs, r_prs = ps("p_rs", [128, 512], F32, stack=stB)

    PT = [(s_pta, r_pta), (s_ptb, r_ptb)]

    r_ptk = [[p.res() for _ in range(16)], [p.res() for _ in range(16)]]

    for qc in range(4):
        nkt = 4 * qc + 4
        first_rs = [True]
        kts = list(range(nkt))
        groups = [kts[i:i + 3] for i in range(0, nkt, 3)]

        def emit_st_group(hp, g, qc=qc):
            for side in range(2):
                head = 2 * hp + side
                pst, rpst = p_stab[side]
                for j, kt in enumerate(g):
                    def mmst(j=j, kt=kt, side=side, hp=hp, qc=qc, pst=pst):
                        return lambda e: e.matmul(
                            pst[:, j * 512:(j + 1) * 512],
                            s_kt[side * 64:(side + 1) * 64,
                                 hp * 2048 + kt * 128:
                                 hp * 2048 + (kt + 1) * 128],
                            s_qt[side * 64:(side + 1) * 64,
                                 hp * 2048 + qc * 512:
                                 hp * 2048 + (qc + 1) * 512],
                            start=True, stop=True,
                            tile_position=(side * 64, 0),
                            skip_group_check=True)
                    p.step("tensor", mmst(), reads=[r_kt, r_qt],
                           writes=[rpst], tag=f"st{qc}.{hp}.{side}.{kt}")
                ptb, _ = PT[side]
                def expf(g=g, side=side, head=head, pst=pst, ptb=ptb):
                    ln = len(g)
                    return lambda e: e.activation(
                        ptb[:, g[0] * 512:(g[0] + ln) * 512],
                        pst[:, 0:ln * 512], AF.Exp, scale=0.125,
                        bias=s_bneg[:, head:head + 1])
                p.step("scalar", expf(), reads=[rpst, r_bneg],
                       writes=[r_ptk[side][kt2] for kt2 in g],
                       tag=f"exp{qc}.{hp}.{side}")
            for kt in g:
                if kt < 4 * qc:
                    continue
                dcol = kt * 128 - qc * 512
                for side in range(2):
                    ptb, _ = PT[side]
                    if dcol > 0:
                        def msf(kt=kt, dcol=dcol, ptb=ptb):
                            return lambda e: e.memset(
                                ptb[:, kt * 512:kt * 512 + dcol], 0.0)
                        p.step("vector", msf(), writes=[r_ptk[side][kt]],
                               tag=f"ms{qc}.{hp}.{side}.{kt}")
                    def trif(kt=kt, dcol=dcol, ptb=ptb):
                        return lambda e: e.tensor_mul(
                            ptb[:, kt * 512 + dcol:kt * 512 + dcol + 128],
                            ptb[:, kt * 512 + dcol:kt * 512 + dcol + 128],
                            s_tri[:])
                    p.step("vector", trif(), reads=[r_tri],
                           writes=[r_ptk[side][kt]],
                           tag=f"tri{qc}.{hp}.{side}.{kt}")

        def emit_pv_group(hp, g, qc=qc):
            for kt in g:
                for side in range(2):
                    head = 2 * hp + side
                    ptb, _ = PT[side]
                    def pvf(kt=kt, side=side, head=head, ptb=ptb):
                        return lambda e: e.matmul(
                            p_o[side * 64:(side + 1) * 64, :],
                            s_v[:, kt * 1024 + head * 64:
                                kt * 1024 + (head + 1) * 64],
                            ptb[:, kt * 512:(kt + 1) * 512],
                            start=(kt == 0), stop=False,
                            tile_position=(0, side * 64),
                            skip_group_check=True)
                    p.step("tensor", pvf(), reads=[r_ptk[side][kt], r_v],
                           writes=[r_po], tag=f"pv{qc}.{hp}.{side}.{kt}")
                    def rsf(kt=kt, side=side, head=head, ptb=ptb,
                            st=first_rs[0]):
                        return lambda e: e.matmul(
                            p_rs[0:16, :],
                            s_ecat[:, head * 16:(head + 1) * 16],
                            ptb[:, kt * 512:(kt + 1) * 512],
                            start=st, stop=False,
                            skip_group_check=True)
                    p.step("tensor", rsf(), reads=[r_ptk[side][kt], r_ecat],
                           writes=[r_prs], tag=f"rs{qc}.{hp}.{side}.{kt}")
                    first_rs[0] = False
            if g is groups[-1]:
                def evo(hp=hp):
                    return lambda e: e.tensor_copy(
                        s_attn[:, hp * 512:(hp + 1) * 512], p_o[:])
                p.step("vector", evo(), reads=[r_po], writes=[r_attn],
                       tag=f"aev{qc}.{hp}")

        jobs = [(hp, g) for hp in range(8) for g in groups]
        for i, (hp, g) in enumerate(jobs):
            emit_st_group(hp, g)
            if i >= 1:
                emit_pv_group(*jobs[i - 1])
        emit_pv_group(*jobs[-1])

        p.step("vector",
               lambda e: e.reciprocal(s_r16[:], p_rs[0:16, :]),
               reads=[r_prs], writes=[r_r16], tag=f"recip{qc}")
        for hp in range(8):
            def rb0(hp=hp):
                return lambda e: e.matmul(
                    p_o[:], s_sel[:, hp * 128:(hp + 1) * 128], s_r16[:],
                    start=True, stop=True, skip_group_check=True)
            p.step("tensor", rb0(), reads=[r_r16, r_sel], writes=[r_po],
                   tag=f"rb{qc}.{hp}")
            def divf(hp=hp):
                return lambda e: e.tensor_mul(
                    s_attn[:, hp * 512:(hp + 1) * 512],
                    s_attn[:, hp * 512:(hp + 1) * 512], p_o[:])
            p.step("vector", divf(), reads=[r_po, r_attn], writes=[r_attn],
                   tag=f"div{qc}.{hp}")
        p_wo = p_stab[0][0]
        r_pwo = p_stab[0][1]
        for qsub in range(4):
            t = qc * 4 + qsub
            xbuf, rxbuf = s_xt[t % 2]
            dma(xbuf[:], x_in[t * 128:(t + 1) * 128, :], writes=[rxbuf],
                tag=f"xb{t}")
            for dmc in range(8):
                for h2 in range(2):
                    def mwo(dmc=dmc, h2=h2, qsub=qsub):
                        return lambda e: e.matmul(
                            p_wo[:, h2 * 512:(h2 + 1) * 512],
                            s_attn[:, dmc * 512 + qsub * 128:
                                   dmc * 512 + (qsub + 1) * 128],
                            s_wob[:, dmc * 1024 + h2 * 512:
                                  dmc * 1024 + (h2 + 1) * 512],
                            start=(dmc == 0), stop=(dmc == 7),
                            skip_group_check=True)
                    p.step("tensor", mwo(), reads=[r_attn, r_wob],
                           writes=[r_pwo], tag=f"wo{t}.{dmc}.{h2}")
            p.step("vector",
                   lambda e: e.tensor_add(s_t1[:], p_wo[:, 0:1024],
                                          s_bob[:]),
                   reads=[r_pwo, r_bob], writes=[r_t1], tag=f"woev{t}")
            xpb, rxpb = s_xp[t % 2]
            p.step("vector",
                   lambda e, xb=xbuf, xp=xpb: e.tensor_add(xp[:], s_t1[:],
                                                           xb[:]),
                   reads=[r_t1, rxbuf], writes=[rxpb], tag=f"res1.{t}")
            dma(scr_x1[t * 128:(t + 1) * 128, :], xpb[:], reads=[rxpb],
                writes=[r_sx1[t]], tag=f"sx1.{t}")
            dy, rdy = s_dummy[t % 2]
            p.step("scalar",
                   lambda e, xp=xpb, dy=dy: e.activation(dy[:], xp[:],
                                                         AF.Square),
                   reads=[rxpb], writes=[rdy], tag=f"bsq{t}")
            p.step("vector",
                   lambda e, dy=dy, t=t: e.tensor_reduce(
                       s_c1sq[:, t:t + 1], dy[:], AX.X, ALU.add),
                   reads=[rdy], writes=[r_c1sq], tag=f"bqr{t}")
            p.step("vector",
                   lambda e, xp=xpb, t=t: e.tensor_reduce(
                       s_c1sum[:, t:t + 1], xp[:], AX.X, ALU.add),
                   reads=[rxpb], writes=[r_c1sum], tag=f"bsr{t}")

    if stage == "B":
        d_x1 = dbg_out("d_x1", [S, DM])
        toks = [dma(d_x1, scr_x1, reads=r_sx1[:], tag="dbgx1")]
        p.step("sync", lambda e: e.nop(), extra_deps=toks, tag="fin")
        stB.close()
        p.emit(nc, top)
        return nc, dbg, top

    stB.close()
    p.add_fence([r_pta, r_ptb, r_attn, r_wob, r_bob, r_r16, r_sel, r_t1,
                 r_po, r_prs, p_stab[0][1], p_stab[1][1],
                 s_xp[0][1], s_xp[1][1]])
    stQKV.close()
    p.add_fence([r_qt, r_kt, r_v])

    stC = ExitStack()
    s_w1b, r_w1b = sb("c_w1b", [128, 8 * INNER], BF16, stack=stC)
    s_w2b, r_w2b = sb("c_w2b", [128, 32 * DM], BF16, stack=stC)
    s_mu2, r_mu2 = sb("c_mu2", [128, 16], F32, stack=stC)
    s_rs2, r_rs2 = sb("c_rs2", [128, 16], F32, stack=stC)
    s_c1x = [sb(f"c_x{i}", [128, DM], F32, stack=stC) for i in range(4)]
    _xn2 = sb("c_xn2", [128, DM], BF16, stack=stC)
    s_xn2 = [_xn2, _xn2]
    s_xn2t = [sb(f"c_xn2t{i}", [128, 8 * 256], BF16, stack=stC)
              for i in range(2)]
    s_h1c = [sb(f"c_h1c{i}", [128, 256], BF16, stack=stC) for i in range(3)]
    _xpp = sb("c_xpp", [128, DM], F32, stack=stC)
    s_xpp = [_xpp, _xpp]
    s_wq1, r_wq1 = sb("c_wq1", [128, 1024], F32, stack=stC)
    s_wqbf, r_wqbf = sb("c_wqbf", [128, 1024], BF16, stack=stC)
    s_wtx, r_wtx = sb("c_wtx", [128, 1024], BF16, stack=stC)
    s_wstc = [sb(f"c_wst{i}", [128, 1024], F32, stack=stC) for i in range(2)]
    p_h = [ps(f"p_h{i}", [128, 512], F32, stack=stC) for i in range(3)]
    p_mlp = [ps(f"p_mlp{i}", [128, 1024], F32, stack=stC) for i in range(2)]
    p_trc, r_ptrc = ps("p_trc", [128, 1024], BF16, stack=stC)

    dma(s_w1b[:].rearrange("p (c n) -> p c n", c=8),
        w1_in.rearrange("(c q) n -> q c n", q=128),
        writes=[r_w1b], eng="gpsimd", tag="ldw1")
    dma(s_w2b[:].rearrange("p (c n) -> p c n", c=32),
        w2_in.rearrange("(c q) n -> q c n", q=128),
        writes=[r_w2b], eng="gpsimd", tag="ldw2")

    wq2_jobs = []
    for widx, (w_ap, scr, rscr) in enumerate(
            ((wg_in, scr_wgt, r_swg), (wu_in, scr_wut, r_swu),
             (wd_in, scr_wdt, r_swd))):
        for ch in range(32):
            wq2_jobs.append((widx, w_ap, scr, rscr, ch))
    wq2_pos = [0]

    def wq2_pump(n):
        for _ in range(n):
            if wq2_pos[0] >= len(wq2_jobs):
                return
            widx, w_ap, scr, rscr, ch = wq2_jobs[wq2_pos[0]]
            wq2_pos[0] += 1
            buf, rbuf = s_wstc[wq2_pos[0] % 2]
            scol = 3 + widx if widx < 2 else 5
            if widx < 2:
                src = w_ap[ch * 128:(ch + 1) * 128, :]
                nin = 1024
            else:
                src = w_ap[(ch % 8) * 128:((ch % 8) + 1) * 128,
                           (ch // 8) * 1024:((ch // 8) + 1) * 1024]
                nin = 1024
            dma(buf[:], src, writes=[rbuf], tag=f"w2l{widx}.{ch}")
            def q1(buf=buf, scol=scol):
                return lambda e: e.tensor_scalar(
                    s_wq1[:], buf[:], s_wcol[:, scol:scol + 1], None,
                    ALU.mult)
            p.step("vector", q1(), reads=[rbuf, r_wcol], writes=[r_wq1],
                   tag=f"w2q1.{widx}.{ch}")
            p.step("vector",
                   lambda e: e.tensor_scalar(s_wq1[:], s_wq1[:], MAGIC,
                                             None, ALU.add),
                   reads=[r_wq1], writes=[r_wq1], tag=f"w2q1b.{widx}.{ch}")
            p.step("vector",
                   lambda e: e.tensor_scalar(s_wq1[:], s_wq1[:], MAGIC - 1.0,
                                             MAGIC + 1.0, ALU.max, ALU.min),
                   reads=[r_wq1], writes=[r_wq1], tag=f"w2q2.{widx}.{ch}")
            p.step("vector",
                   lambda e: e.tensor_scalar(s_wqbf[:], s_wq1[:], MAGIC,
                                             None, ALU.subtract),
                   reads=[r_wq1], writes=[r_wqbf], tag=f"w2q3.{widx}.{ch}")
            for c in range(8):
                def tr(c=c):
                    return lambda e: e.transpose(
                        p_trc[:, c * 128:(c + 1) * 128],
                        s_wqbf[:, c * 128:(c + 1) * 128], s_ident[:])
                p.step("tensor", tr(), reads=[r_wqbf, r_ident],
                       writes=[r_ptrc], tag=f"w2t.{widx}.{ch}.{c}")
            for half in range(2):
                def cp(half=half):
                    return lambda e: e.tensor_copy(
                        s_wtx[:, half * 512:(half + 1) * 512],
                        p_trc[:, half * 512:(half + 1) * 512])
                p.step("vector", cp(), reads=[r_ptrc], writes=[r_wtx],
                       tag=f"w2c.{widx}.{ch}.{half}")
            if widx < 2:
                dst = scr[:, ch * 128:(ch + 1) * 128].rearrange(
                    "(c q) o -> q c o", q=128)
            else:
                dst = scr[(ch // 8) * 1024:(ch // 8 + 1) * 1024,
                          (ch % 8) * 128:((ch % 8) + 1) * 128].rearrange(
                    "(c q) o -> q c o", q=128)
            tok = dma(dst, s_wtx[:].rearrange("p (c o) -> p c o", c=8),
                      reads=[r_wtx], writes=[rscr],
                      tag=f"w2s{widx}.{ch}")
            wq2_store_toks[widx].append(tok)

    batch_stats(s_c1sum, r_c1sum, s_c1sq, r_c1sq, s_mu2, r_mu2, s_rs2,
                r_rs2, "c1")

    xtiles_by_qg = {}

    def c2_prep(qg):
        xnt2, rxnt2 = s_xn2t[qg % 2]
        xtiles = []
        for qs in range(2):
            t = qg * 2 + qs
            xbuf, rxbuf = s_c1x[(qg % 2) * 2 + qs]
            dma(xbuf[:], scr_x1[t * 128:(t + 1) * 128, :], reads=[r_sx1[t]],
                writes=[rxbuf], tag=f"c2x{t}")
            xtiles.append((xbuf, rxbuf))
            xn2, rxn2 = s_xn2[qs]
            p.step("vector",
                   lambda e, xb=xbuf, t=t, xn2=xn2: e.tensor_scalar(
                       xn2[:], xb[:], s_mu2[:, t:t + 1],
                       s_rs2[:, t:t + 1], ALU.subtract, ALU.mult),
                   reads=[rxbuf, r_mu2, r_rs2], writes=[rxn2],
                   tag=f"c2n{t}")
            for c in range(8):
                def tr(c=c, xn2=xn2):
                    return lambda e: e.transpose(
                        p_trc[:, c * 128:(c + 1) * 128],
                        xn2[:, c * 128:(c + 1) * 128], s_ident[:])
                p.step("tensor", tr(), reads=[rxn2, r_ident],
                       writes=[r_ptrc], tag=f"c2t{t}.{c}")
            for c in range(8):
                def ev(c=c, qs=qs, xnt2=xnt2):
                    return lambda e: e.activation(
                        xnt2[:, c * 256 + qs * 128:c * 256 + qs * 128 + 128],
                        p_trc[:, c * 128:(c + 1) * 128], AF.Identity,
                        scale=s_gb2[:, c:c + 1], bias=s_gb2[:, 8 + c:9 + c])
                p.step("scalar", ev(), reads=[r_ptrc, r_gb2],
                       writes=[rxnt2], tag=f"c2e{t}.{c}")
        xtiles_by_qg[qg] = xtiles

    def c2_w1(qg, ic):
        xnt2, rxnt2 = s_xn2t[qg % 2]
        ph, rph = p_h[ic % 3]
        h1c, rh1c = s_h1c[ic % 3]
        for dinc in range(8):
            def m1(dinc=dinc, ic=ic, ph=ph, xnt2=xnt2):
                return lambda e: e.matmul(
                    ph[:, 0:256],
                    s_w1b[:, dinc * INNER + ic * 128:
                          dinc * INNER + (ic + 1) * 128],
                    xnt2[:, dinc * 256:(dinc + 1) * 256],
                    start=(dinc == 0), stop=(dinc == 7),
                    skip_group_check=True)
            p.step("tensor", m1(), reads=[r_w1b, rxnt2], writes=[rph],
                   tag=f"w1.{qg}.{ic}.{dinc}")
        def gel(ic=ic, ph=ph, h1c=h1c):
            return lambda e: e.activation(h1c[:], ph[:, 0:256], AF.Gelu,
                                          bias=s_b1c[:, ic:ic + 1])
        p.step("scalar", gel(), reads=[rph, r_b1c], writes=[rh1c],
               tag=f"gel{qg}.{ic}")

    def c2_w2(qg, ic):
        h1c, rh1c = s_h1c[ic % 3]
        for qs in range(2):
            pm, rpm = p_mlp[qs]
            for h2 in range(2):
                def m2(ic=ic, qs=qs, h2=h2, pm=pm, h1c=h1c):
                    return lambda e: e.matmul(
                        pm[:, h2 * 512:(h2 + 1) * 512],
                        h1c[:, qs * 128:(qs + 1) * 128],
                        s_w2b[:, ic * DM + h2 * 512:
                              ic * DM + (h2 + 1) * 512],
                        start=(ic == 0), stop=(ic == 31),
                        skip_group_check=True)
                p.step("tensor", m2(), reads=[rh1c, r_w2b],
                       writes=[rpm], tag=f"w2m{qg}.{ic}.{qs}.{h2}")

    def c2_evac(qg):
        for qs in range(2):
            t = qg * 2 + qs
            pm, rpm = p_mlp[qs]
            xb, rxb = xtiles_by_qg[qg][qs]
            xpp, rxpp = s_xpp[qs]
            x0b, rx0b = s_xt[qs]
            dma(x0b[:], x_in[t * 128:(t + 1) * 128, :], writes=[rx0b],
                tag=f"c2x0{t}")
            p.step("vector",
                   lambda e, pm=pm, xpp=xpp: e.tensor_add(
                       xpp[:], pm[:], s_b2b[:]),
                   reads=[rpm, r_b2b], writes=[rxpp], tag=f"c2b{t}")
            p.step("vector",
                   lambda e, xpp=xpp, xb=xb: e.tensor_add(xpp[:], xpp[:],
                                                          xb[:]),
                   reads=[rxb, rxpp], writes=[rxpp], tag=f"c2r{t}")
            p.step("vector",
                   lambda e, xpp=xpp, x0b=x0b: e.tensor_add(xpp[:], xpp[:],
                                                            x0b[:]),
                   reads=[rx0b, rxpp], writes=[rxpp], tag=f"c2r2{t}")
            dma(scr_x1[t * 128:(t + 1) * 128, :], xpp[:], reads=[rxpp],
                writes=[r_sx1[t]], tag=f"c2s{t}")

    NIC = INNER // 128
    c2_prep(0)
    jobs = [(qg, ic) for qg in range(8) for ic in range(NIC)]
    DEPTH = 2
    for i, (qg, ic) in enumerate(jobs):
        if i >= DEPTH:
            pqg, pic = jobs[i - DEPTH]
            c2_w2(pqg, pic)
            if pic == NIC - 1:
                c2_evac(pqg)
        if ic == NIC - 1 and qg < 7:
            c2_prep(qg + 1)
        c2_w1(qg, ic)
        wq2_pump(1)
    for i in range(len(jobs) - DEPTH, len(jobs)):
        pqg, pic = jobs[i]
        c2_w2(pqg, pic)
        if pic == NIC - 1:
            c2_evac(pqg)
    wq2_pump(200)


    stC.close()
    p.add_fence([r_w1b, r_w2b, r_wq1, r_wqbf, r_wtx, _xn2[1],
                 p_h[0][1], p_h[1][1], p_h[2][1], p_mlp[0][1], p_mlp[1][1], r_ptrc]
                + [b[1] for b in (s_c1x + s_xpp + s_wstc + s_xn2t
                                  + s_h1c)])

    stD = ExitStack()
    s_wgtb, r_wgtb = sb("d_wgtb", [128, 8 * INNER], BF16, stack=stD)
    s_wutb, r_wutb = sb("d_wutb", [128, 8 * INNER], BF16, stack=stD)
    s_x2t = [sb(f"d_x2t{i}", [128, DM], F32, stack=stD) for i in range(2)]
    _q1 = sb("d_q1", [128, DM], F32, stack=stD)
    s_q1 = [_q1, _q1]
    s_xq = [sb(f"d_xq{i}", [128, DM], BF16, stack=stD) for i in range(2)]
    s_xqt = [sb(f"d_xqt{i}", [128, DM], BF16, stack=stD) for i in range(3)]
    s_sg, r_sg = sb("d_sg", [128, 512], F32, stack=stD)
    s_h, r_h = sb("d_h", [128, INNER], F32, stack=stD)
    _q2 = sb("d_q2", [128, 1024], F32, stack=stD)
    s_q2 = [_q2, _q1]
    s_xq2 = [sb(f"d_xq2{i}", [128, 1024], BF16, stack=stD) for i in range(2)]
    s_hm, r_hm = sb("d_hm", [128, 16], F32, stack=stD)
    p_g = [ps(f"p_g{i}", [128, 512], F32, stack=stD) for i in range(2)]
    p_u = [ps(f"p_u{i}", [128, 512], F32, stack=stD) for i in range(2)]
    p_trd = [ps(f"p_trd{i}", [128, 1024], BF16, stack=stD) for i in range(3)]

    r_wgtc = [p.res() for _ in range(8)]
    r_wutc = [p.res() for _ in range(8)]

    def ldw_chunk(c):
        dma(s_wgtb[:, c * INNER:(c + 1) * INNER],
            scr_wgt[c * 128:(c + 1) * 128, :],
            reads=[r_swg], writes=[r_wgtc[c]], tag=f"ldwgt{c}",
            extra_deps=wq2_store_toks[0])
        dma(s_wutb[:, c * INNER:(c + 1) * INNER],
            scr_wut[c * 128:(c + 1) * 128, :],
            reads=[r_swu], writes=[r_wutc[c]], tag=f"ldwut{c}",
            extra_deps=wq2_store_toks[1])




    for t in range(NT):
        xbuf, rxbuf = s_x2t[t % 2]
        dy, rdy = s_dummy[t % 2]
        dma(xbuf[:], scr_x1[t * 128:(t + 1) * 128, :], reads=[r_sx1[t]],
            writes=[rxbuf], tag=f"c3x{t}")
        p.step("scalar",
               lambda e, xb=xbuf, dy=dy: e.activation(dy[:], xb[:],
                                                      AF.Square),
               reads=[rxbuf], writes=[rdy], tag=f"c3sq{t}")
        p.step("vector",
               lambda e, dy=dy, t=t: e.tensor_reduce(
                   s_c3sq[:, t:t + 1], dy[:], AX.X, ALU.add),
               reads=[rdy], writes=[r_c3sq], tag=f"c3qr{t}")
        p.step("vector",
               lambda e, xb=xbuf, t=t: e.tensor_reduce(
                   s_c3sum[:, t:t + 1], xb[:], AX.X, ALU.add),
               reads=[rxbuf], writes=[r_c3sum], tag=f"c3sr{t}")
        p.step("vector",
               lambda e, t=t: e.tensor_scalar(
                   s_mu3[:, t:t + 1], s_c3sum[:, t:t + 1], 1.0 / DM, None,
                   ALU.mult),
               reads=[r_c3sum], writes=[r_mu3], tag=f"c3mu{t}",
               sync_out=True)
        p.step("vector",
               lambda e, xb=xbuf, dy=dy, t=t: e.tensor_scalar(
                   dy[:], xb[:], s_mu3[:, t:t + 1], None, ALU.subtract),
               reads=[rxbuf, r_mu3, rdy], writes=[rdy], tag=f"c3d{t}")
        p.step("vector",
               lambda e, dy=dy, t=t: e.tensor_reduce(
                   s_axm[:, t:t + 1], dy[:], AX.X, ALU.max,
                   apply_absolute_value=True),
               reads=[rdy], writes=[r_axm], tag=f"c3am{t}")
        if t < 8:
            ldw_chunk(t)
    p.step("vector",
           lambda e: e.tensor_scalar(s_smal[:, 16:32], s_c3sq[:], 1.0 / DM,
                                     None, ALU.mult),
           reads=[r_c3sq], writes=[r_smal], tag="c3ms")
    p.step("vector",
           lambda e: e.tensor_mul(s_smal[:, 32:48], s_mu3[:], s_mu3[:]),
           reads=[r_mu3, r_smal], writes=[r_smal], tag="c3m2")
    p.step("vector",
           lambda e: e.scalar_tensor_tensor(
               s_smal[:, 16:32], s_smal[:, 16:32], LNEPS,
               s_smal[:, 32:48], ALU.add, ALU.subtract),
           reads=[r_smal], writes=[r_smal], tag="c3vp", sync_out=True)
    p.step("vector",
           lambda e: e.reciprocal(s_smal[:, 32:48], s_smal[:, 16:32]),
           reads=[r_smal], writes=[r_smal], tag="c3rc", sync_out=True)
    p.step("scalar",
           lambda e: e.activation(s_smal[:, 32:48], s_smal[:, 32:48],
                                  AF.Sqrt),
           reads=[r_smal], writes=[r_smal], tag="c3rq", sync_out=True)
    p.step("vector",
           lambda e: e.tensor_mul(s_smal[:, 0:16], s_smal[:, 32:48],
                                  s_smal[:, 32:48]),
           reads=[r_smal], writes=[r_smal], tag="c3n1")
    p.step("vector",
           lambda e: e.tensor_mul(s_smal[:, 0:16], s_smal[:, 0:16],
                                  s_smal[:, 16:32]),
           reads=[r_smal], writes=[r_smal], tag="c3n2")
    p.step("vector",
           lambda e: e.tensor_scalar(s_smal[:, 0:16], s_smal[:, 0:16],
                                     -0.5, 1.5, ALU.mult, ALU.add),
           reads=[r_smal], writes=[r_smal], tag="c3n3")
    p.step("vector",
           lambda e: e.tensor_mul(s_rs3[:], s_smal[:, 32:48],
                                  s_smal[:, 0:16]),
           reads=[r_smal], writes=[r_rs3], tag="c3n4", sync_out=True)
    p.step("vector",
           lambda e: e.tensor_mul(s_axm[:], s_axm[:], s_rs3[:]),
           reads=[r_rs3, r_axm], writes=[r_axm], tag="c3ax")
    p.step("vector",
           lambda e: e.tensor_scalar(s_smal[:, 0:16], s_axm[:], 1e-5, None,
                                     ALU.max),
           reads=[r_axm], writes=[r_smal], tag="dqmc")
    p.step("vector",
           lambda e: e.reciprocal(s_smal[:, 16:32], s_smal[:, 0:16]),
           reads=[r_smal], writes=[r_smal], tag="dqrc", sync_out=True)
    p.step("vector",
           lambda e: e.tensor_scalar(s_dsx[:], s_smal[:, 16:32], 127.0,
                                     None, ALU.mult),
           reads=[r_smal], writes=[r_dsx], tag="dqsx")
    p.step("vector",
           lambda e: e.tensor_scalar(s_smal[:, 32:48], s_smal[:, 0:16],
                                     1.0 / 127.0, None, ALU.mult),
           reads=[r_smal], writes=[r_smal], tag="dqos")
    p.step("vector",
           lambda e: e.tensor_scalar(s_dcg[:], s_smal[:, 32:48],
                                     s_wcol[:, 6:7], None, ALU.mult),
           reads=[r_smal, r_wcol], writes=[r_dcg], tag="dqcg")
    p.step("vector",
           lambda e: e.tensor_scalar(s_dcu[:], s_smal[:, 32:48],
                                     s_wcol[:, 7:8], None, ALU.mult),
           reads=[r_smal, r_wcol], writes=[r_dcu], tag="dqcu")

    rx2c = [[p.res() for _ in range(2)] for _ in range(2)]

    def d2_pre(t):
        xbuf, rxbuf = s_x2t[t % 2]
        for ch in range(2):
            dma(xbuf[:, ch * 512:(ch + 1) * 512],
                scr_x1[t * 128:(t + 1) * 128, ch * 512:(ch + 1) * 512],
                reads=[r_sx1[t]], writes=[rx2c[t % 2][ch], rxbuf],
                tag=f"d2x{t}.{ch}")
        p.step("vector",
               lambda e, xb=xbuf, t=t: e.tensor_scalar(
                   xb[:], xb[:], s_mu3[:, t:t + 1], s_rs3[:, t:t + 1],
                   ALU.subtract, ALU.mult),
               reads=[r_mu3, r_rs3, rx2c[t % 2][0], rx2c[t % 2][1],
                      rxbuf], writes=[rxbuf], tag=f"c3n{t}")
        dma(scr_x2[t * 128:(t + 1) * 128, :], xbuf[:], reads=[rxbuf],
            writes=[r_sx2[t]], tag=f"c3s{t}")

    def d2_stage1a(t):
        xbuf, rxbuf = s_x2t[t % 2]
        q1, rq1 = s_q1[t % 2]
        xq, rxq = s_xq[t % 2]
        p.step("scalar",
               lambda e, xb=xbuf, q1=q1, t=t: e.activation(
                   q1[:], xb[:], AF.Identity, scale=s_dsx[:, t:t + 1]),
               reads=[rxbuf, r_dsx], writes=[rq1], tag=f"d2q1{t}")
        p.step("vector",
               lambda e, q1=q1: e.tensor_scalar(q1[:], q1[:], MAGIC, None,
                                                ALU.add),
               reads=[rq1], writes=[rq1], tag=f"d2q1b{t}")
        p.step("vector",
               lambda e, q1=q1, xq=xq: e.tensor_scalar(
                   xq[:], q1[:], MAGIC, None, ALU.subtract),
               reads=[rq1], writes=[rxq], tag=f"d2q2{t}")

    def d2_stage1b(t):
        pr = t % 3
        xq, rxq = s_xq[t % 2]
        xqt, rxqt = s_xqt[pr]
        ptrd, rptrd = p_trd[pr]
        for c in range(8):
            def tr(c=c, xq=xq, ptrd=ptrd):
                return lambda e: e.transpose(ptrd[:, c * 128:(c + 1) * 128],
                                             xq[:, c * 128:(c + 1) * 128],
                                             s_ident[:])
            p.step("tensor", tr(), reads=[rxq, r_ident], writes=[rptrd],
                   tag=f"d2t{t}.{c}")
        for half in range(2):
            def cp(half=half, xqt=xqt, ptrd=ptrd):
                return lambda e: e.activation(
                    xqt[:, half * 512:(half + 1) * 512],
                    ptrd[:, half * 512:(half + 1) * 512], AF.Identity)
            p.step("scalar", cp(), reads=[rptrd], writes=[rxqt],
                   tag=f"d2tc{t}.{half}")

    def d2_stage2(t):
        pr = t % 3
        xqt, rxqt = s_xqt[pr]
        for oc in range(8):
            pg, rpg = p_g[oc % 2]
            pu, rpu = p_u[oc % 2]
            for dinc in range(8):
                def mg(dinc=dinc, oc=oc, pg=pg, xqt=xqt):
                    return lambda e: e.matmul(
                        pg[:],
                        xqt[:, dinc * 128:(dinc + 1) * 128],
                        s_wgtb[:, dinc * INNER + oc * 512:
                               dinc * INNER + (oc + 1) * 512],
                        start=(dinc == 0), stop=(dinc == 7),
                        skip_group_check=True)
                p.step("tensor", mg(), reads=[rxqt, r_wgtc[dinc]],
                       writes=[rpg], tag=f"d2g{t}.{oc}.{dinc}")
                def mu_(dinc=dinc, oc=oc, pu=pu, xqt=xqt):
                    return lambda e: e.matmul(
                        pu[:],
                        xqt[:, dinc * 128:(dinc + 1) * 128],
                        s_wutb[:, dinc * INNER + oc * 512:
                               dinc * INNER + (oc + 1) * 512],
                        start=(dinc == 0), stop=(dinc == 7),
                        skip_group_check=True)
                p.step("tensor", mu_(), reads=[rxqt, r_wutc[dinc]],
                       writes=[rpu], tag=f"d2u{t}.{oc}.{dinc}")
            def sil(pg=pg, t=t):
                return lambda e: e.activation(s_sg[:], pg[:], AF.Silu,
                                              scale=s_dcg[:, t:t + 1])
            p.step("scalar", sil(), reads=[rpg, r_dcg], writes=[r_sg],
                   tag=f"d2s{t}.{oc}")
            def hmul(oc=oc, pu=pu, t=t):
                return lambda e: e.scalar_tensor_tensor(
                    s_h[:, oc * 512:(oc + 1) * 512], pu[:],
                    s_dcu[:, t:t + 1], s_sg[:], ALU.mult, ALU.mult)
            p.step("vector", hmul(), reads=[rpu, r_dcu, r_sg],
                   writes=[r_h], tag=f"d2h{t}.{oc}")
            def hab(oc=oc, t=t):
                return lambda e: e.tensor_reduce(
                    s_smal[:, 8 + (oc % 2):9 + (oc % 2)],
                    s_h[:, oc * 512:(oc + 1) * 512], AX.X, ALU.max,
                    apply_absolute_value=True)
            p.step("vector", hab(), reads=[r_h], writes=[r_smal],
                   tag=f"d2ha{t}.{oc}", sync_out=True)
            def habm(oc=oc, t=t):
                if oc == 0:
                    return lambda e: e.tensor_copy(s_hm[:, t:t + 1],
                                                   s_smal[:, 8:9])
                return lambda e: e.tensor_max(
                    s_hm[:, t:t + 1], s_hm[:, t:t + 1],
                    s_smal[:, 8 + (oc % 2):9 + (oc % 2)])
            p.step("vector", habm(), reads=[r_smal, r_hm], writes=[r_hm],
                   tag=f"d2hm{t}.{oc}")
        p.step("vector",
               lambda e, t=t: e.tensor_scalar(s_smal[:, 1:2],
                                              s_hm[:, t:t + 1], 1e-5, None,
                                              ALU.max),
               reads=[r_hm], writes=[r_smal], tag=f"d2mc2{t}")
        p.step("vector",
               lambda e: e.reciprocal(s_smal[:, 2:3], s_smal[:, 1:2]),
               reads=[r_smal], writes=[r_smal], tag=f"d2rc2{t}",
               sync_out=True)
        p.step("vector",
               lambda e: e.tensor_scalar(s_smal[:, 3:4], s_smal[:, 2:3],
                                         127.0, None, ALU.mult),
               reads=[r_smal], writes=[r_smal], tag=f"d2sx2{t}",
               sync_out=True)
        p.step("vector",
               lambda e: e.tensor_scalar(s_smal[:, 4:5], s_smal[:, 1:2],
                                         1.0 / 127.0, None, ALU.mult),
               reads=[r_smal], writes=[r_smal], tag=f"d2os2{t}")
        p.step("vector",
               lambda e, t=t: e.tensor_mul(s_cbd[:, t:t + 1],
                                           s_smal[:, 4:5], s_wcol[:, 0:1]),
               reads=[r_smal, r_wcol], writes=[r_cbd], tag=f"d2cd{t}")
        for oc in range(4):
            xq2, rxq2 = s_xq2[oc % 2]
            q2, rq2 = s_q2[oc % 2]
            def q1f(oc=oc, q2=q2):
                return lambda e: e.activation(
                    q2[:], s_h[:, oc * 1024:(oc + 1) * 1024], AF.Identity,
                    scale=s_smal[:, 3:4])
            p.step("scalar", q1f(), reads=[r_h, r_smal], writes=[rq2],
                   tag=f"d2p1{t}.{oc}")
            p.step("vector",
                   lambda e, q2=q2: e.tensor_scalar(q2[:], q2[:], MAGIC,
                                                    None, ALU.add),
                   reads=[rq2], writes=[rq2], tag=f"d2p1b{t}.{oc}")
            p.step("vector", lambda e, xq2=xq2, q2=q2: e.tensor_scalar(
                xq2[:], q2[:], MAGIC, None,
                ALU.subtract), reads=[rq2], writes=[rxq2],
                tag=f"d2p2{t}.{oc}")
            dma(scr_xq2[t * 128:(t + 1) * 128,
                        oc * 1024:(oc + 1) * 1024], xq2[:],
                reads=[rxq2], writes=[r_sxq2[t]], tag=f"d2sx{t}.{oc}")

    d2_pre(0)
    d2_stage1a(0)
    d2_stage1b(0)
    d2_pre(1)
    d2_stage1a(1)
    d2_stage1b(1)
    for t in range(NT):
        if t + 2 < NT:
            d2_pre(t + 2)
            d2_stage1a(t + 2)
        d2_stage2(t)
        if t + 2 < NT:
            d2_stage1b(t + 2)

    if stage == "C":
        d_x2 = dbg_out("d_x2", [S, DM])
        d_wgt = dbg_out("d_wgt", [DM, INNER], BF16)
        d_wdt = dbg_out("d_wdt", [INNER, DM], BF16)
        toks = [dma(d_x2, scr_x2, reads=r_sx2[:], tag="dbgx2"),
                dma(d_wgt, scr_wgt, reads=[r_swg], tag="dbgwgt"),
                dma(d_wdt, scr_wdt, reads=[r_swd], tag="dbgwdt")]
        p.step("sync", lambda e: e.nop(), extra_deps=toks, tag="fin")
        p.emit(nc, top)
        return nc, dbg, top


    stD.close()
    p.add_fence(r_wgtc + r_wutc + [r_sg, r_h, r_hm,
                 p_g[0][1], p_g[1][1], p_u[0][1], p_u[1][1],
                 p_trd[0][1], p_trd[1][1], p_trd[2][1]]
                + [b[1] for b in (s_x2t + s_q1 + s_xq + s_xqt + s_xq2
                                  + [_q2])])

    stE = ExitStack()
    s_wdtb, r_wdtb = sb("e_wdtb", [128, 32 * DM], BF16, stack=stE)
    s_xq2t = [sb(f"e_xq2t{i}", [128, INNER], BF16, stack=stE)
              for i in range(2)]
    s_xq2T = [sb(f"e_xq2T{i}", [128, 32 * 128], BF16, stack=stE)
              for i in range(4)]
    s_ex2 = [sb(f"e_x2{i}", [128, DM], F32, stack=stE) for i in range(4)]
    s_y, r_y = sb("e_y", [128, DM], F32, stack=stE)
    s_o = [sb(f"e_o{i}", [128, DM], F32, stack=stE) for i in range(2)]
    p_d = [ps(f"p_d{i}", [128, 1024], F32, stack=stE) for i in range(2)]
    p_tre = [ps(f"p_tre{i}", [128, 1024], BF16, stack=stE) for i in range(2)]

    r_wdtc = [p.res() for _ in range(32)]
    for c in range(32):
        dma(s_wdtb[:, c * DM:(c + 1) * DM],
            scr_wdt[c * 128:(c + 1) * 128, :],
            reads=[r_swd], writes=[r_wdtc[c]], tag=f"ldwdt{c}",
            extra_deps=wq2_store_toks[2])

    out_toks = []
    rq_c = [[p.res() for _ in range(4)] for _ in range(2)]
    rex_c = [[p.res() for _ in range(2)] for _ in range(4)]

    def d3_prep(t):
        qbuf, rqbuf = s_xq2t[t % 2]
        for quad in range(4):
            dma(qbuf[:, quad * 1024:(quad + 1) * 1024],
                scr_xq2[t * 128:(t + 1) * 128,
                        quad * 1024:(quad + 1) * 1024],
                reads=[r_sxq2[t]], writes=[rq_c[t % 2][quad], rqbuf],
                tag=f"d3q{t}.{quad}")
        xqT, rxqT = s_xq2T[t % 4]
        for quad in range(4):
            ptr, rptr = p_tre[quad % 2]
            for c in range(8):
                cc = quad * 8 + c
                def tr(cc=cc, c=c, ptr=ptr, qbuf=qbuf):
                    return lambda e: e.transpose(
                        ptr[:, c * 128:(c + 1) * 128],
                        qbuf[:, cc * 128:(cc + 1) * 128], s_ident[:])
                p.step("tensor", tr(), reads=[rq_c[t % 2][quad], r_ident],
                       writes=[rptr], tag=f"d3t{t}.{cc}")
            for half in range(2):
                def cp(quad=quad, half=half, ptr=ptr, xqT=xqT):
                    return lambda e: e.tensor_copy(
                        xqT[:, quad * 1024 + half * 512:
                            quad * 1024 + (half + 1) * 512],
                        ptr[:, half * 512:(half + 1) * 512])
                p.step("vector", cp(), reads=[rptr], writes=[rxqT],
                       tag=f"d3c{t}.{quad}.{half}")
        xbuf, rxbuf = s_ex2[t % 4]
        for ch in range(2):
            dma(xbuf[:, ch * 512:(ch + 1) * 512],
                scr_x2[t * 128:(t + 1) * 128, ch * 512:(ch + 1) * 512],
                reads=[r_sx2[t]], writes=[rex_c[t % 4][ch], rxbuf],
                tag=f"d3x{t}.{ch}")

    def d3_main(t):
        xqT, rxqT = s_xq2T[t % 4]
        xbuf, rxbuf = s_ex2[t % 4]
        pd, rpd = p_d[t % 2]
        for i2c in range(32):
            for h2 in range(2):
                def md(i2c=i2c, h2=h2, pd=pd, xqT=xqT):
                    return lambda e: e.matmul(
                        pd[:, h2 * 512:(h2 + 1) * 512],
                        xqT[:, i2c * 128:(i2c + 1) * 128],
                        s_wdtb[:, i2c * DM + h2 * 512:
                               i2c * DM + (h2 + 1) * 512],
                        start=(i2c == 0), stop=(i2c == 31),
                        skip_group_check=True)
                p.step("tensor", md(), reads=[rxqT, r_wdtc[i2c]],
                       writes=[rpd], tag=f"d3m{t}.{i2c}.{h2}")
        p.step("vector",
               lambda e, pd=pd, xb=xbuf, t=t: e.scalar_tensor_tensor(
                   s_y[:], pd[:], s_cbd[:, t:t + 1], xb[:], ALU.mult,
                   ALU.add),
               reads=[rpd, r_cbd, rex_c[t % 4][0], rex_c[t % 4][1],
                      rxbuf], writes=[r_y], tag=f"d3y{t}")
        dy3, rdy3 = s_dummy[t % 2]
        p.step("scalar",
               lambda e, dy3=dy3: e.activation(dy3[:], s_y[:], AF.Square),
               reads=[r_y], writes=[rdy3], tag=f"d3sq{t}")
        p.step("vector",
               lambda e, dy3=dy3: e.tensor_reduce(s_smal[:, 0:1], dy3[:],
                                                  AX.X, ALU.add),
               reads=[rdy3], writes=[r_smal], tag=f"d3sq{t}r",
               sync_out=True)
        p.step("vector",
               lambda e: e.tensor_reduce(s_smal[:, 1:2], s_y[:], AX.X,
                                         ALU.add),
               reads=[r_y], writes=[r_smal], tag=f"d3su{t}", sync_out=True)
        p.step("vector",
               lambda e: e.tensor_scalar(s_smal[:, 6:7], s_smal[:, 1:2],
                                         1.0 / DM, None, ALU.mult),
               reads=[r_smal], writes=[r_smal], tag=f"d3mu{t}")
        p.step("vector",
               lambda e: e.tensor_scalar(s_smal[:, 2:3], s_smal[:, 0:1],
                                         1.0 / DM, None, ALU.mult),
               reads=[r_smal], writes=[r_smal], tag=f"d3ms{t}")
        p.step("vector",
               lambda e: e.tensor_mul(s_smal[:, 3:4], s_smal[:, 6:7],
                                      s_smal[:, 6:7]),
               reads=[r_smal], writes=[r_smal], tag=f"d3mm{t}")
        p.step("vector",
               lambda e: e.tensor_sub(s_smal[:, 4:5], s_smal[:, 2:3],
                                      s_smal[:, 3:4]),
               reads=[r_smal], writes=[r_smal], tag=f"d3v{t}",
               flush=lambda e: e.tensor_copy(s_vfl[:, 0:1], s_smal[:, 4:5]))
        p.step("scalar",
               lambda e: e.activation(s_smal[:, 5:6], s_smal[:, 4:5], AF.Ln,
                                      bias=s_cvec[:, 0:1]),
               reads=[r_smal, r_cvec], writes=[r_smal], tag=f"d3l{t}")
        p.step("scalar",
               lambda e: e.activation(s_smal[:, 7:8], s_smal[:, 5:6],
                                      AF.Exp, scale=-0.5),
               reads=[r_smal], writes=[r_smal], tag=f"d3r{t}",
               flush=lambda e: e.copy(s_sfl[:, 0:1], s_smal[:, 7:8]))
        obuf, robuf = s_o[t % 2]
        p.step("vector",
               lambda e, ob=obuf: e.tensor_scalar(
                   ob[:], s_y[:], s_smal[:, 6:7], s_smal[:, 7:8],
                   ALU.subtract, ALU.mult),
               reads=[r_y, r_smal], writes=[robuf], tag=f"d3n{t}")
        out_toks.append(dma(out_t[t * 128:(t + 1) * 128, :], obuf[:],
                            reads=[robuf], tag=f"out{t}"))

    d3_prep(0)
    d3_prep(1)
    for tp in range(0, NT, 2):
        if tp + 2 < NT:
            d3_prep(tp + 2)
            d3_prep(tp + 3)
        d3_main(tp)
        d3_main(tp + 1)
    p.step("sync", lambda e: e.nop(), extra_deps=out_toks, tag="fin")
    stE.close()
    p.emit(nc, top)
    return nc, dbg, top


_CACHE = {}


def _get_program(stage="full"):
    if stage not in _CACHE:
        _CACHE[stage] = build_program(stage)
    return _CACHE[stage]


def make_in_maps(inputs):
    c = _host_consts()
    f32 = np.float32
    base = dict(
        ln1_g=np.ascontiguousarray(inputs["L0_attn_ln1_g"], f32),
        ln1_b=np.ascontiguousarray(inputs["L0_attn_ln1_b"], f32),
        wqkv=np.ascontiguousarray(inputs["L0_attn_Wqkv"], f32),
        bqkv=np.ascontiguousarray(inputs["L0_attn_bqkv"], f32),
        wo=np.ascontiguousarray(inputs["L0_attn_Wo"], f32),
        bo=np.ascontiguousarray(inputs["L0_attn_bo"], f32),
        ln2_g=np.ascontiguousarray(inputs["L0_attn_ln2_g"], f32),
        ln2_b=np.ascontiguousarray(inputs["L0_attn_ln2_b"], f32),
        w1=np.ascontiguousarray(inputs["L0_attn_W1"], f32),
        b1=np.ascontiguousarray(inputs["L0_attn_b1"], f32),
        w2=np.ascontiguousarray(inputs["L0_attn_W2"], f32),
        b2=np.ascontiguousarray(inputs["L0_attn_b2"], f32),
        wg=np.ascontiguousarray(inputs["L0_ffn_w_gate"], f32),
        wu=np.ascontiguousarray(inputs["L0_ffn_w_up"], f32),
        wd=np.ascontiguousarray(inputs["L0_ffn_w_down"], f32),
        cosF=c["cosF"], sinnF=c["sinnF"], trimask=c["trimask"],
        ident=c["ident"], identf=c["identf"], onesf=c["onesf"],
        ecat=c["ecat"], cvec=c["cvec"], sel=c["sel"], onescol=c["onescol"],
    )
    x = np.ascontiguousarray(inputs["L0_attn_x"], f32)
    maps = []
    for core in range(NCORES):
        m = dict(base)
        m["x"] = np.ascontiguousarray(x[core])
        maps.append(m)
    return maps


def kernel(**inputs):
    nc, dbg, top = _get_program("full")
    maps = make_in_maps(inputs)
    res = run_bass_kernel_spmd(nc, maps, core_ids=list(range(NCORES)))
    out = np.stack([res.results[i]["out"] for i in range(NCORES)], axis=0)
    return out.astype(np.float32)


# version 36
# speedup vs baseline: 1.5198x; 1.0006x over previous
import numpy as np
import ml_dtypes
import concourse.bass as bass
import concourse.mybir as mybir
from concourse.bass_utils import run_bass_kernel_spmd

F32 = mybir.dt.float32
BF16 = mybir.dt.bfloat16
AF = mybir.ActivationFunctionType
ALU = mybir.AluOpType
AX = mybir.AxisListType

MAGIC = 12582912.0
S, DM, H, HD, INNER = 2048, 1024, 16, 64, 4096
NT = S // 128
NCORES = 8
LNEPS = 1e-5

ENGINES = ("tensor", "vector", "scalar", "gpsimd", "sync")


class _Step:
    __slots__ = ("idx", "eng", "fn", "deps", "marked", "semval", "dma",
                 "dma_sem", "dma_val", "tag", "sync_out", "flush")

    def __init__(self, idx, eng, fn, deps, dma, tag, sync_out=False,
                 flush=None):
        self.idx, self.eng, self.fn, self.deps = idx, eng, fn, deps
        self.marked = False
        self.semval = None
        self.dma = dma
        self.dma_sem = None
        self.dma_val = None
        self.tag = tag
        self.sync_out = sync_out
        self.flush = flush


class Res:
    __slots__ = ("writers", "readers")

    def __init__(self, fence=()):
        self.writers = list(fence)
        self.readers = []


class Prog:
    def __init__(self):
        self.steps = []
        self.per_engine = {e: [] for e in ENGINES}
        self.fence = []

    def res(self):
        return Res(self.fence)

    def step(self, eng, fn, reads=(), writes=(), dma=False, tag="",
             extra_deps=(), sync_out=False, flush=None):
        deps = list(extra_deps)
        for r in reads:
            deps += r.writers
        for w in writes:
            deps += w.writers
            deps += w.readers
        s = _Step(len(self.steps), eng, fn, deps, dma, tag, sync_out, flush)
        self.steps.append(s)
        self.per_engine[eng].append(s)
        for r in reads:
            r.readers.append(s)
        for w in writes:
            w.writers = [s]
            w.readers = []
        return s

    def add_fence(self, res_list):
        toks = []
        for r in res_list:
            toks += r.writers
            toks += r.readers
        toks += self.fence
        latest = {}
        dmas = {}
        for t in toks:
            if t.dma:
                dmas[id(t)] = t
            else:
                if t.eng not in latest or t.idx > latest[t.eng].idx:
                    latest[t.eng] = t
        self.fence = list(latest.values()) + list(dmas.values())

    def finalize(self):
        for s in self.steps:
            dd = []
            seen = set()
            for d in s.deps:
                if d is s or id(d) in seen:
                    continue
                seen.add(id(d))
                dd.append(d)
                if d.dma:
                    pass
                else:
                    d.marked = True
            s.deps = dd
        for eng in ENGINES:
            c = 0
            for s in self.per_engine[eng]:
                if s.marked:
                    c += 1
                    s.semval = c

    def emit(self, nc, stack, n_dma_sems=48):
        psem = {e: stack.enter_context(nc.semaphore(f"prog_{e}"))
                for e in ENGINES}
        dsems = [stack.enter_context(nc.semaphore(f"dma{i}"))
                 for i in range(n_dma_sems)]
        dcount = [0] * n_dma_sems
        pool = {"sync": list(range(0, 32)), "gpsimd": list(range(32, 48))}
        di = {"sync": 0, "gpsimd": 0}
        for s in self.steps:
            if s.dma:
                pl = pool[s.eng]
                i = pl[di[s.eng]]
                s.dma_sem = i
                dcount[i] += 1
                s.dma_val = 16 * dcount[i]
                di[s.eng] = (di[s.eng] + 1) % len(pl)
        self.finalize()
        block = nc.Block()

        def make_body(eng_name):
            def body(eng):
                hw = {e: 0 for e in ENGINES}
                dhw = [0] * n_dma_sems
                for s in self.per_engine[eng_name]:
                    ew = {}
                    dw = {}
                    for d in s.deps:
                        if d.dma:
                            dw[d.dma_sem] = max(dw.get(d.dma_sem, 0),
                                                d.dma_val)
                        elif d.eng == "tensor" and eng_name == "tensor":
                            continue
                        else:
                            ew[d.eng] = max(ew.get(d.eng, 0), d.semval)
                    if s.dma and s.dma_val > 16:
                        dw[s.dma_sem] = max(dw.get(s.dma_sem, 0),
                                            s.dma_val - 16)
                    for e, v in sorted(ew.items()):
                        if v > hw[e]:
                            eng.wait_ge(psem[e], v)
                            hw[e] = v
                    for q, v in sorted(dw.items()):
                        if v > dhw[q]:
                            eng.wait_ge(dsems[q], v)
                            dhw[q] = v
                    inst = s.fn(eng)
                    if s.dma:
                        inst.then_inc(dsems[s.dma_sem], 16)
                    if s.marked:
                        inst.then_inc(psem[eng_name], 1)
            return body

        with block:
            block.tensor(make_body("tensor"))
            block.vector(make_body("vector"))
            block.scalar(make_body("scalar"))
            block.gpsimd(make_body("gpsimd"))
            block.sync(make_body("sync"))


def _host_consts():
    t = np.arange(S, dtype=np.float64)
    inv = 1.0 / 10000.0 ** (np.arange(0, HD, 2, dtype=np.float64) / HD)
    fr = t[:, None] * inv[None, :]
    emb = np.concatenate([fr, fr], axis=1)
    cos = np.cos(emb).astype(np.float32)
    sin = np.sin(emb).astype(np.float32)
    sinn = sin.copy()
    sinn[:, :32] *= -1.0
    cosF = np.tile(cos, (1, H)).astype(np.float32)
    sinnF = np.tile(sinn, (1, H)).astype(np.float32)
    trimask = np.triu(np.ones((128, 128), np.float32)).astype(ml_dtypes.bfloat16)
    ident = np.eye(128, dtype=ml_dtypes.bfloat16)
    identf = np.eye(128, dtype=np.float32)
    onesf = np.ones((1, 128), np.float32)
    ecat = np.zeros((128, 256), np.float32)
    for h in range(16):
        ecat[:, h * 16 + h] = 1.0
    ecat = ecat.astype(ml_dtypes.bfloat16)
    onescol = np.ones((128, 1), np.float32)
    sel = np.zeros((16, 8 * 128), np.float32)
    for hp in range(8):
        sel[2 * hp, hp * 128:hp * 128 + 64] = 1.0
        sel[2 * hp + 1, hp * 128 + 64:hp * 128 + 128] = 1.0
    cvec = np.zeros((128, 2), np.float32)
    cvec[:, 0] = LNEPS
    cvec[:, 1] = MAGIC
    return dict(cosF=cosF, sinnF=sinnF, trimask=trimask, ident=ident,
                identf=identf, onesf=onesf, ecat=ecat, cvec=cvec, sel=sel,
                onescol=onescol)


def build_program(stage="full"):
    from contextlib import ExitStack
    nc = bass.Bass("TRN2", target_bir_lowering=False)
    p = Prog()
    top = ExitStack()

    def din(name, shape, dtype=F32):
        return nc.dram_tensor(name, shape, dtype, kind="ExternalInput").ap()

    x_in = din("x", [S, DM])
    ln1g_in = din("ln1_g", [DM]); ln1b_in = din("ln1_b", [DM])
    wqkv_in = din("wqkv", [DM, 3 * DM]); bqkv_in = din("bqkv", [3 * DM])
    wo_in = din("wo", [DM, DM]); bo_in = din("bo", [DM])
    ln2g_in = din("ln2_g", [DM]); ln2b_in = din("ln2_b", [DM])
    w1_in = din("w1", [DM, INNER]); b1_in = din("b1", [INNER])
    w2_in = din("w2", [INNER, DM]); b2_in = din("b2", [DM])
    wg_in = din("wg", [INNER, DM]); wu_in = din("wu", [INNER, DM])
    wd_in = din("wd", [DM, INNER])
    cos_in = din("cosF", [S, DM]); sinn_in = din("sinnF", [S, DM])
    tri_in = din("trimask", [128, 128], BF16)
    ident_in = din("ident", [128, 128], BF16)
    identf_in = din("identf", [128, 128], F32)
    onesf_in = din("onesf", [1, 128])
    ecat_in = din("ecat", [128, 256], BF16)
    cvec_in = din("cvec", [128, 2])
    sel_in = din("sel", [16, 8 * 128])
    onescol_in = din("onescol", [128, 1])

    out_t = nc.dram_tensor("out", [S, DM], F32, kind="ExternalOutput").ap()
    dbg = {}

    def dbg_out(name, shape, dtype=F32):
        ap = nc.dram_tensor(name, shape, dtype, kind="ExternalOutput").ap()
        dbg[name] = ap
        return ap

    scr_x1 = nc.dram_tensor("scr_x1", [S, DM], F32, kind="Internal").ap()
    scr_x2 = nc.dram_tensor("scr_x2", [S, DM], F32, kind="Internal").ap()
    scr_xq2 = nc.dram_tensor("scr_xq2", [S, INNER], BF16, kind="Internal").ap()
    scr_wgt = nc.dram_tensor("scr_wgt", [DM, INNER], BF16, kind="Internal").ap()
    scr_wut = nc.dram_tensor("scr_wut", [DM, INNER], BF16, kind="Internal").ap()
    scr_wdt = nc.dram_tensor("scr_wdt", [INNER, DM], BF16, kind="Internal").ap()
    scr_row = nc.dram_tensor("scr_row", [16, 512], F32, kind="Internal").ap()
    scr_vec = nc.dram_tensor("scr_vec", [1, 64], F32, kind="Internal").ap()

    def sb(name, shape, dtype=F32, stack=None):
        h = (stack or top).enter_context(nc.sbuf_tensor(name, shape, dtype))
        return h, p.res()

    def ps(name, shape, dtype=F32, stack=None):
        h = (stack or top).enter_context(nc.psum_tensor(name, shape, dtype))
        return h, p.res()

    def dma(dst_ap, src_ap, reads=(), writes=(), eng="sync", tag="",
            extra_deps=()):
        if eng == "sync" and "DRam" in type(dst_ap.tensor).__name__:
            eng = "gpsimd"
        return p.step(eng, lambda e: e.dma_start(out=dst_ap, in_=src_ap),
                      reads=reads, writes=writes, dma=True, tag=tag,
                      extra_deps=extra_deps)

    r_sx1 = [p.res() for _ in range(NT)]
    r_sx2 = [p.res() for _ in range(NT)]
    r_sxq2 = [p.res() for _ in range(NT)]
    r_swg = p.res(); r_swu = p.res(); r_swd = p.res()
    wq2_store_toks = [[], [], []]
    r_srow = p.res(); r_svec = p.res()

    s_ident, r_ident = sb("c_ident", [128, 128], BF16)
    s_identf, r_identf = sb("c_identf", [128, 128], F32)
    s_tri, r_tri = sb("c_tri", [128, 128], BF16)
    s_onesf, r_onesf = sb("c_onesf", [1, 128], F32)
    s_ecat, r_ecat = sb("c_ecat", [128, 256], BF16)
    s_gb1, r_gb1 = sb("c_gb1", [128, 16], F32)
    s_gb2, r_gb2 = sb("c_gb2", [128, 16], F32)
    s_b1c, r_b1c = sb("c_b1c", [128, 32], F32)
    s_b2b, r_b2b = sb("c_b2b", [128, DM], F32)
    s_mu1, r_mu1 = sb("g_mu1", [128, 16], F32)
    s_rs1, r_rs1 = sb("g_rs1", [128, 16], F32)
    s_nrm, r_nrm = sb("g_nrm", [128, 32], F32)
    s_bneg, r_bneg = sb("g_bneg", [128, 16], F32)
    s_wsum, r_wsum = sb("g_wsum", [1, 8], F32)
    s_wrow, r_wrow = sb("g_wrow", [1, 8], F32)
    s_wcol, r_wcol = sb("g_wcol", [128, 8], F32)
    s_cbd, r_cbd = sb("g_cbd", [128, 16], F32)
    s_row1, r_row1 = sb("g_row1", [1, DM], F32)
    s_gbt, r_gbt = sb("g_gbt", [32, 128], F32)
    s_dummy = [sb(f"g_dummy{i}", [128, DM], F32) for i in range(2)]
    s_cvec, r_cvec = sb("c_cvec", [128, 2], F32)
    s_onescol, r_onescol = sb("c_onescol", [128, 1], F32)
    s_wacc, r_wacc = sb("g_wacc", [128, 4], F32)
    s_w31, r_w31 = sb("g_w31", [3, 1], F32)
    s_vfl, _r_vfl = sb("g_vfl", [128, 16], F32)
    s_sfl, _r_sfl = sb("g_sfl", [128, 16], F32)
    s_smal, r_smal = sb("g_smal", [128, 48], F32)
    s_hrow, r_hrow = sb("g_hrow", [1, 32], F32)
    s_xt = [sb(f"g_xt{i}", [128, DM], F32) for i in range(2)]
    s_c1sum, r_c1sum = sb("g_c1sum", [128, 16], F32)
    s_c1sq, r_c1sq = sb("g_c1sq", [128, 16], F32)
    s_c3sum, r_c3sum = sb("g_c3sum", [128, 16], F32)
    s_c3sq, r_c3sq = sb("g_c3sq", [128, 16], F32)
    s_mu3, r_mu3 = sb("g_mu3", [128, 16], F32)
    s_rs3, r_rs3 = sb("g_rs3", [128, 16], F32)
    s_axm, r_axm = sb("g_axm", [128, 16], F32)
    s_dsx, r_dsx = sb("g_dsx", [128, 16], F32)
    s_dcg, r_dcg = sb("g_dcg", [128, 16], F32)
    s_dcu, r_dcu = sb("g_dcu", [128, 16], F32)

    stQKV = ExitStack()
    s_qt, r_qt = sb("a_QT", [128, 8 * S], BF16, stack=stQKV)
    s_kt, r_kt = sb("a_KT", [128, 8 * S], BF16, stack=stQKV)
    s_v, r_v = sb("a_V", [128, NT * DM], BF16, stack=stQKV)

    pro = ExitStack()
    p_mix, r_pmix = ps("p_mix", [128, 512], F32, stack=pro)

    dma(s_ident[:], ident_in, writes=[r_ident])
    dma(s_identf[:], identf_in, writes=[r_identf])
    dma(s_tri[:], tri_in, writes=[r_tri])
    dma(s_onesf[:], onesf_in, writes=[r_onesf])
    dma(s_ecat[:], ecat_in, writes=[r_ecat])
    dma(s_cvec[:], cvec_in, writes=[r_cvec])
    dma(s_onescol[:], onescol_in, writes=[r_onescol])

    def load_cols(dsts, src_ap, nchunks):
        dst, dst_res, c0 = dsts
        dma(s_gbt[0:nchunks, :], src_ap.rearrange("(c q) -> c q", q=128),
            writes=[r_gbt])
        p.step("tensor",
               lambda e: e.matmul(p_mix[:, 0:nchunks], s_gbt[0:nchunks, :],
                                  s_identf[0:nchunks, 0:nchunks],
                                  start=True, stop=True),
               reads=[r_gbt, r_identf], writes=[r_pmix])
        p.step("vector",
               lambda e: e.tensor_copy(dst[:, c0:c0 + nchunks],
                                       p_mix[:, 0:nchunks]),
               reads=[r_pmix], writes=[dst_res],
               flush=lambda e: e.tensor_copy(s_vfl[:, 0:1],
                                             dst[:, c0:c0 + 1]))

    load_cols((s_gb1, r_gb1, 0), ln1g_in, 8)
    load_cols((s_gb1, r_gb1, 8), ln1b_in, 8)
    load_cols((s_gb2, r_gb2, 0), ln2g_in, 8)
    load_cols((s_gb2, r_gb2, 8), ln2b_in, 8)
    load_cols((s_b1c, r_b1c, 0), b1_in, 32)

    def bcast_row(dst, dst_res, src_row_ap, n):
        dma(s_row1[0:1, 0:n], src_row_ap, writes=[r_row1])
        for j in range(0, n, 512):
            w = min(512, n - j)
            def mk(j=j, w=w):
                return lambda e: e.matmul(p_mix[:, 0:w], s_onesf[:],
                                          s_row1[0:1, j:j + w],
                                          start=True, stop=True)
            p.step("tensor", mk(), reads=[r_row1, r_onesf], writes=[r_pmix])
            def cp(j=j, w=w):
                return lambda e: e.tensor_copy(dst[:, j:j + w],
                                               p_mix[:, 0:w])
            p.step("vector", cp(), reads=[r_pmix], writes=[dst_res])

    bcast_row(s_b2b, r_b2b, b2_in.rearrange("(a n) -> a n", a=1), DM)

    stA = ExitStack()
    s_wph, r_wph = sb("a_wph", [128, 8 * 1024], BF16, stack=stA)
    s_pbias, r_pbias = sb("a_pbias", [128, 1024], F32, stack=stA)
    s_xn = [sb(f"a_xn{i}", [128, 1024], BF16, stack=stA) for i in range(2)]
    s_xnt = [sb(f"a_xnt{i}", [128, 1024], BF16, stack=stA) for i in range(2)]
    xnt_cres = [[p.res() for _ in range(8)] for _ in range(2)]
    ptr_cres = [[p.res() for _ in range(8)] for _ in range(2)]
    s_a0b = [sb(f"a_a0b{i}", [128, 1024], F32, stack=stA) for i in range(2)]
    s_qb = [sb(f"a_qb{i}", [128, 1024], F32, stack=stA) for i in range(2)]
    s_cost = [sb(f"a_cost{i}", [128, 1024], F32, stack=stA) for i in range(2)]
    s_sint = [sb(f"a_sint{i}", [128, 1024], F32, stack=stA) for i in range(2)]
    _tc = sb("a_tcos", [128, 1024], F32, stack=stA)
    _ts_ = sb("a_tsin", [128, 1024], F32, stack=stA)
    s_tcos = [_tc, _tc]
    s_tsin = [_ts_, _ts_]
    s_rope = [sb(f"a_rope{i}", [128, 1024], BF16, stack=stA) for i in range(2)]
    _sq_ = sb("a_sqbf", [128, 1024], BF16, stack=stA)
    s_sqbf = [_sq_, _sq_]
    s_nrmt = [sb(f"a_nrmt{i}", [128, 16], F32, stack=stA) for i in range(2)]
    s_asum, r_asum = sb("a_asum", [128, 16], F32, stack=stA)
    s_asq, r_asq = sb("a_asq", [128, 16], F32, stack=stA)
    s_wst = [sb(f"a_wst{i}", [128, 1024], F32, stack=stA) for i in range(2)]

    p_m = [ps(f"p_m{i}", [128, 1024], F32, stack=stA) for i in range(2)]
    p_tr = [ps(f"p_tr{i}", [128, 1024], BF16, stack=stA) for i in range(2)]


    p.step("vector", lambda e: e.memset(s_nrm[:], 0.0), writes=[r_nrm])


    wq_jobs = []
    for widx, w_ap in enumerate((wg_in, wu_in, wd_in)):
        for ch in range(32):
            wq_jobs.append((widx, w_ap, ch))
    wq_pos = [0]

    def wmean_pump(n):
        for _ in range(n):
            if wq_pos[0] >= len(wq_jobs):
                return
            widx, w_ap, ch = wq_jobs[wq_pos[0]]
            wq_pos[0] += 1
            buf, rbuf = s_wst[wq_pos[0] % 2]
            if widx < 2:
                src = w_ap[ch * 128:(ch + 1) * 128, :]
            else:
                src = w_ap.rearrange("m (c n) -> (m c) n", c=4)[
                    ch * 128:(ch + 1) * 128, :]
            dma(buf[:], src, writes=[rbuf], tag=f"wm{widx}.{ch}")
            p.step("vector",
                   lambda e, b=buf: e.tensor_reduce(
                       s_smal[:, 40:41], b[:], AX.X, ALU.add,
                       apply_absolute_value=True),
                   reads=[rbuf], writes=[r_smal], tag=f"wabs{widx}.{ch}",
                   sync_out=True)
            if ch == 0:
                def acc0(widx=widx):
                    return lambda e: e.tensor_copy(
                        s_wacc[:, widx:widx + 1], s_smal[:, 40:41])
            else:
                def acc0(widx=widx):
                    return lambda e: e.tensor_add(
                        s_wacc[:, widx:widx + 1], s_wacc[:, widx:widx + 1],
                        s_smal[:, 40:41])
            p.step("vector", acc0(), reads=[r_smal, r_wacc], writes=[r_wacc],
                   tag=f"wacc{widx}.{ch}")

    a0bufs = [s_xt[0], s_xt[1], s_a0b[0], s_a0b[1]]
    for t in range(NT):
        xbuf, rxbuf = a0bufs[t % 4]
        dy, rdy = s_dummy[t % 2]
        dma(xbuf[:], x_in[t * 128:(t + 1) * 128, :], writes=[rxbuf],
            tag=f"a0x{t}")
        p.step("scalar",
               lambda e, xb=xbuf, dy=dy: e.activation(dy[:], xb[:],
                                                      AF.Square),
               reads=[rxbuf], writes=[rdy], tag=f"a0sq{t}")
        p.step("vector",
               lambda e, dy=dy, t=t: e.tensor_reduce(
                   s_asq[:, t:t + 1], dy[:], AX.X, ALU.add),
               reads=[rdy], writes=[r_asq], tag=f"a0qr{t}")
        p.step("vector",
               lambda e, xb=xbuf, t=t: e.tensor_reduce(
                   s_asum[:, t:t + 1], xb[:], AX.X, ALU.add),
               reads=[rxbuf], writes=[r_asum], tag=f"a0sr{t}")

    def batch_stats(src_sum, r_srcsum, src_sq, r_srcsq, dst_mu, r_dstmu,
                    dst_rs, r_dstrs, pre):
        p.step("vector",
               lambda e: e.tensor_scalar(dst_mu[:], src_sum[:], 1.0 / DM,
                                         None, ALU.mult),
               reads=[r_srcsum], writes=[r_dstmu], tag=pre + "mu")
        p.step("vector",
               lambda e: e.tensor_scalar(s_smal[:, 0:16], src_sq[:],
                                         1.0 / DM, None, ALU.mult),
               reads=[r_srcsq], writes=[r_smal], tag=pre + "ms")
        p.step("vector",
               lambda e: e.tensor_mul(s_smal[:, 16:32], dst_mu[:],
                                      dst_mu[:]),
               reads=[r_dstmu], writes=[r_smal], tag=pre + "mm")
        p.step("vector",
               lambda e: e.tensor_sub(s_smal[:, 0:16], s_smal[:, 0:16],
                                      s_smal[:, 16:32]),
               reads=[r_smal], writes=[r_smal], tag=pre + "v")
        p.step("scalar",
               lambda e: e.activation(s_smal[:, 16:32], s_smal[:, 0:16],
                                      AF.Ln, bias=s_cvec[:, 0:1]),
               reads=[r_smal, r_cvec], writes=[r_smal], tag=pre + "l")
        p.step("scalar",
               lambda e: e.activation(dst_rs[:], s_smal[:, 16:32], AF.Exp,
                                      scale=-0.5),
               reads=[r_smal], writes=[r_dstrs], tag=pre + "r")

    batch_stats(s_asum, r_asum, s_asq, r_asq, s_mu1, r_mu1, s_rs1, r_rs1,
                "a0")

    for phase in range(3):
        dma(s_wph[:].rearrange("p (c n) -> p c n", c=8),
            wqkv_in[:, phase * 1024:(phase + 1) * 1024].rearrange(
                "(c q) n -> q c n", q=128),
            writes=[r_wph], eng="gpsimd", tag=f"wqkv{phase}")
        bcast_row(s_pbias, r_pbias,
                  bqkv_in.rearrange("(a n) -> a n", a=1)[
                      0:1, phase * 1024:(phase + 1) * 1024], DM)

        def a_stage1(t, phase=phase):
            pr = t % 2
            xbuf, rxbuf = s_xt[pr]
            dma(xbuf[:], x_in[t * 128:(t + 1) * 128, :], writes=[rxbuf],
                tag=f"x{phase}.{t}")
            if phase != 2:
                cbuf, rcbuf = s_cost[pr]
                sbuf_, rsbuf_ = s_sint[pr]
                dma(cbuf[:], cos_in[t * 128:(t + 1) * 128, :],
                    writes=[rcbuf], tag=f"cos{phase}.{t}")
                dma(sbuf_[:], sinn_in[t * 128:(t + 1) * 128, :],
                    writes=[rsbuf_], tag=f"sin{phase}.{t}")
            xn, rxn = s_xn[pr]
            xnt, rxnt = s_xnt[pr]
            ptr, rptr = p_tr[pr]
            p.step("vector",
                   lambda e, xb=xbuf, xn=xn, t=t: e.tensor_scalar(
                       xn[:], xb[:], s_mu1[:, t:t + 1], s_rs1[:, t:t + 1],
                       ALU.subtract, ALU.mult),
                   reads=[rxbuf, r_mu1, r_rs1], writes=[rxn],
                   tag=f"xn{phase}.{t}")
            for c in range(8):
                def tr(c=c, xn=xn, ptr=ptr):
                    return lambda e: e.transpose(
                        ptr[:, c * 128:(c + 1) * 128],
                        xn[:, c * 128:(c + 1) * 128], s_ident[:])
                p.step("tensor", tr(), reads=[rxn, r_ident],
                       writes=[rptr], tag=f"trx{phase}.{t}.{c}")
                def ev(c=c, ptr=ptr, xnt=xnt):
                    return lambda e: e.activation(
                        xnt[:, c * 128:(c + 1) * 128],
                        ptr[:, c * 128:(c + 1) * 128], AF.Identity,
                        scale=s_gb1[:, c:c + 1],
                        bias=s_gb1[:, 8 + c:9 + c])
                p.step("scalar", ev(), reads=[rptr, r_gb1],
                       writes=[xnt_cres[pr][c]], tag=f"evx{phase}.{t}.{c}")
            pm, rpm = p_m[pr]
            for dinc in range(8):
                for h2 in range(2):
                    def mm(dinc=dinc, h2=h2, pm=pm, xnt=xnt):
                        return lambda e: e.matmul(
                            pm[:, h2 * 512:(h2 + 1) * 512],
                            xnt[:, dinc * 128:(dinc + 1) * 128],
                            s_wph[:, dinc * 1024 + h2 * 512:
                                  dinc * 1024 + (h2 + 1) * 512],
                            start=(dinc == 0), stop=(dinc == 7),
                            skip_group_check=True)
                    p.step("tensor", mm(),
                           reads=[xnt_cres[pr][dinc], r_wph],
                           writes=[rpm], tag=f"qkv{phase}.{t}.{dinc}.{h2}")

        def a_stage2(t, phase=phase):
            pr = t % 2
            pm, rpm = p_m[pr]
            ptr, rptr = p_tr[pr]
            if phase == 2:
                for h2 in range(2):
                    def ev2(h2=h2, t=t, pm=pm):
                        return lambda e: e.tensor_add(
                            s_v[:, t * 1024 + h2 * 512:
                                t * 1024 + (h2 + 1) * 512],
                            pm[:, h2 * 512:(h2 + 1) * 512],
                            s_pbias[:, h2 * 512:(h2 + 1) * 512])
                    p.step("vector", ev2(), reads=[rpm, r_pbias],
                           writes=[r_v], tag=f"vev{t}.{h2}")
                return
            cbuf, rcbuf = s_cost[pr]
            sbuf_, rsbuf_ = s_sint[pr]
            qb, rqb = s_qb[pr]
            p.step("vector",
                   lambda e, pm=pm, qb=qb: e.tensor_add(qb[:], pm[:],
                                                        s_pbias[:]),
                   reads=[rpm, r_pbias], writes=[rqb],
                   tag=f"qbev{phase}.{t}")
            sq, rsq = s_sqbf[pr]
            nt_, rnt_ = s_nrmt[pr]
            p.step("scalar",
                   lambda e, qb=qb, sq=sq: e.activation(sq[:], qb[:],
                                                        AF.Square),
                   reads=[rqb], writes=[rsq], tag=f"qsq{phase}.{t}")
            p.step("vector",
                   lambda e, sq=sq, nt_=nt_: e.tensor_reduce(
                       nt_[:], sq[:].rearrange("q (h c) -> q h c", h=16,
                                               c=64),
                       AX.X, ALU.add),
                   reads=[rsq], writes=[rnt_], tag=f"qnr{phase}.{t}",
                   sync_out=True)
            c0 = phase * 16
            p.step("vector",
                   lambda e, c0=c0, nt_=nt_: e.tensor_max(
                       s_nrm[:, c0:c0 + 16], s_nrm[:, c0:c0 + 16], nt_[:]),
                   reads=[rnt_, r_nrm], writes=[r_nrm],
                   tag=f"qmx{phase}.{t}")
            tc, rtc = s_tcos[pr]
            tsn, rtsn = s_tsin[pr]
            rp, rrp = s_rope[pr]
            p.step("vector",
                   lambda e, cbuf=cbuf, qb=qb, tc=tc: e.tensor_mul(
                       tc[:], qb[:], cbuf[:]),
                   reads=[rqb, rcbuf], writes=[rtc], tag=f"rc{phase}.{t}")
            qv = qb[:].rearrange("q (h t2 c) -> q h t2 c", h=16, t2=2, c=32)
            sv = sbuf_[:].rearrange("q (h t2 c) -> q h t2 c", h=16, t2=2,
                                    c=32)
            tv = tsn[:].rearrange("q (h t2 c) -> q h t2 c", h=16, t2=2,
                                  c=32)
            p.step("vector",
                   lambda e, qv=qv, sv=sv, tv=tv: e.tensor_mul(
                       tv[:, :, 0, :], qv[:, :, 1, :], sv[:, :, 0, :]),
                   reads=[rqb, rsbuf_], writes=[rtsn],
                   tag=f"rs0{phase}.{t}")
            p.step("vector",
                   lambda e, qv=qv, sv=sv, tv=tv: e.tensor_mul(
                       tv[:, :, 1, :], qv[:, :, 0, :], sv[:, :, 1, :]),
                   reads=[rqb, rsbuf_], writes=[rtsn],
                   tag=f"rs1{phase}.{t}")
            p.step("vector",
                   lambda e, tc=tc, tsn=tsn, rp=rp: e.tensor_add(
                       rp[:], tc[:], tsn[:]),
                   reads=[rtc, rtsn], writes=[rrp], tag=f"ra{phase}.{t}")
            for c in range(8):
                def tr2(c=c, rp=rp, ptr=ptr):
                    return lambda e: e.transpose(
                        ptr[:, c * 128:(c + 1) * 128],
                        rp[:, c * 128:(c + 1) * 128], s_ident[:])
                p.step("tensor", tr2(), reads=[rrp, r_ident],
                       writes=[rptr], tag=f"trq{phase}.{t}.{c}")
            dstT, rdstT = (s_qt, r_qt) if phase == 0 else (s_kt, r_kt)
            for half in range(2):
                dv = dstT[:].rearrange("d (c q) -> d c q", c=8, q=S)[
                    :, half * 4:(half + 1) * 4, t * 128:(t + 1) * 128]
                def cp(half=half, dv=dv, ptr=ptr):
                    return lambda e: e.activation(
                        dv, ptr[:, half * 512:(half + 1) * 512],
                        AF.Identity)
                p.step("scalar", cp(), reads=[rptr],
                       writes=[rdstT], tag=f"qtc{phase}.{t}.{half}")

        for t in range(NT):
            a_stage1(t)
            if t >= 1:
                a_stage2(t - 1)
            wmean_pump(2)
        a_stage2(NT - 1)
    wmean_pump(100)

    p.step("tensor",
           lambda e: e.matmul(p_mix[0:32, 0:128], s_nrm[:], s_identf[:],
                              start=True, stop=True),
           reads=[r_nrm, r_identf], writes=[r_pmix], tag="nrmT")
    s_hn, r_hn = sb("a_hn", [32, 1], F32, stack=stA)
    p.step("vector",
           lambda e: e.tensor_reduce(s_hn[:], p_mix[0:32, 0:128], AX.X,
                                     ALU.max),
           reads=[r_pmix], writes=[r_hn], tag="nrmmax", sync_out=True)
    dma(scr_vec[0:1, 0:32].rearrange("a (q b) -> q (a b)", q=32, b=1),
        s_hn[:], reads=[r_hn], writes=[r_svec], tag="bnc0")
    dma(s_hrow[:], scr_vec[0:1, 0:32], reads=[r_svec], writes=[r_hrow],
        tag="bnc1")
    p.step("vector",
           lambda e: e.tensor_add(s_hrow[0:1, 0:16], s_hrow[0:1, 0:16],
                                  s_hrow[0:1, 16:32]),
           reads=[r_hrow], writes=[r_hrow], tag="bsum")
    p.step("vector",
           lambda e: e.tensor_scalar(s_hrow[0:1, 0:16], s_hrow[0:1, 0:16],
                                     -1.0 / 16.0, None, ALU.mult),
           reads=[r_hrow], writes=[r_hrow], tag="bneg",
           flush=lambda e: e.tensor_copy(s_vfl[0:1, 0:8], s_hrow[0:1, 0:8]))
    p.step("tensor",
           lambda e: e.matmul(p_mix[:, 0:16], s_onesf[:], s_hrow[0:1, 0:16],
                              start=True, stop=True),
           reads=[r_hrow, r_onesf], writes=[r_pmix], tag="bnegT")
    p.step("vector",
           lambda e: e.tensor_copy(s_bneg[:], p_mix[:, 0:16]),
           reads=[r_pmix], writes=[r_bneg], tag="bnegC",
           flush=lambda e: e.tensor_copy(s_vfl[:, 0:1], s_bneg[:, 0:1]))

    p.step("tensor",
           lambda e: e.matmul(p_mix[0:3, 0:1], s_wacc[:, 0:3],
                              s_onescol[:], start=True, stop=True),
           reads=[r_wacc, r_onescol], writes=[r_pmix], tag="wsumT")
    p.step("vector",
           lambda e: e.tensor_copy(s_w31[:], p_mix[0:3, 0:1]),
           reads=[r_pmix], writes=[r_w31], tag="wsumC")
    p.step("tensor",
           lambda e: e.matmul(p_mix[0:1, 4:7], s_w31[:],
                              s_identf[0:3, 0:3], start=True, stop=True),
           reads=[r_w31, r_identf], writes=[r_pmix], tag="wrowT")
    p.step("vector",
           lambda e: e.tensor_copy(s_wsum[0:1, 0:3], p_mix[0:1, 4:7]),
           reads=[r_pmix], writes=[r_wsum], tag="wrowC")
    p.step("vector",
           lambda e: e.tensor_scalar(s_wrow[0:1, 0:3], s_wsum[0:1, 0:3],
                                     1.0 / (INNER * DM), 1e-5, ALU.mult,
                                     ALU.max),
           reads=[r_wsum], writes=[r_wrow], tag="wmean", sync_out=True)
    p.step("vector",
           lambda e: e.reciprocal(s_wrow[0:1, 3:6], s_wrow[0:1, 0:3]),
           reads=[r_wrow], writes=[r_wrow], tag="wrec", sync_out=True)
    p.step("vector",
           lambda e: e.reciprocal(s_wrow[0:1, 6:8], s_wrow[0:1, 3:5]),
           reads=[r_wrow], writes=[r_wrow], tag="wrec2", sync_out=True)
    p.step("vector",
           lambda e: e.reciprocal(s_wrow[0:1, 0:1], s_wrow[0:1, 5:6]),
           reads=[r_wrow], writes=[r_wrow], tag="wrec3", sync_out=True,
           flush=lambda e: e.tensor_copy(s_vfl[0:1, 0:8], s_wrow[0:1, 0:8]))
    p.step("tensor",
           lambda e: e.matmul(p_mix[:, 0:8], s_onesf[:], s_wrow[0:1, 0:8],
                              start=True, stop=True),
           reads=[r_wrow, r_onesf], writes=[r_pmix], tag="wcolT")
    p.step("vector",
           lambda e: e.tensor_copy(s_wcol[:], p_mix[:, 0:8]),
           reads=[r_pmix], writes=[r_wcol], tag="wcolC",
           flush=lambda e: e.tensor_copy(s_vfl[:, 0:1], s_wcol[:, 0:1]))

    if stage == "A":
        d_qt = dbg_out("d_qt", [128, 8 * S])
        d_kt = dbg_out("d_kt", [128, 8 * S])
        d_v = dbg_out("d_v", [128, NT * DM])
        d_bneg = dbg_out("d_bneg", [128, 16])
        d_wrow = dbg_out("d_wrow", [1, 8])
        d_mu = dbg_out("d_mu", [128, 16])
        d_rs = dbg_out("d_rs", [128, 16])
        d_wph = dbg_out("d_wph", [128, 8 * 1024])
        d_qb = dbg_out("d_qb", [128, 1024])
        d_xnt = dbg_out("d_xnt", [128, 1024])
        d_rope = dbg_out("d_rope", [128, 1024])
        d_pbias = dbg_out("d_pbias", [128, 1024])
        toks = [
            dma(d_qt, s_qt[:], reads=[r_qt], eng="gpsimd", tag="dbgq"),
            dma(d_kt, s_kt[:], reads=[r_kt], eng="gpsimd", tag="dbgk"),
            dma(d_v, s_v[:], reads=[r_v], eng="gpsimd", tag="dbgv"),
            dma(d_bneg, s_bneg[:], reads=[r_bneg], tag="dbgb"),
            dma(d_wrow, s_wrow[:], reads=[r_wrow], tag="dbgw"),
            dma(d_mu, s_mu1[:], reads=[r_mu1], tag="dbgmu"),
            dma(d_rs, s_rs1[:], reads=[r_rs1], tag="dbgrs"),
            dma(d_wph, s_wph[:], reads=[r_wph], eng="gpsimd", tag="dbgwp"),
            dma(d_qb, s_qb[1][0][:], reads=[s_qb[1][1]], tag="dbgqb"),
            dma(d_xnt, s_xnt[1][0][:], reads=xnt_cres[1], eng="gpsimd", tag="dbgxnt"),
            dma(d_rope, s_rope[1][0][:], reads=[s_rope[1][1]], eng="gpsimd", tag="dbgro"),
            dma(d_pbias, s_pbias[:], reads=[r_pbias], tag="dbgpb"),
        ]
        p.step("sync", lambda e: e.nop(), extra_deps=toks, tag="fin")
        stA.close(); pro.close()
        p.emit(nc, top)
        return nc, dbg, top

    stA.close()
    p.add_fence([r_pmix])
    pro.close()
    p.add_fence([r_wph, r_pbias, p_m[0][1], p_m[1][1],
                 r_asum, r_asq]
                + ptr_cres[0] + ptr_cres[1] + xnt_cres[0] + xnt_cres[1]
                + [b[1] for b in (s_cost + s_sint + s_wst + s_xn + s_xnt
                                  + s_a0b
                                  + s_qb + s_tcos + s_tsin + s_rope
                                  + s_sqbf + s_nrmt)])

    stB = ExitStack()
    s_pta, r_pta = sb("b_pta", [128, 16 * 512], BF16, stack=stB)
    s_ptb, r_ptb = sb("b_ptb", [128, 16 * 512], BF16, stack=stB)
    s_attn, r_attn = sb("b_attn", [128, 8 * 512], BF16, stack=stB)
    s_wob, r_wob = sb("b_wob", [128, 8 * 1024], BF16, stack=stB)
    s_bob, r_bob = sb("b_bob", [128, DM], F32, stack=stB)
    s_r16, r_r16 = sb("b_r16", [16, 512], F32, stack=stB)
    s_sel, r_sel = sb("b_sel", [16, 8 * 128], F32, stack=stB)
    s_t1, r_t1 = sb("b_t1", [128, DM], F32, stack=stB)
    s_xp = [sb(f"b_xp{i}", [128, DM], F32, stack=stB) for i in range(2)]

    dma(s_wob[:].rearrange("p (c n) -> p c n", c=8),
        wo_in.rearrange("(c q) n -> q c n", q=128),
        writes=[r_wob], eng="gpsimd", tag="ldwo")
    dma(s_sel[:], sel_in, writes=[r_sel], tag="ldsel")
    bcast2 = ExitStack()
    p_bc2, r_pbc2 = ps("p_bc2", [128, 512], F32, stack=bcast2)
    dma(s_row1[0:1, 0:DM], bo_in.rearrange("(a n) -> a n", a=1),
        writes=[r_row1], tag="ldbo")
    for j in range(2):
        def mk(j=j):
            return lambda e: e.matmul(p_bc2[:, 0:512], s_onesf[:],
                                      s_row1[0:1, j * 512:(j + 1) * 512],
                                      start=True, stop=True)
        p.step("tensor", mk(), reads=[r_row1, r_onesf], writes=[r_pbc2])
        def cp(j=j):
            return lambda e: e.tensor_copy(s_bob[:, j * 512:(j + 1) * 512],
                                           p_bc2[:, 0:512])
        p.step("vector", cp(), reads=[r_pbc2], writes=[r_bob])
    bcast2.close()
    p.add_fence([r_pbc2])

    p_stab = [ps(f"p_st{sd}", [128, 1536], F32, stack=stB) for sd in range(2)]
    p_o, r_po = ps("p_o", [128, 512], F32, stack=stB)
    p_rs, r_prs = ps("p_rs", [128, 512], F32, stack=stB)

    PT = [(s_pta, r_pta), (s_ptb, r_ptb)]

    r_ptk = [[p.res() for _ in range(16)], [p.res() for _ in range(16)]]

    for qc in range(4):
        nkt = 4 * qc + 4
        first_rs = [True]
        kts = list(range(nkt))
        groups = [kts[i:i + 3] for i in range(0, nkt, 3)]

        def emit_st_group(hp, g, qc=qc):
            for side in range(2):
                head = 2 * hp + side
                pst, rpst = p_stab[side]
                for j, kt in enumerate(g):
                    def mmst(j=j, kt=kt, side=side, hp=hp, qc=qc, pst=pst):
                        return lambda e: e.matmul(
                            pst[:, j * 512:(j + 1) * 512],
                            s_kt[side * 64:(side + 1) * 64,
                                 hp * 2048 + kt * 128:
                                 hp * 2048 + (kt + 1) * 128],
                            s_qt[side * 64:(side + 1) * 64,
                                 hp * 2048 + qc * 512:
                                 hp * 2048 + (qc + 1) * 512],
                            start=True, stop=True,
                            tile_position=(side * 64, 0),
                            skip_group_check=True)
                    p.step("tensor", mmst(), reads=[r_kt, r_qt],
                           writes=[rpst], tag=f"st{qc}.{hp}.{side}.{kt}")
                ptb, _ = PT[side]
                def expf(g=g, side=side, head=head, pst=pst, ptb=ptb):
                    ln = len(g)
                    return lambda e: e.activation(
                        ptb[:, g[0] * 512:(g[0] + ln) * 512],
                        pst[:, 0:ln * 512], AF.Exp, scale=0.125,
                        bias=s_bneg[:, head:head + 1])
                p.step("scalar", expf(), reads=[rpst, r_bneg],
                       writes=[r_ptk[side][kt2] for kt2 in g],
                       tag=f"exp{qc}.{hp}.{side}")
            for kt in g:
                if kt < 4 * qc:
                    continue
                dcol = kt * 128 - qc * 512
                for side in range(2):
                    ptb, _ = PT[side]
                    if dcol > 0:
                        def msf(kt=kt, dcol=dcol, ptb=ptb):
                            return lambda e: e.memset(
                                ptb[:, kt * 512:kt * 512 + dcol], 0.0)
                        p.step("vector", msf(), writes=[r_ptk[side][kt]],
                               tag=f"ms{qc}.{hp}.{side}.{kt}")
                    def trif(kt=kt, dcol=dcol, ptb=ptb):
                        return lambda e: e.tensor_mul(
                            ptb[:, kt * 512 + dcol:kt * 512 + dcol + 128],
                            ptb[:, kt * 512 + dcol:kt * 512 + dcol + 128],
                            s_tri[:])
                    p.step("vector", trif(), reads=[r_tri],
                           writes=[r_ptk[side][kt]],
                           tag=f"tri{qc}.{hp}.{side}.{kt}")

        def emit_pv_group(hp, g, qc=qc):
            for kt in g:
                for side in range(2):
                    head = 2 * hp + side
                    ptb, _ = PT[side]
                    def pvf(kt=kt, side=side, head=head, ptb=ptb):
                        return lambda e: e.matmul(
                            p_o[side * 64:(side + 1) * 64, :],
                            s_v[:, kt * 1024 + head * 64:
                                kt * 1024 + (head + 1) * 64],
                            ptb[:, kt * 512:(kt + 1) * 512],
                            start=(kt == 0), stop=False,
                            tile_position=(0, side * 64),
                            skip_group_check=True)
                    p.step("tensor", pvf(), reads=[r_ptk[side][kt], r_v],
                           writes=[r_po], tag=f"pv{qc}.{hp}.{side}.{kt}")
                    def rsf(kt=kt, side=side, head=head, ptb=ptb,
                            st=first_rs[0]):
                        return lambda e: e.matmul(
                            p_rs[0:16, :],
                            s_ecat[:, head * 16:(head + 1) * 16],
                            ptb[:, kt * 512:(kt + 1) * 512],
                            start=st, stop=False,
                            skip_group_check=True)
                    p.step("tensor", rsf(), reads=[r_ptk[side][kt], r_ecat],
                           writes=[r_prs], tag=f"rs{qc}.{hp}.{side}.{kt}")
                    first_rs[0] = False
            if g is groups[-1]:
                def evo(hp=hp):
                    return lambda e: e.tensor_copy(
                        s_attn[:, hp * 512:(hp + 1) * 512], p_o[:])
                p.step("vector", evo(), reads=[r_po], writes=[r_attn],
                       tag=f"aev{qc}.{hp}")

        jobs = [(hp, g) for hp in range(8) for g in groups]
        for i, (hp, g) in enumerate(jobs):
            emit_st_group(hp, g)
            if i >= 1:
                emit_pv_group(*jobs[i - 1])
        emit_pv_group(*jobs[-1])

        p.step("vector",
               lambda e: e.reciprocal(s_r16[:], p_rs[0:16, :]),
               reads=[r_prs], writes=[r_r16], tag=f"recip{qc}")
        for hp in range(8):
            def rb0(hp=hp):
                return lambda e: e.matmul(
                    p_o[:], s_sel[:, hp * 128:(hp + 1) * 128], s_r16[:],
                    start=True, stop=True, skip_group_check=True)
            p.step("tensor", rb0(), reads=[r_r16, r_sel], writes=[r_po],
                   tag=f"rb{qc}.{hp}")
            def divf(hp=hp):
                return lambda e: e.tensor_mul(
                    s_attn[:, hp * 512:(hp + 1) * 512],
                    s_attn[:, hp * 512:(hp + 1) * 512], p_o[:])
            p.step("vector", divf(), reads=[r_po, r_attn], writes=[r_attn],
                   tag=f"div{qc}.{hp}")
        p_wo = p_stab[0][0]
        r_pwo = p_stab[0][1]
        for qsub in range(4):
            t = qc * 4 + qsub
            xbuf, rxbuf = s_xt[t % 2]
            dma(xbuf[:], x_in[t * 128:(t + 1) * 128, :], writes=[rxbuf],
                tag=f"xb{t}")
            for dmc in range(8):
                for h2 in range(2):
                    def mwo(dmc=dmc, h2=h2, qsub=qsub):
                        return lambda e: e.matmul(
                            p_wo[:, h2 * 512:(h2 + 1) * 512],
                            s_attn[:, dmc * 512 + qsub * 128:
                                   dmc * 512 + (qsub + 1) * 128],
                            s_wob[:, dmc * 1024 + h2 * 512:
                                  dmc * 1024 + (h2 + 1) * 512],
                            start=(dmc == 0), stop=(dmc == 7),
                            skip_group_check=True)
                    p.step("tensor", mwo(), reads=[r_attn, r_wob],
                           writes=[r_pwo], tag=f"wo{t}.{dmc}.{h2}")
            p.step("vector",
                   lambda e: e.tensor_add(s_t1[:], p_wo[:, 0:1024],
                                          s_bob[:]),
                   reads=[r_pwo, r_bob], writes=[r_t1], tag=f"woev{t}")
            xpb, rxpb = s_xp[t % 2]
            p.step("vector",
                   lambda e, xb=xbuf, xp=xpb: e.tensor_add(xp[:], s_t1[:],
                                                           xb[:]),
                   reads=[r_t1, rxbuf], writes=[rxpb], tag=f"res1.{t}")
            dma(scr_x1[t * 128:(t + 1) * 128, :], xpb[:], reads=[rxpb],
                writes=[r_sx1[t]], tag=f"sx1.{t}")
            dy, rdy = s_dummy[t % 2]
            p.step("scalar",
                   lambda e, xp=xpb, dy=dy: e.activation(dy[:], xp[:],
                                                         AF.Square),
                   reads=[rxpb], writes=[rdy], tag=f"bsq{t}")
            p.step("vector",
                   lambda e, dy=dy, t=t: e.tensor_reduce(
                       s_c1sq[:, t:t + 1], dy[:], AX.X, ALU.add),
                   reads=[rdy], writes=[r_c1sq], tag=f"bqr{t}")
            p.step("vector",
                   lambda e, xp=xpb, t=t: e.tensor_reduce(
                       s_c1sum[:, t:t + 1], xp[:], AX.X, ALU.add),
                   reads=[rxpb], writes=[r_c1sum], tag=f"bsr{t}")

    if stage == "B":
        d_x1 = dbg_out("d_x1", [S, DM])
        toks = [dma(d_x1, scr_x1, reads=r_sx1[:], tag="dbgx1")]
        p.step("sync", lambda e: e.nop(), extra_deps=toks, tag="fin")
        stB.close()
        p.emit(nc, top)
        return nc, dbg, top

    stB.close()
    p.add_fence([r_pta, r_ptb, r_attn, r_wob, r_bob, r_r16, r_sel, r_t1,
                 r_po, r_prs, p_stab[0][1], p_stab[1][1],
                 s_xp[0][1], s_xp[1][1]])
    stQKV.close()
    p.add_fence([r_qt, r_kt, r_v])

    stC = ExitStack()
    s_w1b, r_w1b = sb("c_w1b", [128, 8 * INNER], BF16, stack=stC)
    s_w2b, r_w2b = sb("c_w2b", [128, 32 * DM], BF16, stack=stC)
    s_mu2, r_mu2 = sb("c_mu2", [128, 16], F32, stack=stC)
    s_rs2, r_rs2 = sb("c_rs2", [128, 16], F32, stack=stC)
    s_c1x = [sb(f"c_x{i}", [128, DM], F32, stack=stC) for i in range(4)]
    _xn2 = sb("c_xn2", [128, DM], BF16, stack=stC)
    s_xn2 = [_xn2, _xn2]
    s_xn2t = [sb(f"c_xn2t{i}", [128, 8 * 256], BF16, stack=stC)
              for i in range(2)]
    s_h1c = [sb(f"c_h1c{i}", [128, 256], BF16, stack=stC) for i in range(3)]
    _xpp = sb("c_xpp", [128, DM], F32, stack=stC)
    s_xpp = [_xpp, _xpp]
    s_wq1, r_wq1 = sb("c_wq1", [128, 1024], F32, stack=stC)
    s_wqbf, r_wqbf = sb("c_wqbf", [128, 1024], BF16, stack=stC)
    s_wtx, r_wtx = sb("c_wtx", [128, 1024], BF16, stack=stC)
    s_wstc = [sb(f"c_wst{i}", [128, 1024], F32, stack=stC) for i in range(2)]
    p_h = [ps(f"p_h{i}", [128, 512], F32, stack=stC) for i in range(3)]
    p_mlp = [ps(f"p_mlp{i}", [128, 1024], F32, stack=stC) for i in range(2)]
    p_trc, r_ptrc = ps("p_trc", [128, 1024], BF16, stack=stC)

    r_w1c = [p.res() for _ in range(8)]
    for c in range(8):
        dma(s_w1b[:, c * INNER:(c + 1) * INNER],
            w1_in[c * 128:(c + 1) * 128, :],
            writes=[r_w1c[c], r_w1b], eng="gpsimd", tag=f"ldw1.{c}")
    r_w2c = [p.res() for _ in range(32)]
    for c in range(32):
        dma(s_w2b[:, c * DM:(c + 1) * DM],
            w2_in[c * 128:(c + 1) * 128, :],
            writes=[r_w2c[c], r_w2b], eng="gpsimd", tag=f"ldw2.{c}")

    wq2_jobs = []
    for widx, (w_ap, scr, rscr) in enumerate(
            ((wg_in, scr_wgt, r_swg), (wu_in, scr_wut, r_swu),
             (wd_in, scr_wdt, r_swd))):
        for ch in range(32):
            wq2_jobs.append((widx, w_ap, scr, rscr, ch))
    wq2_pos = [0]

    def wq2_pump(n):
        for _ in range(n):
            if wq2_pos[0] >= len(wq2_jobs):
                return
            widx, w_ap, scr, rscr, ch = wq2_jobs[wq2_pos[0]]
            wq2_pos[0] += 1
            buf, rbuf = s_wstc[wq2_pos[0] % 2]
            scol = 3 + widx if widx < 2 else 5
            if widx < 2:
                src = w_ap[ch * 128:(ch + 1) * 128, :]
                nin = 1024
            else:
                src = w_ap[(ch % 8) * 128:((ch % 8) + 1) * 128,
                           (ch // 8) * 1024:((ch // 8) + 1) * 1024]
                nin = 1024
            dma(buf[:], src, writes=[rbuf], tag=f"w2l{widx}.{ch}")
            def q1(buf=buf, scol=scol):
                return lambda e: e.tensor_scalar(
                    s_wq1[:], buf[:], s_wcol[:, scol:scol + 1], None,
                    ALU.mult)
            p.step("vector", q1(), reads=[rbuf, r_wcol], writes=[r_wq1],
                   tag=f"w2q1.{widx}.{ch}")
            p.step("vector",
                   lambda e: e.tensor_scalar(s_wq1[:], s_wq1[:], MAGIC,
                                             None, ALU.add),
                   reads=[r_wq1], writes=[r_wq1], tag=f"w2q1b.{widx}.{ch}")
            p.step("vector",
                   lambda e: e.tensor_scalar(s_wq1[:], s_wq1[:], MAGIC - 1.0,
                                             MAGIC + 1.0, ALU.max, ALU.min),
                   reads=[r_wq1], writes=[r_wq1], tag=f"w2q2.{widx}.{ch}")
            p.step("vector",
                   lambda e: e.tensor_scalar(s_wqbf[:], s_wq1[:], MAGIC,
                                             None, ALU.subtract),
                   reads=[r_wq1], writes=[r_wqbf], tag=f"w2q3.{widx}.{ch}")
            for c in range(8):
                def tr(c=c):
                    return lambda e: e.transpose(
                        p_trc[:, c * 128:(c + 1) * 128],
                        s_wqbf[:, c * 128:(c + 1) * 128], s_ident[:])
                p.step("tensor", tr(), reads=[r_wqbf, r_ident],
                       writes=[r_ptrc], tag=f"w2t.{widx}.{ch}.{c}")
            for half in range(2):
                def cp(half=half):
                    return lambda e: e.tensor_copy(
                        s_wtx[:, half * 512:(half + 1) * 512],
                        p_trc[:, half * 512:(half + 1) * 512])
                p.step("vector", cp(), reads=[r_ptrc], writes=[r_wtx],
                       tag=f"w2c.{widx}.{ch}.{half}")
            if widx < 2:
                dst = scr[:, ch * 128:(ch + 1) * 128].rearrange(
                    "(c q) o -> q c o", q=128)
            else:
                dst = scr[(ch // 8) * 1024:(ch // 8 + 1) * 1024,
                          (ch % 8) * 128:((ch % 8) + 1) * 128].rearrange(
                    "(c q) o -> q c o", q=128)
            tok = dma(dst, s_wtx[:].rearrange("p (c o) -> p c o", c=8),
                      reads=[r_wtx], writes=[rscr],
                      tag=f"w2s{widx}.{ch}")
            wq2_store_toks[widx].append(tok)

    batch_stats(s_c1sum, r_c1sum, s_c1sq, r_c1sq, s_mu2, r_mu2, s_rs2,
                r_rs2, "c1")

    xtiles_by_qg = {}

    def c2_prep(qg):
        xnt2, rxnt2 = s_xn2t[qg % 2]
        xtiles = []
        for qs in range(2):
            t = qg * 2 + qs
            xbuf, rxbuf = s_c1x[(qg % 2) * 2 + qs]
            dma(xbuf[:], scr_x1[t * 128:(t + 1) * 128, :], reads=[r_sx1[t]],
                writes=[rxbuf], tag=f"c2x{t}")
            xtiles.append((xbuf, rxbuf))
            xn2, rxn2 = s_xn2[qs]
            p.step("vector",
                   lambda e, xb=xbuf, t=t, xn2=xn2: e.tensor_scalar(
                       xn2[:], xb[:], s_mu2[:, t:t + 1],
                       s_rs2[:, t:t + 1], ALU.subtract, ALU.mult),
                   reads=[rxbuf, r_mu2, r_rs2], writes=[rxn2],
                   tag=f"c2n{t}")
            for c in range(8):
                def tr(c=c, xn2=xn2):
                    return lambda e: e.transpose(
                        p_trc[:, c * 128:(c + 1) * 128],
                        xn2[:, c * 128:(c + 1) * 128], s_ident[:])
                p.step("tensor", tr(), reads=[rxn2, r_ident],
                       writes=[r_ptrc], tag=f"c2t{t}.{c}")
            for c in range(8):
                def ev(c=c, qs=qs, xnt2=xnt2):
                    return lambda e: e.activation(
                        xnt2[:, c * 256 + qs * 128:c * 256 + qs * 128 + 128],
                        p_trc[:, c * 128:(c + 1) * 128], AF.Identity,
                        scale=s_gb2[:, c:c + 1], bias=s_gb2[:, 8 + c:9 + c])
                p.step("scalar", ev(), reads=[r_ptrc, r_gb2],
                       writes=[rxnt2], tag=f"c2e{t}.{c}")
        xtiles_by_qg[qg] = xtiles

    def c2_w1(qg, ic):
        xnt2, rxnt2 = s_xn2t[qg % 2]
        ph, rph = p_h[ic % 3]
        h1c, rh1c = s_h1c[ic % 3]
        for dinc in range(8):
            def m1(dinc=dinc, ic=ic, ph=ph, xnt2=xnt2):
                return lambda e: e.matmul(
                    ph[:, 0:256],
                    s_w1b[:, dinc * INNER + ic * 128:
                          dinc * INNER + (ic + 1) * 128],
                    xnt2[:, dinc * 256:(dinc + 1) * 256],
                    start=(dinc == 0), stop=(dinc == 7),
                    skip_group_check=True)
            p.step("tensor", m1(), reads=[r_w1c[dinc], rxnt2],
                   writes=[rph], tag=f"w1.{qg}.{ic}.{dinc}")
        def gel(ic=ic, ph=ph, h1c=h1c):
            return lambda e: e.activation(h1c[:], ph[:, 0:256], AF.Gelu,
                                          bias=s_b1c[:, ic:ic + 1])
        p.step("scalar", gel(), reads=[rph, r_b1c], writes=[rh1c],
               tag=f"gel{qg}.{ic}")

    def c2_w2(qg, ic):
        h1c, rh1c = s_h1c[ic % 3]
        for qs in range(2):
            pm, rpm = p_mlp[qs]
            for h2 in range(2):
                def m2(ic=ic, qs=qs, h2=h2, pm=pm, h1c=h1c):
                    return lambda e: e.matmul(
                        pm[:, h2 * 512:(h2 + 1) * 512],
                        h1c[:, qs * 128:(qs + 1) * 128],
                        s_w2b[:, ic * DM + h2 * 512:
                              ic * DM + (h2 + 1) * 512],
                        start=(ic == 0), stop=(ic == 31),
                        skip_group_check=True)
                p.step("tensor", m2(), reads=[rh1c, r_w2c[ic]],
                       writes=[rpm], tag=f"w2m{qg}.{ic}.{qs}.{h2}")

    def c2_evac(qg):
        for qs in range(2):
            t = qg * 2 + qs
            pm, rpm = p_mlp[qs]
            xb, rxb = xtiles_by_qg[qg][qs]
            xpp, rxpp = s_xpp[qs]
            x0b, rx0b = s_xt[qs]
            dma(x0b[:], x_in[t * 128:(t + 1) * 128, :], writes=[rx0b],
                tag=f"c2x0{t}")
            p.step("vector",
                   lambda e, pm=pm, xpp=xpp: e.tensor_add(
                       xpp[:], pm[:], s_b2b[:]),
                   reads=[rpm, r_b2b], writes=[rxpp], tag=f"c2b{t}")
            p.step("vector",
                   lambda e, xpp=xpp, xb=xb: e.tensor_add(xpp[:], xpp[:],
                                                          xb[:]),
                   reads=[rxb, rxpp], writes=[rxpp], tag=f"c2r{t}")
            p.step("vector",
                   lambda e, xpp=xpp, x0b=x0b: e.tensor_add(xpp[:], xpp[:],
                                                            x0b[:]),
                   reads=[rx0b, rxpp], writes=[rxpp], tag=f"c2r2{t}")
            dma(scr_x1[t * 128:(t + 1) * 128, :], xpp[:], reads=[rxpp],
                writes=[r_sx1[t]], tag=f"c2s{t}")

    NIC = INNER // 128
    c2_prep(0)
    jobs = [(qg, ic) for qg in range(8) for ic in range(NIC)]
    DEPTH = 2
    for i, (qg, ic) in enumerate(jobs):
        if i >= DEPTH:
            pqg, pic = jobs[i - DEPTH]
            c2_w2(pqg, pic)
            if pic == NIC - 1:
                c2_evac(pqg)
        if ic == NIC - 1 and qg < 7:
            c2_prep(qg + 1)
        c2_w1(qg, ic)
        wq2_pump(1)
    for i in range(len(jobs) - DEPTH, len(jobs)):
        pqg, pic = jobs[i]
        c2_w2(pqg, pic)
        if pic == NIC - 1:
            c2_evac(pqg)
    wq2_pump(200)


    stC.close()
    p.add_fence([r_w1b, r_w2b, r_wq1, r_wqbf, r_wtx, _xn2[1],
                 p_h[0][1], p_h[1][1], p_h[2][1], p_mlp[0][1], p_mlp[1][1], r_ptrc]
                + [b[1] for b in (s_c1x + s_xpp + s_wstc + s_xn2t
                                  + s_h1c)])

    stD = ExitStack()
    s_wgtb, r_wgtb = sb("d_wgtb", [128, 8 * INNER], BF16, stack=stD)
    s_wutb, r_wutb = sb("d_wutb", [128, 8 * INNER], BF16, stack=stD)
    s_x2t = [sb(f"d_x2t{i}", [128, DM], F32, stack=stD) for i in range(2)]
    _q1 = sb("d_q1", [128, DM], F32, stack=stD)
    s_q1 = [_q1, _q1]
    s_xq = [sb(f"d_xq{i}", [128, DM], BF16, stack=stD) for i in range(2)]
    s_xqt = [sb(f"d_xqt{i}", [128, DM], BF16, stack=stD) for i in range(3)]
    s_sg, r_sg = sb("d_sg", [128, 512], F32, stack=stD)
    s_h, r_h = sb("d_h", [128, INNER], F32, stack=stD)
    _q2 = sb("d_q2", [128, 1024], F32, stack=stD)
    s_q2 = [_q2, _q1]
    s_xq2 = [sb(f"d_xq2{i}", [128, 1024], BF16, stack=stD) for i in range(2)]
    s_hm, r_hm = sb("d_hm", [128, 16], F32, stack=stD)
    p_g = [ps(f"p_g{i}", [128, 512], F32, stack=stD) for i in range(3)]
    p_u = [ps(f"p_u{i}", [128, 512], F32, stack=stD) for i in range(3)]
    p_trd = [ps(f"p_trd{i}", [128, 1024], BF16, stack=stD) for i in range(2)]

    r_wgtc = [p.res() for _ in range(8)]
    r_wutc = [p.res() for _ in range(8)]

    def ldw_chunk(c):
        dma(s_wgtb[:, c * INNER:(c + 1) * INNER],
            scr_wgt[c * 128:(c + 1) * 128, :],
            reads=[r_swg], writes=[r_wgtc[c]], tag=f"ldwgt{c}",
            extra_deps=wq2_store_toks[0])
        dma(s_wutb[:, c * INNER:(c + 1) * INNER],
            scr_wut[c * 128:(c + 1) * 128, :],
            reads=[r_swu], writes=[r_wutc[c]], tag=f"ldwut{c}",
            extra_deps=wq2_store_toks[1])




    for t in range(NT):
        xbuf, rxbuf = s_x2t[t % 2]
        dy, rdy = s_dummy[t % 2]
        dma(xbuf[:], scr_x1[t * 128:(t + 1) * 128, :], reads=[r_sx1[t]],
            writes=[rxbuf], tag=f"c3x{t}")
        p.step("scalar",
               lambda e, xb=xbuf, dy=dy: e.activation(dy[:], xb[:],
                                                      AF.Square),
               reads=[rxbuf], writes=[rdy], tag=f"c3sq{t}")
        p.step("vector",
               lambda e, dy=dy, t=t: e.tensor_reduce(
                   s_c3sq[:, t:t + 1], dy[:], AX.X, ALU.add),
               reads=[rdy], writes=[r_c3sq], tag=f"c3qr{t}")
        p.step("vector",
               lambda e, xb=xbuf, t=t: e.tensor_reduce(
                   s_c3sum[:, t:t + 1], xb[:], AX.X, ALU.add),
               reads=[rxbuf], writes=[r_c3sum], tag=f"c3sr{t}")
        p.step("vector",
               lambda e, t=t: e.tensor_scalar(
                   s_mu3[:, t:t + 1], s_c3sum[:, t:t + 1], 1.0 / DM, None,
                   ALU.mult),
               reads=[r_c3sum], writes=[r_mu3], tag=f"c3mu{t}",
               sync_out=True)
        p.step("vector",
               lambda e, xb=xbuf, dy=dy, t=t: e.tensor_scalar(
                   dy[:], xb[:], s_mu3[:, t:t + 1], None, ALU.subtract),
               reads=[rxbuf, r_mu3, rdy], writes=[rdy], tag=f"c3d{t}")
        p.step("vector",
               lambda e, dy=dy, t=t: e.tensor_reduce(
                   s_axm[:, t:t + 1], dy[:], AX.X, ALU.max,
                   apply_absolute_value=True),
               reads=[rdy], writes=[r_axm], tag=f"c3am{t}")
        if t < 8:
            ldw_chunk(t)
    p.step("vector",
           lambda e: e.tensor_scalar(s_smal[:, 16:32], s_c3sq[:], 1.0 / DM,
                                     None, ALU.mult),
           reads=[r_c3sq], writes=[r_smal], tag="c3ms")
    p.step("vector",
           lambda e: e.tensor_mul(s_smal[:, 32:48], s_mu3[:], s_mu3[:]),
           reads=[r_mu3, r_smal], writes=[r_smal], tag="c3m2")
    p.step("vector",
           lambda e: e.scalar_tensor_tensor(
               s_smal[:, 16:32], s_smal[:, 16:32], LNEPS,
               s_smal[:, 32:48], ALU.add, ALU.subtract),
           reads=[r_smal], writes=[r_smal], tag="c3vp", sync_out=True)
    p.step("vector",
           lambda e: e.reciprocal(s_smal[:, 32:48], s_smal[:, 16:32]),
           reads=[r_smal], writes=[r_smal], tag="c3rc", sync_out=True)
    p.step("scalar",
           lambda e: e.activation(s_smal[:, 32:48], s_smal[:, 32:48],
                                  AF.Sqrt),
           reads=[r_smal], writes=[r_smal], tag="c3rq", sync_out=True)
    p.step("vector",
           lambda e: e.tensor_mul(s_smal[:, 0:16], s_smal[:, 32:48],
                                  s_smal[:, 32:48]),
           reads=[r_smal], writes=[r_smal], tag="c3n1")
    p.step("vector",
           lambda e: e.tensor_mul(s_smal[:, 0:16], s_smal[:, 0:16],
                                  s_smal[:, 16:32]),
           reads=[r_smal], writes=[r_smal], tag="c3n2")
    p.step("vector",
           lambda e: e.tensor_scalar(s_smal[:, 0:16], s_smal[:, 0:16],
                                     -0.5, 1.5, ALU.mult, ALU.add),
           reads=[r_smal], writes=[r_smal], tag="c3n3")
    p.step("vector",
           lambda e: e.tensor_mul(s_rs3[:], s_smal[:, 32:48],
                                  s_smal[:, 0:16]),
           reads=[r_smal], writes=[r_rs3], tag="c3n4", sync_out=True)
    p.step("vector",
           lambda e: e.tensor_mul(s_axm[:], s_axm[:], s_rs3[:]),
           reads=[r_rs3, r_axm], writes=[r_axm], tag="c3ax")
    p.step("vector",
           lambda e: e.tensor_scalar(s_smal[:, 0:16], s_axm[:], 1e-5, None,
                                     ALU.max),
           reads=[r_axm], writes=[r_smal], tag="dqmc")
    p.step("vector",
           lambda e: e.reciprocal(s_smal[:, 16:32], s_smal[:, 0:16]),
           reads=[r_smal], writes=[r_smal], tag="dqrc", sync_out=True)
    p.step("vector",
           lambda e: e.tensor_scalar(s_dsx[:], s_smal[:, 16:32], 127.0,
                                     None, ALU.mult),
           reads=[r_smal], writes=[r_dsx], tag="dqsx")
    p.step("vector",
           lambda e: e.tensor_scalar(s_smal[:, 32:48], s_smal[:, 0:16],
                                     1.0 / 127.0, None, ALU.mult),
           reads=[r_smal], writes=[r_smal], tag="dqos")
    p.step("vector",
           lambda e: e.tensor_scalar(s_dcg[:], s_smal[:, 32:48],
                                     s_wcol[:, 6:7], None, ALU.mult),
           reads=[r_smal, r_wcol], writes=[r_dcg], tag="dqcg")
    p.step("vector",
           lambda e: e.tensor_scalar(s_dcu[:], s_smal[:, 32:48],
                                     s_wcol[:, 7:8], None, ALU.mult),
           reads=[r_smal, r_wcol], writes=[r_dcu], tag="dqcu")

    rx2c = [[p.res() for _ in range(2)] for _ in range(2)]

    def d2_pre(t):
        xbuf, rxbuf = s_x2t[t % 2]
        for ch in range(2):
            dma(xbuf[:, ch * 512:(ch + 1) * 512],
                scr_x1[t * 128:(t + 1) * 128, ch * 512:(ch + 1) * 512],
                reads=[r_sx1[t]], writes=[rx2c[t % 2][ch], rxbuf],
                tag=f"d2x{t}.{ch}")
        p.step("vector",
               lambda e, xb=xbuf, t=t: e.tensor_scalar(
                   xb[:], xb[:], s_mu3[:, t:t + 1], s_rs3[:, t:t + 1],
                   ALU.subtract, ALU.mult),
               reads=[r_mu3, r_rs3, rx2c[t % 2][0], rx2c[t % 2][1],
                      rxbuf], writes=[rxbuf], tag=f"c3n{t}")
        dma(scr_x2[t * 128:(t + 1) * 128, :], xbuf[:], reads=[rxbuf],
            writes=[r_sx2[t]], tag=f"c3s{t}")

    def d2_stage1a(t):
        xbuf, rxbuf = s_x2t[t % 2]
        q1, rq1 = s_q1[t % 2]
        xq, rxq = s_xq[t % 2]
        p.step("scalar",
               lambda e, xb=xbuf, q1=q1, t=t: e.activation(
                   q1[:], xb[:], AF.Identity, scale=s_dsx[:, t:t + 1]),
               reads=[rxbuf, r_dsx], writes=[rq1], tag=f"d2q1{t}")
        p.step("vector",
               lambda e, q1=q1: e.tensor_scalar(q1[:], q1[:], MAGIC, None,
                                                ALU.add),
               reads=[rq1], writes=[rq1], tag=f"d2q1b{t}")
        p.step("vector",
               lambda e, q1=q1, xq=xq: e.tensor_scalar(
                   xq[:], q1[:], MAGIC, None, ALU.subtract),
               reads=[rq1], writes=[rxq], tag=f"d2q2{t}")

    def d2_stage1b(t):
        pr = t % 3
        xq, rxq = s_xq[t % 2]
        xqt, rxqt = s_xqt[pr]
        ptrd, rptrd = p_trd[t % 2]
        for c in range(8):
            def tr(c=c, xq=xq, ptrd=ptrd):
                return lambda e: e.transpose(ptrd[:, c * 128:(c + 1) * 128],
                                             xq[:, c * 128:(c + 1) * 128],
                                             s_ident[:])
            p.step("tensor", tr(), reads=[rxq, r_ident], writes=[rptrd],
                   tag=f"d2t{t}.{c}")
        for half in range(2):
            def cp(half=half, xqt=xqt, ptrd=ptrd):
                return lambda e: e.activation(
                    xqt[:, half * 512:(half + 1) * 512],
                    ptrd[:, half * 512:(half + 1) * 512], AF.Identity)
            p.step("scalar", cp(), reads=[rptrd], writes=[rxqt],
                   tag=f"d2tc{t}.{half}")

    def d2_stage2(t):
        pr = t % 3
        xqt, rxqt = s_xqt[pr]
        for oc in range(8):
            pg, rpg = p_g[oc % 3]
            pu, rpu = p_u[oc % 3]
            for dinc in range(8):
                def mg(dinc=dinc, oc=oc, pg=pg, xqt=xqt):
                    return lambda e: e.matmul(
                        pg[:],
                        xqt[:, dinc * 128:(dinc + 1) * 128],
                        s_wgtb[:, dinc * INNER + oc * 512:
                               dinc * INNER + (oc + 1) * 512],
                        start=(dinc == 0), stop=(dinc == 7),
                        skip_group_check=True)
                p.step("tensor", mg(), reads=[rxqt, r_wgtc[dinc]],
                       writes=[rpg], tag=f"d2g{t}.{oc}.{dinc}")
                def mu_(dinc=dinc, oc=oc, pu=pu, xqt=xqt):
                    return lambda e: e.matmul(
                        pu[:],
                        xqt[:, dinc * 128:(dinc + 1) * 128],
                        s_wutb[:, dinc * INNER + oc * 512:
                               dinc * INNER + (oc + 1) * 512],
                        start=(dinc == 0), stop=(dinc == 7),
                        skip_group_check=True)
                p.step("tensor", mu_(), reads=[rxqt, r_wutc[dinc]],
                       writes=[rpu], tag=f"d2u{t}.{oc}.{dinc}")
            def sil(pg=pg, t=t):
                return lambda e: e.activation(s_sg[:], pg[:], AF.Silu,
                                              scale=s_dcg[:, t:t + 1])
            p.step("scalar", sil(), reads=[rpg, r_dcg], writes=[r_sg],
                   tag=f"d2s{t}.{oc}")
            def hmul(oc=oc, pu=pu, t=t):
                return lambda e: e.scalar_tensor_tensor(
                    s_h[:, oc * 512:(oc + 1) * 512], pu[:],
                    s_dcu[:, t:t + 1], s_sg[:], ALU.mult, ALU.mult)
            p.step("vector", hmul(), reads=[rpu, r_dcu, r_sg],
                   writes=[r_h], tag=f"d2h{t}.{oc}")
            def hab(oc=oc, t=t):
                return lambda e: e.tensor_reduce(
                    s_smal[:, 8 + (oc % 2):9 + (oc % 2)],
                    s_h[:, oc * 512:(oc + 1) * 512], AX.X, ALU.max,
                    apply_absolute_value=True)
            p.step("vector", hab(), reads=[r_h], writes=[r_smal],
                   tag=f"d2ha{t}.{oc}", sync_out=True)
            def habm(oc=oc, t=t):
                if oc == 0:
                    return lambda e: e.tensor_copy(s_hm[:, t:t + 1],
                                                   s_smal[:, 8:9])
                return lambda e: e.tensor_max(
                    s_hm[:, t:t + 1], s_hm[:, t:t + 1],
                    s_smal[:, 8 + (oc % 2):9 + (oc % 2)])
            p.step("vector", habm(), reads=[r_smal, r_hm], writes=[r_hm],
                   tag=f"d2hm{t}.{oc}")
        p.step("vector",
               lambda e, t=t: e.tensor_scalar(s_smal[:, 1:2],
                                              s_hm[:, t:t + 1], 1e-5, None,
                                              ALU.max),
               reads=[r_hm], writes=[r_smal], tag=f"d2mc2{t}")
        p.step("vector",
               lambda e: e.reciprocal(s_smal[:, 2:3], s_smal[:, 1:2]),
               reads=[r_smal], writes=[r_smal], tag=f"d2rc2{t}",
               sync_out=True)
        p.step("vector",
               lambda e: e.tensor_scalar(s_smal[:, 3:4], s_smal[:, 2:3],
                                         127.0, None, ALU.mult),
               reads=[r_smal], writes=[r_smal], tag=f"d2sx2{t}",
               sync_out=True)
        p.step("vector",
               lambda e: e.tensor_scalar(s_smal[:, 4:5], s_smal[:, 1:2],
                                         1.0 / 127.0, None, ALU.mult),
               reads=[r_smal], writes=[r_smal], tag=f"d2os2{t}")
        p.step("vector",
               lambda e, t=t: e.tensor_mul(s_cbd[:, t:t + 1],
                                           s_smal[:, 4:5], s_wcol[:, 0:1]),
               reads=[r_smal, r_wcol], writes=[r_cbd], tag=f"d2cd{t}")
        for oc in range(4):
            xq2, rxq2 = s_xq2[oc % 2]
            q2, rq2 = s_q2[oc % 2]
            def q1f(oc=oc, q2=q2):
                return lambda e: e.activation(
                    q2[:], s_h[:, oc * 1024:(oc + 1) * 1024], AF.Identity,
                    scale=s_smal[:, 3:4])
            p.step("scalar", q1f(), reads=[r_h, r_smal], writes=[rq2],
                   tag=f"d2p1{t}.{oc}")
            p.step("vector",
                   lambda e, q2=q2: e.tensor_scalar(q2[:], q2[:], MAGIC,
                                                    None, ALU.add),
                   reads=[rq2], writes=[rq2], tag=f"d2p1b{t}.{oc}")
            p.step("vector", lambda e, xq2=xq2, q2=q2: e.tensor_scalar(
                xq2[:], q2[:], MAGIC, None,
                ALU.subtract), reads=[rq2], writes=[rxq2],
                tag=f"d2p2{t}.{oc}")
            dma(scr_xq2[t * 128:(t + 1) * 128,
                        oc * 1024:(oc + 1) * 1024], xq2[:],
                reads=[rxq2], writes=[r_sxq2[t]], tag=f"d2sx{t}.{oc}")

    d2_pre(0)
    d2_stage1a(0)
    d2_stage1b(0)
    d2_pre(1)
    d2_stage1a(1)
    d2_stage1b(1)
    for t in range(NT):
        if t + 2 < NT:
            d2_pre(t + 2)
            d2_stage1a(t + 2)
        d2_stage2(t)
        if t + 2 < NT:
            d2_stage1b(t + 2)

    if stage == "C":
        d_x2 = dbg_out("d_x2", [S, DM])
        d_wgt = dbg_out("d_wgt", [DM, INNER], BF16)
        d_wdt = dbg_out("d_wdt", [INNER, DM], BF16)
        toks = [dma(d_x2, scr_x2, reads=r_sx2[:], tag="dbgx2"),
                dma(d_wgt, scr_wgt, reads=[r_swg], tag="dbgwgt"),
                dma(d_wdt, scr_wdt, reads=[r_swd], tag="dbgwdt")]
        p.step("sync", lambda e: e.nop(), extra_deps=toks, tag="fin")
        p.emit(nc, top)
        return nc, dbg, top


    stD.close()
    p.add_fence(r_wgtc + r_wutc + [r_sg, r_h, r_hm,
                 p_g[0][1], p_g[1][1], p_g[2][1],
                 p_u[0][1], p_u[1][1], p_u[2][1],
                 p_trd[0][1], p_trd[1][1]]
                + [b[1] for b in (s_x2t + s_q1 + s_xq + s_xqt + s_xq2
                                  + [_q2])])

    stE = ExitStack()
    s_wdtb, r_wdtb = sb("e_wdtb", [128, 32 * DM], BF16, stack=stE)
    s_xq2t = [sb(f"e_xq2t{i}", [128, INNER], BF16, stack=stE)
              for i in range(2)]
    s_xq2T = [sb(f"e_xq2T{i}", [128, 32 * 128], BF16, stack=stE)
              for i in range(4)]
    s_ex2 = [sb(f"e_x2{i}", [128, DM], F32, stack=stE) for i in range(4)]
    s_y, r_y = sb("e_y", [128, DM], F32, stack=stE)
    s_o = [sb(f"e_o{i}", [128, DM], F32, stack=stE) for i in range(2)]
    p_d = [ps(f"p_d{i}", [128, 1024], F32, stack=stE) for i in range(2)]
    p_tre = [ps(f"p_tre{i}", [128, 1024], BF16, stack=stE) for i in range(2)]

    r_wdtc = [p.res() for _ in range(32)]

    def ld_wdt_chunks():
        for c in range(32):
            dma(s_wdtb[:, c * DM:(c + 1) * DM],
                scr_wdt[c * 128:(c + 1) * 128, :],
                reads=[r_swd], writes=[r_wdtc[c]], tag=f"ldwdt{c}",
                extra_deps=wq2_store_toks[2])

    out_toks = []
    rq_c = [[p.res() for _ in range(4)] for _ in range(2)]
    rex_c = [[p.res() for _ in range(2)] for _ in range(4)]

    def d3_prep(t):
        qbuf, rqbuf = s_xq2t[t % 2]
        for quad in range(4):
            dma(qbuf[:, quad * 1024:(quad + 1) * 1024],
                scr_xq2[t * 128:(t + 1) * 128,
                        quad * 1024:(quad + 1) * 1024],
                reads=[r_sxq2[t]], writes=[rq_c[t % 2][quad], rqbuf],
                tag=f"d3q{t}.{quad}")
        xqT, rxqT = s_xq2T[t % 4]
        for quad in range(4):
            ptr, rptr = p_tre[quad % 2]
            for c in range(8):
                cc = quad * 8 + c
                def tr(cc=cc, c=c, ptr=ptr, qbuf=qbuf):
                    return lambda e: e.transpose(
                        ptr[:, c * 128:(c + 1) * 128],
                        qbuf[:, cc * 128:(cc + 1) * 128], s_ident[:])
                p.step("tensor", tr(), reads=[rq_c[t % 2][quad], r_ident],
                       writes=[rptr], tag=f"d3t{t}.{cc}")
            for half in range(2):
                def cp(quad=quad, half=half, ptr=ptr, xqT=xqT):
                    return lambda e: e.tensor_copy(
                        xqT[:, quad * 1024 + half * 512:
                            quad * 1024 + (half + 1) * 512],
                        ptr[:, half * 512:(half + 1) * 512])
                p.step("vector", cp(), reads=[rptr], writes=[rxqT],
                       tag=f"d3c{t}.{quad}.{half}")
        xbuf, rxbuf = s_ex2[t % 4]
        for ch in range(2):
            dma(xbuf[:, ch * 512:(ch + 1) * 512],
                scr_x2[t * 128:(t + 1) * 128, ch * 512:(ch + 1) * 512],
                reads=[r_sx2[t]], writes=[rex_c[t % 4][ch], rxbuf],
                tag=f"d3x{t}.{ch}")

    def d3_main(t):
        xqT, rxqT = s_xq2T[t % 4]
        xbuf, rxbuf = s_ex2[t % 4]
        pd, rpd = p_d[t % 2]
        for i2c in range(32):
            for h2 in range(2):
                def md(i2c=i2c, h2=h2, pd=pd, xqT=xqT):
                    return lambda e: e.matmul(
                        pd[:, h2 * 512:(h2 + 1) * 512],
                        xqT[:, i2c * 128:(i2c + 1) * 128],
                        s_wdtb[:, i2c * DM + h2 * 512:
                               i2c * DM + (h2 + 1) * 512],
                        start=(i2c == 0), stop=(i2c == 31),
                        skip_group_check=True)
                p.step("tensor", md(), reads=[rxqT, r_wdtc[i2c]],
                       writes=[rpd], tag=f"d3m{t}.{i2c}.{h2}")
        p.step("vector",
               lambda e, pd=pd, xb=xbuf, t=t: e.scalar_tensor_tensor(
                   s_y[:], pd[:], s_cbd[:, t:t + 1], xb[:], ALU.mult,
                   ALU.add),
               reads=[rpd, r_cbd, rex_c[t % 4][0], rex_c[t % 4][1],
                      rxbuf], writes=[r_y], tag=f"d3y{t}")
        dy3, rdy3 = s_dummy[t % 2]
        p.step("scalar",
               lambda e, dy3=dy3: e.activation(dy3[:], s_y[:], AF.Square),
               reads=[r_y], writes=[rdy3], tag=f"d3sq{t}")
        p.step("vector",
               lambda e, dy3=dy3: e.tensor_reduce(s_smal[:, 0:1], dy3[:],
                                                  AX.X, ALU.add),
               reads=[rdy3], writes=[r_smal], tag=f"d3sq{t}r",
               sync_out=True)
        p.step("vector",
               lambda e: e.tensor_reduce(s_smal[:, 1:2], s_y[:], AX.X,
                                         ALU.add),
               reads=[r_y], writes=[r_smal], tag=f"d3su{t}", sync_out=True)
        p.step("vector",
               lambda e: e.tensor_scalar(s_smal[:, 6:7], s_smal[:, 1:2],
                                         1.0 / DM, None, ALU.mult),
               reads=[r_smal], writes=[r_smal], tag=f"d3mu{t}")
        p.step("vector",
               lambda e: e.tensor_scalar(s_smal[:, 2:3], s_smal[:, 0:1],
                                         1.0 / DM, None, ALU.mult),
               reads=[r_smal], writes=[r_smal], tag=f"d3ms{t}")
        p.step("vector",
               lambda e: e.tensor_mul(s_smal[:, 3:4], s_smal[:, 6:7],
                                      s_smal[:, 6:7]),
               reads=[r_smal], writes=[r_smal], tag=f"d3mm{t}")
        p.step("vector",
               lambda e: e.tensor_sub(s_smal[:, 4:5], s_smal[:, 2:3],
                                      s_smal[:, 3:4]),
               reads=[r_smal], writes=[r_smal], tag=f"d3v{t}",
               flush=lambda e: e.tensor_copy(s_vfl[:, 0:1], s_smal[:, 4:5]))
        p.step("scalar",
               lambda e: e.activation(s_smal[:, 5:6], s_smal[:, 4:5], AF.Ln,
                                      bias=s_cvec[:, 0:1]),
               reads=[r_smal, r_cvec], writes=[r_smal], tag=f"d3l{t}")
        p.step("scalar",
               lambda e: e.activation(s_smal[:, 7:8], s_smal[:, 5:6],
                                      AF.Exp, scale=-0.5),
               reads=[r_smal], writes=[r_smal], tag=f"d3r{t}",
               flush=lambda e: e.copy(s_sfl[:, 0:1], s_smal[:, 7:8]))
        obuf, robuf = s_o[t % 2]
        p.step("vector",
               lambda e, ob=obuf: e.tensor_scalar(
                   ob[:], s_y[:], s_smal[:, 6:7], s_smal[:, 7:8],
                   ALU.subtract, ALU.mult),
               reads=[r_y, r_smal], writes=[robuf], tag=f"d3n{t}")
        out_toks.append(dma(out_t[t * 128:(t + 1) * 128, :], obuf[:],
                            reads=[robuf], tag=f"out{t}"))

    d3_prep(0)
    d3_prep(1)
    ld_wdt_chunks()
    for tp in range(0, NT, 2):
        if tp + 2 < NT:
            d3_prep(tp + 2)
            d3_prep(tp + 3)
        d3_main(tp)
        d3_main(tp + 1)
    p.step("sync", lambda e: e.nop(), extra_deps=out_toks, tag="fin")
    stE.close()
    p.emit(nc, top)
    return nc, dbg, top


_CACHE = {}


def _get_program(stage="full"):
    if stage not in _CACHE:
        _CACHE[stage] = build_program(stage)
    return _CACHE[stage]


def make_in_maps(inputs):
    c = _host_consts()
    f32 = np.float32
    base = dict(
        ln1_g=np.ascontiguousarray(inputs["L0_attn_ln1_g"], f32),
        ln1_b=np.ascontiguousarray(inputs["L0_attn_ln1_b"], f32),
        wqkv=np.ascontiguousarray(inputs["L0_attn_Wqkv"], f32),
        bqkv=np.ascontiguousarray(inputs["L0_attn_bqkv"], f32),
        wo=np.ascontiguousarray(inputs["L0_attn_Wo"], f32),
        bo=np.ascontiguousarray(inputs["L0_attn_bo"], f32),
        ln2_g=np.ascontiguousarray(inputs["L0_attn_ln2_g"], f32),
        ln2_b=np.ascontiguousarray(inputs["L0_attn_ln2_b"], f32),
        w1=np.ascontiguousarray(inputs["L0_attn_W1"], f32),
        b1=np.ascontiguousarray(inputs["L0_attn_b1"], f32),
        w2=np.ascontiguousarray(inputs["L0_attn_W2"], f32),
        b2=np.ascontiguousarray(inputs["L0_attn_b2"], f32),
        wg=np.ascontiguousarray(inputs["L0_ffn_w_gate"], f32),
        wu=np.ascontiguousarray(inputs["L0_ffn_w_up"], f32),
        wd=np.ascontiguousarray(inputs["L0_ffn_w_down"], f32),
        cosF=c["cosF"], sinnF=c["sinnF"], trimask=c["trimask"],
        ident=c["ident"], identf=c["identf"], onesf=c["onesf"],
        ecat=c["ecat"], cvec=c["cvec"], sel=c["sel"], onescol=c["onescol"],
    )
    x = np.ascontiguousarray(inputs["L0_attn_x"], f32)
    maps = []
    for core in range(NCORES):
        m = dict(base)
        m["x"] = np.ascontiguousarray(x[core])
        maps.append(m)
    return maps


def kernel(**inputs):
    nc, dbg, top = _get_program("full")
    maps = make_in_maps(inputs)
    res = run_bass_kernel_spmd(nc, maps, core_ids=list(range(NCORES)))
    out = np.stack([res.results[i]["out"] for i in range(NCORES)], axis=0)
    return out.astype(np.float32)
